# version 32
# baseline (speedup 1.0000x reference)
import contextlib
import numpy as np
import ml_dtypes
import concourse.bass as bass
import concourse.mybir as mybir
from concourse.bass_utils import run_bass_kernel_spmd

F32 = mybir.dt.float32
BF16 = mybir.dt.bfloat16
AF = mybir.ActivationFunctionType
ALU = mybir.AluOpType
AX = mybir.AxisListType
BF16NP = ml_dtypes.bfloat16

T = 2048
D = 1024
DT = D // 128
TT = T // 128
NCORES = 8
NH = 16
DH = 64
LN_EPS = 1e-5

ENGINES = ("tensor", "vector", "scalar", "gpsimd", "sync")
DMA_QUEUES = ("sync", "gpsimd", "scalar")


class Plan:

    def __init__(self, nc):
        self.nc = nc
        self.ops = []
        self.by_engine = {e: [] for e in ENGINES}
        self.same_engine_wait = True
        self.dma_count = {q: 0 for q in DMA_QUEUES}

    def op(self, eng, fn, waits=(), tag=""):
        oid = len(self.ops)
        rec = dict(oid=oid, eng=eng, fn=fn, waits=[w for w in waits if w is not None],
                   dma=False, tag=tag, inc=False)
        self.ops.append(rec)
        self.by_engine[eng].append(rec)
        return oid

    def dma(self, queue, out, in_, waits=(), tag=""):
        oid = len(self.ops)

        def fn(e, out=out, in_=in_):
            return e.dma_start(out=out, in_=in_)
        self.dma_count[queue] += 16
        rec = dict(oid=oid, eng=queue, fn=fn, waits=[w for w in waits if w is not None],
                   dma=True, tag=tag, inc=True, dma_target=self.dma_count[queue])
        self.ops.append(rec)
        self.by_engine[queue].append(rec)
        return oid

    def finalize_and_emit(self, block, sems, dma_sems, final_waits):
        for rec in self.ops:
            for w in rec["waits"]:
                self.ops[w]["inc"] = True
        for e in ENGINES:
            c = 0
            for rec in self.by_engine[e]:
                if not rec["dma"] and rec["inc"]:
                    c += 1
                    rec["eng_target"] = c

        def wait_list(rec):
            agg = {}
            for w in rec["waits"]:
                p = self.ops[w]
                if p["dma"]:
                    key = ("dma", p["eng"])
                    val = p["dma_target"]
                else:
                    if p["eng"] == rec["eng"] and rec["eng"] == "tensor":
                        continue
                    if p["eng"] == rec["eng"] and not self.same_engine_wait:
                        continue
                    key = ("eng", p["eng"])
                    val = p["eng_target"]
                if agg.get(key, 0) < val:
                    agg[key] = val
            return agg

        def run_engine(e, ename):
            seen = {}
            for rec in self.by_engine[ename]:
                for (kind, src), val in wait_list(rec).items():
                    if seen.get((kind, src), -1) >= val:
                        continue
                    seen[(kind, src)] = val
                    sem = dma_sems[src] if kind == "dma" else sems[src]
                    e.wait_ge(sem, val)
                ins = rec["fn"](e)
                if rec["dma"]:
                    ins.then_inc(dma_sems[rec["eng"]], 16)
                elif rec["inc"]:
                    ins.then_inc(sems[rec["eng"]], 1)
            for (kind, src), val in final_waits.get(ename, {}).items():
                if seen.get((kind, src), -1) >= val:
                    continue
                sem = dma_sems[src] if kind == "dma" else sems[src]
                e.wait_ge(sem, val)

        block.sync(lambda e: run_engine(e, "sync"))
        block.vector(lambda e: run_engine(e, "vector"))
        block.scalar(lambda e: run_engine(e, "scalar"))
        block.gpsimd(lambda e: run_engine(e, "gpsimd"))
        block.tensor(lambda e: run_engine(e, "tensor"))

    def final_wait_spec(self, opids, engine="sync"):
        agg = {}
        for oid in opids:
            p = self.ops[oid]
            assert p["dma"]
            key = ("dma", p["eng"])
            val = p["dma_target"]
            if agg.get(key, 0) < val:
                agg[key] = val
        return {engine: agg}


class Buf:
    __slots__ = ("w", "r")

    def __init__(self):
        self.w = []
        self.r = []

    def read_waits(self):
        return list(self.w)

    def write_waits(self):
        return list(self.w) + list(self.r)

    def note_read(self, oid):
        self.r.append(oid)

    def note_write(self, oid):
        self.w = [oid]
        self.r = []

    def note_write_acc(self, oid):
        self.w.append(oid)
        self.r = []


def ceil_div(a, b):
    return (a + b - 1) // b


def host_constants():
    c = {}
    c["identity_f32"] = np.eye(128, dtype=np.float32)
    c["identity_bf16"] = np.eye(128).astype(BF16NP)
    c["ones_f32"] = np.ones((128, 128), dtype=np.float32)
    c["ones_bf16"] = np.ones((128, 128)).astype(BF16NP)
    m = (np.arange(128)[None, :] >= np.arange(128)[:, None]).astype(np.float32)
    c["tri_bf16"] = m.astype(BF16NP)
    fv = np.zeros((128, 8), np.float32)
    fv[:, 0] = LN_EPS
    fv[:, 1] = 1.0 / D
    fv[:, 2] = -0.5
    fv[:, 3] = 1.0
    fv[:, 4] = -1.0
    fv[:, 5] = 0.125
    c["fvals"] = fv
    rev = np.zeros((128, 128), np.float32)
    rev[np.arange(128), 127 - np.arange(128)] = 1.0
    c["rev_bf16"] = rev.astype(BF16NP)
    return c


class Ctx:
    def __init__(self, nc, plan, st):
        self.nc = nc
        self.plan = plan
        self.st = st
        self.sb = {}
        self.ps = {}
        self.bufs = {}
        self.fused = False
        self.arena_cols = 0
        self.cursor = 0
        self.sub_names = []
        self.offsets = {}
        self.prev_tokens = []
        self.max_cursor = 0

    def begin_sublayer(self, tag=""):
        if not self.fused:
            return
        toks = list(self.prev_tokens)
        for n in self.sub_names:
            b = self.bufs.get(n)
            if b:
                toks += b.w + b.r
            self.sb.pop(n, None)
            self.bufs.pop(n, None)
        self.prev_tokens = sorted(set(toks))
        self.sub_names = []
        self.cursor = 1024
        self.sub_tag = tag

    def salloc(self, name, shape, dt=F32):
        if not self.fused:
            return self.sbuf(name, shape, dt)
        p = shape[0]
        n = shape[1]
        f32n = n if dt == F32 else (n + 1) // 2
        f32n = (f32n + 15) & ~15
        assert self.cursor + f32n <= self.arena_cols,             f"arena overflow at {name} ({self.sub_tag}): {self.cursor}+{f32n}>{self.arena_cols}"
        self.offsets[name] = self.cursor
        h = self.sb["arena"]
        if dt == F32:
            ap = h[0:p, self.cursor:self.cursor + n]
        else:
            ap = h[0:p, self.cursor:self.cursor + (n + 1) // 2].bitcast(dt)[0:p, 0:n]
        self.cursor += f32n
        self.max_cursor = max(self.max_cursor, self.cursor)
        self.sb[name] = ap
        b = Buf()
        b.w = list(self.prev_tokens)
        self.bufs[name] = b
        self.sub_names.append(name)
        return ap

    def salloc_in(self, host, name, shape, dt=F32, off_f32=0):
        if not self.fused:
            return self.sbuf(name, shape, dt)
        base = self.offsets[host] + off_f32
        p, n = shape
        h = self.sb["arena"]
        if dt == F32:
            ap = h[0:p, base:base + n]
        else:
            ap = h[0:p, base:base + (n + 1) // 2].bitcast(dt)[0:p, 0:n]
        self.sb[name] = ap
        b = Buf()
        b.w = list(self.prev_tokens)
        self.bufs[name] = b
        self.sub_names.append(name)
        return ap

    def salloc_block(self, nelem):
        if not self.fused:
            return None
        base = self.cursor
        assert base + nelem <= self.arena_cols, f"arena overflow block {nelem}"
        self.cursor += nelem
        self.max_cursor = max(self.max_cursor, self.cursor)
        return base

    def carve_rel(self, base, name, off, nelem, dt=F32, p=128):
        if base is None:
            return self.carve("mb_arena", name, off, nelem, dt)[0:p, :]
        h = self.sb["arena"]
        if dt == F32:
            ap = h[0:p, base + off:base + off + nelem]
        else:
            ap = h[0:p, base + off:base + off + (nelem + 1) // 2].bitcast(dt)[0:p, 0:nelem]
        self.sb[name] = ap
        b = Buf()
        b.w = list(self.prev_tokens)
        self.bufs[name] = b
        self.sub_names.append(name)
        return ap

    def sbuf(self, name, shape, dt=F32):
        h = self.st.enter_context(self.nc.sbuf_tensor(name, list(shape), dt))
        self.sb[name] = h
        self.bufs[name] = Buf()
        return h

    def psum(self, name, shape, dt=F32):
        h = self.st.enter_context(self.nc.psum_tensor(name, list(shape), dt))
        self.ps[name] = h
        self.bufs[name] = Buf()
        return h

    def buf(self, name):
        return self.bufs.setdefault(name, Buf())

    def carve(self, arena, name, c0, nelem, dt=F32):
        h = self.sb[arena]
        if dt == F32:
            ap = h[:, c0:c0 + nelem]
        else:
            ap = h[:, c0:c0 + (nelem + 1) // 2].bitcast(dt)[:, 0:nelem]
        self.sb[name] = ap
        self.bufs[name] = Buf()
        return ap

    def arena_reset(self, names, handoff_from):
        toks = []
        for n in handoff_from:
            b = self.bufs.get(n)
            if b:
                toks += b.w + b.r
        for n in names:
            self.bufs[n].w = list(toks)



def emit_load_const(C, name, dram_ap, shape, dt, persist=False):
    if name in C.sb:
        h = C.sb[name]
    elif persist or not C.fused:
        h = C.sbuf(name, shape, dt)
    else:
        h = C.salloc(name, shape, dt)
    oid = C.plan.dma("sync", h[0:shape[0], :], dram_ap[:, :],
                     waits=C.buf(name).write_waits(), tag=f"load_{name}")
    C.buf(name).note_write(oid)
    return h


def emit_in_transform(C, x_tok_dram, xname="x_res"):
    P = C.plan
    x_res = C.sb[xname]
    stage = C.salloc("io_stage", (128, 2 * 1024), F32)
    pt = C.ps["pbig"]
    ident = C.sb["identity_f32"]
    bident = C.buf("identity_f32")
    bst = C.buf("io_stage")
    bpt = C.buf("pbig")
    bx = C.buf(xname)
    xv = x_tok_dram.rearrange("(a p) d -> p a d", p=128)
    for j in range(DT):
        for g in range(2):
            sl = (j * 2 + g) % 2
            ld = P.dma("sync", stage[:, sl * 1024:(sl + 1) * 1024]
                       .rearrange("p (a c) -> p a c", a=8),
                       xv[:, g * 8:(g + 1) * 8, j * 128:(j + 1) * 128],
                       waits=bst.write_waits(), tag="in_ld")
            bst.note_write_acc(ld)
            for ii in range(8):
                tp = P.op("tensor", lambda e, ii=ii, sl=sl: e.transpose(
                    pt[:, ii * 128:(ii + 1) * 128],
                    stage[:, sl * 1024 + ii * 128:sl * 1024 + (ii + 1) * 128],
                    ident[:, :]),
                    waits=[ld] + bident.read_waits() + bpt.write_waits(), tag="in_tp")
                bst.note_read(tp)
                bident.note_read(tp)
                bpt.note_write_acc(tp)
            ev = P.op("vector", lambda e, j=j, g=g: e.tensor_copy(
                x_res[:, j * T + g * 1024:j * T + (g + 1) * 1024], pt[:, :]),
                waits=bpt.w + bx.write_waits(), tag="in_ev")
            bpt.note_read(ev)
            bx.note_write_acc(ev)


def emit_out_transform(C, out_dram, xname="x_res"):
    P = C.plan
    x_res = C.sb[xname]
    stage = C.salloc("io_stage2", (128, 2 * D), F32)
    pt = C.ps["pbig"]
    ident = C.sb["identity_f32"]
    bident = C.buf("identity_f32")
    bst = C.buf("io_stage2")
    bpt = C.buf("pbig")
    bx = C.buf(xname)
    stores = []
    for i in range(TT):
        sl = i % 2
        for g in range(2):
            for jj in range(4):
                j = g * 4 + jj
                tp = P.op("tensor", lambda e, i=i, j=j, jj=jj: e.transpose(
                    pt[:, jj * 128:(jj + 1) * 128],
                    x_res[:, j * T + i * 128: j * T + (i + 1) * 128], ident[:, :]),
                    waits=bx.read_waits() + bident.read_waits() + bpt.write_waits(),
                    tag="out_tp")
                bx.note_read(tp)
                bident.note_read(tp)
                bpt.note_write_acc(tp)
            ev = P.op("vector", lambda e, sl=sl, g=g: e.tensor_copy(
                stage[:, sl * D + g * 512:sl * D + (g + 1) * 512], pt[:, 0:512]),
                waits=bpt.w + bst.write_waits(), tag="out_ev")
            bpt.note_read(ev)
            bst.note_write_acc(ev)
        sto = P.dma("sync", out_dram[i * 128:(i + 1) * 128, :], stage[:, sl * D:(sl + 1) * D],
                    waits=bst.w, tag="out_st")
        bst.note_read(sto)
        stores.append(sto)
    return stores



def emit_ln(C, src_name, dst_name):
    if "ln_sq" not in C.sb:
        C.salloc("ln_sq", (128, 512), F32)
    if "ln_rows" not in C.sb:
        C.salloc("ln_rows", (1, 1536), F32)
    """LayerNorm feature-major: src f32 [128, DT*T] -> dst bf16 [128, DT*T].
    Chunked by 512 tokens. Uses: ln_sq [128,512] f32, ln_rows [1, 3*T] f32,
    ln_ps psum [1, 1024] f32, ln_bc psum [128, 1024] f32, pa.. unused."""
    P = C.plan
    x = C.sb[src_name]
    y = C.sb[dst_name]
    bx = C.buf(src_name)
    by = C.buf(dst_name)
    ones = C.sb["ones_f32"]
    bones = C.buf("ones_f32")
    sq = C.sb["ln_sq"]; bsq = C.buf("ln_sq")
    rows = C.sb["ln_rows"]; brows = C.buf("ln_rows")
    psr_a = C.ps["pc"]; bps_a = C.buf("pc")
    psr_b = C.ps["pd"]; bps_b = C.buf("pd")
    pb = C.ps["pbig"]; bpb = C.buf("pbig")
    t1 = C.sb["ln_t1"]; bt1 = C.buf("ln_t1")

    for ch in range(4):
        s = ch * 512
        sqb = sq.bitcast(BF16)
        onesb_ = C.sb["ones_bf16"]
        bonesb_ = C.buf("ones_bf16")
        for j in range(DT):
            cv = P.op("scalar", lambda e, j=j, s=s, sqb=sqb: e.activation(
                sqb[:, 0:512], x[:, j * T + s:j * T + s + 512], AF.Copy),
                waits=bx.read_waits() + bsq.write_waits(), tag="ln_cast")
            bx.note_read(cv)
            bsq.note_write(cv)
            sqo = P.op("vector", lambda e, sqb=sqb: e.tensor_mul(
                sqb[:, 512:1024], sqb[:, 0:512], sqb[:, 0:512]),
                waits=bsq.w, tag="ln_sq")
            bsq.note_write_acc(sqo)
            mm = P.op("tensor", lambda e, j=j, sqb=sqb: e.matmul(
                psr_a[0:1, 0:512], onesb_[:, 0:1], sqb[:, 0:512],
                start=(j == 0), stop=(j == DT - 1)),
                waits=bsq.w + bonesb_.read_waits() +
                      (bps_a.write_waits() if j == 0 else []), tag="ln_sum")
            bsq.note_read(mm)
            bonesb_.note_read(mm)
            bps_a.note_write_acc(mm)
            mm2 = P.op("tensor", lambda e, j=j, sqb=sqb: e.matmul(
                psr_b[0:1, 0:512], onesb_[:, 0:1], sqb[:, 512:1024],
                start=(j == 0), stop=(j == DT - 1)),
                waits=bsq.w + bonesb_.read_waits() +
                      (bps_b.write_waits() if j == 0 else []), tag="ln_sumsq")
            bsq.note_read(mm2)
            bps_b.note_write_acc(mm2)
        R_MEAN, R_RSTD, R_M2 = 0, 512, 1024
        fv = C.sb["fvals"]
        bfv = C.buf("fvals")
        m_mean = P.op("scalar", lambda e, s=s: e.activation(
            rows[0:1, R_MEAN:R_MEAN + 512], psr_a[0:1, 0:512], AF.Copy, scale=fv[0:1, 1:2]),
            waits=bps_a.w + brows.write_waits() + bfv.read_waits(), tag="ln_mean")
        bps_a.note_read(m_mean)
        brows.note_write_acc(m_mean)
        m_var = P.op("scalar", lambda e, s=s: e.activation(
            rows[0:1, R_RSTD:R_RSTD + 512], psr_b[0:1, 0:512], AF.Copy, scale=fv[0:1, 1:2]),
            waits=bps_b.w + [m_mean], tag="ln_var0")
        bps_b.note_read(m_var)
        brows.note_write_acc(m_var)
        sqm = P.op("vector", lambda e, s=s: e.tensor_mul(
            rows[0:1, R_M2:R_M2 + 512], rows[0:1, R_MEAN:R_MEAN + 512],
            rows[0:1, R_MEAN:R_MEAN + 512]),
            waits=[m_mean], tag="ln_m2a")
        subv = P.op("vector", lambda e, s=s: e.tensor_sub(
            rows[0:1, R_RSTD:R_RSTD + 512], rows[0:1, R_RSTD:R_RSTD + 512],
            rows[0:1, R_M2:R_M2 + 512]),
            waits=[sqm, m_var], tag="ln_m2b")
        lnv = P.op("scalar", lambda e, s=s: e.activation(
            rows[0:1, R_RSTD:R_RSTD + 512], rows[0:1, R_RSTD:R_RSTD + 512],
            AF.Ln, bias=fv[0:1, 0:1]), waits=[subv], tag="ln_ln")
        rstd = P.op("scalar", lambda e, s=s: e.activation(
            rows[0:1, R_RSTD:R_RSTD + 512], rows[0:1, R_RSTD:R_RSTD + 512],
            AF.Exp, scale=fv[0:1, 2:3]), waits=[lnv], tag="ln_exp")
        m2 = P.op("vector", lambda e, s=s: e.tensor_mul(
            rows[0:1, R_M2:R_M2 + 512], rows[0:1, R_MEAN:R_MEAN + 512],
            rows[0:1, R_RSTD:R_RSTD + 512]),
            waits=[rstd, sqm, m_mean], tag="ln_m2")
        brows.note_write_acc(m2)
        brows.w = [m2, rstd]
        b1 = P.op("tensor", lambda e, s=s: e.matmul(
            pb[:, 0:512], ones[0:1, :], rows[0:1, R_RSTD:R_RSTD + 512],
            start=True, stop=True),
            waits=[rstd, m2] + bones.read_waits() + bpb.write_waits(), tag="ln_bc1")
        b2 = P.op("tensor", lambda e, s=s: e.matmul(
            pb[:, 512:1024], ones[0:1, :], rows[0:1, R_M2:R_M2 + 512],
            start=True, stop=True),
            waits=[m2] + bones.read_waits() + bpb.write_waits(), tag="ln_bc2")
        brows.note_read(b1)
        brows.note_read(b2)
        bones.note_read(b1)
        bpb.note_write(b1)
        bpb.w = [b1, b2]
        for j in range(DT):
            mu = P.op("vector", lambda e, j=j, s=s: e.tensor_mul(
                t1[:, 0:512], x[:, j * T + s:j * T + s + 512], pb[:, 0:512]),
                waits=bx.read_waits() + bpb.w + bt1.write_waits(), tag="ln_n1")
            bx.note_read(mu)
            bpb.note_read(mu)
            bt1.note_write(mu)
            su = P.op("vector", lambda e, j=j, s=s: e.tensor_sub(
                y[:, j * T + s:j * T + s + 512], t1[:, 0:512], pb[:, 512:1024]),
                waits=bt1.w + bpb.w + by.write_waits(), tag="ln_n2")
            bt1.note_read(su)
            bpb.note_read(su)
            by.note_write_acc(su)


class WeightLoader:

    def __init__(self, C, name, dram_ap, rows, cols, transpose, queue="gpsimd"):
        self.C, self.name = C, name
        self.dram = dram_ap
        self.rows, self.cols = rows, cols
        self.transpose = transpose
        self.queue = queue
        if transpose:
            self.KT, self.MT = ceil_div(cols, 128), ceil_div(rows, 128)
        else:
            self.KT, self.MT = ceil_div(rows, 128), ceil_div(cols, 128)

    def prefetch_all(self):
        C, P = self.C, self.C.plan
        if not self.transpose:
            h = C.salloc(f"w_{self.name}", (128, self.KT * self.cols), BF16)
            b = C.buf(f"w_{self.name}")
            for kt in range(self.KT):
                r0 = kt * 128
                r1 = min(self.rows, r0 + 128)
                ld = P.dma(self.queue, h[0:r1 - r0, kt * self.cols:(kt + 1) * self.cols],
                           self.dram[r0:r1, :], waits=b.write_waits(), tag=f"w_{self.name}_ld")
                b.note_write_acc(ld)
            self.h, self.b = h, b
        else:
            nm = f"w_{self.name}"
            if nm in C.sb:
                h = C.sb[nm]
                b = C.buf(nm)
            else:
                h = C.salloc(nm, (128, self.KT * self.rows), BF16)
                b = C.buf(nm)
            if "ws_nat" in C.sb:
                natname = "ws_nat"
            else:
                if "wt_nat" not in C.sb:
                    C.sbuf("wt_nat", (128, 2048), BF16)
                natname = "wt_nat"
            if "wt_ps" not in C.ps:
                C.psum("wt_ps", (128, 1024), BF16)
            nat = C.sb[natname]
            bn = C.buf(natname)
            pt = C.ps["wt_ps"]
            bp = C.buf("wt_ps")
            identb = C.sb["identity_bf16"]
            bib = C.buf("identity_bf16")
            ncolblk = ceil_div(self.cols, 128)
            for mt in range(self.MT):
                r0 = mt * 128
                r1 = min(self.rows, r0 + 128)
                nr = r1 - r0
                ld = P.dma(self.queue, nat[0:nr, 0:self.cols], self.dram[r0:r1, :],
                           waits=bn.write_waits(), tag=f"w_{self.name}_ld")
                bn.note_write(ld)
                for g0 in range(0, ncolblk, 8):
                    g1 = min(ncolblk, g0 + 8)
                    for kt in range(g0, g1):
                        c0 = kt * 128
                        c1 = min(self.cols, c0 + 128)
                        tp = P.op("tensor", lambda e, kt=kt, c0=c0, c1=c1, g0=g0, nr=nr: e.transpose(
                            pt[0:c1 - c0, (kt - g0) * 128:(kt - g0) * 128 + nr],
                            nat[0:nr, c0:c1], identb[0:nr, 0:nr]),
                            waits=bn.read_waits() + bib.read_waits() + bp.write_waits(),
                            tag=f"w_{self.name}_tp")
                        bn.note_read(tp)
                        bib.note_read(tp)
                        bp.note_write_acc(tp)
                    for kt in range(g0, g1):
                        c0 = kt * 128
                        c1 = min(self.cols, c0 + 128)
                        ev = P.op("vector", lambda e, kt=kt, g0=g0, c0=c0, c1=c1, r0=r0, r1=r1: e.tensor_copy(
                            h[0:c1 - c0, kt * self.rows + r0: kt * self.rows + r1],
                            pt[0:c1 - c0, (kt - g0) * 128:(kt - g0) * 128 + (r1 - r0)]),
                            waits=bp.w + b.write_waits(), tag=f"w_{self.name}_ev")
                        bp.note_read(ev)
                        b.note_write_acc(ev)
            self.h, self.b = h, b

    def tile(self, kt, mt):
        if self.transpose:
            k1 = min(self.cols - kt * 128, 128)
            m0 = mt * 128
            m1 = min(self.rows, m0 + 128)
            return self.h[0:k1, kt * self.rows + m0: kt * self.rows + m1], self.b
        else:
            k1 = min(self.rows - kt * 128, 128)
            m0 = mt * 128
            m1 = min(self.cols, m0 + 128)
            return self.h[0:k1, kt * self.cols + m0: kt * self.cols + m1], self.b


def emit_proj_fm(C, wl, xsrc_name, out_cb, out_rows, psname, tag="proj"):
    P = C.plan
    x = C.sb[xsrc_name]
    bx = C.buf(xsrc_name)
    KT = wl.KT
    MT = ceil_div(out_rows, 128)
    for mt in range(MT):
        for ch in range(4):
            pname = psname[(mt * 4 + ch) % len(psname)]
            ps = C.ps[pname]
            bps = C.buf(pname)
            s = ch * 512
            for kt in range(KT):
                tile, bw = wl.tile(kt, mt)
                mm = P.op("tensor", lambda e, tile=tile, kt=kt, s=s, ps=ps: e.matmul(
                    ps[0:128, 0:512], tile, x[:, kt * T + s: kt * T + s + 512],
                    start=(kt == 0), stop=(kt == KT - 1)),
                    waits=bw.read_waits() + bx.read_waits() +
                          (bps.write_waits() if kt == 0 else []), tag=tag + "_mm")
                bw.note_read(mm)
                bx.note_read(mm)
                bps.note_write_acc(mm)
            out_cb(mt, ch, ps, bps)


def emit_ffn_rwkv(C, wr, wk, wv):
    P = C.plan
    CH = 512
    xl = C.sb["x_ln"]; bxl = C.buf("x_ln")
    xr = C.sb["ffn_xr"]; bxr = C.buf("ffn_xr")
    xk = C.sb["ffn_xk"]; bxk = C.buf("ffn_xk")
    dd = C.sb["ffn_d"]; bd = C.buf("ffn_d")
    mu = C.sb["mu_rk"]; bmu = C.buf("mu_rk")
    r = C.sb["ffn_r"]; br = C.buf("ffn_r")
    k2 = C.sb["ffn_k2"]; bk2 = C.buf("ffn_k2")
    xres = C.sb["x_res"]; bxres = C.buf("x_res")
    t1 = C.sb["ln_t1"]; bt1 = C.buf("ln_t1")

    def chunk_proj(wl, src, bsrc, evac, tag):
        for mt in range(DT):
            pname = ("pa", "pb")[mt % 2]
            ps = C.ps[pname]
            bps = C.buf(pname)
            tiles = wl.mt_tiles(mt) if hasattr(wl, "mt_tiles") else None
            for kt in range(DT):
                tile, bw = tiles[kt] if tiles else wl.tile(kt, mt)
                mm = P.op("tensor", lambda e, tile=tile, kt=kt, ps=ps: e.matmul(
                    ps[0:128, 0:CH], tile, src[:, kt * CH:(kt + 1) * CH],
                    start=(kt == 0), stop=(kt == DT - 1)),
                    waits=bw.read_waits() + bsrc.read_waits() +
                          (bps.write_waits() if kt == 0 else []), tag=tag + "_mm")
                bw.note_read(mm)
                bsrc.note_read(mm)
                bps.note_write_acc(mm)
            evac(mt, ps, bps)

    for ch in range(T // CH):
        s = ch * CH
        for j in range(DT):
            if ch == 0:
                c0 = P.op("vector", lambda e, j=j: e.tensor_copy(dd[:, 0:1], xl[:, j * T:j * T + 1]),
                          waits=bxl.read_waits() + bd.write_waits(), tag="ffn_d0")
                bxl.note_read(c0)
                bd.note_write(c0)
                c1 = P.op("vector", lambda e, j=j: e.tensor_sub(
                    dd[:, 1:CH], xl[:, j * T + 1:j * T + CH], xl[:, j * T:j * T + CH - 1]),
                    waits=bxl.read_waits() + [c0], tag="ffn_d1")
                bxl.note_read(c1)
                bd.note_write_acc(c1)
            else:
                c1 = P.op("vector", lambda e, j=j, s=s: e.tensor_sub(
                    dd[:, 0:CH], xl[:, j * T + s:j * T + s + CH],
                    xl[:, j * T + s - 1:j * T + s + CH - 1]),
                    waits=bxl.read_waits() + bd.write_waits(), tag="ffn_d1")
                bxl.note_read(c1)
                bd.note_write(c1)
            for (dst, bdst, mucol, tg) in ((xr, bxr, j, "ffn_xr"), (xk, bxk, DT + j, "ffn_xk")):
                if ch == 0:
                    m0 = P.op("vector", lambda e, j=j, dst=dst, mucol=mucol: e.tensor_scalar(
                        dst[:, j * CH:j * CH + 1], dd[:, 0:1], mu[:, mucol:mucol + 1], None, ALU.mult),
                        waits=bd.w + bmu.read_waits() + bdst.write_waits(), tag=tg + "0")
                    bd.note_read(m0)
                    bmu.note_read(m0)
                    bdst.note_write(m0)
                    m1 = P.op("vector", lambda e, j=j, dst=dst, mucol=mucol: e.scalar_tensor_tensor(
                        dst[:, j * CH + 1:(j + 1) * CH], dd[:, 1:CH], mu[:, mucol:mucol + 1],
                        xl[:, j * T:j * T + CH - 1], ALU.mult, ALU.add),
                        waits=bd.w + bxl.read_waits() + bmu.read_waits() + [m0], tag=tg + "1")
                    bd.note_read(m1)
                    bxl.note_read(m1)
                    bmu.note_read(m1)
                    bdst.note_write_acc(m1)
                else:
                    m1 = P.op("vector", lambda e, j=j, s=s, dst=dst, mucol=mucol: e.scalar_tensor_tensor(
                        dst[:, j * CH:(j + 1) * CH], dd[:, 0:CH], mu[:, mucol:mucol + 1],
                        xl[:, j * T + s - 1:j * T + s + CH - 1], ALU.mult, ALU.add),
                        waits=bd.w + bxl.read_waits() + bmu.read_waits() + bdst.write_waits(), tag=tg + "1")
                    bd.note_read(m1)
                    bxl.note_read(m1)
                    bmu.note_read(m1)
                    bdst.note_write(m1)

        def evac_r(mt, ps, bps):
            ev = P.op("scalar", lambda e, mt=mt, ps=ps: e.activation(
                r[:, mt * CH:(mt + 1) * CH], ps[0:128, 0:CH], AF.Sigmoid),
                waits=bps.w + br.write_waits(), tag="ffn_r_ev")
            bps.note_read(ev)
            br.note_write_acc(ev)
        chunk_proj(wr, xr, bxr, evac_r, "ffn_r")

        def evac_k(mt, ps, bps):
            e1 = P.op("scalar", lambda e, ps=ps: e.activation(
                t1[:, 0:CH], ps[0:128, 0:CH], AF.Relu),
                waits=bps.w + bt1.write_waits(), tag="ffn_k_relu")
            bps.note_read(e1)
            bt1.note_write(e1)
            e2 = P.op("vector", lambda e, mt=mt: e.tensor_mul(
                k2[:, mt * CH:(mt + 1) * CH], t1[:, 0:CH], t1[:, 0:CH]),
                waits=bt1.w + bk2.write_waits(), tag="ffn_k_sq")
            bt1.note_read(e2)
            bk2.note_write_acc(e2)
        chunk_proj(wk, xk, bxk, evac_k, "ffn_k")

        def evac_v(mt, ps, bps):
            e1 = P.op("vector", lambda e, mt=mt, ps=ps: e.tensor_mul(
                t1[:, 512:1024], ps[0:128, 0:CH], r[:, mt * CH:(mt + 1) * CH]),
                waits=bps.w + br.read_waits() + bt1.write_waits(), tag="ffn_v_mul")
            bps.note_read(e1)
            br.note_read(e1)
            bt1.note_write(e1)
            e2 = P.op("vector", lambda e, mt=mt, s=s: e.tensor_add(
                xres[:, mt * T + s:mt * T + s + CH], xres[:, mt * T + s:mt * T + s + CH],
                t1[:, 512:1024]),
                waits=bt1.w + bxres.write_waits(), tag="ffn_v_add")
            bt1.note_read(e2)
            bxres.note_write_acc(e2)
        chunk_proj(wv, k2, bk2, evac_v, "ffn_v")


def emit_ffn_small(C, w_in, w_out):
    P = C.plan
    xl = C.sb["x_ln"]; bxl = C.buf("x_ln")
    xres = C.sb["x_res"]; bxres = C.buf("x_res")
    vec = C.sb["l1ffn_vec"]; bvec = C.buf("l1ffn_vec")
    bout = C.sb["l1ffn_bout"]; bbout = C.buf("l1ffn_bout")
    s_t = C.sb["l1ffn_s"]; bs = C.buf("l1ffn_s")
    h_t = C.sb["l1ffn_h"]; bh = C.buf("l1ffn_h")
    h_t2 = C.sb["l1ffn_h2"]; bh2 = C.buf("l1ffn_h2")
    g_t = C.sb["l1ffn_g"]; bg = C.buf("l1ffn_g")
    gb_t = C.sb["l1ffn_gb"]; bgb = C.buf("l1ffn_gb")
    t1 = C.sb["ln_t1"]; bt1 = C.buf("ln_t1")

    for ch in range(4):
        s = ch * 512
        ps = C.ps["pa"]
        bps = C.buf("pa")
        for kt in range(DT):
            tile, bw = w_in.tile(kt, 0)
            mm = P.op("tensor", lambda e, tile=tile, kt=kt, s=s, ps=ps: e.matmul(
                ps[0:64, 0:512], tile, xl[:, kt * T + s:kt * T + s + 512],
                start=(kt == 0), stop=(kt == DT - 1)),
                waits=bw.read_waits() + bxl.read_waits() +
                      (bps.write_waits() if kt == 0 else []), tag="l1f_s")
            bw.note_read(mm)
            bxl.note_read(mm)
            bps.note_write_acc(mm)
        ev = P.op("vector", lambda e, s=s, ps=ps: e.tensor_scalar_add(
            s_t[:, s:s + 512], ps[0:64, 0:512], vec[:, 48:49]),
            waits=bps.w + bvec.read_waits() + bs.write_waits(), tag="l1f_s_ev")
        bps.note_read(ev)
        bvec.note_read(ev)
        bs.note_write_acc(ev)
    for i in range(16):
        hc_t, bhc_t = (h_t, bh) if i % 2 == 0 else (h_t2, bh2)
        hi = P.op("scalar", lambda e, i=i, hc_t=hc_t: e.activation(
            hc_t[:, :], s_t[:, :], AF.Relu, scale=vec[:, i:i + 1], bias=vec[:, 16 + i:16 + i + 1]),
            waits=bs.w + bvec.read_waits() + bhc_t.write_waits(), tag="l1f_h")
        bs.note_read(hi)
        bvec.note_read(hi)
        bhc_t.note_write(hi)
        if i == 0:
            gi = P.op("vector", lambda e, hc_t=hc_t: e.tensor_scalar(
                g_t[:, :], hc_t[:, :], vec[:, 32:33], vec[:, 49:50], ALU.mult, ALU.add),
                waits=bhc_t.w + bvec.read_waits() + bg.write_waits(), tag="l1f_g0")
        else:
            gi = P.op("vector", lambda e, i=i, hc_t=hc_t: e.scalar_tensor_tensor(
                g_t[:, :], hc_t[:, :], vec[:, 32 + i:32 + i + 1], g_t[:, :], ALU.mult, ALU.add),
                waits=bhc_t.w + bvec.read_waits() + bg.write_waits() + bg.read_waits(), tag="l1f_g")
        bhc_t.note_read(gi)
        bvec.note_read(gi)
        bg.note_write(gi)
    gc = P.op("vector", lambda e: e.tensor_copy(gb_t[:, :], g_t[:, :]),
              waits=bg.w + bgb.write_waits(), tag="l1f_gc")
    bg.note_read(gc)
    bgb.note_write(gc)
    for mt in range(DT):
        for ch in range(4):
            s = ch * 512
            ps = C.ps["pb"]
            bps = C.buf("pb")
            tile, bw = w_out.tile(0, mt)
            mm = P.op("tensor", lambda e, tile=tile, s=s, ps=ps: e.matmul(
                ps[0:128, 0:512], tile, gb_t[:, s:s + 512], start=True, stop=True),
                waits=bw.read_waits() + bgb.read_waits() + bps.write_waits(), tag="l1f_o")
            bw.note_read(mm)
            bgb.note_read(mm)
            bps.note_write(mm)
            e1 = P.op("vector", lambda e, mt=mt, ps=ps: e.tensor_scalar_add(
                t1[:, 0:512], ps[0:128, 0:512], bout[:, mt:mt + 1]),
                waits=bps.w + bbout.read_waits() + bt1.write_waits(), tag="l1f_o_b")
            bps.note_read(e1)
            bbout.note_read(e1)
            bt1.note_write(e1)
            e2 = P.op("vector", lambda e, mt=mt, s=s: e.tensor_add(
                xres[:, mt * T + s:mt * T + s + 512], xres[:, mt * T + s:mt * T + s + 512],
                t1[:, 0:512]),
                waits=bt1.w + bxres.write_waits(), tag="l1f_o_add")
            bt1.note_read(e2)
            bxres.note_write_acc(e2)


_CONSTS = host_constants()

SUBLAYER_SEQ = ["l0_attn_a", "l0_ffn", "l0_attn_b", "l0_ffn2",
                "l1_mamba", "l1_ffn", "l1_attn_b", "l1_ffn2"]



def _ensure_ws(C, slotw):
    if "ws_nat" not in C.sb:
        C.ws_slotw = slotw
        C.sbuf("ws_nat", (128, 2 * slotw), BF16)
        C.sbuf("ws_wt", (128, 2 * slotw), BF16)
        C.sbuf("ws_vT", (128, 8 * 128), BF16)


class WStreamT:

    def __init__(self, C, name, dram, rows, cols):
        self.C, self.name, self.dram = C, name, dram
        self.rows, self.cols = rows, cols
        self.KT = cols // 128
        _ensure_ws(C, max(cols, 1024))
        assert cols <= C.ws_slotw
        self.slot = 0

    def mt_tiles(self, mt, row0=0):
        C, P = self.C, self.C.plan
        sl = self.slot
        self.slot ^= 1
        nat = C.sb["ws_nat"]
        bn = C.buf("ws_nat")
        wt = C.sb["ws_wt"]
        bwt = C.buf("ws_wt")
        pt = C.ps["wt_ps"]
        bp = C.buf("wt_ps")
        identb = C.sb["identity_bf16"]
        r0 = row0 + mt * 128
        SLOTW = self.C.ws_slotw
        ld = P.dma("gpsimd", nat[:, sl * SLOTW: sl * SLOTW + self.cols],
                   self.dram[r0:r0 + 128, :], waits=bn.write_waits(), tag=f"{self.name}_ld")
        bn.note_write_acc(ld)
        for g0 in range(0, self.KT, 8):
            g1 = min(self.KT, g0 + 8)
            for kt in range(g0, g1):
                tp = P.op("tensor", lambda e, kt=kt, sl=sl, g0=g0: e.transpose(
                    pt[:, (kt - g0) * 128:(kt - g0 + 1) * 128],
                    nat[:, sl * SLOTW + kt * 128: sl * SLOTW + (kt + 1) * 128], identb[:, :]),
                    waits=[ld] + C.buf("identity_bf16").read_waits() + bp.write_waits(),
                    tag=f"{self.name}_tp")
                bn.note_read(tp)
                bp.note_write_acc(tp)
            ev = P.op("vector", lambda e, sl=sl, g0=g0, g1=g1: e.tensor_copy(
                wt[:, sl * SLOTW + g0 * 128: sl * SLOTW + g1 * 128],
                pt[:, 0:(g1 - g0) * 128]),
                waits=bp.w + bwt.write_waits(), tag=f"{self.name}_ev")
            bp.note_read(ev)
            bwt.note_write_acc(ev)
        return [(wt[:, sl * SLOTW + kt * 128: sl * SLOTW + (kt + 1) * 128], bwt)
                for kt in range(self.KT)]

    def v_block(self, g, row0, width=512):
        C, P = self.C, self.C.plan
        nat = C.sb["ws_nat"]
        bn = C.buf("ws_nat")
        vt = C.sb["ws_vT"]
        bvt = C.buf("ws_vT")
        pt = C.ps["wt_ps"]
        bp = C.buf("wt_ps")
        identb = C.sb["identity_bf16"]
        for rb in range(width // 128):
            r0 = row0 + g * width + rb * 128
            sl = self.slot
            self.slot ^= 1
            SLOTW = self.C.ws_slotw
            ld = P.dma("gpsimd", nat[:, sl * SLOTW: sl * SLOTW + self.cols],
                       self.dram[r0:r0 + 128, :], waits=bn.write_waits(), tag=f"{self.name}_vld")
            bn.note_write_acc(ld)
            for kt in range(self.KT):
                tp = P.op("tensor", lambda e, kt=kt, sl=sl: e.transpose(
                    pt[:, kt * 128:(kt + 1) * 128],
                    nat[:, sl * SLOTW + kt * 128: sl * SLOTW + (kt + 1) * 128], identb[:, :]),
                    waits=[ld] + bp.write_waits(), tag=f"{self.name}_vtp")
                bn.note_read(tp)
                bp.note_write_acc(tp)
            ev = P.op("vector", lambda e, rb=rb, width=width: e.tensor_copy(
                vt.rearrange("p (k w) -> p k w", w=width)[:, :, rb * 128:(rb + 1) * 128],
                pt.rearrange("p (k w) -> p k w", w=128)[:, 0:self.KT, :]),
                waits=bp.w + bvt.write_waits(), tag=f"{self.name}_vev")
            bp.note_read(ev)
            bvt.note_write_acc(ev)
        return vt, bvt

    def v_slice(self, vt, kt, width=512):
        return vt[:, kt * width:(kt + 1) * width]


class WStreamN:

    def __init__(self, C, name, dram, cin, cout):
        self.C, self.name, self.dram = C, name, dram
        self.cin, self.cout = cin, cout
        self.KT = cin // 128
        _ensure_ws(C, max(cin, 1024))
        assert cin <= C.ws_slotw
        self.slot = 0

    def mt_tiles(self, mt, row0=0):
        C, P = self.C, self.C.plan
        sl = self.slot
        self.slot ^= 1
        SLOTW = self.C.ws_slotw
        wt = C.sb["ws_wt"]
        bwt = C.buf("ws_wt")
        c0 = row0 + mt * 128
        for kt in range(self.KT):
            ld = P.dma("gpsimd", wt[:, sl * SLOTW + kt * 128: sl * SLOTW + (kt + 1) * 128],
                       self.dram[kt * 128:(kt + 1) * 128, c0:c0 + 128],
                       waits=bwt.write_waits() if kt == 0 else [], tag=f"{self.name}_ld")
            bwt.note_write_acc(ld)
        return [(wt[:, sl * SLOTW + kt * 128: sl * SLOTW + (kt + 1) * 128], bwt)
                for kt in range(self.KT)]

    def v_block(self, g, row0, width=512):
        C, P = self.C, self.C.plan
        vt = C.sb["ws_vT"]
        bvt = C.buf("ws_vT")
        c0 = row0 + g * width
        for kt in range(self.KT):
            ld = P.dma("gpsimd", vt[:, kt * width:(kt + 1) * width],
                       self.dram[kt * 128:(kt + 1) * 128, c0:c0 + width],
                       waits=bvt.write_waits() if kt == 0 else [], tag=f"{self.name}_vld")
            bvt.note_write_acc(ld)
        return vt, bvt

    def v_slice(self, vt, kt, width=512):
        return vt[:, kt * width:(kt + 1) * width]


def emit_attn(C, kind, streams, has_bias, causal, rope=False, relbias=False):
    P = C.plan
    xl = C.sb["x_ln"]; bxl = C.buf("x_ln")
    xres = C.sb["x_res"]; bxres = C.buf("x_res")
    v = C.sb["att_v"]; bv = C.buf("att_v")
    o = C.sb["att_o"]; bo = C.buf("att_o")
    pbuf = C.sb["att_p"]; bp = C.buf("att_p")
    for _nm in ("att_p0", "att_p1", "att_p2", "att_p3", "att_p4", "att_p5"):
        C.buf(_nm).w = sorted(set(C.buf(_nm).w + bp.write_waits()))
    den = C.sb["att_den"]; bden = C.buf("att_den")
    rbs = C.sb["att_rb"]; brb = C.buf("att_rb")
    onesf = C.sb["ones_f32"]; bonesf = C.buf("ones_f32")
    onesb = C.sb["ones_bf16"]; bonesb = C.buf("ones_bf16")
    tri = C.sb["tri_bf16"]; btri = C.buf("tri_bf16")
    fv = C.sb["fvals"]; bfv = C.buf("fvals")
    qk = [C.sb["att_qk0"], C.sb["att_qk1"]]
    bqk = [C.buf("att_qk0"), C.buf("att_qk1")]

    (s_qk, q_row0, k_row0) = streams["qk"]
    (s_v, v_row0) = streams["v"]
    (s_o,) = streams["o"]
    relbias_fn = streams.get("relbias_fn")
    if relbias:
        d2 = C.sb["rb_d2"]
        bd2 = C.buf("rb_d2")
    if has_bias:
        qkb = C.sb["att_qkb"]; bqkb = C.buf("att_qkb")
    if rope:
        cosb = C.sb["rope_cos"]; bcos = C.buf("rope_cos")
        sinb = C.sb["rope_sin"]; bsin = C.buf("rope_sin")
        rmat = C.sb["rope_R"]; brm = C.buf("rope_R")
        t1 = C.sb["ln_t1"]; bt1 = C.buf("ln_t1")

    ms = P.op("vector", lambda e: e.memset(v[:, :], 1.0), waits=bv.write_waits(), tag="att_v_ones")
    bv.note_write(ms)

    VSL = TT * 130
    for hpi in range(DT):
        sl = hpi % 2
        dst, bdst = qk[sl], bqk[sl]
        PS4 = ("pa", "pb", "pc", "pd")
        vtiles = s_v.mt_tiles(hpi, row0=v_row0)
        for kt in range(DT):
            tile, bw = vtiles[kt]
            for ch in range(4):
                s = ch * 512
                ps = C.ps[PS4[ch]]
                bps = C.buf(PS4[ch])
                mm = P.op("tensor", lambda e, tile=tile, kt=kt, s=s, ps=ps: e.matmul(
                    ps[0:128, 0:512], tile, xl[:, kt * T + s: kt * T + s + 512],
                    start=(kt == 0), stop=(kt == DT - 1)),
                    waits=bw.read_waits() + bxl.read_waits() +
                          (bps.write_waits() if kt == 0 else []), tag="att_vf_mm")
                bw.note_read(mm)
                bxl.note_read(mm)
                bps.note_write_acc(mm)
        for ch in range(4):
            s = ch * 512
            ps = C.ps[PS4[ch]]
            bps = C.buf(PS4[ch])
            if has_bias:
                ev = P.op("vector", lambda e, s=s, ps=ps, hpi=hpi: e.tensor_scalar_add(
                    dst[:, s:s + 512], ps[0:128, 0:512], qkb[:, 16 + hpi:16 + hpi + 1]),
                    waits=bps.w + bqkb.read_waits() + bdst.write_waits(), tag="att_vf_ev")
                bqkb.note_read(ev)
            else:
                ev = P.op("vector", lambda e, s=s, ps=ps: e.tensor_copy(
                    dst[:, s:s + 512], ps[0:128, 0:512]),
                    waits=bps.w + bdst.write_waits(), tag="att_vf_ev")
            bps.note_read(ev)
            bdst.note_write_acc(ev)
        identb_ = C.sb["identity_bf16"]
        wtp = C.ps["wt_ps"]
        bwtp = C.buf("wt_ps")
        for g in range(2):
            for ii in range(8):
                i = g * 8 + ii
                tp = P.op("tensor", lambda e, i=i, ii=ii: e.transpose(
                    wtp[:, ii * 128:(ii + 1) * 128], dst[:, i * 128:(i + 1) * 128],
                    identb_[:, :]),
                    waits=bdst.read_waits() + bdst.w + C.buf("identity_bf16").read_waits() +
                          bwtp.write_waits(), tag="att_vt")
                bdst.note_read(tp)
                C.buf("identity_bf16").note_read(tp)
                bwtp.note_write_acc(tp)
            ev = P.op("vector", lambda e, g=g, sl=sl: e.tensor_copy(
                v[:, sl * VSL + g * 8 * 130: sl * VSL + (g + 1) * 8 * 130]
                    .rearrange("p (a h c) -> p a h c", h=2, c=65)[:, :, :, 0:64],
                wtp[:, 0:1024].rearrange("p (a h c) -> p a h c", h=2, c=64)),
                waits=bwtp.w + bv.write_waits(), tag="att_v_ev")
            bwtp.note_read(ev)
            bv.note_write_acc(ev)
        PS4 = ("pa", "pb", "pc", "pd")
        for half, row0, bcol0 in ((0, q_row0, 0), (1, k_row0, DT)):
            tiles = s_qk.mt_tiles(hpi, row0=row0)
            for kt in range(DT):
                tile, bw = tiles[kt]
                for ch in range(4):
                    s = ch * 512
                    ps = C.ps[PS4[ch]]
                    bps = C.buf(PS4[ch])
                    mm = P.op("tensor", lambda e, tile=tile, kt=kt, s=s, ps=ps: e.matmul(
                        ps[0:128, 0:512], tile, xl[:, kt * T + s: kt * T + s + 512],
                        start=(kt == 0), stop=(kt == DT - 1)),
                        waits=bw.read_waits() + bxl.read_waits() +
                              (bps.write_waits() if kt == 0 else []), tag="att_qk_mm")
                    bw.note_read(mm)
                    bxl.note_read(mm)
                    bps.note_write_acc(mm)
            for ch in range(4):
                s = ch * 512
                ps = C.ps[PS4[ch]]
                bps = C.buf(PS4[ch])
                col = half * T + s
                if has_bias:
                    ev = P.op("vector", lambda e, dst=dst, col=col, ps=ps, bc=bcol0, hpi=hpi:
                        e.tensor_scalar_add(dst[:, col:col + 512], ps[0:128, 0:512],
                                            qkb[:, bc + hpi:bc + hpi + 1]),
                        waits=bps.w + bqkb.read_waits() + bdst.write_waits(), tag="att_qk_ev")
                    bqkb.note_read(ev)
                else:
                    ev = P.op("vector", lambda e, dst=dst, col=col, ps=ps: e.tensor_copy(
                        dst[:, col:col + 512], ps[0:128, 0:512]),
                        waits=bps.w + bdst.write_waits(), tag="att_qk_ev")
                bps.note_read(ev)
                bdst.note_write_acc(ev)
        if rope:
            for half in range(2):
                for ch in range(4):
                    s = half * T + ch * 512
                    rs = ch * 512
                    ps = C.ps[("pa", "pb")[ch % 2]]
                    bps = C.buf(("pa", "pb")[ch % 2])
                    mm = P.op("tensor", lambda e, dst=dst, s=s, ps=ps: e.matmul(
                        ps[0:128, 0:512], rmat[:, :], dst[:, s:s + 512],
                        start=True, stop=True),
                        waits=brm.read_waits() + bdst.read_waits() + bps.write_waits(), tag="rope_mm")
                    brm.note_read(mm)
                    bdst.note_read(mm)
                    bps.note_write(mm)
                    e1 = P.op("vector", lambda e, rs=rs, ps=ps: e.tensor_mul(
                        t1[:, 0:512], ps[0:128, 0:512], sinb[:, rs:rs + 512]),
                        waits=bps.w + bsin.read_waits() + bt1.write_waits(), tag="rope_m1")
                    bps.note_read(e1)
                    bsin.note_read(e1)
                    bt1.note_write(e1)
                    e2 = P.op("vector", lambda e, dst=dst, s=s, rs=rs: e.tensor_mul(
                        t1[:, 512:1024], dst[:, s:s + 512], cosb[:, rs:rs + 512]),
                        waits=bdst.read_waits() + bcos.read_waits() + [e1], tag="rope_m2")
                    bdst.note_read(e2)
                    bcos.note_read(e2)
                    bt1.note_write_acc(e2)
                    e3 = P.op("vector", lambda e, dst=dst, s=s: e.tensor_add(
                        dst[:, s:s + 512], t1[:, 0:512], t1[:, 512:1024]),
                        waits=bt1.w + bdst.write_waits(), tag="rope_m3")
                    bt1.note_read(e3)
                    bdst.note_write_acc(e3)

        if relbias:
            relbias_fn(2 * hpi + 0, 0)
            relbias_fn(2 * hpi + 1, 1)
        bp_slots = [C.buf(f"att_p{i}") for i in range(6)]
        AVPS = ("pa", "pb")
        SPS = ("pc", "pd")
        for qc in range(4):
            ki_hi = (4 * qc + 4) if causal else TT

            def emit_S(ki, hh):
                m = ki - 4 * qc
                v0 = m * 128 if (causal and m >= 0) else 0
                hoff = 64 * hh
                ps_s = C.ps[SPS[hh]]
                bss = C.buf(SPS[hh])
                mm = P.op("tensor", lambda e, dst=dst, hoff=hoff, ki=ki, qc=qc, v0=v0, ps_s=ps_s:
                    e.matmul(
                        ps_s[0:128, v0:512],
                        dst[hoff:hoff + 64, T + ki * 128:T + (ki + 1) * 128],
                        dst[hoff:hoff + 64, qc * 512 + v0:(qc + 1) * 512],
                        start=True, stop=True),
                    waits=bdst.read_waits() + bss.write_waits(), tag="att_s")
                bdst.note_read(mm)
                bss.note_write(mm)

            def emit_P(ki, hh):
                m = ki - 4 * qc
                v0 = m * 128 if (causal and m >= 0) else 0
                ps_s = C.ps[SPS[hh]]
                bss = C.buf(SPS[hh])
                slot = (ki % 3) + 3 * hh
                bsl = bp_slots[slot]
                ex = P.op("scalar", lambda e, slot=slot, v0=v0, ps_s=ps_s: e.activation(
                    pbuf[:, slot * 512 + v0:(slot + 1) * 512], ps_s[0:128, v0:512], AF.Exp,
                    scale=fv[:, 5:6]),
                    waits=bss.w + bsl.write_waits() + bfv.read_waits(), tag="att_exp")
                bss.note_read(ex)
                bsl.note_write(ex)
                if causal and 0 <= m < 4:
                    tm = P.op("vector", lambda e, slot=slot, v0=v0: e.tensor_mul(
                        pbuf[:, slot * 512 + v0:slot * 512 + v0 + 128],
                        pbuf[:, slot * 512 + v0:slot * 512 + v0 + 128], tri[:, :]),
                        waits=bsl.w + btri.read_waits(), tag="att_tri")
                    btri.note_read(tm)
                    bsl.note_write_acc(tm)
                if relbias and (ki * 128 + 127) - (qc * 512) >= 1026:
                    c0 = qc * 512 + 2047 - (ki * 128 + 127)
                    wm = P.op("vector", lambda e, slot=slot, c0=c0, hh=hh: e.tensor_mul(
                        pbuf[:, slot * 512:(slot + 1) * 512],
                        pbuf[:, slot * 512:(slot + 1) * 512],
                        d2[:, hh * 2048 + c0:hh * 2048 + c0 + 512]),
                        waits=bsl.w + bd2.read_waits(), tag="att_wedge")
                    bd2.note_read(wm)
                    bsl.note_write_acc(wm)

            def emit_AV(ki, hh):
                m = ki - 4 * qc
                v0 = m * 128 if (causal and m >= 0) else 0
                slot = (ki % 3) + 3 * hh
                bsl = bp_slots[slot]
                ps_av = C.ps[AVPS[hh]]
                bav = C.buf(AVPS[hh])
                av = P.op("tensor", lambda e, sl=sl, hh=hh, ki=ki, v0=v0, slot=slot, ps_av=ps_av, last=(ki == ki_hi - 1):
                    e.matmul(
                        ps_av[0:65, v0:512],
                        v[:, sl * VSL + ki * 130 + hh * 65: sl * VSL + ki * 130 + (hh + 1) * 65],
                        pbuf[:, slot * 512 + v0:(slot + 1) * 512],
                        start=(ki == 0), stop=last),
                    waits=bv.read_waits() + bsl.w +
                          (bav.write_waits() if ki == 0 else []), tag="att_av")
                bv.note_read(av)
                bsl.note_read(av)
                bav.note_write_acc(av)

            for hh in range(2):
                emit_S(0, hh)
            for hh in range(2):
                emit_P(0, hh)
            for ki in range(ki_hi):
                if ki + 1 < ki_hi:
                    emit_S(ki + 1, 0)
                    emit_S(ki + 1, 1)
                emit_AV(ki, 0)
                emit_AV(ki, 1)
                if ki + 1 < ki_hi:
                    emit_P(ki + 1, 0)
                    emit_P(ki + 1, 1)
            for hh in range(2):
                hoff = 64 * hh
                ps_av = C.ps[AVPS[hh]]
                bav = C.buf(AVPS[hh])
                d1 = P.op("scalar", lambda e, ps_av=ps_av: e.activation(
                    den[64:65, 0:512], ps_av[64:65, 0:512], AF.Ln),
                    waits=bav.w + bden.write_waits(), tag="att_den")
                bav.note_read(d1)
                bden.note_write(d1)
                denb = den.bitcast(BF16)
                d2o = P.op("scalar", lambda e, denb=denb: e.activation(
                    denb[64:65, 1024:1536], den[64:65, 0:512], AF.Exp, scale=fv[64:65, 4:5]),
                    waits=bden.w + bfv.read_waits(), tag="att_rec")
                bfv.note_read(d2o)
                bden.note_write_acc(d2o)
                rb = P.op("tensor", lambda e, denb=denb: e.matmul(
                    C.ps["pbig"][0:64, 0:512], onesb[64:65, 0:64], denb[64:65, 1024:1536],
                    start=True, stop=True),
                    waits=bden.w + bonesb.read_waits() + C.buf("pbig").write_waits(), tag="att_rb_mm")
                bden.note_read(rb)
                bonesb.note_read(rb)
                C.buf("pbig").note_write(rb)
                rc = P.op("vector", lambda e: e.tensor_copy(rbs[0:64, 0:512], C.ps["pbig"][0:64, 0:512]),
                          waits=C.buf("pbig").w + brb.write_waits(), tag="att_rb_cp")
                C.buf("pbig").note_read(rc)
                brb.note_write(rc)
                oe = P.op("vector", lambda e, hoff=hoff, hpi=hpi, qc=qc, ps_av=ps_av: e.tensor_mul(
                    o[hoff:hoff + 64, hpi * T + qc * 512:hpi * T + (qc + 1) * 512],
                    ps_av[0:64, 0:512], rbs[0:64, 0:512]),
                    waits=bav.w + brb.w + bo.write_waits(), tag="att_o_ev")
                bav.note_read(oe)
                brb.note_read(oe)
                bo.note_write_acc(oe)

    ob = C.sb["att_ob"]; bob = C.buf("att_ob")
    PS4 = ("pa", "pb", "pc", "pd")
    for mt in range(DT):
        tiles = s_o.mt_tiles(mt)
        for kt in range(DT):
            tile, bw = tiles[kt]
            for ch in range(4):
                s = ch * 512
                ps = C.ps[PS4[ch]]
                bps = C.buf(PS4[ch])
                mm = P.op("tensor", lambda e, tile=tile, kt=kt, s=s, ps=ps: e.matmul(
                    ps[0:128, 0:512], tile, o[:, kt * T + s: kt * T + s + 512],
                    start=(kt == 0), stop=(kt == DT - 1)),
                    waits=bw.read_waits() + bo.read_waits() +
                          (bps.write_waits() if kt == 0 else []), tag="att_o_mm")
                bw.note_read(mm)
                bo.note_read(mm)
                bps.note_write_acc(mm)
        for ch in range(4):
            s = ch * 512
            ps = C.ps[PS4[ch]]
            bps = C.buf(PS4[ch])
            ev = P.op("vector", lambda e, mt=mt, s=s, ps=ps: e.scalar_tensor_tensor(
                xres[:, mt * T + s:mt * T + s + 512], ps[0:128, 0:512], ob[:, mt:mt + 1],
                xres[:, mt * T + s:mt * T + s + 512], ALU.add, ALU.add),
                waits=bps.w + bob.read_waits() + bxres.write_waits(), tag="att_res")
            bps.note_read(ev)
            bob.note_read(ev)
            bxres.note_write_acc(ev)


def _alloc_attn_common(C, has_bias):
    C.salloc("att_qk0", (128, 2 * T), BF16)
    C.salloc("att_qk1", (128, 2 * T), BF16)
    C.salloc("att_v", (128, 2 * TT * 130), BF16)
    C.salloc("att_o", (128, DT * T), BF16)
    C.salloc("att_p", (128, 6 * 512), BF16)
    C.salloc("att_den", (65, 768), F32)
    C.salloc("att_rb", (64, 512), F32)
    if "wt_ps" not in C.ps:
        C.psum("wt_ps", (128, 1024), BF16)


def _load_bias_cols(C, din, name, vec):
    arr = np.ascontiguousarray(vec.reshape(DT, 128).T, dtype=np.float32)
    d = din(name + "_in", arr)
    return emit_load_const(C, name, d, arr.shape, F32)


def _load_vbias(C, din, name, vec):
    arr = np.ascontiguousarray(vec.reshape(1, 1024), dtype=np.float32)
    d = din(name + "_in", arr)
    h = C.salloc(name, (1, 1024), BF16)
    oid = C.plan.dma("gpsimd", h[0:1, :], d[0:1, :],
                     waits=C.buf(name).write_waits(), tag=f"load_{name}")
    C.buf(name).note_write(oid)
    return h


def prep_attn_l0a(C, W, din):
    _alloc_attn_common(C, True)
    b = W["L0_attn_a_b_qkv"]
    qkb = np.concatenate([b[0:1024].reshape(DT, 128).T, b[1024:2048].reshape(DT, 128).T,
                          b[2048:3072].reshape(DT, 128).T], axis=1)
    dq = din("att_qkb_in", np.ascontiguousarray(qkb, dtype=np.float32))
    emit_load_const(C, "att_qkb", dq, qkb.shape, F32)
    _load_bias_cols(C, din, "att_ob", W["L0_attn_a_b_o"])
    s_qkv = WStreamT(C, "wqkv", din("W_qkv", W["L0_attn_a_W_qkv"]), 3072, 1024)
    s_o = WStreamT(C, "wo", din("W_o", W["L0_attn_a_W_o"]), 1024, 1024)
    streams = {"qk": (s_qkv, 0, 1024), "v": (s_qkv, 2048), "o": (s_o,)}
    return lambda: emit_attn(C, "l0a", streams, has_bias=True, causal=True)


def prep_attn_l1b(C, W, din):
    _alloc_attn_common(C, False)
    zeros = np.zeros((128, DT), np.float32)
    dz = din("att_ob_in", zeros)
    emit_load_const(C, "att_ob", dz, zeros.shape, F32)
    pos = np.arange(T, dtype=np.float64)
    didx = np.arange(128) % 64
    inv = 1.0 / (10000.0 ** ((didx % 32) * 2.0 / 64.0))
    ang = inv[:, None] * pos[None, :]
    cosb = np.cos(ang).astype(np.float32)
    sinb = np.sin(ang).astype(np.float32)
    nc_din_bf16(C, din, "rope_cos", cosb)
    nc_din_bf16(C, din, "rope_sin", sinb)
    R = np.zeros((128, 128), np.float32)
    for half in range(2):
        b0 = 64 * half
        for m in range(64):
            if m < 32:
                R[b0 + m + 32, b0 + m] = -1.0
            else:
                R[b0 + m - 32, b0 + m] = 1.0
    dr = nc_din_bf16(C, din, "rope_R", R)
    s_qkv = WStreamT(C, "w1", din("W1", W["L1_attn_b_W1"]), 3072, 1024)
    s_o = WStreamT(C, "w2", din("W2", W["L1_attn_b_W2"]), 1024, 1024)
    streams = {"qk": (s_qkv, 0, 1024), "v": (s_qkv, 2048), "o": (s_o,)}
    return lambda: emit_attn(C, "l1b", streams, has_bias=False, causal=True, rope=True)


def nc_din_bf16(C, din, name, arr_f32):
    arr = np.ascontiguousarray(arr_f32, dtype=np.float32)
    d = din(name + "_in", arr)
    h = C.salloc(name, arr.shape, BF16)
    oid = C.plan.dma("gpsimd", h[0:arr.shape[0], :], d[:, :],
                     waits=C.buf(name).write_waits(), tag=f"load_{name}")
    C.buf(name).note_write(oid)
    return h


class QKDispatch:

    def __init__(self, sq, sk):
        self.sq, self.sk = sq, sk

    def mt_tiles(self, mt, row0=0):
        return (self.sq if row0 == 0 else self.sk).mt_tiles(mt, row0=0)


def prep_attn_l0b(C, W, din):
    _alloc_attn_common(C, True)
    qkb = np.concatenate([W["L0_attn_b_bq"].reshape(DT, 128).T,
                          W["L0_attn_b_bk"].reshape(DT, 128).T,
                          W["L0_attn_b_bv"].reshape(DT, 128).T], axis=1)
    dq = din("att_qkb_in", np.ascontiguousarray(qkb, dtype=np.float32))
    emit_load_const(C, "att_qkb", dq, qkb.shape, F32)
    _load_bias_cols(C, din, "att_ob", W["L0_attn_b_bo"])
    sq = WStreamN(C, "wq", din("Wq", W["L0_attn_b_Wq"]), 1024, 1024)
    sk = WStreamN(C, "wk", din("Wk", W["L0_attn_b_Wk"]), 1024, 1024)
    sv = WStreamN(C, "wv", din("Wv", W["L0_attn_b_Wv"]), 1024, 1024)
    so = WStreamN(C, "wo2", din("Wo", W["L0_attn_b_Wo"]), 1024, 1024)
    P = C.plan
    nc = C.nc
    rt_d = din("rel_table", W["L0_attn_b_rel_table"])
    eg = C.salloc_in("att_o", "rb_eg", (16, 2176), F32, 0)
    beg = C.buf("rb_eg")
    rt = C.salloc_in("att_o", "rb_rt", (128, 8 * 16), F32, 2176)
    brt = C.buf("rb_rt")
    C.salloc("rb_d2", (128, 2 * 2048), BF16)
    egpad_d = nc.dram_tensor("rb_egpad_d", [16, 2175], BF16, kind="Internal")
    egpad_ap = egpad_d.ap()
    ident = C.sb["identity_f32"]
    pbig = C.ps["pbig"]
    bpb = C.buf("pbig")

    def emit_rel_pre():
        ms = P.op("vector", lambda e: e.memset(eg[:, :], 1.0), waits=beg.write_waits(), tag="rb_ms")
        beg.note_write(ms)
        for t in range(8):
            r0 = t * 128
            r1 = min(1023, r0 + 128)
            nr = r1 - r0
            ld = P.dma("sync", rt[0:nr, t * 16:(t + 1) * 16], rt_d[r0:r1, :],
                       waits=brt.write_waits(), tag="rb_ld")
            brt.note_write_acc(ld)
            tp = P.op("tensor", lambda e, t=t, nr=nr: e.transpose(
                pbig[0:16, t * 128:t * 128 + nr], rt[0:nr, t * 16:(t + 1) * 16],
                ident[0:nr, 0:nr]),
                waits=[ld] + C.buf("identity_f32").read_waits() + bpb.write_waits(), tag="rb_tp")
            brt.note_read(tp)
            bpb.note_write_acc(tp)
        ev = P.op("vector", lambda e: e.tensor_copy(eg[0:16, 0:1023], pbig[0:16, 0:1023]),
                  waits=bpb.w + [ms], tag="rb_ev")
        bpb.note_read(ev)
        beg.note_write_acc(ev)
        ch = P.op("vector", lambda e: e.tensor_copy(eg[0:16, 2175:2176], eg[0:16, 1022:1023]),
                  waits=beg.w, tag="rb_ch")
        beg.note_read(ch)
        beg.note_write_acc(ch)
        sb = P.op("vector", lambda e: e.tensor_single_scalar(
            eg[0:16, 0:1023], eg[0:16, 0:1023], eg[0:16, 2175:2176], ALU.subtract),
            waits=beg.w, tag="rb_sub")
        beg.note_write_acc(sb)
        ex = P.op("scalar", lambda e: e.activation(eg[0:16, 0:1023], eg[0:16, 0:1023], AF.Exp),
                  waits=beg.w, tag="rb_exp")
        beg.note_write_acc(ex)
        st = P.dma("gpsimd", egpad_ap[:, :], eg[0:16, 0:2175], waits=beg.w + beg.r, tag="rb_st")
        beg.note_read(st)
        C.buf("rb_egpad_d").note_write(st)

    rev = C.sb["rev_bf16"]

    def relbias_fn(h, dslot=0):
        old_toks = []
        for nm in ("att_den", "att_rb", "rb_d2f"):
            b = C.bufs.get(nm)
            if b:
                old_toks += b.w + b.r
        if C.fused:
            d2f = C.salloc_in("att_den", "rb_d2f", (128, 2048), BF16, 0)
        else:
            d2f = C.sb["rb_d2f"] if "rb_d2f" in C.sb else C.sbuf("rb_d2f", (128, 2048), BF16)
        bd2 = C.buf("rb_d2")
        bd2f = C.buf("rb_d2f")
        bd2f.w = sorted(set(bd2f.w + old_toks))
        d2h = C.sb["rb_d2"]
        srcap = bass.AP(egpad_d, h * 2175, [[1, 128], [1, 2048]])
        dm = P.dma("scalar", d2f[:, :], srcap,
                   waits=C.buf("rb_egpad_d").read_waits() + bd2f.write_waits(), tag="rb_d2f")
        bd2f.note_write(dm)
        for cch in range(2):
            s = cch * 1024
            for half in range(2):
                mm = P.op("tensor", lambda e, s=s, half=half, d2f=d2f: e.matmul(
                    C.ps["pbig"][:, half * 512:(half + 1) * 512], rev[:, :],
                    d2f[:, s + half * 512:s + (half + 1) * 512], start=True, stop=True),
                    waits=bd2f.w + C.buf("rev_bf16").read_waits() +
                          C.buf("pbig").write_waits(), tag="rb_rev")
                bd2f.note_read(mm)
                C.buf("rev_bf16").note_read(mm)
                C.buf("pbig").note_write_acc(mm)
            ev = P.op("vector", lambda e, s=s, dslot=dslot: e.tensor_copy(
                d2h[:, dslot * 2048 + s:dslot * 2048 + s + 1024], C.ps["pbig"][:, 0:1024]),
                waits=C.buf("pbig").w + bd2.write_waits(), tag="rb_rev_ev")
            C.buf("pbig").note_read(ev)
            bd2.note_write_acc(ev)

    emit_rel_pre()
    if C.fused:
        C.arena_reset(["att_o"], ["rb_eg", "rb_rt", "att_o"])
    streams = {"qk": (QKDispatch(sq, sk), 0, 1), "v": (sv, 0), "o": (so,),
               "relbias_fn": relbias_fn}
    return lambda: emit_attn(C, "l0b", streams, has_bias=True, causal=False, relbias=True)



def prep_mamba(C, W, din):
    nc, P = C.nc, C.plan
    def cols(name, vec, per):
        arr = np.ascontiguousarray(vec.reshape(-1, 128).T, dtype=np.float32)
        d = din(name + "_in", arr)
        return emit_load_const(C, name, d, arr.shape, F32)
    cols("mb_cvb", W["L1_attn_a_conv_b"], 16)
    cols("mb_dtb", W["L1_attn_a_dt_proj_b"], 16)
    cols("mb_dcol", W["L1_attn_a_D"], 16)
    cvw = np.ascontiguousarray(
        W["L1_attn_a_conv_w"].reshape(16, 128, 4).transpose(1, 0, 2).reshape(128, 64),
        dtype=np.float32)
    emit_load_const(C, "mb_cvw", din("mb_cvw_in", cvw), cvw.shape, F32)
    alog = np.ascontiguousarray(
        W["L1_attn_a_A_log"].reshape(16, 128, 16).transpose(1, 0, 2).reshape(128, 256),
        dtype=np.float32)
    C.salloc("mb_cb", (128, 16 * 1024), BF16)
    if C.fused:
        C.sb["mb_alog"] = C.salloc_in("mb_cb", "mb_alog", (128, 256), F32, 0)
        alog_ap = emit_load_const(C, "mb_alog", din("mb_alog_in", alog), alog.shape, F32)
    else:
        alog_ap = emit_load_const(C, "mb_alog", din("mb_alog_in", alog), alog.shape, F32)
    acol = C.salloc("mb_acol", (128, 256), F32)
    e1 = P.op("scalar", lambda e: e.activation(acol[:, :], alog_ap[0:128, :], AF.Exp),
              waits=C.buf("mb_alog").read_waits(), tag="mb_acol_e")
    C.buf("mb_alog").note_read(e1)
    C.buf("mb_acol").note_write(e1)
    e2 = P.op("vector", lambda e: e.tensor_scalar_mul(acol[:, :], acol[:, :], C.sb["fvals"][:, 4:5]),
              waits=C.buf("mb_acol").w + C.buf("fvals").read_waits(), tag="mb_acol_n")
    C.buf("mb_acol").note_write(e2)
    if C.fused:
        C.arena_reset(["mb_cb"], ["mb_alog", "mb_cb"])
    eh = np.zeros((96, 32 * 128), np.float32)
    for n in range(16):
        eh[64 + n, n * 128:(n + 1) * 128] = 1.0
        eh[80 + n, (16 + n) * 128:(17 + n) * 128] = 1.0
    ehot_d = din("mb_ehot_in", eh)
    _ensure_ws(C, 2048)
    w_in = WStreamT(C, "win4", din("in_proj_w", W["L1_attn_a_in_proj_w"]), 4096, 1024)
    w_xp = WeightLoader(C, "wxp", din("x_proj_w", W["L1_attn_a_x_proj_w"]), 96, 2048, True)
    w_dt = WeightLoader(C, "wdt", din("dt_proj_w", W["L1_attn_a_dt_proj_w"]), 2048, 64, True)
    w_out = WStreamT(C, "wout4", din("out_proj_w", W["L1_attn_a_out_proj_w"]), 1024, 2048)
    w_dt.prefetch_all()
    xc_d = nc.dram_tensor("mb_xc_d", [2048, 2048], BF16, kind="Internal").ap()
    z_d = nc.dram_tensor("mb_z_d", [2048, 2048], BF16, kind="Internal").ap()
    g_d = nc.dram_tensor("mb_g_d", [2048, 2048], BF16, kind="Internal").ap()
    return lambda: emit_mamba(C, w_in, w_xp, w_dt, w_out, xc_d, z_d, g_d, ehot_d)


def emit_mamba(C, w_in, w_xp, w_dt, w_out, xc_d, z_d, g_d, ehot_d):
    P = C.plan
    xl = C.sb["x_ln"]; bxl = C.buf("x_ln")
    xres = C.sb["x_res"]; bxres = C.buf("x_res")
    fv = C.sb["fvals"]; bfv = C.buf("fvals")
    acol = C.sb["mb_acol"]; bacol = C.buf("mb_acol")
    cvw = C.sb["mb_cvw"]; cvb = C.sb["mb_cvb"]
    dtb = C.sb["mb_dtb"]; dcol = C.sb["mb_dcol"]
    ehot = C.sb["ws_nat"]; behot = C.buf("ws_nat")
    cbsb = C.sb["mb_cb"]; bcb = C.buf("mb_cb")
    bxcd = C.buf("mb_xc_d"); bzd = C.buf("mb_z_d"); bgd = C.buf("mb_g_d")
    identb = C.sb["identity_bf16"]

    mb0 = C.salloc_block(9728) if C.fused else None
    C.carve_rel(mb0, "w_wxp", 4100, 1536, BF16)
    w_xp.prefetch_all()
    xe = C.carve_rel(mb0, "mb_xe", 0, 2052, F32)
    bxe = C.buf("mb_xe")
    acc = C.carve_rel(mb0, "mb_acc", 2052, 2048, F32)
    bacc = C.buf("mb_acc")
    zstg = C.carve_rel(mb0, "mb_zstg", 2052, 2048, BF16)
    bzstg = C.buf("mb_zstg")
    xdblb = C.carve_rel(mb0, "mb_xdblb", 8704, 2048, BF16)
    bxdblb = C.buf("mb_xdblb")
    C.arena_reset(["mb_xe", "mb_acc", "mb_zstg", "mb_xdblb"],
                  ["ln_t1", "ln_sq", "ln_rows"])
    XPPS = (("pc", 0), ("pd", 0), ("pbig", 0), ("pbig", 512))
    for mt in range(32):
        dt = mt % 16
        is_x = mt < 16
        if mt == 16:
            C.arena_reset(["mb_zstg"], ["mb_xe", "mb_acc"])
        tiles = w_in.mt_tiles(mt)
        if is_x:
            z3 = P.op("vector", lambda e: e.memset(xe[:, 0:3], 0.0),
                      waits=bxe.write_waits(), tag="mb_xe0")
            bxe.note_write(z3)
        PS3 = ("pa", "pb", "pe")
        for ch in range(4):
            s = ch * 512
            ps = C.ps[PS3[ch % 3]]
            bps = C.buf(PS3[ch % 3])
            for kt in range(DT):
                tile, bw = tiles[kt]
                mm = P.op("tensor", lambda e, tile=tile, kt=kt, s=s, ps=ps: e.matmul(
                    ps[0:128, 0:512], tile, xl[:, kt * T + s:kt * T + s + 512],
                    start=(kt == 0), stop=(kt == DT - 1)),
                    waits=bw.read_waits() + bxl.read_waits() +
                          (bps.write_waits() if kt == 0 else []), tag="mb_inp")
                bw.note_read(mm)
                bxl.note_read(mm)
                bps.note_write_acc(mm)
            if is_x:
                ev = P.op("scalar", lambda e, s=s, ps=ps: e.activation(
                    xe[:, 3 + s:3 + s + 512], ps[0:128, 0:512], AF.Copy),
                    waits=bps.w + bxe.write_waits(), tag="mb_xe_ev")
                bps.note_read(ev)
                bxe.note_write_acc(ev)
            else:
                ev = P.op("scalar", lambda e, s=s, ps=ps: e.activation(
                    zstg[:, s:s + 512], ps[0:128, 0:512], AF.Silu),
                    waits=bps.w + bzstg.write_waits(), tag="mb_z_ev")
                bps.note_read(ev)
                bzstg.note_write_acc(ev)
        if is_x:
            c0 = P.op("vector", lambda e, dt=dt: e.tensor_scalar(
                acc[:, :], xe[:, 3:2051], cvw[:, dt * 4 + 3:dt * 4 + 4], None, ALU.mult),
                waits=bxe.w + C.buf("mb_cvw").read_waits() + bacc.write_waits(), tag="mb_cv3")
            bxe.note_read(c0)
            bacc.note_write(c0)
            for j in (2, 1, 0):
                cj = P.op("vector", lambda e, dt=dt, j=j: e.scalar_tensor_tensor(
                    acc[:, :], xe[:, j:j + 2048], cvw[:, dt * 4 + j:dt * 4 + j + 1],
                    acc[:, :], ALU.mult, ALU.add),
                    waits=bxe.w + bacc.w + C.buf("mb_cvw").read_waits(), tag="mb_cv")
                bxe.note_read(cj)
                bacc.note_write(cj)
            xco = xe[:, 0:1024].bitcast(BF16)
            sl = P.op("scalar", lambda e, dt=dt: e.activation(
                xco[:, 0:2048], acc[:, :], AF.Silu, bias=cvb[:, dt:dt + 1]),
                waits=bacc.w + C.buf("mb_cvb").read_waits() + bxe.write_waits(), tag="mb_silu")
            bacc.note_read(sl)
            bxe.note_write(sl)
            st = P.dma("sync", xc_d[dt * 128:(dt + 1) * 128, :], xco[:, 0:2048],
                       waits=[sl] + bxcd.write_waits(), tag="mb_xc_st")
            bxe.note_read(st)
            bxcd.note_write_acc(st)
            for ch in range(4):
                s = ch * 512
                pnm, po = XPPS[ch]
                ps = C.ps[pnm]
                bps = C.buf(pnm)
                tile, bw = w_xp.tile(dt, 0)
                mm = P.op("tensor", lambda e, tile=tile, s=s, ps=ps, po=po, dt=dt: e.matmul(
                    ps[0:96, po:po + 512], tile, xco[:, s:s + 512],
                    start=(dt == 0), stop=(dt == 15), skip_group_check=True),
                    waits=bw.read_waits() + [sl] +
                          (bps.write_waits() if dt == 0 else []), tag="mb_xp")
                bw.note_read(mm)
                bxe.note_read(mm)
                bps.note_write_acc(mm)
        else:
            st = P.dma("sync", z_d[dt * 128:(dt + 1) * 128, :], zstg[:, 0:2048],
                       waits=bzstg.w + bzd.write_waits(), tag="mb_z_st")
            bzstg.note_read(st)
            bzd.note_write_acc(st)
    for ch in range(4):
        pnm, po = XPPS[ch]
        ps = C.ps[pnm]
        bps = C.buf(pnm)
        ev = P.op("scalar", lambda e, ch=ch, ps=ps, po=po: e.activation(
            xdblb[0:96, ch * 512:(ch + 1) * 512], ps[0:96, po:po + 512], AF.Copy),
            waits=bps.w + bxdblb.write_waits(), tag="mb_xdbl_ev")
        bps.note_read(ev)
        bxdblb.note_write_acc(ev)
    ehl = P.dma("gpsimd", ehot[0:96, 0:4096], ehot_d[:, :],
                waits=behot.write_waits(), tag="mb_ehot_ld")
    behot.note_write(ehl)

    dA = C.carve_rel(mb0, "mb_dA", 0, 1024, F32)
    bdA = C.buf("mb_dA")
    dBu = C.carve_rel(mb0, "mb_dBu", 1024, 1024, BF16)
    bdBu = C.buf("mb_dBu")
    hbf = C.carve_rel(mb0, "mb_h", 1536, 1024, BF16)
    bh = C.buf("mb_h")
    hc = C.carve_rel(mb0, "mb_hc", 2048, 1024, BF16)
    bhc = C.buf("mb_hc")
    gate1 = C.carve_rel(mb0, "mb_g1", 4096, 1024, F32)
    bg1 = C.buf("mb_g1")
    spt = C.carve_rel(mb0, "mb_spt", 6144, 512, F32)
    bspt = C.buf("mb_spt")
    zg1, bzg1 = C.carve_rel(mb0, "mb_zg0", 2560, 1024, BF16), C.buf("mb_zg0")
    zgs = [(zg1, bzg1), (zg1, bzg1)]
    ubs = [(C.carve_rel(mb0, "mb_u0", 3072, 1024, BF16), C.buf("mb_u0")),
           (C.carve_rel(mb0, "mb_u1", 6656, 1024, BF16), C.buf("mb_u1"))]
    xc2s = [(C.carve_rel(mb0, "mb_xc20", 3584, 1024, BF16), C.buf("mb_xc20")),
            (C.carve_rel(mb0, "mb_xc21", 7168, 1024, BF16), C.buf("mb_xc21"))]
    deltas = [(C.carve_rel(mb0, "mb_delta0", 5120, 1024, F32), C.buf("mb_delta0")),
              (C.carve_rel(mb0, "mb_delta1", 7680, 1024, F32), C.buf("mb_delta1"))]
    if C.fused:
        dA2 = C.salloc_in("ln_sq", "mb_dA2", (128, 1024), F32, 0)
        dBu2 = C.salloc_in("ln_sq", "mb_dBu2", (128, 1024), BF16, 1024)
        hbf2 = C.salloc_in("ln_sq", "mb_h2", (128, 1024), BF16, 1536)
        hc2 = C.salloc("mb_hc2", (128, 1024), BF16)
        C.arena_reset(["mb_dA2", "mb_dBu2", "mb_h2"],
                      ["ln_sq", "ln_rows", "mb_dA2", "mb_dBu2", "mb_h2"])
    else:
        dA2, dBu2, hbf2, hc2 = dA, dBu, hbf, hc
    bdA2 = C.buf("mb_dA2") if C.fused else bdA
    bdBu2 = C.buf("mb_dBu2") if C.fused else bdBu
    bh2 = C.buf("mb_h2") if C.fused else bh
    bhc2 = C.buf("mb_hc2") if C.fused else bhc
    C.arena_reset(["mb_dA", "mb_dBu", "mb_h", "mb_hc", "mb_zg0",
                   "mb_u0", "mb_u1", "mb_xc20", "mb_xc21", "mb_g1", "mb_spt",
                   "mb_delta0", "mb_delta1"],
                  ["mb_xe", "mb_acc", "mb_zstg", "w_wxp"])
    hstate = C.salloc("mb_hst", (128, 256), F32)
    bhst = C.buf("mb_hst")
    bbsb, bbb = xl, bxl
    py = C.ps["pbig"]
    bpy = C.buf("pbig")

    for half in range(2):
        hs = half * 1024
        for n in range(NH):
            for (base, dst, bdst) in ((0, bbsb, bbb), (16, cbsb, bcb)):
                for q in range(2):
                    ps = C.ps[("pa", "pb")[q]]
                    bps = C.buf(("pa", "pb")[q])
                    mm = P.op("tensor", lambda e, base=base, n=n, q=q, hs=hs, ps=ps: e.matmul(
                        ps[0:128, 0:512],
                        ehot[0:96, (base + n) * 128:(base + n + 1) * 128],
                        xdblb[0:96, hs + q * 512:hs + (q + 1) * 512],
                        start=True, stop=True),
                        waits=behot.read_waits() + bxdblb.read_waits() + bps.write_waits(),
                        tag="mb_sel")
                    behot.note_read(mm)
                    bxdblb.note_read(mm)
                    bps.note_write(mm)
                    ev = P.op("scalar", lambda e, n=n, q=q, dst=dst, ps=ps: e.activation(
                        dst[:, n * 1024 + q * 512:n * 1024 + (q + 1) * 512],
                        ps[0:128, 0:512], AF.Copy),
                        waits=bps.w + bdst.write_waits(), tag="mb_sel_ev")
                    bps.note_read(ev)
                    bdst.note_write_acc(ev)
        def emit_ldz(phalf, pdt):
            phs = phalf * 1024
            pzg, pbzg = zgs[pdt % 2]
            ldz = P.dma("gpsimd", pzg[:, 0:1024],
                        z_d[pdt * 128:(pdt + 1) * 128, phs:phs + 1024],
                        waits=bzd.w + pbzg.write_waits(), tag="mb_z_ld")
            bzd.note_read(ldz)
            pbzg.note_write(ldz)

        def prep(phalf, pdt):
            phs = phalf * 1024
            psl = pdt % 2
            pxc, pbxc = xc2s[psl]
            pdl, pbdl = deltas[psl]
            pub, pbub = ubs[psl]
            ldx = P.dma("gpsimd", pxc[:, 0:1024],
                        xc_d[pdt * 128:(pdt + 1) * 128, phs:phs + 1024],
                        waits=bxcd.w + pbxc.write_waits(), tag="mb_xc_ld")
            bxcd.note_read(ldx)
            pbxc.note_write(ldx)
            for ch in range(2):
                s = ch * 512
                ps = C.ps[("pc", "pd")[ch % 2]]
                bps = C.buf(("pc", "pd")[ch % 2])
                tile, bw = w_dt.tile(0, pdt)
                mm = P.op("tensor", lambda e, tile=tile, s=s, phs=phs, ps=ps: e.matmul(
                    ps[0:128, 0:512], tile, xdblb[0:64, phs + s:phs + s + 512],
                    start=True, stop=True),
                    waits=bw.read_waits() + bxdblb.read_waits() + bps.write_waits(), tag="mb_dt")
                bw.note_read(mm)
                bxdblb.note_read(mm)
                bps.note_write(mm)
                o1 = P.op("scalar", lambda e, s=s, pdt=pdt, ps=ps, pdl=pdl: e.activation(
                    pdl[:, s:s + 512], ps[0:128, 0:512], AF.Relu,
                    bias=dtb[:, pdt:pdt + 1]),
                    waits=bps.w + C.buf("mb_dtb").read_waits() + pbdl.write_waits(),
                    tag="mb_sp_r")
                bps.note_read(o1)
                pbdl.note_write_acc(o1)
                o2 = P.op("scalar", lambda e, pdt=pdt, ps=ps: e.activation(
                    spt[:, :], ps[0:128, 0:512], AF.Abs, bias=dtb[:, pdt:pdt + 1]),
                    waits=bps.w + C.buf("mb_dtb").read_waits() + bspt.write_waits(),
                    tag="mb_sp_abs")
                bps.note_read(o2)
                bspt.note_write(o2)
                o3 = P.op("scalar", lambda e: e.activation(spt[:, :], spt[:, :], AF.Exp,
                                                           scale=fv[:, 4:5]),
                          waits=bspt.w + bfv.read_waits(), tag="mb_sp_e")
                bspt.note_write_acc(o3)
                o4 = P.op("scalar", lambda e: e.activation(spt[:, :], spt[:, :], AF.Ln, bias=1.0),
                          waits=[o3], tag="mb_sp_l")
                bspt.note_write_acc(o4)
                o6 = P.op("vector", lambda e, s=s, pdl=pdl: e.tensor_add(
                    pdl[:, s:s + 512], pdl[:, s:s + 512], spt[:, :]),
                    waits=bspt.w + pbdl.write_waits() + pbdl.read_waits(), tag="mb_sp_f")
                bspt.note_read(o6)
                pbdl.note_write_acc(o6)
            uo = P.op("vector", lambda e, pub=pub, pdl=pdl, pxc=pxc: e.tensor_mul(
                pub[:, 0:1024], pdl[:, 0:1024], pxc[:, 0:1024]),
                      waits=pbdl.w + pbxc.w + pbub.write_waits(), tag="mb_u")
            pbdl.note_read(uo)
            pbxc.note_read(uo)
            pbub.note_write(uo)

        for dt in range(16):
            sl = dt % 2
            xcb2, bxc2 = xc2s[sl]
            zg, bzg = zgs[sl]
            delta, bdelta = deltas[sl]
            ub, bub = ubs[sl]
            if dt == 0 and half == 0:
                prep(0, 0)
                emit_ldz(0, 0)
            for n in range(NH):
                if n == 3:
                    if dt < 15:
                        prep(half, dt + 1)
                    elif half == 0:
                        prep(1, 0)
                dslot = n % 2
                dA_c, bdA_c = (dA, bdA) if dslot == 0 else (dA2, bdA2)
                da = P.op("scalar", lambda e, dt=dt, n=n, dA_c=dA_c, delta=delta: e.activation(
                    dA_c[:, 0:1024], delta[:, 0:1024], AF.Exp,
                    scale=acol[:, dt * 16 + n:dt * 16 + n + 1]),
                    waits=bdelta.read_waits() + bacol.read_waits() + bdA_c.write_waits(),
                    tag="mb_dA")
                bdelta.note_read(da)
                bacol.note_read(da)
                bdA_c.note_write(da)
                dslot = n % 2
                dBu_c, bdBu_c = (dBu, bdBu) if dslot == 0 else (dBu2, bdBu2)
                db = P.op("vector", lambda e, n=n, dBu_c=dBu_c, ub=ub: e.tensor_mul(
                    dBu_c[:, 0:1024], ub[:, 0:1024], bbsb[:, n * 1024:(n + 1) * 1024]),
                    waits=bub.read_waits() + bbb.read_waits() + bdBu_c.write_waits(), tag="mb_dBu")
                bub.note_read(db)
                bbb.note_read(db)
                bdBu_c.note_write(db)
                if half == 0:
                    ini = 0.0
                    iw = []
                else:
                    ini = hstate[:, dt * 16 + n:dt * 16 + n + 1]
                    iw = bhst.read_waits()
                h_c, bh_c = (hbf, bh) if dslot == 0 else (hbf2, bh2)
                hc_c, bhc_c = (hc, bhc) if dslot == 0 else (hc2, bhc2)
                sc = P.op("vector", lambda e, ini=ini, dA_c=dA_c, dBu_c=dBu_c, h_c=h_c: e.tensor_tensor_scan(
                    h_c[:, 0:1024], dA_c[:, 0:1024], dBu_c[:, 0:1024], ini, ALU.mult, ALU.add),
                    waits=bdA_c.w + bdBu_c.w + iw + bh_c.write_waits(), tag="mb_scan")
                bdA_c.note_read(sc)
                bdBu_c.note_read(sc)
                if half == 1:
                    bhst.note_read(sc)
                bh_c.note_write(sc)
                if half == 0:
                    hs_cp = P.op("scalar", lambda e, dt=dt, n=n, h_c=h_c: e.activation(
                        hstate[:, dt * 16 + n:dt * 16 + n + 1], h_c[:, 1023:1024], AF.Copy),
                        waits=bh_c.w + bhst.write_waits(), tag="mb_hst")
                    bh_c.note_read(hs_cp)
                    bhst.note_write_acc(hs_cp)
                hco = P.op("vector", lambda e, n=n, h_c=h_c, hc_c=hc_c: e.tensor_mul(
                    hc_c[:, 0:1024], h_c[:, 0:1024], cbsb[:, n * 1024:(n + 1) * 1024]),
                    waits=bh_c.w + bcb.read_waits() + bhc_c.write_waits(), tag="mb_hc")
                bh_c.note_read(hco)
                bcb.note_read(hco)
                bhc_c.note_write(hco)
                for q in range(2):
                    mm = P.op("tensor", lambda e, q=q, n=n, hc_c=hc_c: e.matmul(
                        py[:, q * 512:(q + 1) * 512], identb[:, :],
                        hc_c[:, q * 512:(q + 1) * 512],
                        start=(n == 0), stop=(n == NH - 1)),
                        waits=C.buf("identity_bf16").read_waits() + bhc_c.w +
                              (bpy.write_waits() if n == 0 else []), tag="mb_y")
                    C.buf("identity_bf16").note_read(mm)
                    bhc_c.note_read(mm)
                    bpy.note_write_acc(mm)
            g1a = P.op("scalar", lambda e, dt=dt, xcb2=xcb2: e.activation(
                gate1[:, 0:1024], xcb2[:, 0:1024], AF.Copy, scale=dcol[:, dt:dt + 1]),
                waits=bxc2.read_waits() + C.buf("mb_dcol").read_waits() +
                      bg1.write_waits(), tag="mb_g1a")
            bxc2.note_read(g1a)
            bg1.note_write(g1a)
            g1b = P.op("vector", lambda e: e.tensor_add(
                gate1[:, 0:1024], gate1[:, 0:1024], py[:, 0:1024]),
                waits=bg1.w + bpy.w, tag="mb_g1b")
            bpy.note_read(g1b)
            bg1.note_write_acc(g1b)
            go = P.op("vector", lambda e, zg=zg: e.tensor_mul(hc[:, 0:1024], gate1[:, 0:1024],
                                                              zg[:, 0:1024]),
                      waits=bg1.w + bzg.read_waits() + bhc.write_waits(), tag="mb_gate")
            bg1.note_read(go)
            bzg.note_read(go)
            bhc.note_write(go)
            st = P.dma("sync", g_d[dt * 128:(dt + 1) * 128, hs:hs + 1024], hc[:, 0:1024],
                       waits=bhc.w + bgd.write_waits(), tag="mb_g_st")
            bhc.note_read(st)
            bgd.note_write_acc(st)
            if dt < 15:
                emit_ldz(half, dt + 1)
            elif half == 0:
                emit_ldz(1, 0)

    for dtk in range(16):
        dst, bdst = (bbsb, bbb) if dtk < 8 else (cbsb, bcb)
        off = (dtk % 8) * 2048
        ld = P.dma("gpsimd", dst[:, off:off + 2048], g_d[dtk * 128:(dtk + 1) * 128, :],
                   waits=bgd.w + bdst.write_waits(), tag="mb_g_ld")
        bgd.note_read(ld)
        bdst.note_write_acc(ld)
    PS4 = ("pa", "pb", "pc", "pd")
    for mt in range(DT):
        tiles = w_out.mt_tiles(mt)
        for kt in range(16):
            tile, bw = tiles[kt]
            hsb, bsrc = (bbsb, bbb) if kt < 8 else (cbsb, bcb)
            for ch in range(4):
                s = ch * 512
                ps = C.ps[PS4[ch]]
                bps = C.buf(PS4[ch])
                srcap = hsb[:, (kt % 8) * 2048 + s:(kt % 8) * 2048 + s + 512]
                mm = P.op("tensor", lambda e, tile=tile, srcap=srcap, ps=ps, kt=kt: e.matmul(
                    ps[0:128, 0:512], tile, srcap, start=(kt == 0), stop=(kt == 15)),
                    waits=bw.read_waits() + bsrc.read_waits() +
                          (bps.write_waits() if kt == 0 else []), tag="mb_out")
                bw.note_read(mm)
                bsrc.note_read(mm)
                bps.note_write_acc(mm)
        for ch in range(4):
            s = ch * 512
            ps = C.ps[PS4[ch]]
            bps = C.buf(PS4[ch])
            ev = P.op("vector", lambda e, mt=mt, s=s, ps=ps: e.tensor_add(
                xres[:, mt * T + s:mt * T + s + 512], xres[:, mt * T + s:mt * T + s + 512],
                ps[0:128, 0:512]),
                waits=bps.w + bxres.write_waits(), tag="mb_res")
            bps.note_read(ev)
            bxres.note_write_acc(ev)


def prep_ffn_rwkv(C, W, din):
    C.salloc("ffn_xr", (128, DT * 512), BF16)
    C.salloc("ffn_xk", (128, DT * 512), BF16)
    C.salloc("ffn_d", (128, 512), BF16)
    C.salloc("ffn_r", (128, DT * 512), BF16)
    C.salloc("ffn_k2", (128, DT * 512), BF16)
    mu_cols = np.concatenate([W["L0_ffn_mu_r"].reshape(DT, 128).T,
                              W["L0_ffn_mu_k"].reshape(DT, 128).T], axis=1)
    dmu = din("mu_rk_in", np.ascontiguousarray(mu_cols, dtype=np.float32))
    emit_load_const(C, "mu_rk", dmu, mu_cols.shape, F32)
    wr = WeightLoader(C, "wr", din("W_r", W["L0_ffn_W_r"]), D, D, True)
    wk = WeightLoader(C, "wk", din("W_k", W["L0_ffn_W_k"]), D, D, True)
    wv = WStreamT(C, "wvs", din("W_v", W["L0_ffn_W_v"]), D, D)
    wr.prefetch_all()
    wk.prefetch_all()
    return lambda: emit_ffn_rwkv(C, wr, wk, wv)


def prep_ffn_small(C, W, din):
    v = np.zeros((64, 50), np.float32)
    v[:, 0:16] = W["L1_ffn_W1"]
    v[:, 16:32] = W["L1_ffn_b1"]
    v[:, 32:48] = W["L1_ffn_W2"]
    v[:, 48] = W["L1_ffn_bin_"]
    v[:, 49] = W["L1_ffn_b2"]
    dv = din("l1ffn_vec_in", v)
    emit_load_const(C, "l1ffn_vec", dv, v.shape, F32)
    bo = np.ascontiguousarray(W["L1_ffn_bout"].reshape(DT, 128).T, dtype=np.float32)
    dbo = din("l1ffn_bout_in", bo)
    emit_load_const(C, "l1ffn_bout", dbo, bo.shape, F32)
    C.salloc("l1ffn_s", (64, T), F32)
    C.salloc("l1ffn_h", (64, T), F32)
    C.salloc("l1ffn_h2", (64, T), F32)
    C.salloc("l1ffn_g", (64, T), F32)
    C.salloc("l1ffn_gb", (64, T), BF16)
    w_in = WeightLoader(C, "win", din("Win", W["L1_ffn_Win"]), D, 64, False)
    w_out = WeightLoader(C, "wout", din("Wout", W["L1_ffn_Wout"]), 64, D, False)
    w_in.prefetch_all()
    w_out.prefetch_all()
    return lambda: emit_ffn_small(C, w_in, w_out)


SUBLAYER_PREP = {
    "l0_ffn": prep_ffn_rwkv,
    "l0_ffn2": prep_ffn_rwkv,
    "l1_ffn": prep_ffn_small,
    "l1_ffn2": prep_ffn_small,
    "l0_attn_a": prep_attn_l0a,
    "l0_attn_b": prep_attn_l0b,
    "l1_attn_b": prep_attn_l1b,
    "l1_mamba": prep_mamba,
}


def build_stage(stage, W):
    nc = bass.Bass("TRN2", target_bir_lowering=False, debug=False)
    st = contextlib.ExitStack()
    plan = Plan(nc)
    C = Ctx(nc, plan, st)
    feeds = {}

    def din(name, arr):
        arr = np.ascontiguousarray(arr, dtype=np.float32)
        ap = nc.dram_tensor(name, list(arr.shape), F32, kind="ExternalInput").ap()
        feeds[name] = arr
        return ap

    with st:
        sems = {e: st.enter_context(nc.semaphore("s_" + e)) for e in ENGINES}
        dma_sems = {q: st.enter_context(nc.semaphore("d_" + q)) for q in DMA_QUEUES}

        for name, arr in _CONSTS.items():
            dt = BF16 if arr.dtype == BF16NP else F32
            d = nc.dram_tensor("c_" + name, list(arr.shape), dt, kind="ExternalInput").ap()
            feeds["c_" + name] = arr
            emit_load_const(C, name, d, arr.shape, dt)

        C.sbuf("x_res", (128, DT * T), F32)
        final_stores = []

        if stage == "in_tx":
            C.psum("pbig", (128, 1024), F32)
            x_in = nc.dram_tensor("x_in", [T, D], F32, kind="ExternalInput").ap()
            feeds["x_in"] = "X_TOKEN"
            xfm_out = nc.dram_tensor("xfm_out", [128, DT * T], F32, kind="ExternalOutput").ap()
            emit_in_transform(C, x_in)
            bx = C.buf("x_res")
            sto = plan.dma("sync", xfm_out[:, :], C.sb["x_res"][:, :],
                           waits=bx.w, tag="xfm_st")
            final_stores.append(sto)
        elif stage == "out_tx":
            C.psum("pbig", (128, 1024), F32)
            xfm_in = nc.dram_tensor("xfm_in", [128, DT * T], F32, kind="ExternalInput").ap()
            feeds["xfm_in"] = "XFM"
            out = nc.dram_tensor("out", [T, D], F32, kind="ExternalOutput").ap()
            ld = plan.dma("sync", C.sb["x_res"][:, :], xfm_in[:, :], tag="xfm_ld")
            C.buf("x_res").note_write(ld)
            final_stores += emit_out_transform(C, out)
        else:
            xfm_in = nc.dram_tensor("xfm_in", [128, DT * T], F32, kind="ExternalInput").ap()
            feeds["xfm_in"] = "XFM"
            xfm_out = nc.dram_tensor("xfm_out", [128, DT * T], F32, kind="ExternalOutput").ap()
            ld = plan.dma("sync", C.sb["x_res"][:, :], xfm_in[:, :], tag="xfm_ld")
            C.buf("x_res").note_write(ld)

            C.sbuf("x_ln", (128, DT * T), BF16)
            if stage == "l1_mamba":
                C.sbuf("mb_arena", (128, 9728), F32)
                C.carve("mb_arena", "ln_t1", 0, 1024, F32)
                C.carve("mb_arena", "ln_sq", 1024, 512, F32)
                C.carve("mb_arena", "ln_rows", 1536, 1536, F32)
            else:
                C.sbuf("ln_sq", (128, 512), F32)
                C.sbuf("ln_rows", (1, 1536), F32)
                C.sbuf("ln_t1", (128, 1024), F32)
            C.psum("pbig", (128, 1024), F32)
            C.psum("pa", (128, 512), F32)
            C.psum("pb", (128, 512), F32)
            C.psum("pc", (128, 512), F32)
            C.psum("pd", (128, 512), F32)
            C.psum("pe", (128, 512), F32)

            run_sub = SUBLAYER_PREP[stage](C, W, din)
            emit_ln(C, "x_res", "x_ln")
            run_sub()

            bx = C.buf("x_res")
            sto = plan.dma("sync", xfm_out[:, :], C.sb["x_res"][:, :],
                           waits=bx.write_waits(), tag="xfm_st")
            final_stores.append(sto)

        fw = plan.final_wait_spec(final_stores, engine="sync")
        with nc.Block() as block:
            plan.finalize_and_emit(block, sems, dma_sems, fw)
    return nc, feeds


def run_stage(stage, W, percore):
    nc, feeds = build_stage(stage, W)
    in_maps = []
    for core in range(NCORES):
        m = {}
        for name, arr in feeds.items():
            m[name] = percore[arr][core] if isinstance(arr, str) else arr
        in_maps.append(m)
    return run_bass_kernel_spmd(nc, in_maps, core_ids=list(range(NCORES)))


def build_full(W, seq=None):
    if seq is None:
        seq = SUBLAYER_SEQ
    nc = bass.Bass("TRN2", target_bir_lowering=False, debug=False)
    st = contextlib.ExitStack()
    plan = Plan(nc)
    C = Ctx(nc, plan, st)
    C.fused = True
    feeds = {}

    def din_raw(name, arr):
        arr = np.ascontiguousarray(arr, dtype=np.float32)
        ap = nc.dram_tensor(name, list(arr.shape), F32, kind="ExternalInput").ap()
        feeds[name] = arr
        return ap

    with st:
        sems = {e: st.enter_context(nc.semaphore("s_" + e)) for e in ENGINES}
        dma_sems = {q: st.enter_context(nc.semaphore("d_" + q)) for q in DMA_QUEUES}

        x_in = nc.dram_tensor("x_in", [T, D], F32, kind="ExternalInput").ap()
        feeds["x_in"] = "X_TOKEN"
        out = nc.dram_tensor("out", [T, D], F32, kind="ExternalOutput").ap()

        for name, arr in _CONSTS.items():
            dt = BF16 if arr.dtype == BF16NP else F32
            d = nc.dram_tensor("c_" + name, list(arr.shape), dt, kind="ExternalInput").ap()
            feeds["c_" + name] = arr
            emit_load_const(C, name, d, arr.shape, dt, persist=True)

        C.sbuf("x_res", (128, DT * T), F32)
        C.sbuf("x_ln", (128, DT * T), BF16)
        _ensure_ws(C, 2048)
        C.psum("pa", (128, 512), F32)
        C.psum("pb", (128, 512), F32)
        C.psum("pc", (128, 512), F32)
        C.psum("pd", (128, 512), F32)
        C.psum("pe", (128, 512), F32)
        C.psum("pbig", (128, 1024), F32)
        C.psum("wt_ps", (128, 1024), BF16)

        rem = nc.sbuf_bytes_remaining
        cols = rem // 4 - 64
        C.sbuf("arena", (128, cols), F32)
        C.arena_cols = cols
        C.carve("arena", "ln_t1", 0, 1024, F32)

        C.begin_sublayer("in_tx")
        emit_in_transform(C, x_in)

        for stage in seq:
            C.begin_sublayer(stage)
            pref = stage + "_"
            din = lambda n, a, pref=pref: din_raw(pref + n, a)
            run_sub = SUBLAYER_PREP[stage](C, W, din)
            emit_ln(C, "x_res", "x_ln")
            run_sub()

        C.begin_sublayer("out_tx")
        stores = emit_out_transform(C, out)

        fw = plan.final_wait_spec(stores, engine="sync")
        with nc.Block() as block:
            plan.finalize_and_emit(block, sems, dma_sems, fw)
    return nc, feeds


def kernel(**inputs):
    x = np.ascontiguousarray(np.asarray(inputs["L0_attn_a_x"], dtype=np.float32))
    W = {k: np.ascontiguousarray(np.asarray(v, dtype=np.float32)) for k, v in inputs.items()}
    _ = inputs.get("L0_attn_b_c")

    import os
    if os.environ.get("KERNEL_UNFUSED"):
        res = run_stage("in_tx", W, {"X_TOKEN": [x[b] for b in range(NCORES)]})
        xfm = [res.results[b]["xfm_out"] for b in range(NCORES)]
        for stage in SUBLAYER_SEQ:
            res = run_stage(stage, W, {"XFM": xfm})
            xfm = [res.results[b]["xfm_out"] for b in range(NCORES)]
        res = run_stage("out_tx", W, {"XFM": xfm})
        return np.stack([res.results[b]["out"] for b in range(NCORES)])

    nc, feeds = build_full(W)
    in_maps = []
    for core in range(NCORES):
        m = {}
        for name, arr in feeds.items():
            m[name] = x[core] if isinstance(arr, str) else arr
        in_maps.append(m)
    res = run_bass_kernel_spmd(nc, in_maps, core_ids=list(range(NCORES)))
    return np.stack([res.results[b]["out"] for b in range(NCORES)])


# version 33
# speedup vs baseline: 1.0108x; 1.0108x over previous
import contextlib
import numpy as np
import ml_dtypes
import concourse.bass as bass
import concourse.mybir as mybir
from concourse.bass_utils import run_bass_kernel_spmd

F32 = mybir.dt.float32
BF16 = mybir.dt.bfloat16
AF = mybir.ActivationFunctionType
ALU = mybir.AluOpType
AX = mybir.AxisListType
BF16NP = ml_dtypes.bfloat16

T = 2048
D = 1024
DT = D // 128
TT = T // 128
NCORES = 8
NH = 16
DH = 64
LN_EPS = 1e-5

ENGINES = ("tensor", "vector", "scalar", "gpsimd", "sync")
DMA_QUEUES = ("sync", "gpsimd", "scalar")


class Plan:

    def __init__(self, nc):
        self.nc = nc
        self.ops = []
        self.by_engine = {e: [] for e in ENGINES}
        self.same_engine_wait = True
        self.dma_count = {q: 0 for q in DMA_QUEUES}

    def op(self, eng, fn, waits=(), tag=""):
        oid = len(self.ops)
        rec = dict(oid=oid, eng=eng, fn=fn, waits=[w for w in waits if w is not None],
                   dma=False, tag=tag, inc=False)
        self.ops.append(rec)
        self.by_engine[eng].append(rec)
        return oid

    def dma(self, queue, out, in_, waits=(), tag=""):
        oid = len(self.ops)

        def fn(e, out=out, in_=in_):
            return e.dma_start(out=out, in_=in_)
        self.dma_count[queue] += 16
        rec = dict(oid=oid, eng=queue, fn=fn, waits=[w for w in waits if w is not None],
                   dma=True, tag=tag, inc=True, dma_target=self.dma_count[queue])
        self.ops.append(rec)
        self.by_engine[queue].append(rec)
        return oid

    def finalize_and_emit(self, block, sems, dma_sems, final_waits):
        for rec in self.ops:
            for w in rec["waits"]:
                self.ops[w]["inc"] = True
        for e in ENGINES:
            c = 0
            for rec in self.by_engine[e]:
                if not rec["dma"] and rec["inc"]:
                    c += 1
                    rec["eng_target"] = c

        def wait_list(rec):
            agg = {}
            for w in rec["waits"]:
                p = self.ops[w]
                if p["dma"]:
                    key = ("dma", p["eng"])
                    val = p["dma_target"]
                else:
                    if p["eng"] == rec["eng"] and rec["eng"] == "tensor":
                        continue
                    if p["eng"] == rec["eng"] and not self.same_engine_wait:
                        continue
                    key = ("eng", p["eng"])
                    val = p["eng_target"]
                if agg.get(key, 0) < val:
                    agg[key] = val
            return agg

        def run_engine(e, ename):
            seen = {}
            for rec in self.by_engine[ename]:
                for (kind, src), val in wait_list(rec).items():
                    if seen.get((kind, src), -1) >= val:
                        continue
                    seen[(kind, src)] = val
                    sem = dma_sems[src] if kind == "dma" else sems[src]
                    e.wait_ge(sem, val)
                ins = rec["fn"](e)
                if rec["dma"]:
                    ins.then_inc(dma_sems[rec["eng"]], 16)
                elif rec["inc"]:
                    ins.then_inc(sems[rec["eng"]], 1)
            for (kind, src), val in final_waits.get(ename, {}).items():
                if seen.get((kind, src), -1) >= val:
                    continue
                sem = dma_sems[src] if kind == "dma" else sems[src]
                e.wait_ge(sem, val)

        block.sync(lambda e: run_engine(e, "sync"))
        block.vector(lambda e: run_engine(e, "vector"))
        block.scalar(lambda e: run_engine(e, "scalar"))
        block.gpsimd(lambda e: run_engine(e, "gpsimd"))
        block.tensor(lambda e: run_engine(e, "tensor"))

    def final_wait_spec(self, opids, engine="sync"):
        agg = {}
        for oid in opids:
            p = self.ops[oid]
            assert p["dma"]
            key = ("dma", p["eng"])
            val = p["dma_target"]
            if agg.get(key, 0) < val:
                agg[key] = val
        return {engine: agg}


class Buf:
    __slots__ = ("w", "r")

    def __init__(self):
        self.w = []
        self.r = []

    def read_waits(self):
        return list(self.w)

    def write_waits(self):
        return list(self.w) + list(self.r)

    def note_read(self, oid):
        self.r.append(oid)

    def note_write(self, oid):
        self.w = [oid]
        self.r = []

    def note_write_acc(self, oid):
        self.w.append(oid)
        self.r = []


def ceil_div(a, b):
    return (a + b - 1) // b


def host_constants():
    c = {}
    c["identity_f32"] = np.eye(128, dtype=np.float32)
    c["identity_bf16"] = np.eye(128).astype(BF16NP)
    c["ones_f32"] = np.ones((128, 128), dtype=np.float32)
    c["ones_bf16"] = np.ones((128, 128)).astype(BF16NP)
    m = (np.arange(128)[None, :] >= np.arange(128)[:, None]).astype(np.float32)
    c["tri_bf16"] = m.astype(BF16NP)
    fv = np.zeros((128, 8), np.float32)
    fv[:, 0] = LN_EPS
    fv[:, 1] = 1.0 / D
    fv[:, 2] = -0.5
    fv[:, 3] = 1.0
    fv[:, 4] = -1.0
    fv[:, 5] = 0.125
    c["fvals"] = fv
    rev = np.zeros((128, 128), np.float32)
    rev[np.arange(128), 127 - np.arange(128)] = 1.0
    c["rev_bf16"] = rev.astype(BF16NP)
    return c


class Ctx:
    def __init__(self, nc, plan, st):
        self.nc = nc
        self.plan = plan
        self.st = st
        self.sb = {}
        self.ps = {}
        self.bufs = {}
        self.fused = False
        self.arena_cols = 0
        self.cursor = 0
        self.sub_names = []
        self.offsets = {}
        self.prev_tokens = []
        self.max_cursor = 0

    def begin_sublayer(self, tag=""):
        if not self.fused:
            return
        toks = list(self.prev_tokens)
        for n in self.sub_names:
            b = self.bufs.get(n)
            if b:
                toks += b.w + b.r
            self.sb.pop(n, None)
            self.bufs.pop(n, None)
        self.prev_tokens = sorted(set(toks))
        self.sub_names = []
        self.cursor = 1024
        self.sub_tag = tag

    def salloc(self, name, shape, dt=F32):
        if not self.fused:
            return self.sbuf(name, shape, dt)
        p = shape[0]
        n = shape[1]
        f32n = n if dt == F32 else (n + 1) // 2
        f32n = (f32n + 15) & ~15
        assert self.cursor + f32n <= self.arena_cols,             f"arena overflow at {name} ({self.sub_tag}): {self.cursor}+{f32n}>{self.arena_cols}"
        self.offsets[name] = self.cursor
        h = self.sb["arena"]
        if dt == F32:
            ap = h[0:p, self.cursor:self.cursor + n]
        else:
            ap = h[0:p, self.cursor:self.cursor + (n + 1) // 2].bitcast(dt)[0:p, 0:n]
        self.cursor += f32n
        self.max_cursor = max(self.max_cursor, self.cursor)
        self.sb[name] = ap
        b = Buf()
        b.w = list(self.prev_tokens)
        self.bufs[name] = b
        self.sub_names.append(name)
        return ap

    def salloc_in(self, host, name, shape, dt=F32, off_f32=0):
        if not self.fused:
            return self.sbuf(name, shape, dt)
        base = self.offsets[host] + off_f32
        p, n = shape
        h = self.sb["arena"]
        if dt == F32:
            ap = h[0:p, base:base + n]
        else:
            ap = h[0:p, base:base + (n + 1) // 2].bitcast(dt)[0:p, 0:n]
        self.sb[name] = ap
        b = Buf()
        b.w = list(self.prev_tokens)
        self.bufs[name] = b
        self.sub_names.append(name)
        return ap

    def salloc_block(self, nelem):
        if not self.fused:
            return None
        base = self.cursor
        assert base + nelem <= self.arena_cols, f"arena overflow block {nelem}"
        self.cursor += nelem
        self.max_cursor = max(self.max_cursor, self.cursor)
        return base

    def carve_rel(self, base, name, off, nelem, dt=F32, p=128):
        if base is None:
            return self.carve("mb_arena", name, off, nelem, dt)[0:p, :]
        h = self.sb["arena"]
        if dt == F32:
            ap = h[0:p, base + off:base + off + nelem]
        else:
            ap = h[0:p, base + off:base + off + (nelem + 1) // 2].bitcast(dt)[0:p, 0:nelem]
        self.sb[name] = ap
        b = Buf()
        b.w = list(self.prev_tokens)
        self.bufs[name] = b
        self.sub_names.append(name)
        return ap

    def sbuf(self, name, shape, dt=F32):
        h = self.st.enter_context(self.nc.sbuf_tensor(name, list(shape), dt))
        self.sb[name] = h
        self.bufs[name] = Buf()
        return h

    def psum(self, name, shape, dt=F32):
        h = self.st.enter_context(self.nc.psum_tensor(name, list(shape), dt))
        self.ps[name] = h
        self.bufs[name] = Buf()
        return h

    def buf(self, name):
        return self.bufs.setdefault(name, Buf())

    def carve(self, arena, name, c0, nelem, dt=F32):
        h = self.sb[arena]
        if dt == F32:
            ap = h[:, c0:c0 + nelem]
        else:
            ap = h[:, c0:c0 + (nelem + 1) // 2].bitcast(dt)[:, 0:nelem]
        self.sb[name] = ap
        self.bufs[name] = Buf()
        return ap

    def arena_reset(self, names, handoff_from):
        toks = []
        for n in handoff_from:
            b = self.bufs.get(n)
            if b:
                toks += b.w + b.r
        for n in names:
            self.bufs[n].w = list(toks)



def emit_load_const(C, name, dram_ap, shape, dt, persist=False):
    if name in C.sb:
        h = C.sb[name]
    elif persist or not C.fused:
        h = C.sbuf(name, shape, dt)
    else:
        h = C.salloc(name, shape, dt)
    oid = C.plan.dma("sync", h[0:shape[0], :], dram_ap[:, :],
                     waits=C.buf(name).write_waits(), tag=f"load_{name}")
    C.buf(name).note_write(oid)
    return h


def emit_in_transform(C, x_tok_dram, xname="x_res"):
    P = C.plan
    x_res = C.sb[xname]
    stage = C.salloc("io_stage", (128, 2 * 1024), F32)
    pt = C.ps["pbig"]
    ident = C.sb["identity_f32"]
    bident = C.buf("identity_f32")
    bst = C.buf("io_stage")
    bpt = C.buf("pbig")
    bx = C.buf(xname)
    xv = x_tok_dram.rearrange("(a p) d -> p a d", p=128)
    for j in range(DT):
        for g in range(2):
            sl = (j * 2 + g) % 2
            ld = P.dma("sync", stage[:, sl * 1024:(sl + 1) * 1024]
                       .rearrange("p (a c) -> p a c", a=8),
                       xv[:, g * 8:(g + 1) * 8, j * 128:(j + 1) * 128],
                       waits=bst.write_waits(), tag="in_ld")
            bst.note_write_acc(ld)
            for ii in range(8):
                tp = P.op("tensor", lambda e, ii=ii, sl=sl: e.transpose(
                    pt[:, ii * 128:(ii + 1) * 128],
                    stage[:, sl * 1024 + ii * 128:sl * 1024 + (ii + 1) * 128],
                    ident[:, :]),
                    waits=[ld] + bident.read_waits() + bpt.write_waits(), tag="in_tp")
                bst.note_read(tp)
                bident.note_read(tp)
                bpt.note_write_acc(tp)
            ev = P.op("vector", lambda e, j=j, g=g: e.tensor_copy(
                x_res[:, j * T + g * 1024:j * T + (g + 1) * 1024], pt[:, :]),
                waits=bpt.w + bx.write_waits(), tag="in_ev")
            bpt.note_read(ev)
            bx.note_write_acc(ev)


def emit_out_transform(C, out_dram, xname="x_res"):
    P = C.plan
    x_res = C.sb[xname]
    stage = C.salloc("io_stage2", (128, 2 * D), F32)
    pt = C.ps["pbig"]
    ident = C.sb["identity_f32"]
    bident = C.buf("identity_f32")
    bst = C.buf("io_stage2")
    bpt = C.buf("pbig")
    bx = C.buf(xname)
    stores = []
    for i in range(TT):
        sl = i % 2
        for g in range(2):
            for jj in range(4):
                j = g * 4 + jj
                tp = P.op("tensor", lambda e, i=i, j=j, jj=jj: e.transpose(
                    pt[:, jj * 128:(jj + 1) * 128],
                    x_res[:, j * T + i * 128: j * T + (i + 1) * 128], ident[:, :]),
                    waits=bx.read_waits() + bident.read_waits() + bpt.write_waits(),
                    tag="out_tp")
                bx.note_read(tp)
                bident.note_read(tp)
                bpt.note_write_acc(tp)
            ev = P.op("vector", lambda e, sl=sl, g=g: e.tensor_copy(
                stage[:, sl * D + g * 512:sl * D + (g + 1) * 512], pt[:, 0:512]),
                waits=bpt.w + bst.write_waits(), tag="out_ev")
            bpt.note_read(ev)
            bst.note_write_acc(ev)
        sto = P.dma("sync", out_dram[i * 128:(i + 1) * 128, :], stage[:, sl * D:(sl + 1) * D],
                    waits=bst.w, tag="out_st")
        bst.note_read(sto)
        stores.append(sto)
    return stores



def emit_ln(C, src_name, dst_name):
    if "ln_sq" not in C.sb:
        C.salloc("ln_sq", (128, 512), F32)
    if "ln_rows" not in C.sb:
        C.salloc("ln_rows", (1, 1536), F32)
    """LayerNorm feature-major: src f32 [128, DT*T] -> dst bf16 [128, DT*T].
    Chunked by 512 tokens. Uses: ln_sq [128,512] f32, ln_rows [1, 3*T] f32,
    ln_ps psum [1, 1024] f32, ln_bc psum [128, 1024] f32, pa.. unused."""
    P = C.plan
    x = C.sb[src_name]
    y = C.sb[dst_name]
    bx = C.buf(src_name)
    by = C.buf(dst_name)
    ones = C.sb["ones_f32"]
    bones = C.buf("ones_f32")
    sq = C.sb["ln_sq"]; bsq = C.buf("ln_sq")
    rows = C.sb["ln_rows"]; brows = C.buf("ln_rows")
    psr_a = C.ps["pc"]; bps_a = C.buf("pc")
    psr_b = C.ps["pd"]; bps_b = C.buf("pd")
    pb = C.ps["pbig"]; bpb = C.buf("pbig")
    t1 = C.sb["ln_t1"]; bt1 = C.buf("ln_t1")

    for ch in range(4):
        s = ch * 512
        sqb = sq.bitcast(BF16)
        onesb_ = C.sb["ones_bf16"]
        bonesb_ = C.buf("ones_bf16")
        for j in range(DT):
            cv = P.op("scalar", lambda e, j=j, s=s, sqb=sqb: e.activation(
                sqb[:, 0:512], x[:, j * T + s:j * T + s + 512], AF.Copy),
                waits=bx.read_waits() + bsq.write_waits(), tag="ln_cast")
            bx.note_read(cv)
            bsq.note_write(cv)
            sqo = P.op("vector", lambda e, sqb=sqb: e.tensor_mul(
                sqb[:, 512:1024], sqb[:, 0:512], sqb[:, 0:512]),
                waits=bsq.w, tag="ln_sq")
            bsq.note_write_acc(sqo)
            mm = P.op("tensor", lambda e, j=j, sqb=sqb: e.matmul(
                psr_a[0:1, 0:512], onesb_[:, 0:1], sqb[:, 0:512],
                start=(j == 0), stop=(j == DT - 1)),
                waits=bsq.w + bonesb_.read_waits() +
                      (bps_a.write_waits() if j == 0 else []), tag="ln_sum")
            bsq.note_read(mm)
            bonesb_.note_read(mm)
            bps_a.note_write_acc(mm)
            mm2 = P.op("tensor", lambda e, j=j, sqb=sqb: e.matmul(
                psr_b[0:1, 0:512], onesb_[:, 0:1], sqb[:, 512:1024],
                start=(j == 0), stop=(j == DT - 1)),
                waits=bsq.w + bonesb_.read_waits() +
                      (bps_b.write_waits() if j == 0 else []), tag="ln_sumsq")
            bsq.note_read(mm2)
            bps_b.note_write_acc(mm2)
        R_MEAN, R_RSTD, R_M2 = 0, 512, 1024
        fv = C.sb["fvals"]
        bfv = C.buf("fvals")
        m_mean = P.op("scalar", lambda e, s=s: e.activation(
            rows[0:1, R_MEAN:R_MEAN + 512], psr_a[0:1, 0:512], AF.Copy, scale=fv[0:1, 1:2]),
            waits=bps_a.w + brows.write_waits() + bfv.read_waits(), tag="ln_mean")
        bps_a.note_read(m_mean)
        brows.note_write_acc(m_mean)
        m_var = P.op("scalar", lambda e, s=s: e.activation(
            rows[0:1, R_RSTD:R_RSTD + 512], psr_b[0:1, 0:512], AF.Copy, scale=fv[0:1, 1:2]),
            waits=bps_b.w + [m_mean], tag="ln_var0")
        bps_b.note_read(m_var)
        brows.note_write_acc(m_var)
        sqm = P.op("vector", lambda e, s=s: e.tensor_mul(
            rows[0:1, R_M2:R_M2 + 512], rows[0:1, R_MEAN:R_MEAN + 512],
            rows[0:1, R_MEAN:R_MEAN + 512]),
            waits=[m_mean], tag="ln_m2a")
        subv = P.op("vector", lambda e, s=s: e.tensor_sub(
            rows[0:1, R_RSTD:R_RSTD + 512], rows[0:1, R_RSTD:R_RSTD + 512],
            rows[0:1, R_M2:R_M2 + 512]),
            waits=[sqm, m_var], tag="ln_m2b")
        lnv = P.op("scalar", lambda e, s=s: e.activation(
            rows[0:1, R_RSTD:R_RSTD + 512], rows[0:1, R_RSTD:R_RSTD + 512],
            AF.Ln, bias=fv[0:1, 0:1]), waits=[subv], tag="ln_ln")
        rstd = P.op("scalar", lambda e, s=s: e.activation(
            rows[0:1, R_RSTD:R_RSTD + 512], rows[0:1, R_RSTD:R_RSTD + 512],
            AF.Exp, scale=fv[0:1, 2:3]), waits=[lnv], tag="ln_exp")
        m2 = P.op("vector", lambda e, s=s: e.tensor_mul(
            rows[0:1, R_M2:R_M2 + 512], rows[0:1, R_MEAN:R_MEAN + 512],
            rows[0:1, R_RSTD:R_RSTD + 512]),
            waits=[rstd, sqm, m_mean], tag="ln_m2")
        brows.note_write_acc(m2)
        brows.w = [m2, rstd]
        b1 = P.op("tensor", lambda e, s=s: e.matmul(
            pb[:, 0:512], ones[0:1, :], rows[0:1, R_RSTD:R_RSTD + 512],
            start=True, stop=True),
            waits=[rstd, m2] + bones.read_waits() + bpb.write_waits(), tag="ln_bc1")
        b2 = P.op("tensor", lambda e, s=s: e.matmul(
            pb[:, 512:1024], ones[0:1, :], rows[0:1, R_M2:R_M2 + 512],
            start=True, stop=True),
            waits=[m2] + bones.read_waits() + bpb.write_waits(), tag="ln_bc2")
        brows.note_read(b1)
        brows.note_read(b2)
        bones.note_read(b1)
        bpb.note_write(b1)
        bpb.w = [b1, b2]
        for j in range(DT):
            mu = P.op("vector", lambda e, j=j, s=s: e.tensor_mul(
                t1[:, 0:512], x[:, j * T + s:j * T + s + 512], pb[:, 0:512]),
                waits=bx.read_waits() + bpb.w + bt1.write_waits(), tag="ln_n1")
            bx.note_read(mu)
            bpb.note_read(mu)
            bt1.note_write(mu)
            su = P.op("vector", lambda e, j=j, s=s: e.tensor_sub(
                y[:, j * T + s:j * T + s + 512], t1[:, 0:512], pb[:, 512:1024]),
                waits=bt1.w + bpb.w + by.write_waits(), tag="ln_n2")
            bt1.note_read(su)
            bpb.note_read(su)
            by.note_write_acc(su)


class WeightLoader:

    def __init__(self, C, name, dram_ap, rows, cols, transpose, queue="gpsimd"):
        self.C, self.name = C, name
        self.dram = dram_ap
        self.rows, self.cols = rows, cols
        self.transpose = transpose
        self.queue = queue
        if transpose:
            self.KT, self.MT = ceil_div(cols, 128), ceil_div(rows, 128)
        else:
            self.KT, self.MT = ceil_div(rows, 128), ceil_div(cols, 128)

    def prefetch_all(self):
        C, P = self.C, self.C.plan
        if not self.transpose:
            h = C.salloc(f"w_{self.name}", (128, self.KT * self.cols), BF16)
            b = C.buf(f"w_{self.name}")
            for kt in range(self.KT):
                r0 = kt * 128
                r1 = min(self.rows, r0 + 128)
                ld = P.dma(self.queue, h[0:r1 - r0, kt * self.cols:(kt + 1) * self.cols],
                           self.dram[r0:r1, :], waits=b.write_waits(), tag=f"w_{self.name}_ld")
                b.note_write_acc(ld)
            self.h, self.b = h, b
        else:
            nm = f"w_{self.name}"
            if nm in C.sb:
                h = C.sb[nm]
                b = C.buf(nm)
            else:
                h = C.salloc(nm, (128, self.KT * self.rows), BF16)
                b = C.buf(nm)
            if "ws_nat" in C.sb:
                natname = "ws_nat"
            else:
                if "wt_nat" not in C.sb:
                    C.sbuf("wt_nat", (128, 2048), BF16)
                natname = "wt_nat"
            if "wt_ps" not in C.ps:
                C.psum("wt_ps", (128, 1024), BF16)
            nat = C.sb[natname]
            bn = C.buf(natname)
            pt = C.ps["wt_ps"]
            bp = C.buf("wt_ps")
            identb = C.sb["identity_bf16"]
            bib = C.buf("identity_bf16")
            ncolblk = ceil_div(self.cols, 128)
            for mt in range(self.MT):
                r0 = mt * 128
                r1 = min(self.rows, r0 + 128)
                nr = r1 - r0
                ld = P.dma(self.queue, nat[0:nr, 0:self.cols], self.dram[r0:r1, :],
                           waits=bn.write_waits(), tag=f"w_{self.name}_ld")
                bn.note_write(ld)
                for g0 in range(0, ncolblk, 8):
                    g1 = min(ncolblk, g0 + 8)
                    for kt in range(g0, g1):
                        c0 = kt * 128
                        c1 = min(self.cols, c0 + 128)
                        tp = P.op("tensor", lambda e, kt=kt, c0=c0, c1=c1, g0=g0, nr=nr: e.transpose(
                            pt[0:c1 - c0, (kt - g0) * 128:(kt - g0) * 128 + nr],
                            nat[0:nr, c0:c1], identb[0:nr, 0:nr]),
                            waits=bn.read_waits() + bib.read_waits() + bp.write_waits(),
                            tag=f"w_{self.name}_tp")
                        bn.note_read(tp)
                        bib.note_read(tp)
                        bp.note_write_acc(tp)
                    for kt in range(g0, g1):
                        c0 = kt * 128
                        c1 = min(self.cols, c0 + 128)
                        ev = P.op("vector", lambda e, kt=kt, g0=g0, c0=c0, c1=c1, r0=r0, r1=r1: e.tensor_copy(
                            h[0:c1 - c0, kt * self.rows + r0: kt * self.rows + r1],
                            pt[0:c1 - c0, (kt - g0) * 128:(kt - g0) * 128 + (r1 - r0)]),
                            waits=bp.w + b.write_waits(), tag=f"w_{self.name}_ev")
                        bp.note_read(ev)
                        b.note_write_acc(ev)
            self.h, self.b = h, b

    def tile(self, kt, mt):
        if self.transpose:
            k1 = min(self.cols - kt * 128, 128)
            m0 = mt * 128
            m1 = min(self.rows, m0 + 128)
            return self.h[0:k1, kt * self.rows + m0: kt * self.rows + m1], self.b
        else:
            k1 = min(self.rows - kt * 128, 128)
            m0 = mt * 128
            m1 = min(self.cols, m0 + 128)
            return self.h[0:k1, kt * self.cols + m0: kt * self.cols + m1], self.b


def emit_proj_fm(C, wl, xsrc_name, out_cb, out_rows, psname, tag="proj"):
    P = C.plan
    x = C.sb[xsrc_name]
    bx = C.buf(xsrc_name)
    KT = wl.KT
    MT = ceil_div(out_rows, 128)
    for mt in range(MT):
        for ch in range(4):
            pname = psname[(mt * 4 + ch) % len(psname)]
            ps = C.ps[pname]
            bps = C.buf(pname)
            s = ch * 512
            for kt in range(KT):
                tile, bw = wl.tile(kt, mt)
                mm = P.op("tensor", lambda e, tile=tile, kt=kt, s=s, ps=ps: e.matmul(
                    ps[0:128, 0:512], tile, x[:, kt * T + s: kt * T + s + 512],
                    start=(kt == 0), stop=(kt == KT - 1)),
                    waits=bw.read_waits() + bx.read_waits() +
                          (bps.write_waits() if kt == 0 else []), tag=tag + "_mm")
                bw.note_read(mm)
                bx.note_read(mm)
                bps.note_write_acc(mm)
            out_cb(mt, ch, ps, bps)


class WvPrefetch:

    def __init__(self, C, name, dram):
        self.C, self.name, self.dram = C, name, dram

    def prefetch_all(self):
        C, P = self.C, self.C.plan
        assert C.ws_slotw >= 2048
        nat = C.sb["ws_nat"]
        wt = C.sb["ws_wt"]
        vt = C.sb["ws_vT"]
        bvt = C.buf("ws_vT")
        pt = C.ps["wt_ps"]
        bp = C.buf("wt_ps")
        identb = C.sb["identity_bf16"]
        self.dsts = [(nat, C.buf("ws_nat")), (wt, C.buf("ws_wt"))]
        for mt in range(8):
            ld = P.dma("gpsimd", vt[:, 0:1024], self.dram[mt * 128:(mt + 1) * 128, :],
                       waits=bvt.write_waits(), tag=f"{self.name}_ld")
            bvt.note_write(ld)
            for g in range(2):
                for kk in range(4):
                    kt = g * 4 + kk
                    tp = P.op("tensor", lambda e, kt=kt, kk=kk: e.transpose(
                        pt[:, kk * 128:(kk + 1) * 128], vt[:, kt * 128:(kt + 1) * 128],
                        identb[:, :]),
                        waits=bvt.w + C.buf("identity_bf16").read_waits() + bp.write_waits(),
                        tag=f"{self.name}_tp")
                    bvt.note_read(tp)
                    C.buf("identity_bf16").note_read(tp)
                    bp.note_write_acc(tp)
                dsth, bdst = self.dsts[g]
                ev = P.op("vector", lambda e, mt=mt, dsth=dsth: e.tensor_copy(
                    dsth.rearrange("p (k w) -> p k w", w=1024)[:, :, mt * 128:(mt + 1) * 128],
                    pt[:, 0:512].rearrange("p (k w) -> p k w", w=128)),
                    waits=bp.w + bdst.write_waits(), tag=f"{self.name}_ev")
                bp.note_read(ev)
                bdst.note_write_acc(ev)

    def mt_tiles(self, mt, row0=0):
        out = []
        for kt in range(8):
            dsth, bdst = self.dsts[kt // 4]
            c0 = (kt % 4) * 1024 + mt * 128
            out.append((dsth[:, c0:c0 + 128], bdst))
        return out


def emit_ffn_rwkv(C, wr, wk, wv):
    P = C.plan
    CH = 512
    xl = C.sb["x_ln"]; bxl = C.buf("x_ln")
    xr = C.sb["ffn_xr"]; bxr = C.buf("ffn_xr")
    xk = C.sb["ffn_xk"]; bxk = C.buf("ffn_xk")
    dd = C.sb["ffn_d"]; bd = C.buf("ffn_d")
    mu = C.sb["mu_rk"]; bmu = C.buf("mu_rk")
    r = C.sb["ffn_r"]; br = C.buf("ffn_r")
    k2 = C.sb["ffn_k2"]; bk2 = C.buf("ffn_k2")
    xres = C.sb["x_res"]; bxres = C.buf("x_res")
    t1 = C.sb["ln_t1"]; bt1 = C.buf("ln_t1")

    def chunk_proj(wl, src, bsrc, evac, tag):
        for mt in range(DT):
            pname = ("pa", "pb")[mt % 2]
            ps = C.ps[pname]
            bps = C.buf(pname)
            tiles = wl.mt_tiles(mt) if hasattr(wl, "mt_tiles") else None
            for kt in range(DT):
                tile, bw = tiles[kt] if tiles else wl.tile(kt, mt)
                mm = P.op("tensor", lambda e, tile=tile, kt=kt, ps=ps: e.matmul(
                    ps[0:128, 0:CH], tile, src[:, kt * CH:(kt + 1) * CH],
                    start=(kt == 0), stop=(kt == DT - 1)),
                    waits=bw.read_waits() + bsrc.read_waits() +
                          (bps.write_waits() if kt == 0 else []), tag=tag + "_mm")
                bw.note_read(mm)
                bsrc.note_read(mm)
                bps.note_write_acc(mm)
            evac(mt, ps, bps)

    for ch in range(T // CH):
        s = ch * CH
        for j in range(DT):
            if ch == 0:
                c0 = P.op("vector", lambda e, j=j: e.tensor_copy(dd[:, 0:1], xl[:, j * T:j * T + 1]),
                          waits=bxl.read_waits() + bd.write_waits(), tag="ffn_d0")
                bxl.note_read(c0)
                bd.note_write(c0)
                c1 = P.op("vector", lambda e, j=j: e.tensor_sub(
                    dd[:, 1:CH], xl[:, j * T + 1:j * T + CH], xl[:, j * T:j * T + CH - 1]),
                    waits=bxl.read_waits() + [c0], tag="ffn_d1")
                bxl.note_read(c1)
                bd.note_write_acc(c1)
            else:
                c1 = P.op("vector", lambda e, j=j, s=s: e.tensor_sub(
                    dd[:, 0:CH], xl[:, j * T + s:j * T + s + CH],
                    xl[:, j * T + s - 1:j * T + s + CH - 1]),
                    waits=bxl.read_waits() + bd.write_waits(), tag="ffn_d1")
                bxl.note_read(c1)
                bd.note_write(c1)
            for (dst, bdst, mucol, tg) in ((xr, bxr, j, "ffn_xr"), (xk, bxk, DT + j, "ffn_xk")):
                if ch == 0:
                    m0 = P.op("vector", lambda e, j=j, dst=dst, mucol=mucol: e.tensor_scalar(
                        dst[:, j * CH:j * CH + 1], dd[:, 0:1], mu[:, mucol:mucol + 1], None, ALU.mult),
                        waits=bd.w + bmu.read_waits() + bdst.write_waits(), tag=tg + "0")
                    bd.note_read(m0)
                    bmu.note_read(m0)
                    bdst.note_write(m0)
                    m1 = P.op("vector", lambda e, j=j, dst=dst, mucol=mucol: e.scalar_tensor_tensor(
                        dst[:, j * CH + 1:(j + 1) * CH], dd[:, 1:CH], mu[:, mucol:mucol + 1],
                        xl[:, j * T:j * T + CH - 1], ALU.mult, ALU.add),
                        waits=bd.w + bxl.read_waits() + bmu.read_waits() + [m0], tag=tg + "1")
                    bd.note_read(m1)
                    bxl.note_read(m1)
                    bmu.note_read(m1)
                    bdst.note_write_acc(m1)
                else:
                    m1 = P.op("vector", lambda e, j=j, s=s, dst=dst, mucol=mucol: e.scalar_tensor_tensor(
                        dst[:, j * CH:(j + 1) * CH], dd[:, 0:CH], mu[:, mucol:mucol + 1],
                        xl[:, j * T + s - 1:j * T + s + CH - 1], ALU.mult, ALU.add),
                        waits=bd.w + bxl.read_waits() + bmu.read_waits() + bdst.write_waits(), tag=tg + "1")
                    bd.note_read(m1)
                    bxl.note_read(m1)
                    bmu.note_read(m1)
                    bdst.note_write(m1)

        def evac_r(mt, ps, bps):
            ev = P.op("scalar", lambda e, mt=mt, ps=ps: e.activation(
                r[:, mt * CH:(mt + 1) * CH], ps[0:128, 0:CH], AF.Sigmoid),
                waits=bps.w + br.write_waits(), tag="ffn_r_ev")
            bps.note_read(ev)
            br.note_write_acc(ev)
        chunk_proj(wr, xr, bxr, evac_r, "ffn_r")

        def evac_k(mt, ps, bps):
            e1 = P.op("scalar", lambda e, ps=ps: e.activation(
                t1[:, 0:CH], ps[0:128, 0:CH], AF.Relu),
                waits=bps.w + bt1.write_waits(), tag="ffn_k_relu")
            bps.note_read(e1)
            bt1.note_write(e1)
            e2 = P.op("vector", lambda e, mt=mt: e.tensor_mul(
                k2[:, mt * CH:(mt + 1) * CH], t1[:, 0:CH], t1[:, 0:CH]),
                waits=bt1.w + bk2.write_waits(), tag="ffn_k_sq")
            bt1.note_read(e2)
            bk2.note_write_acc(e2)
        chunk_proj(wk, xk, bxk, evac_k, "ffn_k")

        def evac_v(mt, ps, bps):
            e1 = P.op("vector", lambda e, mt=mt, ps=ps: e.tensor_mul(
                t1[:, 512:1024], ps[0:128, 0:CH], r[:, mt * CH:(mt + 1) * CH]),
                waits=bps.w + br.read_waits() + bt1.write_waits(), tag="ffn_v_mul")
            bps.note_read(e1)
            br.note_read(e1)
            bt1.note_write(e1)
            e2 = P.op("vector", lambda e, mt=mt, s=s: e.tensor_add(
                xres[:, mt * T + s:mt * T + s + CH], xres[:, mt * T + s:mt * T + s + CH],
                t1[:, 512:1024]),
                waits=bt1.w + bxres.write_waits(), tag="ffn_v_add")
            bt1.note_read(e2)
            bxres.note_write_acc(e2)
        chunk_proj(wv, k2, bk2, evac_v, "ffn_v")


def emit_ffn_small(C, w_in, w_out):
    P = C.plan
    xl = C.sb["x_ln"]; bxl = C.buf("x_ln")
    xres = C.sb["x_res"]; bxres = C.buf("x_res")
    vec = C.sb["l1ffn_vec"]; bvec = C.buf("l1ffn_vec")
    bout = C.sb["l1ffn_bout"]; bbout = C.buf("l1ffn_bout")
    s_t = C.sb["l1ffn_s"]; bs = C.buf("l1ffn_s")
    h_t = C.sb["l1ffn_h"]; bh = C.buf("l1ffn_h")
    h_t2 = C.sb["l1ffn_h2"]; bh2 = C.buf("l1ffn_h2")
    g_t = C.sb["l1ffn_g"]; bg = C.buf("l1ffn_g")
    gb_t = C.sb["l1ffn_gb"]; bgb = C.buf("l1ffn_gb")
    t1 = C.sb["ln_t1"]; bt1 = C.buf("ln_t1")

    for ch in range(4):
        s = ch * 512
        ps = C.ps["pa"]
        bps = C.buf("pa")
        for kt in range(DT):
            tile, bw = w_in.tile(kt, 0)
            mm = P.op("tensor", lambda e, tile=tile, kt=kt, s=s, ps=ps: e.matmul(
                ps[0:64, 0:512], tile, xl[:, kt * T + s:kt * T + s + 512],
                start=(kt == 0), stop=(kt == DT - 1)),
                waits=bw.read_waits() + bxl.read_waits() +
                      (bps.write_waits() if kt == 0 else []), tag="l1f_s")
            bw.note_read(mm)
            bxl.note_read(mm)
            bps.note_write_acc(mm)
        ev = P.op("vector", lambda e, s=s, ps=ps: e.tensor_scalar_add(
            s_t[:, s:s + 512], ps[0:64, 0:512], vec[:, 48:49]),
            waits=bps.w + bvec.read_waits() + bs.write_waits(), tag="l1f_s_ev")
        bps.note_read(ev)
        bvec.note_read(ev)
        bs.note_write_acc(ev)
    for i in range(16):
        hc_t, bhc_t = (h_t, bh) if i % 2 == 0 else (h_t2, bh2)
        hi = P.op("scalar", lambda e, i=i, hc_t=hc_t: e.activation(
            hc_t[:, :], s_t[:, :], AF.Relu, scale=vec[:, i:i + 1], bias=vec[:, 16 + i:16 + i + 1]),
            waits=bs.w + bvec.read_waits() + bhc_t.write_waits(), tag="l1f_h")
        bs.note_read(hi)
        bvec.note_read(hi)
        bhc_t.note_write(hi)
        if i == 0:
            gi = P.op("vector", lambda e, hc_t=hc_t: e.tensor_scalar(
                g_t[:, :], hc_t[:, :], vec[:, 32:33], vec[:, 49:50], ALU.mult, ALU.add),
                waits=bhc_t.w + bvec.read_waits() + bg.write_waits(), tag="l1f_g0")
        else:
            gi = P.op("vector", lambda e, i=i, hc_t=hc_t: e.scalar_tensor_tensor(
                g_t[:, :], hc_t[:, :], vec[:, 32 + i:32 + i + 1], g_t[:, :], ALU.mult, ALU.add),
                waits=bhc_t.w + bvec.read_waits() + bg.write_waits() + bg.read_waits(), tag="l1f_g")
        bhc_t.note_read(gi)
        bvec.note_read(gi)
        bg.note_write(gi)
    gc = P.op("vector", lambda e: e.tensor_copy(gb_t[:, :], g_t[:, :]),
              waits=bg.w + bgb.write_waits(), tag="l1f_gc")
    bg.note_read(gc)
    bgb.note_write(gc)
    for mt in range(DT):
        for ch in range(4):
            s = ch * 512
            ps = C.ps["pb"]
            bps = C.buf("pb")
            tile, bw = w_out.tile(0, mt)
            mm = P.op("tensor", lambda e, tile=tile, s=s, ps=ps: e.matmul(
                ps[0:128, 0:512], tile, gb_t[:, s:s + 512], start=True, stop=True),
                waits=bw.read_waits() + bgb.read_waits() + bps.write_waits(), tag="l1f_o")
            bw.note_read(mm)
            bgb.note_read(mm)
            bps.note_write(mm)
            e1 = P.op("vector", lambda e, mt=mt, ps=ps: e.tensor_scalar_add(
                t1[:, 0:512], ps[0:128, 0:512], bout[:, mt:mt + 1]),
                waits=bps.w + bbout.read_waits() + bt1.write_waits(), tag="l1f_o_b")
            bps.note_read(e1)
            bbout.note_read(e1)
            bt1.note_write(e1)
            e2 = P.op("vector", lambda e, mt=mt, s=s: e.tensor_add(
                xres[:, mt * T + s:mt * T + s + 512], xres[:, mt * T + s:mt * T + s + 512],
                t1[:, 0:512]),
                waits=bt1.w + bxres.write_waits(), tag="l1f_o_add")
            bt1.note_read(e2)
            bxres.note_write_acc(e2)


_CONSTS = host_constants()

SUBLAYER_SEQ = ["l0_attn_a", "l0_ffn", "l0_attn_b", "l0_ffn2",
                "l1_mamba", "l1_ffn", "l1_attn_b", "l1_ffn2"]



def _ensure_ws(C, slotw):
    if "ws_nat" not in C.sb:
        C.ws_slotw = slotw
        C.sbuf("ws_nat", (128, 2 * slotw), BF16)
        C.sbuf("ws_wt", (128, 2 * slotw), BF16)
        C.sbuf("ws_vT", (128, 8 * 128), BF16)


class WStreamT:

    def __init__(self, C, name, dram, rows, cols):
        self.C, self.name, self.dram = C, name, dram
        self.rows, self.cols = rows, cols
        self.KT = cols // 128
        _ensure_ws(C, max(cols, 1024))
        assert cols <= C.ws_slotw
        self.slot = 0

    def mt_tiles(self, mt, row0=0):
        C, P = self.C, self.C.plan
        sl = self.slot
        self.slot ^= 1
        nat = C.sb["ws_nat"]
        bn = C.buf("ws_nat")
        wt = C.sb["ws_wt"]
        bwt = C.buf("ws_wt")
        pt = C.ps["wt_ps"]
        bp = C.buf("wt_ps")
        identb = C.sb["identity_bf16"]
        r0 = row0 + mt * 128
        SLOTW = self.C.ws_slotw
        ld = P.dma("gpsimd", nat[:, sl * SLOTW: sl * SLOTW + self.cols],
                   self.dram[r0:r0 + 128, :], waits=bn.write_waits(), tag=f"{self.name}_ld")
        bn.note_write_acc(ld)
        for g0 in range(0, self.KT, 8):
            g1 = min(self.KT, g0 + 8)
            for kt in range(g0, g1):
                tp = P.op("tensor", lambda e, kt=kt, sl=sl, g0=g0: e.transpose(
                    pt[:, (kt - g0) * 128:(kt - g0 + 1) * 128],
                    nat[:, sl * SLOTW + kt * 128: sl * SLOTW + (kt + 1) * 128], identb[:, :]),
                    waits=[ld] + C.buf("identity_bf16").read_waits() + bp.write_waits(),
                    tag=f"{self.name}_tp")
                bn.note_read(tp)
                bp.note_write_acc(tp)
            ev = P.op("vector", lambda e, sl=sl, g0=g0, g1=g1: e.tensor_copy(
                wt[:, sl * SLOTW + g0 * 128: sl * SLOTW + g1 * 128],
                pt[:, 0:(g1 - g0) * 128]),
                waits=bp.w + bwt.write_waits(), tag=f"{self.name}_ev")
            bp.note_read(ev)
            bwt.note_write_acc(ev)
        return [(wt[:, sl * SLOTW + kt * 128: sl * SLOTW + (kt + 1) * 128], bwt)
                for kt in range(self.KT)]

    def v_block(self, g, row0, width=512):
        C, P = self.C, self.C.plan
        nat = C.sb["ws_nat"]
        bn = C.buf("ws_nat")
        vt = C.sb["ws_vT"]
        bvt = C.buf("ws_vT")
        pt = C.ps["wt_ps"]
        bp = C.buf("wt_ps")
        identb = C.sb["identity_bf16"]
        for rb in range(width // 128):
            r0 = row0 + g * width + rb * 128
            sl = self.slot
            self.slot ^= 1
            SLOTW = self.C.ws_slotw
            ld = P.dma("gpsimd", nat[:, sl * SLOTW: sl * SLOTW + self.cols],
                       self.dram[r0:r0 + 128, :], waits=bn.write_waits(), tag=f"{self.name}_vld")
            bn.note_write_acc(ld)
            for kt in range(self.KT):
                tp = P.op("tensor", lambda e, kt=kt, sl=sl: e.transpose(
                    pt[:, kt * 128:(kt + 1) * 128],
                    nat[:, sl * SLOTW + kt * 128: sl * SLOTW + (kt + 1) * 128], identb[:, :]),
                    waits=[ld] + bp.write_waits(), tag=f"{self.name}_vtp")
                bn.note_read(tp)
                bp.note_write_acc(tp)
            ev = P.op("vector", lambda e, rb=rb, width=width: e.tensor_copy(
                vt.rearrange("p (k w) -> p k w", w=width)[:, :, rb * 128:(rb + 1) * 128],
                pt.rearrange("p (k w) -> p k w", w=128)[:, 0:self.KT, :]),
                waits=bp.w + bvt.write_waits(), tag=f"{self.name}_vev")
            bp.note_read(ev)
            bvt.note_write_acc(ev)
        return vt, bvt

    def v_slice(self, vt, kt, width=512):
        return vt[:, kt * width:(kt + 1) * width]


class WStreamN:

    def __init__(self, C, name, dram, cin, cout):
        self.C, self.name, self.dram = C, name, dram
        self.cin, self.cout = cin, cout
        self.KT = cin // 128
        _ensure_ws(C, max(cin, 1024))
        assert cin <= C.ws_slotw
        self.slot = 0

    def mt_tiles(self, mt, row0=0):
        C, P = self.C, self.C.plan
        sl = self.slot
        self.slot ^= 1
        SLOTW = self.C.ws_slotw
        wt = C.sb["ws_wt"]
        bwt = C.buf("ws_wt")
        c0 = row0 + mt * 128
        for kt in range(self.KT):
            ld = P.dma("gpsimd", wt[:, sl * SLOTW + kt * 128: sl * SLOTW + (kt + 1) * 128],
                       self.dram[kt * 128:(kt + 1) * 128, c0:c0 + 128],
                       waits=bwt.write_waits() if kt == 0 else [], tag=f"{self.name}_ld")
            bwt.note_write_acc(ld)
        return [(wt[:, sl * SLOTW + kt * 128: sl * SLOTW + (kt + 1) * 128], bwt)
                for kt in range(self.KT)]

    def v_block(self, g, row0, width=512):
        C, P = self.C, self.C.plan
        vt = C.sb["ws_vT"]
        bvt = C.buf("ws_vT")
        c0 = row0 + g * width
        for kt in range(self.KT):
            ld = P.dma("gpsimd", vt[:, kt * width:(kt + 1) * width],
                       self.dram[kt * 128:(kt + 1) * 128, c0:c0 + width],
                       waits=bvt.write_waits() if kt == 0 else [], tag=f"{self.name}_vld")
            bvt.note_write_acc(ld)
        return vt, bvt

    def v_slice(self, vt, kt, width=512):
        return vt[:, kt * width:(kt + 1) * width]


def emit_attn(C, kind, streams, has_bias, causal, rope=False, relbias=False):
    P = C.plan
    xl = C.sb["x_ln"]; bxl = C.buf("x_ln")
    xres = C.sb["x_res"]; bxres = C.buf("x_res")
    v = C.sb["att_v"]; bv = C.buf("att_v")
    o = C.sb["att_o"]; bo = C.buf("att_o")
    pbuf = C.sb["att_p"]; bp = C.buf("att_p")
    for _nm in ("att_p0", "att_p1", "att_p2", "att_p3", "att_p4", "att_p5"):
        C.buf(_nm).w = sorted(set(C.buf(_nm).w + bp.write_waits()))
    den = C.sb["att_den"]; bden = C.buf("att_den")
    rbs = C.sb["att_rb"]; brb = C.buf("att_rb")
    onesf = C.sb["ones_f32"]; bonesf = C.buf("ones_f32")
    onesb = C.sb["ones_bf16"]; bonesb = C.buf("ones_bf16")
    tri = C.sb["tri_bf16"]; btri = C.buf("tri_bf16")
    fv = C.sb["fvals"]; bfv = C.buf("fvals")
    qk = [C.sb["att_qk0"], C.sb["att_qk1"]]
    bqk = [C.buf("att_qk0"), C.buf("att_qk1")]

    (s_qk, q_row0, k_row0) = streams["qk"]
    (s_v, v_row0) = streams["v"]
    (s_o,) = streams["o"]
    relbias_fn = streams.get("relbias_fn")
    if relbias:
        d2 = C.sb["rb_d2"]
        bd2 = C.buf("rb_d2")
    if has_bias:
        qkb = C.sb["att_qkb"]; bqkb = C.buf("att_qkb")
    if rope:
        cosb = C.sb["rope_cos"]; bcos = C.buf("rope_cos")
        sinb = C.sb["rope_sin"]; bsin = C.buf("rope_sin")
        rmat = C.sb["rope_R"]; brm = C.buf("rope_R")
        t1 = C.sb["ln_t1"]; bt1 = C.buf("ln_t1")

    ms = P.op("vector", lambda e: e.memset(v[:, :], 1.0), waits=bv.write_waits(), tag="att_v_ones")
    bv.note_write(ms)

    VSL = TT * 130
    for hpi in range(DT):
        sl = hpi % 2
        dst, bdst = qk[sl], bqk[sl]
        PS4 = ("pa", "pb", "pc", "pd")
        vtiles = s_v.mt_tiles(hpi, row0=v_row0)
        for kt in range(DT):
            tile, bw = vtiles[kt]
            for ch in range(4):
                s = ch * 512
                ps = C.ps[PS4[ch]]
                bps = C.buf(PS4[ch])
                mm = P.op("tensor", lambda e, tile=tile, kt=kt, s=s, ps=ps: e.matmul(
                    ps[0:128, 0:512], tile, xl[:, kt * T + s: kt * T + s + 512],
                    start=(kt == 0), stop=(kt == DT - 1)),
                    waits=bw.read_waits() + bxl.read_waits() +
                          (bps.write_waits() if kt == 0 else []), tag="att_vf_mm")
                bw.note_read(mm)
                bxl.note_read(mm)
                bps.note_write_acc(mm)
        for ch in range(4):
            s = ch * 512
            ps = C.ps[PS4[ch]]
            bps = C.buf(PS4[ch])
            if has_bias:
                ev = P.op("vector", lambda e, s=s, ps=ps, hpi=hpi: e.tensor_scalar_add(
                    dst[:, s:s + 512], ps[0:128, 0:512], qkb[:, 16 + hpi:16 + hpi + 1]),
                    waits=bps.w + bqkb.read_waits() + bdst.write_waits(), tag="att_vf_ev")
                bqkb.note_read(ev)
            else:
                ev = P.op("vector", lambda e, s=s, ps=ps: e.tensor_copy(
                    dst[:, s:s + 512], ps[0:128, 0:512]),
                    waits=bps.w + bdst.write_waits(), tag="att_vf_ev")
            bps.note_read(ev)
            bdst.note_write_acc(ev)
        identb_ = C.sb["identity_bf16"]
        wtp = C.ps["wt_ps"]
        bwtp = C.buf("wt_ps")
        for g in range(2):
            for ii in range(8):
                i = g * 8 + ii
                tp = P.op("tensor", lambda e, i=i, ii=ii: e.transpose(
                    wtp[:, ii * 128:(ii + 1) * 128], dst[:, i * 128:(i + 1) * 128],
                    identb_[:, :]),
                    waits=bdst.read_waits() + bdst.w + C.buf("identity_bf16").read_waits() +
                          bwtp.write_waits(), tag="att_vt")
                bdst.note_read(tp)
                C.buf("identity_bf16").note_read(tp)
                bwtp.note_write_acc(tp)
            ev = P.op("vector", lambda e, g=g, sl=sl: e.tensor_copy(
                v[:, sl * VSL + g * 8 * 130: sl * VSL + (g + 1) * 8 * 130]
                    .rearrange("p (a h c) -> p a h c", h=2, c=65)[:, :, :, 0:64],
                wtp[:, 0:1024].rearrange("p (a h c) -> p a h c", h=2, c=64)),
                waits=bwtp.w + bv.write_waits(), tag="att_v_ev")
            bwtp.note_read(ev)
            bv.note_write_acc(ev)
        PS4 = ("pa", "pb", "pc", "pd")
        for half, row0, bcol0 in ((0, q_row0, 0), (1, k_row0, DT)):
            tiles = s_qk.mt_tiles(hpi, row0=row0)
            for kt in range(DT):
                tile, bw = tiles[kt]
                for ch in range(4):
                    s = ch * 512
                    ps = C.ps[PS4[ch]]
                    bps = C.buf(PS4[ch])
                    mm = P.op("tensor", lambda e, tile=tile, kt=kt, s=s, ps=ps: e.matmul(
                        ps[0:128, 0:512], tile, xl[:, kt * T + s: kt * T + s + 512],
                        start=(kt == 0), stop=(kt == DT - 1)),
                        waits=bw.read_waits() + bxl.read_waits() +
                              (bps.write_waits() if kt == 0 else []), tag="att_qk_mm")
                    bw.note_read(mm)
                    bxl.note_read(mm)
                    bps.note_write_acc(mm)
            for ch in range(4):
                s = ch * 512
                ps = C.ps[PS4[ch]]
                bps = C.buf(PS4[ch])
                col = half * T + s
                if has_bias:
                    ev = P.op("vector", lambda e, dst=dst, col=col, ps=ps, bc=bcol0, hpi=hpi:
                        e.tensor_scalar_add(dst[:, col:col + 512], ps[0:128, 0:512],
                                            qkb[:, bc + hpi:bc + hpi + 1]),
                        waits=bps.w + bqkb.read_waits() + bdst.write_waits(), tag="att_qk_ev")
                    bqkb.note_read(ev)
                else:
                    ev = P.op("vector", lambda e, dst=dst, col=col, ps=ps: e.tensor_copy(
                        dst[:, col:col + 512], ps[0:128, 0:512]),
                        waits=bps.w + bdst.write_waits(), tag="att_qk_ev")
                bps.note_read(ev)
                bdst.note_write_acc(ev)
        if rope:
            for half in range(2):
                for ch in range(4):
                    s = half * T + ch * 512
                    rs = ch * 512
                    ps = C.ps[("pa", "pb")[ch % 2]]
                    bps = C.buf(("pa", "pb")[ch % 2])
                    mm = P.op("tensor", lambda e, dst=dst, s=s, ps=ps: e.matmul(
                        ps[0:128, 0:512], rmat[:, :], dst[:, s:s + 512],
                        start=True, stop=True),
                        waits=brm.read_waits() + bdst.read_waits() + bps.write_waits(), tag="rope_mm")
                    brm.note_read(mm)
                    bdst.note_read(mm)
                    bps.note_write(mm)
                    e1 = P.op("vector", lambda e, rs=rs, ps=ps: e.tensor_mul(
                        t1[:, 0:512], ps[0:128, 0:512], sinb[:, rs:rs + 512]),
                        waits=bps.w + bsin.read_waits() + bt1.write_waits(), tag="rope_m1")
                    bps.note_read(e1)
                    bsin.note_read(e1)
                    bt1.note_write(e1)
                    e2 = P.op("vector", lambda e, dst=dst, s=s, rs=rs: e.tensor_mul(
                        t1[:, 512:1024], dst[:, s:s + 512], cosb[:, rs:rs + 512]),
                        waits=bdst.read_waits() + bcos.read_waits() + [e1], tag="rope_m2")
                    bdst.note_read(e2)
                    bcos.note_read(e2)
                    bt1.note_write_acc(e2)
                    e3 = P.op("vector", lambda e, dst=dst, s=s: e.tensor_add(
                        dst[:, s:s + 512], t1[:, 0:512], t1[:, 512:1024]),
                        waits=bt1.w + bdst.write_waits(), tag="rope_m3")
                    bt1.note_read(e3)
                    bdst.note_write_acc(e3)

        if relbias:
            relbias_fn(2 * hpi + 0, 0)
            relbias_fn(2 * hpi + 1, 1)
        bp_slots = [C.buf(f"att_p{i}") for i in range(6)]
        AVPS = ("pa", "pb")
        SPS = ("pc", "pd")
        for qc in range(4):
            ki_hi = (4 * qc + 4) if causal else TT

            def emit_S(ki, hh):
                m = ki - 4 * qc
                v0 = m * 128 if (causal and m >= 0) else 0
                hoff = 64 * hh
                ps_s = C.ps[SPS[hh]]
                bss = C.buf(SPS[hh])
                mm = P.op("tensor", lambda e, dst=dst, hoff=hoff, ki=ki, qc=qc, v0=v0, ps_s=ps_s:
                    e.matmul(
                        ps_s[0:128, v0:512],
                        dst[hoff:hoff + 64, T + ki * 128:T + (ki + 1) * 128],
                        dst[hoff:hoff + 64, qc * 512 + v0:(qc + 1) * 512],
                        start=True, stop=True),
                    waits=bdst.read_waits() + bss.write_waits(), tag="att_s")
                bdst.note_read(mm)
                bss.note_write(mm)

            def emit_P(ki, hh):
                m = ki - 4 * qc
                v0 = m * 128 if (causal and m >= 0) else 0
                ps_s = C.ps[SPS[hh]]
                bss = C.buf(SPS[hh])
                slot = (ki % 3) + 3 * hh
                bsl = bp_slots[slot]
                ex = P.op("scalar", lambda e, slot=slot, v0=v0, ps_s=ps_s: e.activation(
                    pbuf[:, slot * 512 + v0:(slot + 1) * 512], ps_s[0:128, v0:512], AF.Exp,
                    scale=fv[:, 5:6]),
                    waits=bss.w + bsl.write_waits() + bfv.read_waits(), tag="att_exp")
                bss.note_read(ex)
                bsl.note_write(ex)
                if causal and 0 <= m < 4:
                    tm = P.op("vector", lambda e, slot=slot, v0=v0: e.tensor_mul(
                        pbuf[:, slot * 512 + v0:slot * 512 + v0 + 128],
                        pbuf[:, slot * 512 + v0:slot * 512 + v0 + 128], tri[:, :]),
                        waits=bsl.w + btri.read_waits(), tag="att_tri")
                    btri.note_read(tm)
                    bsl.note_write_acc(tm)
                if relbias and (ki * 128 + 127) - (qc * 512) >= 1026:
                    c0 = qc * 512 + 2047 - (ki * 128 + 127)
                    wm = P.op("vector", lambda e, slot=slot, c0=c0, hh=hh: e.tensor_mul(
                        pbuf[:, slot * 512:(slot + 1) * 512],
                        pbuf[:, slot * 512:(slot + 1) * 512],
                        d2[:, hh * 2048 + c0:hh * 2048 + c0 + 512]),
                        waits=bsl.w + bd2.read_waits(), tag="att_wedge")
                    bd2.note_read(wm)
                    bsl.note_write_acc(wm)

            def emit_AV(ki, hh):
                m = ki - 4 * qc
                v0 = m * 128 if (causal and m >= 0) else 0
                slot = (ki % 3) + 3 * hh
                bsl = bp_slots[slot]
                ps_av = C.ps[AVPS[hh]]
                bav = C.buf(AVPS[hh])
                av = P.op("tensor", lambda e, sl=sl, hh=hh, ki=ki, v0=v0, slot=slot, ps_av=ps_av, last=(ki == ki_hi - 1):
                    e.matmul(
                        ps_av[0:65, v0:512],
                        v[:, sl * VSL + ki * 130 + hh * 65: sl * VSL + ki * 130 + (hh + 1) * 65],
                        pbuf[:, slot * 512 + v0:(slot + 1) * 512],
                        start=(ki == 0), stop=last),
                    waits=bv.read_waits() + bsl.w +
                          (bav.write_waits() if ki == 0 else []), tag="att_av")
                bv.note_read(av)
                bsl.note_read(av)
                bav.note_write_acc(av)

            for hh in range(2):
                emit_S(0, hh)
            for hh in range(2):
                emit_P(0, hh)
            for ki in range(ki_hi):
                if ki + 1 < ki_hi:
                    emit_S(ki + 1, 0)
                    emit_S(ki + 1, 1)
                emit_AV(ki, 0)
                emit_AV(ki, 1)
                if ki + 1 < ki_hi:
                    emit_P(ki + 1, 0)
                    emit_P(ki + 1, 1)
            for hh in range(2):
                hoff = 64 * hh
                ps_av = C.ps[AVPS[hh]]
                bav = C.buf(AVPS[hh])
                d1 = P.op("scalar", lambda e, ps_av=ps_av: e.activation(
                    den[64:65, 0:512], ps_av[64:65, 0:512], AF.Ln),
                    waits=bav.w + bden.write_waits(), tag="att_den")
                bav.note_read(d1)
                bden.note_write(d1)
                denb = den.bitcast(BF16)
                d2o = P.op("scalar", lambda e, denb=denb: e.activation(
                    denb[64:65, 1024:1536], den[64:65, 0:512], AF.Exp, scale=fv[64:65, 4:5]),
                    waits=bden.w + bfv.read_waits(), tag="att_rec")
                bfv.note_read(d2o)
                bden.note_write_acc(d2o)
                rb = P.op("tensor", lambda e, denb=denb: e.matmul(
                    C.ps["pbig"][0:64, 0:512], onesb[64:65, 0:64], denb[64:65, 1024:1536],
                    start=True, stop=True),
                    waits=bden.w + bonesb.read_waits() + C.buf("pbig").write_waits(), tag="att_rb_mm")
                bden.note_read(rb)
                bonesb.note_read(rb)
                C.buf("pbig").note_write(rb)
                rc = P.op("vector", lambda e: e.tensor_copy(rbs[0:64, 0:512], C.ps["pbig"][0:64, 0:512]),
                          waits=C.buf("pbig").w + brb.write_waits(), tag="att_rb_cp")
                C.buf("pbig").note_read(rc)
                brb.note_write(rc)
                oe = P.op("vector", lambda e, hoff=hoff, hpi=hpi, qc=qc, ps_av=ps_av: e.tensor_mul(
                    o[hoff:hoff + 64, hpi * T + qc * 512:hpi * T + (qc + 1) * 512],
                    ps_av[0:64, 0:512], rbs[0:64, 0:512]),
                    waits=bav.w + brb.w + bo.write_waits(), tag="att_o_ev")
                bav.note_read(oe)
                brb.note_read(oe)
                bo.note_write_acc(oe)

    ob = C.sb["att_ob"]; bob = C.buf("att_ob")
    PS4 = ("pa", "pb", "pc", "pd")
    for mt in range(DT):
        tiles = s_o.mt_tiles(mt)
        for kt in range(DT):
            tile, bw = tiles[kt]
            for ch in range(4):
                s = ch * 512
                ps = C.ps[PS4[ch]]
                bps = C.buf(PS4[ch])
                mm = P.op("tensor", lambda e, tile=tile, kt=kt, s=s, ps=ps: e.matmul(
                    ps[0:128, 0:512], tile, o[:, kt * T + s: kt * T + s + 512],
                    start=(kt == 0), stop=(kt == DT - 1)),
                    waits=bw.read_waits() + bo.read_waits() +
                          (bps.write_waits() if kt == 0 else []), tag="att_o_mm")
                bw.note_read(mm)
                bo.note_read(mm)
                bps.note_write_acc(mm)
        for ch in range(4):
            s = ch * 512
            ps = C.ps[PS4[ch]]
            bps = C.buf(PS4[ch])
            ev = P.op("vector", lambda e, mt=mt, s=s, ps=ps: e.scalar_tensor_tensor(
                xres[:, mt * T + s:mt * T + s + 512], ps[0:128, 0:512], ob[:, mt:mt + 1],
                xres[:, mt * T + s:mt * T + s + 512], ALU.add, ALU.add),
                waits=bps.w + bob.read_waits() + bxres.write_waits(), tag="att_res")
            bps.note_read(ev)
            bob.note_read(ev)
            bxres.note_write_acc(ev)


def _alloc_attn_common(C, has_bias):
    C.salloc("att_qk0", (128, 2 * T), BF16)
    C.salloc("att_qk1", (128, 2 * T), BF16)
    C.salloc("att_v", (128, 2 * TT * 130), BF16)
    C.salloc("att_o", (128, DT * T), BF16)
    C.salloc("att_p", (128, 6 * 512), BF16)
    C.salloc("att_den", (65, 768), F32)
    C.salloc("att_rb", (64, 512), F32)
    if "wt_ps" not in C.ps:
        C.psum("wt_ps", (128, 1024), BF16)


def _load_bias_cols(C, din, name, vec):
    arr = np.ascontiguousarray(vec.reshape(DT, 128).T, dtype=np.float32)
    d = din(name + "_in", arr)
    return emit_load_const(C, name, d, arr.shape, F32)


def _load_vbias(C, din, name, vec):
    arr = np.ascontiguousarray(vec.reshape(1, 1024), dtype=np.float32)
    d = din(name + "_in", arr)
    h = C.salloc(name, (1, 1024), BF16)
    oid = C.plan.dma("gpsimd", h[0:1, :], d[0:1, :],
                     waits=C.buf(name).write_waits(), tag=f"load_{name}")
    C.buf(name).note_write(oid)
    return h


def prep_attn_l0a(C, W, din):
    _alloc_attn_common(C, True)
    b = W["L0_attn_a_b_qkv"]
    qkb = np.concatenate([b[0:1024].reshape(DT, 128).T, b[1024:2048].reshape(DT, 128).T,
                          b[2048:3072].reshape(DT, 128).T], axis=1)
    dq = din("att_qkb_in", np.ascontiguousarray(qkb, dtype=np.float32))
    emit_load_const(C, "att_qkb", dq, qkb.shape, F32)
    _load_bias_cols(C, din, "att_ob", W["L0_attn_a_b_o"])
    s_qkv = WStreamT(C, "wqkv", din("W_qkv", W["L0_attn_a_W_qkv"]), 3072, 1024)
    s_o = WStreamT(C, "wo", din("W_o", W["L0_attn_a_W_o"]), 1024, 1024)
    streams = {"qk": (s_qkv, 0, 1024), "v": (s_qkv, 2048), "o": (s_o,)}
    return lambda: emit_attn(C, "l0a", streams, has_bias=True, causal=True)


def prep_attn_l1b(C, W, din):
    _alloc_attn_common(C, False)
    zeros = np.zeros((128, DT), np.float32)
    dz = din("att_ob_in", zeros)
    emit_load_const(C, "att_ob", dz, zeros.shape, F32)
    pos = np.arange(T, dtype=np.float64)
    didx = np.arange(128) % 64
    inv = 1.0 / (10000.0 ** ((didx % 32) * 2.0 / 64.0))
    ang = inv[:, None] * pos[None, :]
    cosb = np.cos(ang).astype(np.float32)
    sinb = np.sin(ang).astype(np.float32)
    nc_din_bf16(C, din, "rope_cos", cosb)
    nc_din_bf16(C, din, "rope_sin", sinb)
    R = np.zeros((128, 128), np.float32)
    for half in range(2):
        b0 = 64 * half
        for m in range(64):
            if m < 32:
                R[b0 + m + 32, b0 + m] = -1.0
            else:
                R[b0 + m - 32, b0 + m] = 1.0
    dr = nc_din_bf16(C, din, "rope_R", R)
    s_qkv = WStreamT(C, "w1", din("W1", W["L1_attn_b_W1"]), 3072, 1024)
    s_o = WStreamT(C, "w2", din("W2", W["L1_attn_b_W2"]), 1024, 1024)
    streams = {"qk": (s_qkv, 0, 1024), "v": (s_qkv, 2048), "o": (s_o,)}
    return lambda: emit_attn(C, "l1b", streams, has_bias=False, causal=True, rope=True)


def nc_din_bf16(C, din, name, arr_f32):
    arr = np.ascontiguousarray(arr_f32, dtype=np.float32)
    d = din(name + "_in", arr)
    h = C.salloc(name, arr.shape, BF16)
    oid = C.plan.dma("gpsimd", h[0:arr.shape[0], :], d[:, :],
                     waits=C.buf(name).write_waits(), tag=f"load_{name}")
    C.buf(name).note_write(oid)
    return h


class QKDispatch:

    def __init__(self, sq, sk):
        self.sq, self.sk = sq, sk

    def mt_tiles(self, mt, row0=0):
        return (self.sq if row0 == 0 else self.sk).mt_tiles(mt, row0=0)


def prep_attn_l0b(C, W, din):
    _alloc_attn_common(C, True)
    qkb = np.concatenate([W["L0_attn_b_bq"].reshape(DT, 128).T,
                          W["L0_attn_b_bk"].reshape(DT, 128).T,
                          W["L0_attn_b_bv"].reshape(DT, 128).T], axis=1)
    dq = din("att_qkb_in", np.ascontiguousarray(qkb, dtype=np.float32))
    emit_load_const(C, "att_qkb", dq, qkb.shape, F32)
    _load_bias_cols(C, din, "att_ob", W["L0_attn_b_bo"])
    sq = WStreamN(C, "wq", din("Wq", W["L0_attn_b_Wq"]), 1024, 1024)
    sk = WStreamN(C, "wk", din("Wk", W["L0_attn_b_Wk"]), 1024, 1024)
    sv = WStreamN(C, "wv", din("Wv", W["L0_attn_b_Wv"]), 1024, 1024)
    so = WStreamN(C, "wo2", din("Wo", W["L0_attn_b_Wo"]), 1024, 1024)
    P = C.plan
    nc = C.nc
    rt_d = din("rel_table", W["L0_attn_b_rel_table"])
    eg = C.salloc_in("att_o", "rb_eg", (16, 2176), F32, 0)
    beg = C.buf("rb_eg")
    rt = C.salloc_in("att_o", "rb_rt", (128, 8 * 16), F32, 2176)
    brt = C.buf("rb_rt")
    C.salloc("rb_d2", (128, 2 * 2048), BF16)
    egpad_d = nc.dram_tensor("rb_egpad_d", [16, 2175], BF16, kind="Internal")
    egpad_ap = egpad_d.ap()
    ident = C.sb["identity_f32"]
    pbig = C.ps["pbig"]
    bpb = C.buf("pbig")

    def emit_rel_pre():
        ms = P.op("vector", lambda e: e.memset(eg[:, :], 1.0), waits=beg.write_waits(), tag="rb_ms")
        beg.note_write(ms)
        for t in range(8):
            r0 = t * 128
            r1 = min(1023, r0 + 128)
            nr = r1 - r0
            ld = P.dma("sync", rt[0:nr, t * 16:(t + 1) * 16], rt_d[r0:r1, :],
                       waits=brt.write_waits(), tag="rb_ld")
            brt.note_write_acc(ld)
            tp = P.op("tensor", lambda e, t=t, nr=nr: e.transpose(
                pbig[0:16, t * 128:t * 128 + nr], rt[0:nr, t * 16:(t + 1) * 16],
                ident[0:nr, 0:nr]),
                waits=[ld] + C.buf("identity_f32").read_waits() + bpb.write_waits(), tag="rb_tp")
            brt.note_read(tp)
            bpb.note_write_acc(tp)
        ev = P.op("vector", lambda e: e.tensor_copy(eg[0:16, 0:1023], pbig[0:16, 0:1023]),
                  waits=bpb.w + [ms], tag="rb_ev")
        bpb.note_read(ev)
        beg.note_write_acc(ev)
        ch = P.op("vector", lambda e: e.tensor_copy(eg[0:16, 2175:2176], eg[0:16, 1022:1023]),
                  waits=beg.w, tag="rb_ch")
        beg.note_read(ch)
        beg.note_write_acc(ch)
        sb = P.op("vector", lambda e: e.tensor_single_scalar(
            eg[0:16, 0:1023], eg[0:16, 0:1023], eg[0:16, 2175:2176], ALU.subtract),
            waits=beg.w, tag="rb_sub")
        beg.note_write_acc(sb)
        ex = P.op("scalar", lambda e: e.activation(eg[0:16, 0:1023], eg[0:16, 0:1023], AF.Exp),
                  waits=beg.w, tag="rb_exp")
        beg.note_write_acc(ex)
        st = P.dma("gpsimd", egpad_ap[:, :], eg[0:16, 0:2175], waits=beg.w + beg.r, tag="rb_st")
        beg.note_read(st)
        C.buf("rb_egpad_d").note_write(st)

    rev = C.sb["rev_bf16"]

    def relbias_fn(h, dslot=0):
        old_toks = []
        for nm in ("att_den", "att_rb", "rb_d2f"):
            b = C.bufs.get(nm)
            if b:
                old_toks += b.w + b.r
        if C.fused:
            d2f = C.salloc_in("att_den", "rb_d2f", (128, 2048), BF16, 0)
        else:
            d2f = C.sb["rb_d2f"] if "rb_d2f" in C.sb else C.sbuf("rb_d2f", (128, 2048), BF16)
        bd2 = C.buf("rb_d2")
        bd2f = C.buf("rb_d2f")
        bd2f.w = sorted(set(bd2f.w + old_toks))
        d2h = C.sb["rb_d2"]
        srcap = bass.AP(egpad_d, h * 2175, [[1, 128], [1, 2048]])
        dm = P.dma("scalar", d2f[:, :], srcap,
                   waits=C.buf("rb_egpad_d").read_waits() + bd2f.write_waits(), tag="rb_d2f")
        bd2f.note_write(dm)
        for cch in range(2):
            s = cch * 1024
            for half in range(2):
                mm = P.op("tensor", lambda e, s=s, half=half, d2f=d2f: e.matmul(
                    C.ps["pbig"][:, half * 512:(half + 1) * 512], rev[:, :],
                    d2f[:, s + half * 512:s + (half + 1) * 512], start=True, stop=True),
                    waits=bd2f.w + C.buf("rev_bf16").read_waits() +
                          C.buf("pbig").write_waits(), tag="rb_rev")
                bd2f.note_read(mm)
                C.buf("rev_bf16").note_read(mm)
                C.buf("pbig").note_write_acc(mm)
            ev = P.op("vector", lambda e, s=s, dslot=dslot: e.tensor_copy(
                d2h[:, dslot * 2048 + s:dslot * 2048 + s + 1024], C.ps["pbig"][:, 0:1024]),
                waits=C.buf("pbig").w + bd2.write_waits(), tag="rb_rev_ev")
            C.buf("pbig").note_read(ev)
            bd2.note_write_acc(ev)

    emit_rel_pre()
    if C.fused:
        C.arena_reset(["att_o"], ["rb_eg", "rb_rt", "att_o"])
    streams = {"qk": (QKDispatch(sq, sk), 0, 1), "v": (sv, 0), "o": (so,),
               "relbias_fn": relbias_fn}
    return lambda: emit_attn(C, "l0b", streams, has_bias=True, causal=False, relbias=True)



def prep_mamba(C, W, din):
    nc, P = C.nc, C.plan
    def cols(name, vec, per):
        arr = np.ascontiguousarray(vec.reshape(-1, 128).T, dtype=np.float32)
        d = din(name + "_in", arr)
        return emit_load_const(C, name, d, arr.shape, F32)
    cols("mb_cvb", W["L1_attn_a_conv_b"], 16)
    cols("mb_dtb", W["L1_attn_a_dt_proj_b"], 16)
    cols("mb_dcol", W["L1_attn_a_D"], 16)
    cvw = np.ascontiguousarray(
        W["L1_attn_a_conv_w"].reshape(16, 128, 4).transpose(1, 0, 2).reshape(128, 64),
        dtype=np.float32)
    emit_load_const(C, "mb_cvw", din("mb_cvw_in", cvw), cvw.shape, F32)
    alog = np.ascontiguousarray(
        W["L1_attn_a_A_log"].reshape(16, 128, 16).transpose(1, 0, 2).reshape(128, 256),
        dtype=np.float32)
    C.salloc("mb_cb", (128, 16 * 1024), BF16)
    if C.fused:
        C.sb["mb_alog"] = C.salloc_in("mb_cb", "mb_alog", (128, 256), F32, 0)
        alog_ap = emit_load_const(C, "mb_alog", din("mb_alog_in", alog), alog.shape, F32)
    else:
        alog_ap = emit_load_const(C, "mb_alog", din("mb_alog_in", alog), alog.shape, F32)
    acol = C.salloc("mb_acol", (128, 256), F32)
    e1 = P.op("scalar", lambda e: e.activation(acol[:, :], alog_ap[0:128, :], AF.Exp),
              waits=C.buf("mb_alog").read_waits(), tag="mb_acol_e")
    C.buf("mb_alog").note_read(e1)
    C.buf("mb_acol").note_write(e1)
    e2 = P.op("vector", lambda e: e.tensor_scalar_mul(acol[:, :], acol[:, :], C.sb["fvals"][:, 4:5]),
              waits=C.buf("mb_acol").w + C.buf("fvals").read_waits(), tag="mb_acol_n")
    C.buf("mb_acol").note_write(e2)
    if C.fused:
        C.arena_reset(["mb_cb"], ["mb_alog", "mb_cb"])
    eh = np.zeros((96, 32 * 128), np.float32)
    for n in range(16):
        eh[64 + n, n * 128:(n + 1) * 128] = 1.0
        eh[80 + n, (16 + n) * 128:(17 + n) * 128] = 1.0
    ehot_d = din("mb_ehot_in", eh)
    _ensure_ws(C, 2048)
    w_in = WStreamT(C, "win4", din("in_proj_w", W["L1_attn_a_in_proj_w"]), 4096, 1024)
    w_xp = WeightLoader(C, "wxp", din("x_proj_w", W["L1_attn_a_x_proj_w"]), 96, 2048, True)
    w_dt = WeightLoader(C, "wdt", din("dt_proj_w", W["L1_attn_a_dt_proj_w"]), 2048, 64, True)
    w_out = WStreamT(C, "wout4", din("out_proj_w", W["L1_attn_a_out_proj_w"]), 1024, 2048)
    w_dt.prefetch_all()
    xc_d = nc.dram_tensor("mb_xc_d", [2048, 2048], BF16, kind="Internal").ap()
    z_d = nc.dram_tensor("mb_z_d", [2048, 2048], BF16, kind="Internal").ap()
    g_d = nc.dram_tensor("mb_g_d", [2048, 2048], BF16, kind="Internal").ap()
    return lambda: emit_mamba(C, w_in, w_xp, w_dt, w_out, xc_d, z_d, g_d, ehot_d)


def emit_mamba(C, w_in, w_xp, w_dt, w_out, xc_d, z_d, g_d, ehot_d):
    P = C.plan
    xl = C.sb["x_ln"]; bxl = C.buf("x_ln")
    xres = C.sb["x_res"]; bxres = C.buf("x_res")
    fv = C.sb["fvals"]; bfv = C.buf("fvals")
    acol = C.sb["mb_acol"]; bacol = C.buf("mb_acol")
    cvw = C.sb["mb_cvw"]; cvb = C.sb["mb_cvb"]
    dtb = C.sb["mb_dtb"]; dcol = C.sb["mb_dcol"]
    ehot = C.sb["ws_nat"]; behot = C.buf("ws_nat")
    cbsb = C.sb["mb_cb"]; bcb = C.buf("mb_cb")
    bxcd = C.buf("mb_xc_d"); bzd = C.buf("mb_z_d"); bgd = C.buf("mb_g_d")
    identb = C.sb["identity_bf16"]

    mb0 = C.salloc_block(9728) if C.fused else None
    C.carve_rel(mb0, "w_wxp", 4100, 1536, BF16)
    w_xp.prefetch_all()
    xe = C.carve_rel(mb0, "mb_xe", 0, 2052, F32)
    bxe = C.buf("mb_xe")
    acc = C.carve_rel(mb0, "mb_acc", 2052, 2048, F32)
    bacc = C.buf("mb_acc")
    zstg = C.carve_rel(mb0, "mb_zstg", 2052, 2048, BF16)
    bzstg = C.buf("mb_zstg")
    xdblb = C.carve_rel(mb0, "mb_xdblb", 8704, 2048, BF16)
    bxdblb = C.buf("mb_xdblb")
    C.arena_reset(["mb_xe", "mb_acc", "mb_zstg", "mb_xdblb"],
                  ["ln_t1", "ln_sq", "ln_rows"])
    XPPS = (("pc", 0), ("pd", 0), ("pbig", 0), ("pbig", 512))
    for mt in range(32):
        dt = mt % 16
        is_x = mt < 16
        if mt == 16:
            C.arena_reset(["mb_zstg"], ["mb_xe", "mb_acc"])
        tiles = w_in.mt_tiles(mt)
        if is_x:
            z3 = P.op("vector", lambda e: e.memset(xe[:, 0:3], 0.0),
                      waits=bxe.write_waits(), tag="mb_xe0")
            bxe.note_write(z3)
        PS3 = ("pa", "pb", "pe")
        for ch in range(4):
            s = ch * 512
            ps = C.ps[PS3[ch % 3]]
            bps = C.buf(PS3[ch % 3])
            for kt in range(DT):
                tile, bw = tiles[kt]
                mm = P.op("tensor", lambda e, tile=tile, kt=kt, s=s, ps=ps: e.matmul(
                    ps[0:128, 0:512], tile, xl[:, kt * T + s:kt * T + s + 512],
                    start=(kt == 0), stop=(kt == DT - 1)),
                    waits=bw.read_waits() + bxl.read_waits() +
                          (bps.write_waits() if kt == 0 else []), tag="mb_inp")
                bw.note_read(mm)
                bxl.note_read(mm)
                bps.note_write_acc(mm)
            if is_x:
                ev = P.op("scalar", lambda e, s=s, ps=ps: e.activation(
                    xe[:, 3 + s:3 + s + 512], ps[0:128, 0:512], AF.Copy),
                    waits=bps.w + bxe.write_waits(), tag="mb_xe_ev")
                bps.note_read(ev)
                bxe.note_write_acc(ev)
            else:
                ev = P.op("scalar", lambda e, s=s, ps=ps: e.activation(
                    zstg[:, s:s + 512], ps[0:128, 0:512], AF.Silu),
                    waits=bps.w + bzstg.write_waits(), tag="mb_z_ev")
                bps.note_read(ev)
                bzstg.note_write_acc(ev)
        if is_x:
            c0 = P.op("vector", lambda e, dt=dt: e.tensor_scalar(
                acc[:, :], xe[:, 3:2051], cvw[:, dt * 4 + 3:dt * 4 + 4], None, ALU.mult),
                waits=bxe.w + C.buf("mb_cvw").read_waits() + bacc.write_waits(), tag="mb_cv3")
            bxe.note_read(c0)
            bacc.note_write(c0)
            for j in (2, 1, 0):
                cj = P.op("vector", lambda e, dt=dt, j=j: e.scalar_tensor_tensor(
                    acc[:, :], xe[:, j:j + 2048], cvw[:, dt * 4 + j:dt * 4 + j + 1],
                    acc[:, :], ALU.mult, ALU.add),
                    waits=bxe.w + bacc.w + C.buf("mb_cvw").read_waits(), tag="mb_cv")
                bxe.note_read(cj)
                bacc.note_write(cj)
            xco = xe[:, 0:1024].bitcast(BF16)
            sl = P.op("scalar", lambda e, dt=dt: e.activation(
                xco[:, 0:2048], acc[:, :], AF.Silu, bias=cvb[:, dt:dt + 1]),
                waits=bacc.w + C.buf("mb_cvb").read_waits() + bxe.write_waits(), tag="mb_silu")
            bacc.note_read(sl)
            bxe.note_write(sl)
            st = P.dma("sync", xc_d[dt * 128:(dt + 1) * 128, :], xco[:, 0:2048],
                       waits=[sl] + bxcd.write_waits(), tag="mb_xc_st")
            bxe.note_read(st)
            bxcd.note_write_acc(st)
            for ch in range(4):
                s = ch * 512
                pnm, po = XPPS[ch]
                ps = C.ps[pnm]
                bps = C.buf(pnm)
                tile, bw = w_xp.tile(dt, 0)
                mm = P.op("tensor", lambda e, tile=tile, s=s, ps=ps, po=po, dt=dt: e.matmul(
                    ps[0:96, po:po + 512], tile, xco[:, s:s + 512],
                    start=(dt == 0), stop=(dt == 15), skip_group_check=True),
                    waits=bw.read_waits() + [sl] +
                          (bps.write_waits() if dt == 0 else []), tag="mb_xp")
                bw.note_read(mm)
                bxe.note_read(mm)
                bps.note_write_acc(mm)
        else:
            st = P.dma("sync", z_d[dt * 128:(dt + 1) * 128, :], zstg[:, 0:2048],
                       waits=bzstg.w + bzd.write_waits(), tag="mb_z_st")
            bzstg.note_read(st)
            bzd.note_write_acc(st)
    for ch in range(4):
        pnm, po = XPPS[ch]
        ps = C.ps[pnm]
        bps = C.buf(pnm)
        ev = P.op("scalar", lambda e, ch=ch, ps=ps, po=po: e.activation(
            xdblb[0:96, ch * 512:(ch + 1) * 512], ps[0:96, po:po + 512], AF.Copy),
            waits=bps.w + bxdblb.write_waits(), tag="mb_xdbl_ev")
        bps.note_read(ev)
        bxdblb.note_write_acc(ev)
    ehl = P.dma("gpsimd", ehot[0:96, 0:4096], ehot_d[:, :],
                waits=behot.write_waits(), tag="mb_ehot_ld")
    behot.note_write(ehl)

    dA = C.carve_rel(mb0, "mb_dA", 0, 1024, F32)
    bdA = C.buf("mb_dA")
    dBu = C.carve_rel(mb0, "mb_dBu", 1024, 1024, BF16)
    bdBu = C.buf("mb_dBu")
    hbf = C.carve_rel(mb0, "mb_h", 1536, 1024, BF16)
    bh = C.buf("mb_h")
    hc = C.carve_rel(mb0, "mb_hc", 2048, 1024, BF16)
    bhc = C.buf("mb_hc")
    gate1 = C.carve_rel(mb0, "mb_g1", 4096, 1024, F32)
    bg1 = C.buf("mb_g1")
    spt = C.carve_rel(mb0, "mb_spt", 6144, 512, F32)
    bspt = C.buf("mb_spt")
    zg1, bzg1 = C.carve_rel(mb0, "mb_zg0", 2560, 1024, BF16), C.buf("mb_zg0")
    zgs = [(zg1, bzg1), (zg1, bzg1)]
    ubs = [(C.carve_rel(mb0, "mb_u0", 3072, 1024, BF16), C.buf("mb_u0")),
           (C.carve_rel(mb0, "mb_u1", 6656, 1024, BF16), C.buf("mb_u1"))]
    xc2s = [(C.carve_rel(mb0, "mb_xc20", 3584, 1024, BF16), C.buf("mb_xc20")),
            (C.carve_rel(mb0, "mb_xc21", 7168, 1024, BF16), C.buf("mb_xc21"))]
    deltas = [(C.carve_rel(mb0, "mb_delta0", 5120, 1024, F32), C.buf("mb_delta0")),
              (C.carve_rel(mb0, "mb_delta1", 7680, 1024, F32), C.buf("mb_delta1"))]
    if C.fused:
        dA2 = C.salloc_in("ln_sq", "mb_dA2", (128, 1024), F32, 0)
        dBu2 = C.salloc_in("ln_sq", "mb_dBu2", (128, 1024), BF16, 1024)
        hbf2 = C.salloc_in("ln_sq", "mb_h2", (128, 1024), BF16, 1536)
        hc2 = C.salloc("mb_hc2", (128, 1024), BF16)
        C.arena_reset(["mb_dA2", "mb_dBu2", "mb_h2"],
                      ["ln_sq", "ln_rows", "mb_dA2", "mb_dBu2", "mb_h2"])
    else:
        dA2, dBu2, hbf2, hc2 = dA, dBu, hbf, hc
    bdA2 = C.buf("mb_dA2") if C.fused else bdA
    bdBu2 = C.buf("mb_dBu2") if C.fused else bdBu
    bh2 = C.buf("mb_h2") if C.fused else bh
    bhc2 = C.buf("mb_hc2") if C.fused else bhc
    C.arena_reset(["mb_dA", "mb_dBu", "mb_h", "mb_hc", "mb_zg0",
                   "mb_u0", "mb_u1", "mb_xc20", "mb_xc21", "mb_g1", "mb_spt",
                   "mb_delta0", "mb_delta1"],
                  ["mb_xe", "mb_acc", "mb_zstg", "w_wxp"])
    hstate = C.salloc("mb_hst", (128, 256), F32)
    bhst = C.buf("mb_hst")
    bbsb, bbb = xl, bxl
    py = C.ps["pbig"]
    bpy = C.buf("pbig")

    for half in range(2):
        hs = half * 1024
        for n in range(NH):
            for (base, dst, bdst) in ((0, bbsb, bbb), (16, cbsb, bcb)):
                for q in range(2):
                    ps = C.ps[("pa", "pb")[q]]
                    bps = C.buf(("pa", "pb")[q])
                    mm = P.op("tensor", lambda e, base=base, n=n, q=q, hs=hs, ps=ps: e.matmul(
                        ps[0:128, 0:512],
                        ehot[0:96, (base + n) * 128:(base + n + 1) * 128],
                        xdblb[0:96, hs + q * 512:hs + (q + 1) * 512],
                        start=True, stop=True),
                        waits=behot.read_waits() + bxdblb.read_waits() + bps.write_waits(),
                        tag="mb_sel")
                    behot.note_read(mm)
                    bxdblb.note_read(mm)
                    bps.note_write(mm)
                    ev = P.op("scalar", lambda e, n=n, q=q, dst=dst, ps=ps: e.activation(
                        dst[:, n * 1024 + q * 512:n * 1024 + (q + 1) * 512],
                        ps[0:128, 0:512], AF.Copy),
                        waits=bps.w + bdst.write_waits(), tag="mb_sel_ev")
                    bps.note_read(ev)
                    bdst.note_write_acc(ev)
        def emit_ldz(phalf, pdt):
            phs = phalf * 1024
            pzg, pbzg = zgs[pdt % 2]
            ldz = P.dma("gpsimd", pzg[:, 0:1024],
                        z_d[pdt * 128:(pdt + 1) * 128, phs:phs + 1024],
                        waits=bzd.w + pbzg.write_waits(), tag="mb_z_ld")
            bzd.note_read(ldz)
            pbzg.note_write(ldz)

        def prep(phalf, pdt):
            phs = phalf * 1024
            psl = pdt % 2
            pxc, pbxc = xc2s[psl]
            pdl, pbdl = deltas[psl]
            pub, pbub = ubs[psl]
            ldx = P.dma("gpsimd", pxc[:, 0:1024],
                        xc_d[pdt * 128:(pdt + 1) * 128, phs:phs + 1024],
                        waits=bxcd.w + pbxc.write_waits(), tag="mb_xc_ld")
            bxcd.note_read(ldx)
            pbxc.note_write(ldx)
            for ch in range(2):
                s = ch * 512
                ps = C.ps[("pc", "pd")[ch % 2]]
                bps = C.buf(("pc", "pd")[ch % 2])
                tile, bw = w_dt.tile(0, pdt)
                mm = P.op("tensor", lambda e, tile=tile, s=s, phs=phs, ps=ps: e.matmul(
                    ps[0:128, 0:512], tile, xdblb[0:64, phs + s:phs + s + 512],
                    start=True, stop=True),
                    waits=bw.read_waits() + bxdblb.read_waits() + bps.write_waits(), tag="mb_dt")
                bw.note_read(mm)
                bxdblb.note_read(mm)
                bps.note_write(mm)
                o1 = P.op("scalar", lambda e, s=s, pdt=pdt, ps=ps, pdl=pdl: e.activation(
                    pdl[:, s:s + 512], ps[0:128, 0:512], AF.Relu,
                    bias=dtb[:, pdt:pdt + 1]),
                    waits=bps.w + C.buf("mb_dtb").read_waits() + pbdl.write_waits(),
                    tag="mb_sp_r")
                bps.note_read(o1)
                pbdl.note_write_acc(o1)
                o2 = P.op("scalar", lambda e, pdt=pdt, ps=ps: e.activation(
                    spt[:, :], ps[0:128, 0:512], AF.Abs, bias=dtb[:, pdt:pdt + 1]),
                    waits=bps.w + C.buf("mb_dtb").read_waits() + bspt.write_waits(),
                    tag="mb_sp_abs")
                bps.note_read(o2)
                bspt.note_write(o2)
                o3 = P.op("scalar", lambda e: e.activation(spt[:, :], spt[:, :], AF.Exp,
                                                           scale=fv[:, 4:5]),
                          waits=bspt.w + bfv.read_waits(), tag="mb_sp_e")
                bspt.note_write_acc(o3)
                o4 = P.op("scalar", lambda e: e.activation(spt[:, :], spt[:, :], AF.Ln, bias=1.0),
                          waits=[o3], tag="mb_sp_l")
                bspt.note_write_acc(o4)
                o6 = P.op("vector", lambda e, s=s, pdl=pdl: e.tensor_add(
                    pdl[:, s:s + 512], pdl[:, s:s + 512], spt[:, :]),
                    waits=bspt.w + pbdl.write_waits() + pbdl.read_waits(), tag="mb_sp_f")
                bspt.note_read(o6)
                pbdl.note_write_acc(o6)
            uo = P.op("vector", lambda e, pub=pub, pdl=pdl, pxc=pxc: e.tensor_mul(
                pub[:, 0:1024], pdl[:, 0:1024], pxc[:, 0:1024]),
                      waits=pbdl.w + pbxc.w + pbub.write_waits(), tag="mb_u")
            pbdl.note_read(uo)
            pbxc.note_read(uo)
            pbub.note_write(uo)

        for dt in range(16):
            sl = dt % 2
            xcb2, bxc2 = xc2s[sl]
            zg, bzg = zgs[sl]
            delta, bdelta = deltas[sl]
            ub, bub = ubs[sl]
            if dt == 0 and half == 0:
                prep(0, 0)
                emit_ldz(0, 0)
            for n in range(NH):
                if n == 3:
                    if dt < 15:
                        prep(half, dt + 1)
                    elif half == 0:
                        prep(1, 0)
                dslot = n % 2
                dA_c, bdA_c = (dA, bdA) if dslot == 0 else (dA2, bdA2)
                da = P.op("scalar", lambda e, dt=dt, n=n, dA_c=dA_c, delta=delta: e.activation(
                    dA_c[:, 0:1024], delta[:, 0:1024], AF.Exp,
                    scale=acol[:, dt * 16 + n:dt * 16 + n + 1]),
                    waits=bdelta.read_waits() + bacol.read_waits() + bdA_c.write_waits(),
                    tag="mb_dA")
                bdelta.note_read(da)
                bacol.note_read(da)
                bdA_c.note_write(da)
                dslot = n % 2
                dBu_c, bdBu_c = (dBu, bdBu) if dslot == 0 else (dBu2, bdBu2)
                db = P.op("vector", lambda e, n=n, dBu_c=dBu_c, ub=ub: e.tensor_mul(
                    dBu_c[:, 0:1024], ub[:, 0:1024], bbsb[:, n * 1024:(n + 1) * 1024]),
                    waits=bub.read_waits() + bbb.read_waits() + bdBu_c.write_waits(), tag="mb_dBu")
                bub.note_read(db)
                bbb.note_read(db)
                bdBu_c.note_write(db)
                if half == 0:
                    ini = 0.0
                    iw = []
                else:
                    ini = hstate[:, dt * 16 + n:dt * 16 + n + 1]
                    iw = bhst.read_waits()
                h_c, bh_c = (hbf, bh) if dslot == 0 else (hbf2, bh2)
                hc_c, bhc_c = (hc, bhc) if dslot == 0 else (hc2, bhc2)
                sc = P.op("vector", lambda e, ini=ini, dA_c=dA_c, dBu_c=dBu_c, h_c=h_c: e.tensor_tensor_scan(
                    h_c[:, 0:1024], dA_c[:, 0:1024], dBu_c[:, 0:1024], ini, ALU.mult, ALU.add),
                    waits=bdA_c.w + bdBu_c.w + iw + bh_c.write_waits(), tag="mb_scan")
                bdA_c.note_read(sc)
                bdBu_c.note_read(sc)
                if half == 1:
                    bhst.note_read(sc)
                bh_c.note_write(sc)
                if half == 0:
                    hs_cp = P.op("scalar", lambda e, dt=dt, n=n, h_c=h_c: e.activation(
                        hstate[:, dt * 16 + n:dt * 16 + n + 1], h_c[:, 1023:1024], AF.Copy),
                        waits=bh_c.w + bhst.write_waits(), tag="mb_hst")
                    bh_c.note_read(hs_cp)
                    bhst.note_write_acc(hs_cp)
                hco = P.op("vector", lambda e, n=n, h_c=h_c, hc_c=hc_c: e.tensor_mul(
                    hc_c[:, 0:1024], h_c[:, 0:1024], cbsb[:, n * 1024:(n + 1) * 1024]),
                    waits=bh_c.w + bcb.read_waits() + bhc_c.write_waits(), tag="mb_hc")
                bh_c.note_read(hco)
                bcb.note_read(hco)
                bhc_c.note_write(hco)
                for q in range(2):
                    mm = P.op("tensor", lambda e, q=q, n=n, hc_c=hc_c: e.matmul(
                        py[:, q * 512:(q + 1) * 512], identb[:, :],
                        hc_c[:, q * 512:(q + 1) * 512],
                        start=(n == 0), stop=(n == NH - 1)),
                        waits=C.buf("identity_bf16").read_waits() + bhc_c.w +
                              (bpy.write_waits() if n == 0 else []), tag="mb_y")
                    C.buf("identity_bf16").note_read(mm)
                    bhc_c.note_read(mm)
                    bpy.note_write_acc(mm)
            g1a = P.op("scalar", lambda e, dt=dt, xcb2=xcb2: e.activation(
                gate1[:, 0:1024], xcb2[:, 0:1024], AF.Copy, scale=dcol[:, dt:dt + 1]),
                waits=bxc2.read_waits() + C.buf("mb_dcol").read_waits() +
                      bg1.write_waits(), tag="mb_g1a")
            bxc2.note_read(g1a)
            bg1.note_write(g1a)
            g1b = P.op("vector", lambda e: e.tensor_add(
                gate1[:, 0:1024], gate1[:, 0:1024], py[:, 0:1024]),
                waits=bg1.w + bpy.w, tag="mb_g1b")
            bpy.note_read(g1b)
            bg1.note_write_acc(g1b)
            go = P.op("vector", lambda e, zg=zg: e.tensor_mul(hc[:, 0:1024], gate1[:, 0:1024],
                                                              zg[:, 0:1024]),
                      waits=bg1.w + bzg.read_waits() + bhc.write_waits(), tag="mb_gate")
            bg1.note_read(go)
            bzg.note_read(go)
            bhc.note_write(go)
            st = P.dma("sync", g_d[dt * 128:(dt + 1) * 128, hs:hs + 1024], hc[:, 0:1024],
                       waits=bhc.w + bgd.write_waits(), tag="mb_g_st")
            bhc.note_read(st)
            bgd.note_write_acc(st)
            if dt < 15:
                emit_ldz(half, dt + 1)
            elif half == 0:
                emit_ldz(1, 0)

    for dtk in range(16):
        dst, bdst = (bbsb, bbb) if dtk < 8 else (cbsb, bcb)
        off = (dtk % 8) * 2048
        ld = P.dma("gpsimd", dst[:, off:off + 2048], g_d[dtk * 128:(dtk + 1) * 128, :],
                   waits=bgd.w + bdst.write_waits(), tag="mb_g_ld")
        bgd.note_read(ld)
        bdst.note_write_acc(ld)
    PS4 = ("pa", "pb", "pc", "pd")
    for mt in range(DT):
        tiles = w_out.mt_tiles(mt)
        for kt in range(16):
            tile, bw = tiles[kt]
            hsb, bsrc = (bbsb, bbb) if kt < 8 else (cbsb, bcb)
            for ch in range(4):
                s = ch * 512
                ps = C.ps[PS4[ch]]
                bps = C.buf(PS4[ch])
                srcap = hsb[:, (kt % 8) * 2048 + s:(kt % 8) * 2048 + s + 512]
                mm = P.op("tensor", lambda e, tile=tile, srcap=srcap, ps=ps, kt=kt: e.matmul(
                    ps[0:128, 0:512], tile, srcap, start=(kt == 0), stop=(kt == 15)),
                    waits=bw.read_waits() + bsrc.read_waits() +
                          (bps.write_waits() if kt == 0 else []), tag="mb_out")
                bw.note_read(mm)
                bsrc.note_read(mm)
                bps.note_write_acc(mm)
        for ch in range(4):
            s = ch * 512
            ps = C.ps[PS4[ch]]
            bps = C.buf(PS4[ch])
            ev = P.op("vector", lambda e, mt=mt, s=s, ps=ps: e.tensor_add(
                xres[:, mt * T + s:mt * T + s + 512], xres[:, mt * T + s:mt * T + s + 512],
                ps[0:128, 0:512]),
                waits=bps.w + bxres.write_waits(), tag="mb_res")
            bps.note_read(ev)
            bxres.note_write_acc(ev)


def prep_ffn_rwkv(C, W, din):
    C.salloc("ffn_xr", (128, DT * 512), BF16)
    C.salloc("ffn_xk", (128, DT * 512), BF16)
    C.salloc("ffn_d", (128, 512), BF16)
    C.salloc("ffn_r", (128, DT * 512), BF16)
    C.salloc("ffn_k2", (128, DT * 512), BF16)
    mu_cols = np.concatenate([W["L0_ffn_mu_r"].reshape(DT, 128).T,
                              W["L0_ffn_mu_k"].reshape(DT, 128).T], axis=1)
    dmu = din("mu_rk_in", np.ascontiguousarray(mu_cols, dtype=np.float32))
    emit_load_const(C, "mu_rk", dmu, mu_cols.shape, F32)
    _ensure_ws(C, 2048)
    wr = WeightLoader(C, "wr", din("W_r", W["L0_ffn_W_r"]), D, D, True)
    wk = WeightLoader(C, "wk", din("W_k", W["L0_ffn_W_k"]), D, D, True)
    wv = WvPrefetch(C, "wvs", din("W_v", W["L0_ffn_W_v"]))
    wr.prefetch_all()
    wk.prefetch_all()
    wv.prefetch_all()
    return lambda: emit_ffn_rwkv(C, wr, wk, wv)


def prep_ffn_small(C, W, din):
    v = np.zeros((64, 50), np.float32)
    v[:, 0:16] = W["L1_ffn_W1"]
    v[:, 16:32] = W["L1_ffn_b1"]
    v[:, 32:48] = W["L1_ffn_W2"]
    v[:, 48] = W["L1_ffn_bin_"]
    v[:, 49] = W["L1_ffn_b2"]
    dv = din("l1ffn_vec_in", v)
    emit_load_const(C, "l1ffn_vec", dv, v.shape, F32)
    bo = np.ascontiguousarray(W["L1_ffn_bout"].reshape(DT, 128).T, dtype=np.float32)
    dbo = din("l1ffn_bout_in", bo)
    emit_load_const(C, "l1ffn_bout", dbo, bo.shape, F32)
    C.salloc("l1ffn_s", (64, T), F32)
    C.salloc("l1ffn_h", (64, T), F32)
    C.salloc("l1ffn_h2", (64, T), F32)
    C.salloc("l1ffn_g", (64, T), F32)
    C.salloc("l1ffn_gb", (64, T), BF16)
    w_in = WeightLoader(C, "win", din("Win", W["L1_ffn_Win"]), D, 64, False)
    w_out = WeightLoader(C, "wout", din("Wout", W["L1_ffn_Wout"]), 64, D, False)
    w_in.prefetch_all()
    w_out.prefetch_all()
    return lambda: emit_ffn_small(C, w_in, w_out)


SUBLAYER_PREP = {
    "l0_ffn": prep_ffn_rwkv,
    "l0_ffn2": prep_ffn_rwkv,
    "l1_ffn": prep_ffn_small,
    "l1_ffn2": prep_ffn_small,
    "l0_attn_a": prep_attn_l0a,
    "l0_attn_b": prep_attn_l0b,
    "l1_attn_b": prep_attn_l1b,
    "l1_mamba": prep_mamba,
}


def build_stage(stage, W):
    nc = bass.Bass("TRN2", target_bir_lowering=False, debug=False)
    st = contextlib.ExitStack()
    plan = Plan(nc)
    C = Ctx(nc, plan, st)
    feeds = {}

    def din(name, arr):
        arr = np.ascontiguousarray(arr, dtype=np.float32)
        ap = nc.dram_tensor(name, list(arr.shape), F32, kind="ExternalInput").ap()
        feeds[name] = arr
        return ap

    with st:
        sems = {e: st.enter_context(nc.semaphore("s_" + e)) for e in ENGINES}
        dma_sems = {q: st.enter_context(nc.semaphore("d_" + q)) for q in DMA_QUEUES}

        for name, arr in _CONSTS.items():
            dt = BF16 if arr.dtype == BF16NP else F32
            d = nc.dram_tensor("c_" + name, list(arr.shape), dt, kind="ExternalInput").ap()
            feeds["c_" + name] = arr
            emit_load_const(C, name, d, arr.shape, dt)

        C.sbuf("x_res", (128, DT * T), F32)
        final_stores = []

        if stage == "in_tx":
            C.psum("pbig", (128, 1024), F32)
            x_in = nc.dram_tensor("x_in", [T, D], F32, kind="ExternalInput").ap()
            feeds["x_in"] = "X_TOKEN"
            xfm_out = nc.dram_tensor("xfm_out", [128, DT * T], F32, kind="ExternalOutput").ap()
            emit_in_transform(C, x_in)
            bx = C.buf("x_res")
            sto = plan.dma("sync", xfm_out[:, :], C.sb["x_res"][:, :],
                           waits=bx.w, tag="xfm_st")
            final_stores.append(sto)
        elif stage == "out_tx":
            C.psum("pbig", (128, 1024), F32)
            xfm_in = nc.dram_tensor("xfm_in", [128, DT * T], F32, kind="ExternalInput").ap()
            feeds["xfm_in"] = "XFM"
            out = nc.dram_tensor("out", [T, D], F32, kind="ExternalOutput").ap()
            ld = plan.dma("sync", C.sb["x_res"][:, :], xfm_in[:, :], tag="xfm_ld")
            C.buf("x_res").note_write(ld)
            final_stores += emit_out_transform(C, out)
        else:
            xfm_in = nc.dram_tensor("xfm_in", [128, DT * T], F32, kind="ExternalInput").ap()
            feeds["xfm_in"] = "XFM"
            xfm_out = nc.dram_tensor("xfm_out", [128, DT * T], F32, kind="ExternalOutput").ap()
            ld = plan.dma("sync", C.sb["x_res"][:, :], xfm_in[:, :], tag="xfm_ld")
            C.buf("x_res").note_write(ld)

            C.sbuf("x_ln", (128, DT * T), BF16)
            if stage == "l1_mamba":
                C.sbuf("mb_arena", (128, 9728), F32)
                C.carve("mb_arena", "ln_t1", 0, 1024, F32)
                C.carve("mb_arena", "ln_sq", 1024, 512, F32)
                C.carve("mb_arena", "ln_rows", 1536, 1536, F32)
            else:
                C.sbuf("ln_sq", (128, 512), F32)
                C.sbuf("ln_rows", (1, 1536), F32)
                C.sbuf("ln_t1", (128, 1024), F32)
            C.psum("pbig", (128, 1024), F32)
            C.psum("pa", (128, 512), F32)
            C.psum("pb", (128, 512), F32)
            C.psum("pc", (128, 512), F32)
            C.psum("pd", (128, 512), F32)
            C.psum("pe", (128, 512), F32)

            run_sub = SUBLAYER_PREP[stage](C, W, din)
            emit_ln(C, "x_res", "x_ln")
            run_sub()

            bx = C.buf("x_res")
            sto = plan.dma("sync", xfm_out[:, :], C.sb["x_res"][:, :],
                           waits=bx.write_waits(), tag="xfm_st")
            final_stores.append(sto)

        fw = plan.final_wait_spec(final_stores, engine="sync")
        with nc.Block() as block:
            plan.finalize_and_emit(block, sems, dma_sems, fw)
    return nc, feeds


def run_stage(stage, W, percore):
    nc, feeds = build_stage(stage, W)
    in_maps = []
    for core in range(NCORES):
        m = {}
        for name, arr in feeds.items():
            m[name] = percore[arr][core] if isinstance(arr, str) else arr
        in_maps.append(m)
    return run_bass_kernel_spmd(nc, in_maps, core_ids=list(range(NCORES)))


def build_full(W, seq=None):
    if seq is None:
        seq = SUBLAYER_SEQ
    nc = bass.Bass("TRN2", target_bir_lowering=False, debug=False)
    st = contextlib.ExitStack()
    plan = Plan(nc)
    C = Ctx(nc, plan, st)
    C.fused = True
    feeds = {}

    def din_raw(name, arr):
        arr = np.ascontiguousarray(arr, dtype=np.float32)
        ap = nc.dram_tensor(name, list(arr.shape), F32, kind="ExternalInput").ap()
        feeds[name] = arr
        return ap

    with st:
        sems = {e: st.enter_context(nc.semaphore("s_" + e)) for e in ENGINES}
        dma_sems = {q: st.enter_context(nc.semaphore("d_" + q)) for q in DMA_QUEUES}

        x_in = nc.dram_tensor("x_in", [T, D], F32, kind="ExternalInput").ap()
        feeds["x_in"] = "X_TOKEN"
        out = nc.dram_tensor("out", [T, D], F32, kind="ExternalOutput").ap()

        for name, arr in _CONSTS.items():
            dt = BF16 if arr.dtype == BF16NP else F32
            d = nc.dram_tensor("c_" + name, list(arr.shape), dt, kind="ExternalInput").ap()
            feeds["c_" + name] = arr
            emit_load_const(C, name, d, arr.shape, dt, persist=True)

        C.sbuf("x_res", (128, DT * T), F32)
        C.sbuf("x_ln", (128, DT * T), BF16)
        _ensure_ws(C, 2048)
        C.psum("pa", (128, 512), F32)
        C.psum("pb", (128, 512), F32)
        C.psum("pc", (128, 512), F32)
        C.psum("pd", (128, 512), F32)
        C.psum("pe", (128, 512), F32)
        C.psum("pbig", (128, 1024), F32)
        C.psum("wt_ps", (128, 1024), BF16)

        rem = nc.sbuf_bytes_remaining
        cols = rem // 4 - 64
        C.sbuf("arena", (128, cols), F32)
        C.arena_cols = cols
        C.carve("arena", "ln_t1", 0, 1024, F32)

        C.begin_sublayer("in_tx")
        emit_in_transform(C, x_in)

        for stage in seq:
            C.begin_sublayer(stage)
            pref = stage + "_"
            din = lambda n, a, pref=pref: din_raw(pref + n, a)
            run_sub = SUBLAYER_PREP[stage](C, W, din)
            emit_ln(C, "x_res", "x_ln")
            run_sub()

        C.begin_sublayer("out_tx")
        stores = emit_out_transform(C, out)

        fw = plan.final_wait_spec(stores, engine="sync")
        with nc.Block() as block:
            plan.finalize_and_emit(block, sems, dma_sems, fw)
    return nc, feeds


def kernel(**inputs):
    x = np.ascontiguousarray(np.asarray(inputs["L0_attn_a_x"], dtype=np.float32))
    W = {k: np.ascontiguousarray(np.asarray(v, dtype=np.float32)) for k, v in inputs.items()}
    _ = inputs.get("L0_attn_b_c")

    import os
    if os.environ.get("KERNEL_UNFUSED"):
        res = run_stage("in_tx", W, {"X_TOKEN": [x[b] for b in range(NCORES)]})
        xfm = [res.results[b]["xfm_out"] for b in range(NCORES)]
        for stage in SUBLAYER_SEQ:
            res = run_stage(stage, W, {"XFM": xfm})
            xfm = [res.results[b]["xfm_out"] for b in range(NCORES)]
        res = run_stage("out_tx", W, {"XFM": xfm})
        return np.stack([res.results[b]["out"] for b in range(NCORES)])

    nc, feeds = build_full(W)
    in_maps = []
    for core in range(NCORES):
        m = {}
        for name, arr in feeds.items():
            m[name] = x[core] if isinstance(arr, str) else arr
        in_maps.append(m)
    res = run_bass_kernel_spmd(nc, in_maps, core_ids=list(range(NCORES)))
    return np.stack([res.results[b]["out"] for b in range(NCORES)])


# version 34
# speedup vs baseline: 1.0109x; 1.0001x over previous
import contextlib
import numpy as np
import ml_dtypes
import concourse.bass as bass
import concourse.mybir as mybir
from concourse.bass_utils import run_bass_kernel_spmd

F32 = mybir.dt.float32
BF16 = mybir.dt.bfloat16
AF = mybir.ActivationFunctionType
ALU = mybir.AluOpType
AX = mybir.AxisListType
BF16NP = ml_dtypes.bfloat16

T = 2048
D = 1024
DT = D // 128
TT = T // 128
NCORES = 8
NH = 16
DH = 64
LN_EPS = 1e-5

ENGINES = ("tensor", "vector", "scalar", "gpsimd", "sync")
DMA_QUEUES = ("sync", "gpsimd", "scalar")


class Plan:

    def __init__(self, nc):
        self.nc = nc
        self.ops = []
        self.by_engine = {e: [] for e in ENGINES}
        self.same_engine_wait = True
        self.dma_count = {q: 0 for q in DMA_QUEUES}

    def op(self, eng, fn, waits=(), tag=""):
        oid = len(self.ops)
        rec = dict(oid=oid, eng=eng, fn=fn, waits=[w for w in waits if w is not None],
                   dma=False, tag=tag, inc=False)
        self.ops.append(rec)
        self.by_engine[eng].append(rec)
        return oid

    def dma(self, queue, out, in_, waits=(), tag=""):
        oid = len(self.ops)

        def fn(e, out=out, in_=in_):
            return e.dma_start(out=out, in_=in_)
        self.dma_count[queue] += 16
        rec = dict(oid=oid, eng=queue, fn=fn, waits=[w for w in waits if w is not None],
                   dma=True, tag=tag, inc=True, dma_target=self.dma_count[queue])
        self.ops.append(rec)
        self.by_engine[queue].append(rec)
        return oid

    def finalize_and_emit(self, block, sems, dma_sems, final_waits):
        for rec in self.ops:
            for w in rec["waits"]:
                self.ops[w]["inc"] = True
        for e in ENGINES:
            c = 0
            for rec in self.by_engine[e]:
                if not rec["dma"] and rec["inc"]:
                    c += 1
                    rec["eng_target"] = c

        def wait_list(rec):
            agg = {}
            for w in rec["waits"]:
                p = self.ops[w]
                if p["dma"]:
                    key = ("dma", p["eng"])
                    val = p["dma_target"]
                else:
                    if p["eng"] == rec["eng"] and rec["eng"] == "tensor":
                        continue
                    if p["eng"] == rec["eng"] and not self.same_engine_wait:
                        continue
                    key = ("eng", p["eng"])
                    val = p["eng_target"]
                if agg.get(key, 0) < val:
                    agg[key] = val
            return agg

        def run_engine(e, ename):
            seen = {}
            for rec in self.by_engine[ename]:
                for (kind, src), val in wait_list(rec).items():
                    if seen.get((kind, src), -1) >= val:
                        continue
                    seen[(kind, src)] = val
                    sem = dma_sems[src] if kind == "dma" else sems[src]
                    e.wait_ge(sem, val)
                ins = rec["fn"](e)
                if rec["dma"]:
                    ins.then_inc(dma_sems[rec["eng"]], 16)
                elif rec["inc"]:
                    ins.then_inc(sems[rec["eng"]], 1)
            for (kind, src), val in final_waits.get(ename, {}).items():
                if seen.get((kind, src), -1) >= val:
                    continue
                sem = dma_sems[src] if kind == "dma" else sems[src]
                e.wait_ge(sem, val)

        block.sync(lambda e: run_engine(e, "sync"))
        block.vector(lambda e: run_engine(e, "vector"))
        block.scalar(lambda e: run_engine(e, "scalar"))
        block.gpsimd(lambda e: run_engine(e, "gpsimd"))
        block.tensor(lambda e: run_engine(e, "tensor"))

    def final_wait_spec(self, opids, engine="sync"):
        agg = {}
        for oid in opids:
            p = self.ops[oid]
            assert p["dma"]
            key = ("dma", p["eng"])
            val = p["dma_target"]
            if agg.get(key, 0) < val:
                agg[key] = val
        return {engine: agg}


class Buf:
    __slots__ = ("w", "r")

    def __init__(self):
        self.w = []
        self.r = []

    def read_waits(self):
        return list(self.w)

    def write_waits(self):
        return list(self.w) + list(self.r)

    def note_read(self, oid):
        self.r.append(oid)

    def note_write(self, oid):
        self.w = [oid]
        self.r = []

    def note_write_acc(self, oid):
        self.w.append(oid)
        self.r = []


def ceil_div(a, b):
    return (a + b - 1) // b


def host_constants():
    c = {}
    c["identity_f32"] = np.eye(128, dtype=np.float32)
    c["identity_bf16"] = np.eye(128).astype(BF16NP)
    c["ones_f32"] = np.ones((128, 128), dtype=np.float32)
    c["ones_bf16"] = np.ones((128, 128)).astype(BF16NP)
    m = (np.arange(128)[None, :] >= np.arange(128)[:, None]).astype(np.float32)
    c["tri_bf16"] = m.astype(BF16NP)
    fv = np.zeros((128, 8), np.float32)
    fv[:, 0] = LN_EPS
    fv[:, 1] = 1.0 / D
    fv[:, 2] = -0.5
    fv[:, 3] = 1.0
    fv[:, 4] = -1.0
    fv[:, 5] = 0.125
    c["fvals"] = fv
    rev = np.zeros((128, 128), np.float32)
    rev[np.arange(128), 127 - np.arange(128)] = 1.0
    c["rev_bf16"] = rev.astype(BF16NP)
    return c


class Ctx:
    def __init__(self, nc, plan, st):
        self.nc = nc
        self.plan = plan
        self.st = st
        self.sb = {}
        self.ps = {}
        self.bufs = {}
        self.fused = False
        self.arena_cols = 0
        self.cursor = 0
        self.sub_names = []
        self.offsets = {}
        self.prev_tokens = []
        self.max_cursor = 0

    def begin_sublayer(self, tag=""):
        if not self.fused:
            return
        toks = list(self.prev_tokens)
        for n in self.sub_names:
            b = self.bufs.get(n)
            if b:
                toks += b.w + b.r
            self.sb.pop(n, None)
            self.bufs.pop(n, None)
        self.prev_tokens = sorted(set(toks))
        self.sub_names = []
        self.cursor = 1024
        self.sub_tag = tag

    def salloc(self, name, shape, dt=F32):
        if not self.fused:
            return self.sbuf(name, shape, dt)
        p = shape[0]
        n = shape[1]
        f32n = n if dt == F32 else (n + 1) // 2
        f32n = (f32n + 15) & ~15
        assert self.cursor + f32n <= self.arena_cols,             f"arena overflow at {name} ({self.sub_tag}): {self.cursor}+{f32n}>{self.arena_cols}"
        self.offsets[name] = self.cursor
        h = self.sb["arena"]
        if dt == F32:
            ap = h[0:p, self.cursor:self.cursor + n]
        else:
            ap = h[0:p, self.cursor:self.cursor + (n + 1) // 2].bitcast(dt)[0:p, 0:n]
        self.cursor += f32n
        self.max_cursor = max(self.max_cursor, self.cursor)
        self.sb[name] = ap
        b = Buf()
        b.w = list(self.prev_tokens)
        self.bufs[name] = b
        self.sub_names.append(name)
        return ap

    def salloc_in(self, host, name, shape, dt=F32, off_f32=0):
        if not self.fused:
            return self.sbuf(name, shape, dt)
        base = self.offsets[host] + off_f32
        p, n = shape
        h = self.sb["arena"]
        if dt == F32:
            ap = h[0:p, base:base + n]
        else:
            ap = h[0:p, base:base + (n + 1) // 2].bitcast(dt)[0:p, 0:n]
        self.sb[name] = ap
        b = Buf()
        b.w = list(self.prev_tokens)
        self.bufs[name] = b
        self.sub_names.append(name)
        return ap

    def salloc_block(self, nelem):
        if not self.fused:
            return None
        base = self.cursor
        assert base + nelem <= self.arena_cols, f"arena overflow block {nelem}"
        self.cursor += nelem
        self.max_cursor = max(self.max_cursor, self.cursor)
        return base

    def carve_rel(self, base, name, off, nelem, dt=F32, p=128):
        if base is None:
            return self.carve("mb_arena", name, off, nelem, dt)[0:p, :]
        h = self.sb["arena"]
        if dt == F32:
            ap = h[0:p, base + off:base + off + nelem]
        else:
            ap = h[0:p, base + off:base + off + (nelem + 1) // 2].bitcast(dt)[0:p, 0:nelem]
        self.sb[name] = ap
        b = Buf()
        b.w = list(self.prev_tokens)
        self.bufs[name] = b
        self.sub_names.append(name)
        return ap

    def sbuf(self, name, shape, dt=F32):
        h = self.st.enter_context(self.nc.sbuf_tensor(name, list(shape), dt))
        self.sb[name] = h
        self.bufs[name] = Buf()
        return h

    def psum(self, name, shape, dt=F32):
        h = self.st.enter_context(self.nc.psum_tensor(name, list(shape), dt))
        self.ps[name] = h
        self.bufs[name] = Buf()
        return h

    def buf(self, name):
        return self.bufs.setdefault(name, Buf())

    def carve(self, arena, name, c0, nelem, dt=F32):
        h = self.sb[arena]
        if dt == F32:
            ap = h[:, c0:c0 + nelem]
        else:
            ap = h[:, c0:c0 + (nelem + 1) // 2].bitcast(dt)[:, 0:nelem]
        self.sb[name] = ap
        self.bufs[name] = Buf()
        return ap

    def arena_reset(self, names, handoff_from):
        toks = []
        for n in handoff_from:
            b = self.bufs.get(n)
            if b:
                toks += b.w + b.r
        for n in names:
            self.bufs[n].w = list(toks)



def emit_load_const(C, name, dram_ap, shape, dt, persist=False):
    if name in C.sb:
        h = C.sb[name]
    elif persist or not C.fused:
        h = C.sbuf(name, shape, dt)
    else:
        h = C.salloc(name, shape, dt)
    oid = C.plan.dma("sync", h[0:shape[0], :], dram_ap[:, :],
                     waits=C.buf(name).write_waits(), tag=f"load_{name}")
    C.buf(name).note_write(oid)
    return h


def emit_in_transform(C, x_tok_dram, xname="x_res"):
    P = C.plan
    x_res = C.sb[xname]
    stage = C.salloc("io_stage", (128, 2 * 1024), F32)
    pt = C.ps["pbig"]
    ident = C.sb["identity_f32"]
    bident = C.buf("identity_f32")
    bst = C.buf("io_stage")
    bpt = C.buf("pbig")
    bx = C.buf(xname)
    xv = x_tok_dram.rearrange("(a p) d -> p a d", p=128)
    for j in range(DT):
        for g in range(2):
            sl = (j * 2 + g) % 2
            ld = P.dma("sync", stage[:, sl * 1024:(sl + 1) * 1024]
                       .rearrange("p (a c) -> p a c", a=8),
                       xv[:, g * 8:(g + 1) * 8, j * 128:(j + 1) * 128],
                       waits=bst.write_waits(), tag="in_ld")
            bst.note_write_acc(ld)
            for ii in range(8):
                tp = P.op("tensor", lambda e, ii=ii, sl=sl: e.transpose(
                    pt[:, ii * 128:(ii + 1) * 128],
                    stage[:, sl * 1024 + ii * 128:sl * 1024 + (ii + 1) * 128],
                    ident[:, :]),
                    waits=[ld] + bident.read_waits() + bpt.write_waits(), tag="in_tp")
                bst.note_read(tp)
                bident.note_read(tp)
                bpt.note_write_acc(tp)
            ev = P.op("vector", lambda e, j=j, g=g: e.tensor_copy(
                x_res[:, j * T + g * 1024:j * T + (g + 1) * 1024], pt[:, :]),
                waits=bpt.w + bx.write_waits(), tag="in_ev")
            bpt.note_read(ev)
            bx.note_write_acc(ev)


def emit_out_transform(C, out_dram, xname="x_res"):
    P = C.plan
    x_res = C.sb[xname]
    stage = C.salloc("io_stage2", (128, 2 * D), F32)
    pt = C.ps["pbig"]
    ident = C.sb["identity_f32"]
    bident = C.buf("identity_f32")
    bst = C.buf("io_stage2")
    bpt = C.buf("pbig")
    bx = C.buf(xname)
    stores = []
    for i in range(TT):
        sl = i % 2
        for g in range(2):
            for jj in range(4):
                j = g * 4 + jj
                tp = P.op("tensor", lambda e, i=i, j=j, jj=jj: e.transpose(
                    pt[:, jj * 128:(jj + 1) * 128],
                    x_res[:, j * T + i * 128: j * T + (i + 1) * 128], ident[:, :]),
                    waits=bx.read_waits() + bident.read_waits() + bpt.write_waits(),
                    tag="out_tp")
                bx.note_read(tp)
                bident.note_read(tp)
                bpt.note_write_acc(tp)
            ev = P.op("vector", lambda e, sl=sl, g=g: e.tensor_copy(
                stage[:, sl * D + g * 512:sl * D + (g + 1) * 512], pt[:, 0:512]),
                waits=bpt.w + bst.write_waits(), tag="out_ev")
            bpt.note_read(ev)
            bst.note_write_acc(ev)
        sto = P.dma("sync", out_dram[i * 128:(i + 1) * 128, :], stage[:, sl * D:(sl + 1) * D],
                    waits=bst.w, tag="out_st")
        bst.note_read(sto)
        stores.append(sto)
    return stores



def emit_ln(C, src_name, dst_name):
    if "ln_sq" not in C.sb:
        C.salloc("ln_sq", (128, 512), F32)
    if "ln_rows" not in C.sb:
        C.salloc("ln_rows", (1, 1536), F32)
    """LayerNorm feature-major: src f32 [128, DT*T] -> dst bf16 [128, DT*T].
    Chunked by 512 tokens. Uses: ln_sq [128,512] f32, ln_rows [1, 3*T] f32,
    ln_ps psum [1, 1024] f32, ln_bc psum [128, 1024] f32, pa.. unused."""
    P = C.plan
    x = C.sb[src_name]
    y = C.sb[dst_name]
    bx = C.buf(src_name)
    by = C.buf(dst_name)
    ones = C.sb["ones_f32"]
    bones = C.buf("ones_f32")
    sq = C.sb["ln_sq"]; bsq = C.buf("ln_sq")
    rows = C.sb["ln_rows"]; brows = C.buf("ln_rows")
    psr_a = C.ps["pc"]; bps_a = C.buf("pc")
    psr_b = C.ps["pd"]; bps_b = C.buf("pd")
    pb = C.ps["pbig"]; bpb = C.buf("pbig")
    t1 = C.sb["ln_t1"]; bt1 = C.buf("ln_t1")

    for ch in range(4):
        s = ch * 512
        sqb = sq.bitcast(BF16)
        onesb_ = C.sb["ones_bf16"]
        bonesb_ = C.buf("ones_bf16")
        for j in range(DT):
            cv = P.op("scalar", lambda e, j=j, s=s, sqb=sqb: e.activation(
                sqb[:, 0:512], x[:, j * T + s:j * T + s + 512], AF.Copy),
                waits=bx.read_waits() + bsq.write_waits(), tag="ln_cast")
            bx.note_read(cv)
            bsq.note_write(cv)
            sqo = P.op("vector", lambda e, sqb=sqb: e.tensor_mul(
                sqb[:, 512:1024], sqb[:, 0:512], sqb[:, 0:512]),
                waits=bsq.w, tag="ln_sq")
            bsq.note_write_acc(sqo)
            mm = P.op("tensor", lambda e, j=j, sqb=sqb: e.matmul(
                psr_a[0:1, 0:512], onesb_[:, 0:1], sqb[:, 0:512],
                start=(j == 0), stop=(j == DT - 1)),
                waits=bsq.w + bonesb_.read_waits() +
                      (bps_a.write_waits() if j == 0 else []), tag="ln_sum")
            bsq.note_read(mm)
            bonesb_.note_read(mm)
            bps_a.note_write_acc(mm)
            mm2 = P.op("tensor", lambda e, j=j, sqb=sqb: e.matmul(
                psr_b[0:1, 0:512], onesb_[:, 0:1], sqb[:, 512:1024],
                start=(j == 0), stop=(j == DT - 1)),
                waits=bsq.w + bonesb_.read_waits() +
                      (bps_b.write_waits() if j == 0 else []), tag="ln_sumsq")
            bsq.note_read(mm2)
            bps_b.note_write_acc(mm2)
        R_MEAN, R_RSTD, R_M2 = 0, 512, 1024
        fv = C.sb["fvals"]
        bfv = C.buf("fvals")
        m_mean = P.op("scalar", lambda e, s=s: e.activation(
            rows[0:1, R_MEAN:R_MEAN + 512], psr_a[0:1, 0:512], AF.Copy, scale=fv[0:1, 1:2]),
            waits=bps_a.w + brows.write_waits() + bfv.read_waits(), tag="ln_mean")
        bps_a.note_read(m_mean)
        brows.note_write_acc(m_mean)
        m_var = P.op("scalar", lambda e, s=s: e.activation(
            rows[0:1, R_RSTD:R_RSTD + 512], psr_b[0:1, 0:512], AF.Copy, scale=fv[0:1, 1:2]),
            waits=bps_b.w + [m_mean], tag="ln_var0")
        bps_b.note_read(m_var)
        brows.note_write_acc(m_var)
        sqm = P.op("vector", lambda e, s=s: e.tensor_mul(
            rows[0:1, R_M2:R_M2 + 512], rows[0:1, R_MEAN:R_MEAN + 512],
            rows[0:1, R_MEAN:R_MEAN + 512]),
            waits=[m_mean], tag="ln_m2a")
        subv = P.op("vector", lambda e, s=s: e.tensor_sub(
            rows[0:1, R_RSTD:R_RSTD + 512], rows[0:1, R_RSTD:R_RSTD + 512],
            rows[0:1, R_M2:R_M2 + 512]),
            waits=[sqm, m_var], tag="ln_m2b")
        lnv = P.op("scalar", lambda e, s=s: e.activation(
            rows[0:1, R_RSTD:R_RSTD + 512], rows[0:1, R_RSTD:R_RSTD + 512],
            AF.Ln, bias=fv[0:1, 0:1]), waits=[subv], tag="ln_ln")
        rstd = P.op("scalar", lambda e, s=s: e.activation(
            rows[0:1, R_RSTD:R_RSTD + 512], rows[0:1, R_RSTD:R_RSTD + 512],
            AF.Exp, scale=fv[0:1, 2:3]), waits=[lnv], tag="ln_exp")
        m2 = P.op("vector", lambda e, s=s: e.tensor_mul(
            rows[0:1, R_M2:R_M2 + 512], rows[0:1, R_MEAN:R_MEAN + 512],
            rows[0:1, R_RSTD:R_RSTD + 512]),
            waits=[rstd, sqm, m_mean], tag="ln_m2")
        brows.note_write_acc(m2)
        brows.w = [m2, rstd]
        b1 = P.op("tensor", lambda e, s=s: e.matmul(
            pb[:, 0:512], ones[0:1, :], rows[0:1, R_RSTD:R_RSTD + 512],
            start=True, stop=True),
            waits=[rstd, m2] + bones.read_waits() + bpb.write_waits(), tag="ln_bc1")
        b2 = P.op("tensor", lambda e, s=s: e.matmul(
            pb[:, 512:1024], ones[0:1, :], rows[0:1, R_M2:R_M2 + 512],
            start=True, stop=True),
            waits=[m2] + bones.read_waits() + bpb.write_waits(), tag="ln_bc2")
        brows.note_read(b1)
        brows.note_read(b2)
        bones.note_read(b1)
        bpb.note_write(b1)
        bpb.w = [b1, b2]
        for j in range(DT):
            mu = P.op("vector", lambda e, j=j, s=s: e.tensor_mul(
                t1[:, 0:512], x[:, j * T + s:j * T + s + 512], pb[:, 0:512]),
                waits=bx.read_waits() + bpb.w + bt1.write_waits(), tag="ln_n1")
            bx.note_read(mu)
            bpb.note_read(mu)
            bt1.note_write(mu)
            su = P.op("vector", lambda e, j=j, s=s: e.tensor_sub(
                y[:, j * T + s:j * T + s + 512], t1[:, 0:512], pb[:, 512:1024]),
                waits=bt1.w + bpb.w + by.write_waits(), tag="ln_n2")
            bt1.note_read(su)
            bpb.note_read(su)
            by.note_write_acc(su)


class WeightLoader:

    def __init__(self, C, name, dram_ap, rows, cols, transpose, queue="gpsimd"):
        self.C, self.name = C, name
        self.dram = dram_ap
        self.rows, self.cols = rows, cols
        self.transpose = transpose
        self.queue = queue
        if transpose:
            self.KT, self.MT = ceil_div(cols, 128), ceil_div(rows, 128)
        else:
            self.KT, self.MT = ceil_div(rows, 128), ceil_div(cols, 128)

    def prefetch_all(self):
        C, P = self.C, self.C.plan
        if not self.transpose:
            h = C.salloc(f"w_{self.name}", (128, self.KT * self.cols), BF16)
            b = C.buf(f"w_{self.name}")
            for kt in range(self.KT):
                r0 = kt * 128
                r1 = min(self.rows, r0 + 128)
                ld = P.dma(self.queue, h[0:r1 - r0, kt * self.cols:(kt + 1) * self.cols],
                           self.dram[r0:r1, :], waits=b.write_waits(), tag=f"w_{self.name}_ld")
                b.note_write_acc(ld)
            self.h, self.b = h, b
        else:
            nm = f"w_{self.name}"
            if nm in C.sb:
                h = C.sb[nm]
                b = C.buf(nm)
            else:
                h = C.salloc(nm, (128, self.KT * self.rows), BF16)
                b = C.buf(nm)
            if "ws_nat" in C.sb:
                natname = "ws_nat"
            else:
                if "wt_nat" not in C.sb:
                    C.sbuf("wt_nat", (128, 2048), BF16)
                natname = "wt_nat"
            if "wt_ps" not in C.ps:
                C.psum("wt_ps", (128, 1024), BF16)
            nat = C.sb[natname]
            bn = C.buf(natname)
            pt = C.ps["wt_ps"]
            bp = C.buf("wt_ps")
            identb = C.sb["identity_bf16"]
            bib = C.buf("identity_bf16")
            ncolblk = ceil_div(self.cols, 128)
            for mt in range(self.MT):
                r0 = mt * 128
                r1 = min(self.rows, r0 + 128)
                nr = r1 - r0
                ld = P.dma(self.queue, nat[0:nr, 0:self.cols], self.dram[r0:r1, :],
                           waits=bn.write_waits(), tag=f"w_{self.name}_ld")
                bn.note_write(ld)
                for g0 in range(0, ncolblk, 8):
                    g1 = min(ncolblk, g0 + 8)
                    for kt in range(g0, g1):
                        c0 = kt * 128
                        c1 = min(self.cols, c0 + 128)
                        tp = P.op("tensor", lambda e, kt=kt, c0=c0, c1=c1, g0=g0, nr=nr: e.transpose(
                            pt[0:c1 - c0, (kt - g0) * 128:(kt - g0) * 128 + nr],
                            nat[0:nr, c0:c1], identb[0:nr, 0:nr]),
                            waits=bn.read_waits() + bib.read_waits() + bp.write_waits(),
                            tag=f"w_{self.name}_tp")
                        bn.note_read(tp)
                        bib.note_read(tp)
                        bp.note_write_acc(tp)
                    for kt in range(g0, g1):
                        c0 = kt * 128
                        c1 = min(self.cols, c0 + 128)
                        ev = P.op("vector", lambda e, kt=kt, g0=g0, c0=c0, c1=c1, r0=r0, r1=r1: e.tensor_copy(
                            h[0:c1 - c0, kt * self.rows + r0: kt * self.rows + r1],
                            pt[0:c1 - c0, (kt - g0) * 128:(kt - g0) * 128 + (r1 - r0)]),
                            waits=bp.w + b.write_waits(), tag=f"w_{self.name}_ev")
                        bp.note_read(ev)
                        b.note_write_acc(ev)
            self.h, self.b = h, b

    def tile(self, kt, mt):
        if self.transpose:
            k1 = min(self.cols - kt * 128, 128)
            m0 = mt * 128
            m1 = min(self.rows, m0 + 128)
            return self.h[0:k1, kt * self.rows + m0: kt * self.rows + m1], self.b
        else:
            k1 = min(self.rows - kt * 128, 128)
            m0 = mt * 128
            m1 = min(self.cols, m0 + 128)
            return self.h[0:k1, kt * self.cols + m0: kt * self.cols + m1], self.b


def emit_proj_fm(C, wl, xsrc_name, out_cb, out_rows, psname, tag="proj"):
    P = C.plan
    x = C.sb[xsrc_name]
    bx = C.buf(xsrc_name)
    KT = wl.KT
    MT = ceil_div(out_rows, 128)
    for mt in range(MT):
        for ch in range(4):
            pname = psname[(mt * 4 + ch) % len(psname)]
            ps = C.ps[pname]
            bps = C.buf(pname)
            s = ch * 512
            for kt in range(KT):
                tile, bw = wl.tile(kt, mt)
                mm = P.op("tensor", lambda e, tile=tile, kt=kt, s=s, ps=ps: e.matmul(
                    ps[0:128, 0:512], tile, x[:, kt * T + s: kt * T + s + 512],
                    start=(kt == 0), stop=(kt == KT - 1)),
                    waits=bw.read_waits() + bx.read_waits() +
                          (bps.write_waits() if kt == 0 else []), tag=tag + "_mm")
                bw.note_read(mm)
                bx.note_read(mm)
                bps.note_write_acc(mm)
            out_cb(mt, ch, ps, bps)


class WvPrefetch:

    def __init__(self, C, name, dram):
        self.C, self.name, self.dram = C, name, dram

    def prefetch_all(self):
        C, P = self.C, self.C.plan
        assert C.ws_slotw >= 2048
        nat = C.sb["ws_nat"]
        wt = C.sb["ws_wt"]
        vt = C.sb["ws_vT"]
        bvt = C.buf("ws_vT")
        pt = C.ps["wt_ps"]
        bp = C.buf("wt_ps")
        identb = C.sb["identity_bf16"]
        self.dsts = [(nat, C.buf("ws_nat")), (wt, C.buf("ws_wt"))]
        for mt in range(8):
            ld = P.dma("gpsimd", vt[:, 0:1024], self.dram[mt * 128:(mt + 1) * 128, :],
                       waits=bvt.write_waits(), tag=f"{self.name}_ld")
            bvt.note_write(ld)
            for g in range(2):
                for kk in range(4):
                    kt = g * 4 + kk
                    tp = P.op("tensor", lambda e, kt=kt, kk=kk: e.transpose(
                        pt[:, kk * 128:(kk + 1) * 128], vt[:, kt * 128:(kt + 1) * 128],
                        identb[:, :]),
                        waits=bvt.w + C.buf("identity_bf16").read_waits() + bp.write_waits(),
                        tag=f"{self.name}_tp")
                    bvt.note_read(tp)
                    C.buf("identity_bf16").note_read(tp)
                    bp.note_write_acc(tp)
                dsth, bdst = self.dsts[g]
                ev = P.op("vector", lambda e, mt=mt, dsth=dsth: e.tensor_copy(
                    dsth.rearrange("p (k w) -> p k w", w=1024)[:, :, mt * 128:(mt + 1) * 128],
                    pt[:, 0:512].rearrange("p (k w) -> p k w", w=128)),
                    waits=bp.w + bdst.write_waits(), tag=f"{self.name}_ev")
                bp.note_read(ev)
                bdst.note_write_acc(ev)

    def mt_tiles(self, mt, row0=0):
        out = []
        for kt in range(8):
            dsth, bdst = self.dsts[kt // 4]
            c0 = (kt % 4) * 1024 + mt * 128
            out.append((dsth[:, c0:c0 + 128], bdst))
        return out


def emit_ffn_rwkv(C, wr, wk, wv):
    P = C.plan
    CH = 512
    xl = C.sb["x_ln"]; bxl = C.buf("x_ln")
    xr = C.sb["ffn_xr"]; bxr = C.buf("ffn_xr")
    xk = C.sb["ffn_xk"]; bxk = C.buf("ffn_xk")
    dd = C.sb["ffn_d"]; bd = C.buf("ffn_d")
    mu = C.sb["mu_rk"]; bmu = C.buf("mu_rk")
    r = C.sb["ffn_r"]; br = C.buf("ffn_r")
    k2 = C.sb["ffn_k2"]; bk2 = C.buf("ffn_k2")
    xres = C.sb["x_res"]; bxres = C.buf("x_res")
    t1 = C.sb["ln_t1"]; bt1 = C.buf("ln_t1")

    def chunk_proj(wl, src, bsrc, evac, tag):
        for mt in range(DT):
            pname = ("pa", "pb")[mt % 2]
            ps = C.ps[pname]
            bps = C.buf(pname)
            tiles = wl.mt_tiles(mt) if hasattr(wl, "mt_tiles") else None
            for kt in range(DT):
                tile, bw = tiles[kt] if tiles else wl.tile(kt, mt)
                mm = P.op("tensor", lambda e, tile=tile, kt=kt, ps=ps: e.matmul(
                    ps[0:128, 0:CH], tile, src[:, kt * CH:(kt + 1) * CH],
                    start=(kt == 0), stop=(kt == DT - 1)),
                    waits=bw.read_waits() + bsrc.read_waits() +
                          (bps.write_waits() if kt == 0 else []), tag=tag + "_mm")
                bw.note_read(mm)
                bsrc.note_read(mm)
                bps.note_write_acc(mm)
            evac(mt, ps, bps)

    for ch in range(T // CH):
        s = ch * CH
        for j in range(DT):
            if ch == 0:
                c0 = P.op("vector", lambda e, j=j: e.tensor_copy(dd[:, 0:1], xl[:, j * T:j * T + 1]),
                          waits=bxl.read_waits() + bd.write_waits(), tag="ffn_d0")
                bxl.note_read(c0)
                bd.note_write(c0)
                c1 = P.op("vector", lambda e, j=j: e.tensor_sub(
                    dd[:, 1:CH], xl[:, j * T + 1:j * T + CH], xl[:, j * T:j * T + CH - 1]),
                    waits=bxl.read_waits() + [c0], tag="ffn_d1")
                bxl.note_read(c1)
                bd.note_write_acc(c1)
            else:
                c1 = P.op("vector", lambda e, j=j, s=s: e.tensor_sub(
                    dd[:, 0:CH], xl[:, j * T + s:j * T + s + CH],
                    xl[:, j * T + s - 1:j * T + s + CH - 1]),
                    waits=bxl.read_waits() + bd.write_waits(), tag="ffn_d1")
                bxl.note_read(c1)
                bd.note_write(c1)
            for (dst, bdst, mucol, tg) in ((xr, bxr, j, "ffn_xr"), (xk, bxk, DT + j, "ffn_xk")):
                if ch == 0:
                    m0 = P.op("vector", lambda e, j=j, dst=dst, mucol=mucol: e.tensor_scalar(
                        dst[:, j * CH:j * CH + 1], dd[:, 0:1], mu[:, mucol:mucol + 1], None, ALU.mult),
                        waits=bd.w + bmu.read_waits() + bdst.write_waits(), tag=tg + "0")
                    bd.note_read(m0)
                    bmu.note_read(m0)
                    bdst.note_write(m0)
                    m1 = P.op("vector", lambda e, j=j, dst=dst, mucol=mucol: e.scalar_tensor_tensor(
                        dst[:, j * CH + 1:(j + 1) * CH], dd[:, 1:CH], mu[:, mucol:mucol + 1],
                        xl[:, j * T:j * T + CH - 1], ALU.mult, ALU.add),
                        waits=bd.w + bxl.read_waits() + bmu.read_waits() + [m0], tag=tg + "1")
                    bd.note_read(m1)
                    bxl.note_read(m1)
                    bmu.note_read(m1)
                    bdst.note_write_acc(m1)
                else:
                    m1 = P.op("vector", lambda e, j=j, s=s, dst=dst, mucol=mucol: e.scalar_tensor_tensor(
                        dst[:, j * CH:(j + 1) * CH], dd[:, 0:CH], mu[:, mucol:mucol + 1],
                        xl[:, j * T + s - 1:j * T + s + CH - 1], ALU.mult, ALU.add),
                        waits=bd.w + bxl.read_waits() + bmu.read_waits() + bdst.write_waits(), tag=tg + "1")
                    bd.note_read(m1)
                    bxl.note_read(m1)
                    bmu.note_read(m1)
                    bdst.note_write(m1)

        def evac_r(mt, ps, bps):
            ev = P.op("scalar", lambda e, mt=mt, ps=ps: e.activation(
                r[:, mt * CH:(mt + 1) * CH], ps[0:128, 0:CH], AF.Sigmoid),
                waits=bps.w + br.write_waits(), tag="ffn_r_ev")
            bps.note_read(ev)
            br.note_write_acc(ev)
        chunk_proj(wr, xr, bxr, evac_r, "ffn_r")

        def evac_k(mt, ps, bps):
            e1 = P.op("scalar", lambda e, ps=ps: e.activation(
                t1[:, 0:CH], ps[0:128, 0:CH], AF.Relu),
                waits=bps.w + bt1.write_waits(), tag="ffn_k_relu")
            bps.note_read(e1)
            bt1.note_write(e1)
            e2 = P.op("vector", lambda e, mt=mt: e.tensor_mul(
                k2[:, mt * CH:(mt + 1) * CH], t1[:, 0:CH], t1[:, 0:CH]),
                waits=bt1.w + bk2.write_waits(), tag="ffn_k_sq")
            bt1.note_read(e2)
            bk2.note_write_acc(e2)
        chunk_proj(wk, xk, bxk, evac_k, "ffn_k")

        def evac_v(mt, ps, bps):
            e1 = P.op("vector", lambda e, mt=mt, ps=ps: e.tensor_mul(
                t1[:, 512:1024], ps[0:128, 0:CH], r[:, mt * CH:(mt + 1) * CH]),
                waits=bps.w + br.read_waits() + bt1.write_waits(), tag="ffn_v_mul")
            bps.note_read(e1)
            br.note_read(e1)
            bt1.note_write(e1)
            e2 = P.op("vector", lambda e, mt=mt, s=s: e.tensor_add(
                xres[:, mt * T + s:mt * T + s + CH], xres[:, mt * T + s:mt * T + s + CH],
                t1[:, 512:1024]),
                waits=bt1.w + bxres.write_waits(), tag="ffn_v_add")
            bt1.note_read(e2)
            bxres.note_write_acc(e2)
        chunk_proj(wv, k2, bk2, evac_v, "ffn_v")


def emit_ffn_small(C, w_in, w_out):
    P = C.plan
    xl = C.sb["x_ln"]; bxl = C.buf("x_ln")
    xres = C.sb["x_res"]; bxres = C.buf("x_res")
    vec = C.sb["l1ffn_vec"]; bvec = C.buf("l1ffn_vec")
    bout = C.sb["l1ffn_bout"]; bbout = C.buf("l1ffn_bout")
    s_t = C.sb["l1ffn_s"]; bs = C.buf("l1ffn_s")
    h_t = C.sb["l1ffn_h"]; bh = C.buf("l1ffn_h")
    h_t2 = C.sb["l1ffn_h2"]; bh2 = C.buf("l1ffn_h2")
    g_t = C.sb["l1ffn_g"]; bg = C.buf("l1ffn_g")
    gb_t = C.sb["l1ffn_gb"]; bgb = C.buf("l1ffn_gb")
    t1 = C.sb["ln_t1"]; bt1 = C.buf("ln_t1")

    for ch in range(4):
        s = ch * 512
        ps = C.ps["pa"]
        bps = C.buf("pa")
        for kt in range(DT):
            tile, bw = w_in.tile(kt, 0)
            mm = P.op("tensor", lambda e, tile=tile, kt=kt, s=s, ps=ps: e.matmul(
                ps[0:64, 0:512], tile, xl[:, kt * T + s:kt * T + s + 512],
                start=(kt == 0), stop=(kt == DT - 1)),
                waits=bw.read_waits() + bxl.read_waits() +
                      (bps.write_waits() if kt == 0 else []), tag="l1f_s")
            bw.note_read(mm)
            bxl.note_read(mm)
            bps.note_write_acc(mm)
        ev = P.op("vector", lambda e, s=s, ps=ps: e.tensor_scalar_add(
            s_t[:, s:s + 512], ps[0:64, 0:512], vec[:, 48:49]),
            waits=bps.w + bvec.read_waits() + bs.write_waits(), tag="l1f_s_ev")
        bps.note_read(ev)
        bvec.note_read(ev)
        bs.note_write_acc(ev)
    for i in range(16):
        hc_t, bhc_t = (h_t, bh) if i % 2 == 0 else (h_t2, bh2)
        hi = P.op("scalar", lambda e, i=i, hc_t=hc_t: e.activation(
            hc_t[:, :], s_t[:, :], AF.Relu, scale=vec[:, i:i + 1], bias=vec[:, 16 + i:16 + i + 1]),
            waits=bs.w + bvec.read_waits() + bhc_t.write_waits(), tag="l1f_h")
        bs.note_read(hi)
        bvec.note_read(hi)
        bhc_t.note_write(hi)
        if i == 0:
            gi = P.op("vector", lambda e, hc_t=hc_t: e.tensor_scalar(
                g_t[:, :], hc_t[:, :], vec[:, 32:33], vec[:, 49:50], ALU.mult, ALU.add),
                waits=bhc_t.w + bvec.read_waits() + bg.write_waits(), tag="l1f_g0")
        else:
            gi = P.op("vector", lambda e, i=i, hc_t=hc_t: e.scalar_tensor_tensor(
                g_t[:, :], hc_t[:, :], vec[:, 32 + i:32 + i + 1], g_t[:, :], ALU.mult, ALU.add),
                waits=bhc_t.w + bvec.read_waits() + bg.write_waits() + bg.read_waits(), tag="l1f_g")
        bhc_t.note_read(gi)
        bvec.note_read(gi)
        bg.note_write(gi)
    gc = P.op("vector", lambda e: e.tensor_copy(gb_t[:, :], g_t[:, :]),
              waits=bg.w + bgb.write_waits(), tag="l1f_gc")
    bg.note_read(gc)
    bgb.note_write(gc)
    for mt in range(DT):
        for ch in range(4):
            s = ch * 512
            ps = C.ps["pb"]
            bps = C.buf("pb")
            tile, bw = w_out.tile(0, mt)
            mm = P.op("tensor", lambda e, tile=tile, s=s, ps=ps: e.matmul(
                ps[0:128, 0:512], tile, gb_t[:, s:s + 512], start=True, stop=True),
                waits=bw.read_waits() + bgb.read_waits() + bps.write_waits(), tag="l1f_o")
            bw.note_read(mm)
            bgb.note_read(mm)
            bps.note_write(mm)
            e1 = P.op("vector", lambda e, mt=mt, ps=ps: e.tensor_scalar_add(
                t1[:, 0:512], ps[0:128, 0:512], bout[:, mt:mt + 1]),
                waits=bps.w + bbout.read_waits() + bt1.write_waits(), tag="l1f_o_b")
            bps.note_read(e1)
            bbout.note_read(e1)
            bt1.note_write(e1)
            e2 = P.op("vector", lambda e, mt=mt, s=s: e.tensor_add(
                xres[:, mt * T + s:mt * T + s + 512], xres[:, mt * T + s:mt * T + s + 512],
                t1[:, 0:512]),
                waits=bt1.w + bxres.write_waits(), tag="l1f_o_add")
            bt1.note_read(e2)
            bxres.note_write_acc(e2)


_CONSTS = host_constants()

SUBLAYER_SEQ = ["l0_attn_a", "l0_ffn", "l0_attn_b", "l0_ffn2",
                "l1_mamba", "l1_ffn", "l1_attn_b", "l1_ffn2"]



def _ensure_ws(C, slotw):
    if "ws_nat" not in C.sb:
        C.ws_slotw = slotw
        C.sbuf("ws_nat", (128, 2 * slotw), BF16)
        C.sbuf("ws_wt", (128, 2 * slotw), BF16)
        C.sbuf("ws_vT", (128, 8 * 128), BF16)


class WStreamT:

    def __init__(self, C, name, dram, rows, cols):
        self.C, self.name, self.dram = C, name, dram
        self.rows, self.cols = rows, cols
        self.KT = cols // 128
        _ensure_ws(C, max(cols, 1024))
        assert cols <= C.ws_slotw
        self.slot = 0

    def mt_tiles(self, mt, row0=0):
        C, P = self.C, self.C.plan
        sl = self.slot
        self.slot ^= 1
        nat = C.sb["ws_nat"]
        bn = C.buf("ws_nat")
        wt = C.sb["ws_wt"]
        bwt = C.buf("ws_wt")
        pt = C.ps["wt_ps"]
        bp = C.buf("wt_ps")
        identb = C.sb["identity_bf16"]
        r0 = row0 + mt * 128
        SLOTW = self.C.ws_slotw
        ld = P.dma("gpsimd", nat[:, sl * SLOTW: sl * SLOTW + self.cols],
                   self.dram[r0:r0 + 128, :], waits=bn.write_waits(), tag=f"{self.name}_ld")
        bn.note_write_acc(ld)
        for g0 in range(0, self.KT, 8):
            g1 = min(self.KT, g0 + 8)
            for kt in range(g0, g1):
                tp = P.op("tensor", lambda e, kt=kt, sl=sl, g0=g0: e.transpose(
                    pt[:, (kt - g0) * 128:(kt - g0 + 1) * 128],
                    nat[:, sl * SLOTW + kt * 128: sl * SLOTW + (kt + 1) * 128], identb[:, :]),
                    waits=[ld] + C.buf("identity_bf16").read_waits() + bp.write_waits(),
                    tag=f"{self.name}_tp")
                bn.note_read(tp)
                bp.note_write_acc(tp)
            ev = P.op("vector", lambda e, sl=sl, g0=g0, g1=g1: e.tensor_copy(
                wt[:, sl * SLOTW + g0 * 128: sl * SLOTW + g1 * 128],
                pt[:, 0:(g1 - g0) * 128]),
                waits=bp.w + bwt.write_waits(), tag=f"{self.name}_ev")
            bp.note_read(ev)
            bwt.note_write_acc(ev)
        return [(wt[:, sl * SLOTW + kt * 128: sl * SLOTW + (kt + 1) * 128], bwt)
                for kt in range(self.KT)]

    def v_block(self, g, row0, width=512):
        C, P = self.C, self.C.plan
        nat = C.sb["ws_nat"]
        bn = C.buf("ws_nat")
        vt = C.sb["ws_vT"]
        bvt = C.buf("ws_vT")
        pt = C.ps["wt_ps"]
        bp = C.buf("wt_ps")
        identb = C.sb["identity_bf16"]
        for rb in range(width // 128):
            r0 = row0 + g * width + rb * 128
            sl = self.slot
            self.slot ^= 1
            SLOTW = self.C.ws_slotw
            ld = P.dma("gpsimd", nat[:, sl * SLOTW: sl * SLOTW + self.cols],
                       self.dram[r0:r0 + 128, :], waits=bn.write_waits(), tag=f"{self.name}_vld")
            bn.note_write_acc(ld)
            for kt in range(self.KT):
                tp = P.op("tensor", lambda e, kt=kt, sl=sl: e.transpose(
                    pt[:, kt * 128:(kt + 1) * 128],
                    nat[:, sl * SLOTW + kt * 128: sl * SLOTW + (kt + 1) * 128], identb[:, :]),
                    waits=[ld] + bp.write_waits(), tag=f"{self.name}_vtp")
                bn.note_read(tp)
                bp.note_write_acc(tp)
            ev = P.op("vector", lambda e, rb=rb, width=width: e.tensor_copy(
                vt.rearrange("p (k w) -> p k w", w=width)[:, :, rb * 128:(rb + 1) * 128],
                pt.rearrange("p (k w) -> p k w", w=128)[:, 0:self.KT, :]),
                waits=bp.w + bvt.write_waits(), tag=f"{self.name}_vev")
            bp.note_read(ev)
            bvt.note_write_acc(ev)
        return vt, bvt

    def v_slice(self, vt, kt, width=512):
        return vt[:, kt * width:(kt + 1) * width]


class WStreamN:

    def __init__(self, C, name, dram, cin, cout):
        self.C, self.name, self.dram = C, name, dram
        self.cin, self.cout = cin, cout
        self.KT = cin // 128
        _ensure_ws(C, max(cin, 1024))
        assert cin <= C.ws_slotw
        self.slot = 0

    def mt_tiles(self, mt, row0=0):
        C, P = self.C, self.C.plan
        sl = self.slot
        self.slot ^= 1
        SLOTW = self.C.ws_slotw
        wt = C.sb["ws_wt"]
        bwt = C.buf("ws_wt")
        c0 = row0 + mt * 128
        for kt in range(self.KT):
            ld = P.dma("gpsimd", wt[:, sl * SLOTW + kt * 128: sl * SLOTW + (kt + 1) * 128],
                       self.dram[kt * 128:(kt + 1) * 128, c0:c0 + 128],
                       waits=bwt.write_waits() if kt == 0 else [], tag=f"{self.name}_ld")
            bwt.note_write_acc(ld)
        return [(wt[:, sl * SLOTW + kt * 128: sl * SLOTW + (kt + 1) * 128], bwt)
                for kt in range(self.KT)]

    def v_block(self, g, row0, width=512):
        C, P = self.C, self.C.plan
        vt = C.sb["ws_vT"]
        bvt = C.buf("ws_vT")
        c0 = row0 + g * width
        for kt in range(self.KT):
            ld = P.dma("gpsimd", vt[:, kt * width:(kt + 1) * width],
                       self.dram[kt * 128:(kt + 1) * 128, c0:c0 + width],
                       waits=bvt.write_waits() if kt == 0 else [], tag=f"{self.name}_vld")
            bvt.note_write_acc(ld)
        return vt, bvt

    def v_slice(self, vt, kt, width=512):
        return vt[:, kt * width:(kt + 1) * width]


def emit_attn(C, kind, streams, has_bias, causal, rope=False, relbias=False):
    P = C.plan
    xl = C.sb["x_ln"]; bxl = C.buf("x_ln")
    xres = C.sb["x_res"]; bxres = C.buf("x_res")
    v = C.sb["att_v"]; bv = C.buf("att_v")
    o = C.sb["att_o"]; bo = C.buf("att_o")
    pbuf = C.sb["att_p"]; bp = C.buf("att_p")
    for _nm in ("att_p0", "att_p1", "att_p2", "att_p3", "att_p4", "att_p5"):
        C.buf(_nm).w = sorted(set(C.buf(_nm).w + bp.write_waits()))
    den = C.sb["att_den"]; bden = C.buf("att_den")
    rbs = C.sb["att_rb"]; brb = C.buf("att_rb")
    onesf = C.sb["ones_f32"]; bonesf = C.buf("ones_f32")
    onesb = C.sb["ones_bf16"]; bonesb = C.buf("ones_bf16")
    tri = C.sb["tri_bf16"]; btri = C.buf("tri_bf16")
    fv = C.sb["fvals"]; bfv = C.buf("fvals")
    qk = [C.sb["att_qk0"], C.sb["att_qk1"]]
    bqk = [C.buf("att_qk0"), C.buf("att_qk1")]

    (s_qk, q_row0, k_row0) = streams["qk"]
    (s_v, v_row0) = streams["v"]
    (s_o,) = streams["o"]
    relbias_fn = streams.get("relbias_fn")
    if relbias:
        d2 = C.sb["rb_d2"]
        bd2 = C.buf("rb_d2")
    if has_bias:
        qkb = C.sb["att_qkb"]; bqkb = C.buf("att_qkb")
    if rope:
        cosb = C.sb["rope_cos"]; bcos = C.buf("rope_cos")
        sinb = C.sb["rope_sin"]; bsin = C.buf("rope_sin")
        rmat = C.sb["rope_R"]; brm = C.buf("rope_R")
        t1 = C.sb["ln_t1"]; bt1 = C.buf("ln_t1")

    ms = P.op("vector", lambda e: e.memset(v[:, :], 1.0), waits=bv.write_waits(), tag="att_v_ones")
    bv.note_write(ms)

    VSL = TT * 130

    def emit_proj(hpi):
        sl = hpi % 2
        dst, bdst = qk[sl], bqk[sl]
        PS4 = ("pa", "pb", "pc", "pd")
        vtiles = s_v.mt_tiles(hpi, row0=v_row0)
        for kt in range(DT):
            tile, bw = vtiles[kt]
            for ch in range(4):
                s = ch * 512
                ps = C.ps[PS4[ch]]
                bps = C.buf(PS4[ch])
                mm = P.op("tensor", lambda e, tile=tile, kt=kt, s=s, ps=ps: e.matmul(
                    ps[0:128, 0:512], tile, xl[:, kt * T + s: kt * T + s + 512],
                    start=(kt == 0), stop=(kt == DT - 1)),
                    waits=bw.read_waits() + bxl.read_waits() +
                          (bps.write_waits() if kt == 0 else []), tag="att_vf_mm")
                bw.note_read(mm)
                bxl.note_read(mm)
                bps.note_write_acc(mm)
        for ch in range(4):
            s = ch * 512
            ps = C.ps[PS4[ch]]
            bps = C.buf(PS4[ch])
            if has_bias:
                ev = P.op("vector", lambda e, s=s, ps=ps, hpi=hpi: e.tensor_scalar_add(
                    dst[:, s:s + 512], ps[0:128, 0:512], qkb[:, 16 + hpi:16 + hpi + 1]),
                    waits=bps.w + bqkb.read_waits() + bdst.write_waits(), tag="att_vf_ev")
                bqkb.note_read(ev)
            else:
                ev = P.op("vector", lambda e, s=s, ps=ps: e.tensor_copy(
                    dst[:, s:s + 512], ps[0:128, 0:512]),
                    waits=bps.w + bdst.write_waits(), tag="att_vf_ev")
            bps.note_read(ev)
            bdst.note_write_acc(ev)
        identb_ = C.sb["identity_bf16"]
        wtp = C.ps["wt_ps"]
        bwtp = C.buf("wt_ps")
        for g in range(2):
            for ii in range(8):
                i = g * 8 + ii
                tp = P.op("tensor", lambda e, i=i, ii=ii: e.transpose(
                    wtp[:, ii * 128:(ii + 1) * 128], dst[:, i * 128:(i + 1) * 128],
                    identb_[:, :]),
                    waits=bdst.read_waits() + bdst.w + C.buf("identity_bf16").read_waits() +
                          bwtp.write_waits(), tag="att_vt")
                bdst.note_read(tp)
                C.buf("identity_bf16").note_read(tp)
                bwtp.note_write_acc(tp)
            ev = P.op("vector", lambda e, g=g, sl=sl: e.tensor_copy(
                v[:, sl * VSL + g * 8 * 130: sl * VSL + (g + 1) * 8 * 130]
                    .rearrange("p (a h c) -> p a h c", h=2, c=65)[:, :, :, 0:64],
                wtp[:, 0:1024].rearrange("p (a h c) -> p a h c", h=2, c=64)),
                waits=bwtp.w + bv.write_waits(), tag="att_v_ev")
            bwtp.note_read(ev)
            bv.note_write_acc(ev)
        PS4 = ("pa", "pb", "pc", "pd")
        for half, row0, bcol0 in ((0, q_row0, 0), (1, k_row0, DT)):
            tiles = s_qk.mt_tiles(hpi, row0=row0)
            for kt in range(DT):
                tile, bw = tiles[kt]
                for ch in range(4):
                    s = ch * 512
                    ps = C.ps[PS4[ch]]
                    bps = C.buf(PS4[ch])
                    mm = P.op("tensor", lambda e, tile=tile, kt=kt, s=s, ps=ps: e.matmul(
                        ps[0:128, 0:512], tile, xl[:, kt * T + s: kt * T + s + 512],
                        start=(kt == 0), stop=(kt == DT - 1)),
                        waits=bw.read_waits() + bxl.read_waits() +
                              (bps.write_waits() if kt == 0 else []), tag="att_qk_mm")
                    bw.note_read(mm)
                    bxl.note_read(mm)
                    bps.note_write_acc(mm)
            for ch in range(4):
                s = ch * 512
                ps = C.ps[PS4[ch]]
                bps = C.buf(PS4[ch])
                col = half * T + s
                if has_bias:
                    ev = P.op("vector", lambda e, dst=dst, col=col, ps=ps, bc=bcol0, hpi=hpi:
                        e.tensor_scalar_add(dst[:, col:col + 512], ps[0:128, 0:512],
                                            qkb[:, bc + hpi:bc + hpi + 1]),
                        waits=bps.w + bqkb.read_waits() + bdst.write_waits(), tag="att_qk_ev")
                    bqkb.note_read(ev)
                else:
                    ev = P.op("vector", lambda e, dst=dst, col=col, ps=ps: e.tensor_copy(
                        dst[:, col:col + 512], ps[0:128, 0:512]),
                        waits=bps.w + bdst.write_waits(), tag="att_qk_ev")
                bps.note_read(ev)
                bdst.note_write_acc(ev)
        if rope:
            for half in range(2):
                for ch in range(4):
                    s = half * T + ch * 512
                    rs = ch * 512
                    ps = C.ps[("pa", "pb")[ch % 2]]
                    bps = C.buf(("pa", "pb")[ch % 2])
                    mm = P.op("tensor", lambda e, dst=dst, s=s, ps=ps: e.matmul(
                        ps[0:128, 0:512], rmat[:, :], dst[:, s:s + 512],
                        start=True, stop=True),
                        waits=brm.read_waits() + bdst.read_waits() + bps.write_waits(), tag="rope_mm")
                    brm.note_read(mm)
                    bdst.note_read(mm)
                    bps.note_write(mm)
                    e1 = P.op("vector", lambda e, rs=rs, ps=ps: e.tensor_mul(
                        t1[:, 0:512], ps[0:128, 0:512], sinb[:, rs:rs + 512]),
                        waits=bps.w + bsin.read_waits() + bt1.write_waits(), tag="rope_m1")
                    bps.note_read(e1)
                    bsin.note_read(e1)
                    bt1.note_write(e1)
                    e2 = P.op("vector", lambda e, dst=dst, s=s, rs=rs: e.tensor_mul(
                        t1[:, 512:1024], dst[:, s:s + 512], cosb[:, rs:rs + 512]),
                        waits=bdst.read_waits() + bcos.read_waits() + [e1], tag="rope_m2")
                    bdst.note_read(e2)
                    bcos.note_read(e2)
                    bt1.note_write_acc(e2)
                    e3 = P.op("vector", lambda e, dst=dst, s=s: e.tensor_add(
                        dst[:, s:s + 512], t1[:, 0:512], t1[:, 512:1024]),
                        waits=bt1.w + bdst.write_waits(), tag="rope_m3")
                    bt1.note_read(e3)
                    bdst.note_write_acc(e3)

    def emit_core(hpi):
        sl = hpi % 2
        dst, bdst = qk[sl], bqk[sl]
        if relbias:
            relbias_fn(2 * hpi + 0, 0)
            relbias_fn(2 * hpi + 1, 1)
        bp_slots = [C.buf(f"att_p{i}") for i in range(6)]
        AVPS = ("pa", "pb")
        SPS = ("pc", "pd")
        for qc in range(4):
            ki_hi = (4 * qc + 4) if causal else TT

            def emit_S(ki, hh):
                m = ki - 4 * qc
                v0 = m * 128 if (causal and m >= 0) else 0
                hoff = 64 * hh
                ps_s = C.ps[SPS[hh]]
                bss = C.buf(SPS[hh])
                mm = P.op("tensor", lambda e, dst=dst, hoff=hoff, ki=ki, qc=qc, v0=v0, ps_s=ps_s:
                    e.matmul(
                        ps_s[0:128, v0:512],
                        dst[hoff:hoff + 64, T + ki * 128:T + (ki + 1) * 128],
                        dst[hoff:hoff + 64, qc * 512 + v0:(qc + 1) * 512],
                        start=True, stop=True),
                    waits=bdst.read_waits() + bss.write_waits(), tag="att_s")
                bdst.note_read(mm)
                bss.note_write(mm)

            def emit_P(ki, hh):
                m = ki - 4 * qc
                v0 = m * 128 if (causal and m >= 0) else 0
                ps_s = C.ps[SPS[hh]]
                bss = C.buf(SPS[hh])
                slot = (ki % 3) + 3 * hh
                bsl = bp_slots[slot]
                ex = P.op("scalar", lambda e, slot=slot, v0=v0, ps_s=ps_s: e.activation(
                    pbuf[:, slot * 512 + v0:(slot + 1) * 512], ps_s[0:128, v0:512], AF.Exp,
                    scale=fv[:, 5:6]),
                    waits=bss.w + bsl.write_waits() + bfv.read_waits(), tag="att_exp")
                bss.note_read(ex)
                bsl.note_write(ex)
                if causal and 0 <= m < 4:
                    tm = P.op("vector", lambda e, slot=slot, v0=v0: e.tensor_mul(
                        pbuf[:, slot * 512 + v0:slot * 512 + v0 + 128],
                        pbuf[:, slot * 512 + v0:slot * 512 + v0 + 128], tri[:, :]),
                        waits=bsl.w + btri.read_waits(), tag="att_tri")
                    btri.note_read(tm)
                    bsl.note_write_acc(tm)
                if relbias and (ki * 128 + 127) - (qc * 512) >= 1026:
                    c0 = qc * 512 + 2047 - (ki * 128 + 127)
                    wm = P.op("vector", lambda e, slot=slot, c0=c0, hh=hh: e.tensor_mul(
                        pbuf[:, slot * 512:(slot + 1) * 512],
                        pbuf[:, slot * 512:(slot + 1) * 512],
                        d2[:, hh * 2048 + c0:hh * 2048 + c0 + 512]),
                        waits=bsl.w + bd2.read_waits(), tag="att_wedge")
                    bd2.note_read(wm)
                    bsl.note_write_acc(wm)

            def emit_AV(ki, hh):
                m = ki - 4 * qc
                v0 = m * 128 if (causal and m >= 0) else 0
                slot = (ki % 3) + 3 * hh
                bsl = bp_slots[slot]
                ps_av = C.ps[AVPS[hh]]
                bav = C.buf(AVPS[hh])
                av = P.op("tensor", lambda e, sl=sl, hh=hh, ki=ki, v0=v0, slot=slot, ps_av=ps_av, last=(ki == ki_hi - 1):
                    e.matmul(
                        ps_av[0:65, v0:512],
                        v[:, sl * VSL + ki * 130 + hh * 65: sl * VSL + ki * 130 + (hh + 1) * 65],
                        pbuf[:, slot * 512 + v0:(slot + 1) * 512],
                        start=(ki == 0), stop=last),
                    waits=bv.read_waits() + bsl.w +
                          (bav.write_waits() if ki == 0 else []), tag="att_av")
                bv.note_read(av)
                bsl.note_read(av)
                bav.note_write_acc(av)

            for hh in range(2):
                emit_S(0, hh)
            for hh in range(2):
                emit_P(0, hh)
            for ki in range(ki_hi):
                if ki + 1 < ki_hi:
                    emit_S(ki + 1, 0)
                    emit_S(ki + 1, 1)
                emit_AV(ki, 0)
                emit_AV(ki, 1)
                if ki + 1 < ki_hi:
                    emit_P(ki + 1, 0)
                    emit_P(ki + 1, 1)
            for hh in range(2):
                hoff = 64 * hh
                ps_av = C.ps[AVPS[hh]]
                bav = C.buf(AVPS[hh])
                d1 = P.op("scalar", lambda e, ps_av=ps_av: e.activation(
                    den[64:65, 0:512], ps_av[64:65, 0:512], AF.Ln),
                    waits=bav.w + bden.write_waits(), tag="att_den")
                bav.note_read(d1)
                bden.note_write(d1)
                denb = den.bitcast(BF16)
                d2o = P.op("scalar", lambda e, denb=denb: e.activation(
                    denb[64:65, 1024:1536], den[64:65, 0:512], AF.Exp, scale=fv[64:65, 4:5]),
                    waits=bden.w + bfv.read_waits(), tag="att_rec")
                bfv.note_read(d2o)
                bden.note_write_acc(d2o)
                rb = P.op("tensor", lambda e, denb=denb: e.matmul(
                    C.ps["pbig"][0:64, 0:512], onesb[64:65, 0:64], denb[64:65, 1024:1536],
                    start=True, stop=True),
                    waits=bden.w + bonesb.read_waits() + C.buf("pbig").write_waits(), tag="att_rb_mm")
                bden.note_read(rb)
                bonesb.note_read(rb)
                C.buf("pbig").note_write(rb)
                rc = P.op("vector", lambda e: e.tensor_copy(rbs[0:64, 0:512], C.ps["pbig"][0:64, 0:512]),
                          waits=C.buf("pbig").w + brb.write_waits(), tag="att_rb_cp")
                C.buf("pbig").note_read(rc)
                brb.note_write(rc)
                oe = P.op("vector", lambda e, hoff=hoff, hpi=hpi, qc=qc, ps_av=ps_av: e.tensor_mul(
                    o[hoff:hoff + 64, hpi * T + qc * 512:hpi * T + (qc + 1) * 512],
                    ps_av[0:64, 0:512], rbs[0:64, 0:512]),
                    waits=bav.w + brb.w + bo.write_waits(), tag="att_o_ev")
                bav.note_read(oe)
                brb.note_read(oe)
                bo.note_write_acc(oe)

    emit_proj(0)
    for hpi in range(DT):
        if hpi + 1 < DT:
            emit_proj(hpi + 1)
        emit_core(hpi)

    ob = C.sb["att_ob"]; bob = C.buf("att_ob")
    PS4 = ("pa", "pb", "pc", "pd")
    for mt in range(DT):
        tiles = s_o.mt_tiles(mt)
        for kt in range(DT):
            tile, bw = tiles[kt]
            for ch in range(4):
                s = ch * 512
                ps = C.ps[PS4[ch]]
                bps = C.buf(PS4[ch])
                mm = P.op("tensor", lambda e, tile=tile, kt=kt, s=s, ps=ps: e.matmul(
                    ps[0:128, 0:512], tile, o[:, kt * T + s: kt * T + s + 512],
                    start=(kt == 0), stop=(kt == DT - 1)),
                    waits=bw.read_waits() + bo.read_waits() +
                          (bps.write_waits() if kt == 0 else []), tag="att_o_mm")
                bw.note_read(mm)
                bo.note_read(mm)
                bps.note_write_acc(mm)
        for ch in range(4):
            s = ch * 512
            ps = C.ps[PS4[ch]]
            bps = C.buf(PS4[ch])
            ev = P.op("vector", lambda e, mt=mt, s=s, ps=ps: e.scalar_tensor_tensor(
                xres[:, mt * T + s:mt * T + s + 512], ps[0:128, 0:512], ob[:, mt:mt + 1],
                xres[:, mt * T + s:mt * T + s + 512], ALU.add, ALU.add),
                waits=bps.w + bob.read_waits() + bxres.write_waits(), tag="att_res")
            bps.note_read(ev)
            bob.note_read(ev)
            bxres.note_write_acc(ev)


def _alloc_attn_common(C, has_bias):
    C.salloc("att_qk0", (128, 2 * T), BF16)
    C.salloc("att_qk1", (128, 2 * T), BF16)
    C.salloc("att_v", (128, 2 * TT * 130), BF16)
    C.salloc("att_o", (128, DT * T), BF16)
    C.salloc("att_p", (128, 6 * 512), BF16)
    C.salloc("att_den", (65, 768), F32)
    C.salloc("att_rb", (64, 512), F32)
    if "wt_ps" not in C.ps:
        C.psum("wt_ps", (128, 1024), BF16)


def _load_bias_cols(C, din, name, vec):
    arr = np.ascontiguousarray(vec.reshape(DT, 128).T, dtype=np.float32)
    d = din(name + "_in", arr)
    return emit_load_const(C, name, d, arr.shape, F32)


def _load_vbias(C, din, name, vec):
    arr = np.ascontiguousarray(vec.reshape(1, 1024), dtype=np.float32)
    d = din(name + "_in", arr)
    h = C.salloc(name, (1, 1024), BF16)
    oid = C.plan.dma("gpsimd", h[0:1, :], d[0:1, :],
                     waits=C.buf(name).write_waits(), tag=f"load_{name}")
    C.buf(name).note_write(oid)
    return h


def prep_attn_l0a(C, W, din):
    _alloc_attn_common(C, True)
    b = W["L0_attn_a_b_qkv"]
    qkb = np.concatenate([b[0:1024].reshape(DT, 128).T, b[1024:2048].reshape(DT, 128).T,
                          b[2048:3072].reshape(DT, 128).T], axis=1)
    dq = din("att_qkb_in", np.ascontiguousarray(qkb, dtype=np.float32))
    emit_load_const(C, "att_qkb", dq, qkb.shape, F32)
    _load_bias_cols(C, din, "att_ob", W["L0_attn_a_b_o"])
    s_qkv = WStreamT(C, "wqkv", din("W_qkv", W["L0_attn_a_W_qkv"]), 3072, 1024)
    s_o = WStreamT(C, "wo", din("W_o", W["L0_attn_a_W_o"]), 1024, 1024)
    streams = {"qk": (s_qkv, 0, 1024), "v": (s_qkv, 2048), "o": (s_o,)}
    return lambda: emit_attn(C, "l0a", streams, has_bias=True, causal=True)


def prep_attn_l1b(C, W, din):
    _alloc_attn_common(C, False)
    zeros = np.zeros((128, DT), np.float32)
    dz = din("att_ob_in", zeros)
    emit_load_const(C, "att_ob", dz, zeros.shape, F32)
    pos = np.arange(T, dtype=np.float64)
    didx = np.arange(128) % 64
    inv = 1.0 / (10000.0 ** ((didx % 32) * 2.0 / 64.0))
    ang = inv[:, None] * pos[None, :]
    cosb = np.cos(ang).astype(np.float32)
    sinb = np.sin(ang).astype(np.float32)
    nc_din_bf16(C, din, "rope_cos", cosb)
    nc_din_bf16(C, din, "rope_sin", sinb)
    R = np.zeros((128, 128), np.float32)
    for half in range(2):
        b0 = 64 * half
        for m in range(64):
            if m < 32:
                R[b0 + m + 32, b0 + m] = -1.0
            else:
                R[b0 + m - 32, b0 + m] = 1.0
    dr = nc_din_bf16(C, din, "rope_R", R)
    s_qkv = WStreamT(C, "w1", din("W1", W["L1_attn_b_W1"]), 3072, 1024)
    s_o = WStreamT(C, "w2", din("W2", W["L1_attn_b_W2"]), 1024, 1024)
    streams = {"qk": (s_qkv, 0, 1024), "v": (s_qkv, 2048), "o": (s_o,)}
    return lambda: emit_attn(C, "l1b", streams, has_bias=False, causal=True, rope=True)


def nc_din_bf16(C, din, name, arr_f32):
    arr = np.ascontiguousarray(arr_f32, dtype=np.float32)
    d = din(name + "_in", arr)
    h = C.salloc(name, arr.shape, BF16)
    oid = C.plan.dma("gpsimd", h[0:arr.shape[0], :], d[:, :],
                     waits=C.buf(name).write_waits(), tag=f"load_{name}")
    C.buf(name).note_write(oid)
    return h


class QKDispatch:

    def __init__(self, sq, sk):
        self.sq, self.sk = sq, sk

    def mt_tiles(self, mt, row0=0):
        return (self.sq if row0 == 0 else self.sk).mt_tiles(mt, row0=0)


def prep_attn_l0b(C, W, din):
    _alloc_attn_common(C, True)
    qkb = np.concatenate([W["L0_attn_b_bq"].reshape(DT, 128).T,
                          W["L0_attn_b_bk"].reshape(DT, 128).T,
                          W["L0_attn_b_bv"].reshape(DT, 128).T], axis=1)
    dq = din("att_qkb_in", np.ascontiguousarray(qkb, dtype=np.float32))
    emit_load_const(C, "att_qkb", dq, qkb.shape, F32)
    _load_bias_cols(C, din, "att_ob", W["L0_attn_b_bo"])
    sq = WStreamN(C, "wq", din("Wq", W["L0_attn_b_Wq"]), 1024, 1024)
    sk = WStreamN(C, "wk", din("Wk", W["L0_attn_b_Wk"]), 1024, 1024)
    sv = WStreamN(C, "wv", din("Wv", W["L0_attn_b_Wv"]), 1024, 1024)
    so = WStreamN(C, "wo2", din("Wo", W["L0_attn_b_Wo"]), 1024, 1024)
    P = C.plan
    nc = C.nc
    rt_d = din("rel_table", W["L0_attn_b_rel_table"])
    eg = C.salloc_in("att_o", "rb_eg", (16, 2176), F32, 0)
    beg = C.buf("rb_eg")
    rt = C.salloc_in("att_o", "rb_rt", (128, 8 * 16), F32, 2176)
    brt = C.buf("rb_rt")
    C.salloc("rb_d2", (128, 2 * 2048), BF16)
    egpad_d = nc.dram_tensor("rb_egpad_d", [16, 2175], BF16, kind="Internal")
    egpad_ap = egpad_d.ap()
    ident = C.sb["identity_f32"]
    pbig = C.ps["pbig"]
    bpb = C.buf("pbig")

    def emit_rel_pre():
        ms = P.op("vector", lambda e: e.memset(eg[:, :], 1.0), waits=beg.write_waits(), tag="rb_ms")
        beg.note_write(ms)
        for t in range(8):
            r0 = t * 128
            r1 = min(1023, r0 + 128)
            nr = r1 - r0
            ld = P.dma("sync", rt[0:nr, t * 16:(t + 1) * 16], rt_d[r0:r1, :],
                       waits=brt.write_waits(), tag="rb_ld")
            brt.note_write_acc(ld)
            tp = P.op("tensor", lambda e, t=t, nr=nr: e.transpose(
                pbig[0:16, t * 128:t * 128 + nr], rt[0:nr, t * 16:(t + 1) * 16],
                ident[0:nr, 0:nr]),
                waits=[ld] + C.buf("identity_f32").read_waits() + bpb.write_waits(), tag="rb_tp")
            brt.note_read(tp)
            bpb.note_write_acc(tp)
        ev = P.op("vector", lambda e: e.tensor_copy(eg[0:16, 0:1023], pbig[0:16, 0:1023]),
                  waits=bpb.w + [ms], tag="rb_ev")
        bpb.note_read(ev)
        beg.note_write_acc(ev)
        ch = P.op("vector", lambda e: e.tensor_copy(eg[0:16, 2175:2176], eg[0:16, 1022:1023]),
                  waits=beg.w, tag="rb_ch")
        beg.note_read(ch)
        beg.note_write_acc(ch)
        sb = P.op("vector", lambda e: e.tensor_single_scalar(
            eg[0:16, 0:1023], eg[0:16, 0:1023], eg[0:16, 2175:2176], ALU.subtract),
            waits=beg.w, tag="rb_sub")
        beg.note_write_acc(sb)
        ex = P.op("scalar", lambda e: e.activation(eg[0:16, 0:1023], eg[0:16, 0:1023], AF.Exp),
                  waits=beg.w, tag="rb_exp")
        beg.note_write_acc(ex)
        st = P.dma("gpsimd", egpad_ap[:, :], eg[0:16, 0:2175], waits=beg.w + beg.r, tag="rb_st")
        beg.note_read(st)
        C.buf("rb_egpad_d").note_write(st)

    rev = C.sb["rev_bf16"]

    def relbias_fn(h, dslot=0):
        old_toks = []
        for nm in ("att_den", "att_rb", "rb_d2f"):
            b = C.bufs.get(nm)
            if b:
                old_toks += b.w + b.r
        if C.fused:
            d2f = C.salloc_in("att_den", "rb_d2f", (128, 2048), BF16, 0)
        else:
            d2f = C.sb["rb_d2f"] if "rb_d2f" in C.sb else C.sbuf("rb_d2f", (128, 2048), BF16)
        bd2 = C.buf("rb_d2")
        bd2f = C.buf("rb_d2f")
        bd2f.w = sorted(set(bd2f.w + old_toks))
        d2h = C.sb["rb_d2"]
        srcap = bass.AP(egpad_d, h * 2175, [[1, 128], [1, 2048]])
        dm = P.dma("scalar", d2f[:, :], srcap,
                   waits=C.buf("rb_egpad_d").read_waits() + bd2f.write_waits(), tag="rb_d2f")
        bd2f.note_write(dm)
        for cch in range(2):
            s = cch * 1024
            for half in range(2):
                mm = P.op("tensor", lambda e, s=s, half=half, d2f=d2f: e.matmul(
                    C.ps["pbig"][:, half * 512:(half + 1) * 512], rev[:, :],
                    d2f[:, s + half * 512:s + (half + 1) * 512], start=True, stop=True),
                    waits=bd2f.w + C.buf("rev_bf16").read_waits() +
                          C.buf("pbig").write_waits(), tag="rb_rev")
                bd2f.note_read(mm)
                C.buf("rev_bf16").note_read(mm)
                C.buf("pbig").note_write_acc(mm)
            ev = P.op("vector", lambda e, s=s, dslot=dslot: e.tensor_copy(
                d2h[:, dslot * 2048 + s:dslot * 2048 + s + 1024], C.ps["pbig"][:, 0:1024]),
                waits=C.buf("pbig").w + bd2.write_waits(), tag="rb_rev_ev")
            C.buf("pbig").note_read(ev)
            bd2.note_write_acc(ev)

    emit_rel_pre()
    if C.fused:
        C.arena_reset(["att_o"], ["rb_eg", "rb_rt", "att_o"])
    streams = {"qk": (QKDispatch(sq, sk), 0, 1), "v": (sv, 0), "o": (so,),
               "relbias_fn": relbias_fn}
    return lambda: emit_attn(C, "l0b", streams, has_bias=True, causal=False, relbias=True)



def prep_mamba(C, W, din):
    nc, P = C.nc, C.plan
    def cols(name, vec, per):
        arr = np.ascontiguousarray(vec.reshape(-1, 128).T, dtype=np.float32)
        d = din(name + "_in", arr)
        return emit_load_const(C, name, d, arr.shape, F32)
    cols("mb_cvb", W["L1_attn_a_conv_b"], 16)
    cols("mb_dtb", W["L1_attn_a_dt_proj_b"], 16)
    cols("mb_dcol", W["L1_attn_a_D"], 16)
    cvw = np.ascontiguousarray(
        W["L1_attn_a_conv_w"].reshape(16, 128, 4).transpose(1, 0, 2).reshape(128, 64),
        dtype=np.float32)
    emit_load_const(C, "mb_cvw", din("mb_cvw_in", cvw), cvw.shape, F32)
    alog = np.ascontiguousarray(
        W["L1_attn_a_A_log"].reshape(16, 128, 16).transpose(1, 0, 2).reshape(128, 256),
        dtype=np.float32)
    C.salloc("mb_cb", (128, 16 * 1024), BF16)
    if C.fused:
        C.sb["mb_alog"] = C.salloc_in("mb_cb", "mb_alog", (128, 256), F32, 0)
        alog_ap = emit_load_const(C, "mb_alog", din("mb_alog_in", alog), alog.shape, F32)
    else:
        alog_ap = emit_load_const(C, "mb_alog", din("mb_alog_in", alog), alog.shape, F32)
    acol = C.salloc("mb_acol", (128, 256), F32)
    e1 = P.op("scalar", lambda e: e.activation(acol[:, :], alog_ap[0:128, :], AF.Exp),
              waits=C.buf("mb_alog").read_waits(), tag="mb_acol_e")
    C.buf("mb_alog").note_read(e1)
    C.buf("mb_acol").note_write(e1)
    e2 = P.op("vector", lambda e: e.tensor_scalar_mul(acol[:, :], acol[:, :], C.sb["fvals"][:, 4:5]),
              waits=C.buf("mb_acol").w + C.buf("fvals").read_waits(), tag="mb_acol_n")
    C.buf("mb_acol").note_write(e2)
    if C.fused:
        C.arena_reset(["mb_cb"], ["mb_alog", "mb_cb"])
    eh = np.zeros((96, 32 * 128), np.float32)
    for n in range(16):
        eh[64 + n, n * 128:(n + 1) * 128] = 1.0
        eh[80 + n, (16 + n) * 128:(17 + n) * 128] = 1.0
    ehot_d = din("mb_ehot_in", eh)
    _ensure_ws(C, 2048)
    w_in = WStreamT(C, "win4", din("in_proj_w", W["L1_attn_a_in_proj_w"]), 4096, 1024)
    w_xp = WeightLoader(C, "wxp", din("x_proj_w", W["L1_attn_a_x_proj_w"]), 96, 2048, True)
    w_dt = WeightLoader(C, "wdt", din("dt_proj_w", W["L1_attn_a_dt_proj_w"]), 2048, 64, True)
    w_out = WStreamT(C, "wout4", din("out_proj_w", W["L1_attn_a_out_proj_w"]), 1024, 2048)
    w_dt.prefetch_all()
    xc_d = nc.dram_tensor("mb_xc_d", [2048, 2048], BF16, kind="Internal").ap()
    z_d = nc.dram_tensor("mb_z_d", [2048, 2048], BF16, kind="Internal").ap()
    g_d = nc.dram_tensor("mb_g_d", [2048, 2048], BF16, kind="Internal").ap()
    return lambda: emit_mamba(C, w_in, w_xp, w_dt, w_out, xc_d, z_d, g_d, ehot_d)


def emit_mamba(C, w_in, w_xp, w_dt, w_out, xc_d, z_d, g_d, ehot_d):
    P = C.plan
    xl = C.sb["x_ln"]; bxl = C.buf("x_ln")
    xres = C.sb["x_res"]; bxres = C.buf("x_res")
    fv = C.sb["fvals"]; bfv = C.buf("fvals")
    acol = C.sb["mb_acol"]; bacol = C.buf("mb_acol")
    cvw = C.sb["mb_cvw"]; cvb = C.sb["mb_cvb"]
    dtb = C.sb["mb_dtb"]; dcol = C.sb["mb_dcol"]
    ehot = C.sb["ws_nat"]; behot = C.buf("ws_nat")
    cbsb = C.sb["mb_cb"]; bcb = C.buf("mb_cb")
    bxcd = C.buf("mb_xc_d"); bzd = C.buf("mb_z_d"); bgd = C.buf("mb_g_d")
    identb = C.sb["identity_bf16"]

    mb0 = C.salloc_block(9728) if C.fused else None
    C.carve_rel(mb0, "w_wxp", 4100, 1536, BF16)
    w_xp.prefetch_all()
    xe = C.carve_rel(mb0, "mb_xe", 0, 2052, F32)
    bxe = C.buf("mb_xe")
    acc = C.carve_rel(mb0, "mb_acc", 2052, 2048, F32)
    bacc = C.buf("mb_acc")
    zstg = C.carve_rel(mb0, "mb_zstg", 2052, 2048, BF16)
    bzstg = C.buf("mb_zstg")
    xdblb = C.carve_rel(mb0, "mb_xdblb", 8704, 2048, BF16)
    bxdblb = C.buf("mb_xdblb")
    C.arena_reset(["mb_xe", "mb_acc", "mb_zstg", "mb_xdblb"],
                  ["ln_t1", "ln_sq", "ln_rows"])
    XPPS = (("pc", 0), ("pd", 0), ("pbig", 0), ("pbig", 512))
    for mt in range(32):
        dt = mt % 16
        is_x = mt < 16
        if mt == 16:
            C.arena_reset(["mb_zstg"], ["mb_xe", "mb_acc"])
        tiles = w_in.mt_tiles(mt)
        if is_x:
            z3 = P.op("vector", lambda e: e.memset(xe[:, 0:3], 0.0),
                      waits=bxe.write_waits(), tag="mb_xe0")
            bxe.note_write(z3)
        PS3 = ("pa", "pb", "pe")
        for ch in range(4):
            s = ch * 512
            ps = C.ps[PS3[ch % 3]]
            bps = C.buf(PS3[ch % 3])
            for kt in range(DT):
                tile, bw = tiles[kt]
                mm = P.op("tensor", lambda e, tile=tile, kt=kt, s=s, ps=ps: e.matmul(
                    ps[0:128, 0:512], tile, xl[:, kt * T + s:kt * T + s + 512],
                    start=(kt == 0), stop=(kt == DT - 1)),
                    waits=bw.read_waits() + bxl.read_waits() +
                          (bps.write_waits() if kt == 0 else []), tag="mb_inp")
                bw.note_read(mm)
                bxl.note_read(mm)
                bps.note_write_acc(mm)
            if is_x:
                ev = P.op("scalar", lambda e, s=s, ps=ps: e.activation(
                    xe[:, 3 + s:3 + s + 512], ps[0:128, 0:512], AF.Copy),
                    waits=bps.w + bxe.write_waits(), tag="mb_xe_ev")
                bps.note_read(ev)
                bxe.note_write_acc(ev)
            else:
                ev = P.op("scalar", lambda e, s=s, ps=ps: e.activation(
                    zstg[:, s:s + 512], ps[0:128, 0:512], AF.Silu),
                    waits=bps.w + bzstg.write_waits(), tag="mb_z_ev")
                bps.note_read(ev)
                bzstg.note_write_acc(ev)
        if is_x:
            c0 = P.op("vector", lambda e, dt=dt: e.tensor_scalar(
                acc[:, :], xe[:, 3:2051], cvw[:, dt * 4 + 3:dt * 4 + 4], None, ALU.mult),
                waits=bxe.w + C.buf("mb_cvw").read_waits() + bacc.write_waits(), tag="mb_cv3")
            bxe.note_read(c0)
            bacc.note_write(c0)
            for j in (2, 1, 0):
                cj = P.op("vector", lambda e, dt=dt, j=j: e.scalar_tensor_tensor(
                    acc[:, :], xe[:, j:j + 2048], cvw[:, dt * 4 + j:dt * 4 + j + 1],
                    acc[:, :], ALU.mult, ALU.add),
                    waits=bxe.w + bacc.w + C.buf("mb_cvw").read_waits(), tag="mb_cv")
                bxe.note_read(cj)
                bacc.note_write(cj)
            xco = xe[:, 0:1024].bitcast(BF16)
            sl = P.op("scalar", lambda e, dt=dt: e.activation(
                xco[:, 0:2048], acc[:, :], AF.Silu, bias=cvb[:, dt:dt + 1]),
                waits=bacc.w + C.buf("mb_cvb").read_waits() + bxe.write_waits(), tag="mb_silu")
            bacc.note_read(sl)
            bxe.note_write(sl)
            st = P.dma("sync", xc_d[dt * 128:(dt + 1) * 128, :], xco[:, 0:2048],
                       waits=[sl] + bxcd.write_waits(), tag="mb_xc_st")
            bxe.note_read(st)
            bxcd.note_write_acc(st)
            for ch in range(4):
                s = ch * 512
                pnm, po = XPPS[ch]
                ps = C.ps[pnm]
                bps = C.buf(pnm)
                tile, bw = w_xp.tile(dt, 0)
                mm = P.op("tensor", lambda e, tile=tile, s=s, ps=ps, po=po, dt=dt: e.matmul(
                    ps[0:96, po:po + 512], tile, xco[:, s:s + 512],
                    start=(dt == 0), stop=(dt == 15), skip_group_check=True),
                    waits=bw.read_waits() + [sl] +
                          (bps.write_waits() if dt == 0 else []), tag="mb_xp")
                bw.note_read(mm)
                bxe.note_read(mm)
                bps.note_write_acc(mm)
        else:
            st = P.dma("sync", z_d[dt * 128:(dt + 1) * 128, :], zstg[:, 0:2048],
                       waits=bzstg.w + bzd.write_waits(), tag="mb_z_st")
            bzstg.note_read(st)
            bzd.note_write_acc(st)
    for ch in range(4):
        pnm, po = XPPS[ch]
        ps = C.ps[pnm]
        bps = C.buf(pnm)
        ev = P.op("scalar", lambda e, ch=ch, ps=ps, po=po: e.activation(
            xdblb[0:96, ch * 512:(ch + 1) * 512], ps[0:96, po:po + 512], AF.Copy),
            waits=bps.w + bxdblb.write_waits(), tag="mb_xdbl_ev")
        bps.note_read(ev)
        bxdblb.note_write_acc(ev)
    ehl = P.dma("gpsimd", ehot[0:96, 0:4096], ehot_d[:, :],
                waits=behot.write_waits(), tag="mb_ehot_ld")
    behot.note_write(ehl)

    dA = C.carve_rel(mb0, "mb_dA", 0, 1024, F32)
    bdA = C.buf("mb_dA")
    dBu = C.carve_rel(mb0, "mb_dBu", 1024, 1024, BF16)
    bdBu = C.buf("mb_dBu")
    hbf = C.carve_rel(mb0, "mb_h", 1536, 1024, BF16)
    bh = C.buf("mb_h")
    hc = C.carve_rel(mb0, "mb_hc", 2048, 1024, BF16)
    bhc = C.buf("mb_hc")
    gate1 = C.carve_rel(mb0, "mb_g1", 4096, 1024, F32)
    bg1 = C.buf("mb_g1")
    spt = C.carve_rel(mb0, "mb_spt", 6144, 512, F32)
    bspt = C.buf("mb_spt")
    zg1, bzg1 = C.carve_rel(mb0, "mb_zg0", 2560, 1024, BF16), C.buf("mb_zg0")
    zgs = [(zg1, bzg1), (zg1, bzg1)]
    ubs = [(C.carve_rel(mb0, "mb_u0", 3072, 1024, BF16), C.buf("mb_u0")),
           (C.carve_rel(mb0, "mb_u1", 6656, 1024, BF16), C.buf("mb_u1"))]
    xc2s = [(C.carve_rel(mb0, "mb_xc20", 3584, 1024, BF16), C.buf("mb_xc20")),
            (C.carve_rel(mb0, "mb_xc21", 7168, 1024, BF16), C.buf("mb_xc21"))]
    deltas = [(C.carve_rel(mb0, "mb_delta0", 5120, 1024, F32), C.buf("mb_delta0")),
              (C.carve_rel(mb0, "mb_delta1", 7680, 1024, F32), C.buf("mb_delta1"))]
    if C.fused:
        dA2 = C.salloc_in("ln_sq", "mb_dA2", (128, 1024), F32, 0)
        dBu2 = C.salloc_in("ln_sq", "mb_dBu2", (128, 1024), BF16, 1024)
        hbf2 = C.salloc_in("ln_sq", "mb_h2", (128, 1024), BF16, 1536)
        hc2 = C.salloc("mb_hc2", (128, 1024), BF16)
        C.arena_reset(["mb_dA2", "mb_dBu2", "mb_h2"],
                      ["ln_sq", "ln_rows", "mb_dA2", "mb_dBu2", "mb_h2"])
    else:
        dA2, dBu2, hbf2, hc2 = dA, dBu, hbf, hc
    bdA2 = C.buf("mb_dA2") if C.fused else bdA
    bdBu2 = C.buf("mb_dBu2") if C.fused else bdBu
    bh2 = C.buf("mb_h2") if C.fused else bh
    bhc2 = C.buf("mb_hc2") if C.fused else bhc
    C.arena_reset(["mb_dA", "mb_dBu", "mb_h", "mb_hc", "mb_zg0",
                   "mb_u0", "mb_u1", "mb_xc20", "mb_xc21", "mb_g1", "mb_spt",
                   "mb_delta0", "mb_delta1"],
                  ["mb_xe", "mb_acc", "mb_zstg", "w_wxp"])
    hstate = C.salloc("mb_hst", (128, 256), F32)
    bhst = C.buf("mb_hst")
    bbsb, bbb = xl, bxl
    py = C.ps["pbig"]
    bpy = C.buf("pbig")

    for half in range(2):
        hs = half * 1024
        for n in range(NH):
            for (base, dst, bdst) in ((0, bbsb, bbb), (16, cbsb, bcb)):
                for q in range(2):
                    ps = C.ps[("pa", "pb")[q]]
                    bps = C.buf(("pa", "pb")[q])
                    mm = P.op("tensor", lambda e, base=base, n=n, q=q, hs=hs, ps=ps: e.matmul(
                        ps[0:128, 0:512],
                        ehot[0:96, (base + n) * 128:(base + n + 1) * 128],
                        xdblb[0:96, hs + q * 512:hs + (q + 1) * 512],
                        start=True, stop=True),
                        waits=behot.read_waits() + bxdblb.read_waits() + bps.write_waits(),
                        tag="mb_sel")
                    behot.note_read(mm)
                    bxdblb.note_read(mm)
                    bps.note_write(mm)
                    ev = P.op("scalar", lambda e, n=n, q=q, dst=dst, ps=ps: e.activation(
                        dst[:, n * 1024 + q * 512:n * 1024 + (q + 1) * 512],
                        ps[0:128, 0:512], AF.Copy),
                        waits=bps.w + bdst.write_waits(), tag="mb_sel_ev")
                    bps.note_read(ev)
                    bdst.note_write_acc(ev)
        def emit_ldz(phalf, pdt):
            phs = phalf * 1024
            pzg, pbzg = zgs[pdt % 2]
            ldz = P.dma("gpsimd", pzg[:, 0:1024],
                        z_d[pdt * 128:(pdt + 1) * 128, phs:phs + 1024],
                        waits=bzd.w + pbzg.write_waits(), tag="mb_z_ld")
            bzd.note_read(ldz)
            pbzg.note_write(ldz)

        def prep(phalf, pdt):
            phs = phalf * 1024
            psl = pdt % 2
            pxc, pbxc = xc2s[psl]
            pdl, pbdl = deltas[psl]
            pub, pbub = ubs[psl]
            ldx = P.dma("gpsimd", pxc[:, 0:1024],
                        xc_d[pdt * 128:(pdt + 1) * 128, phs:phs + 1024],
                        waits=bxcd.w + pbxc.write_waits(), tag="mb_xc_ld")
            bxcd.note_read(ldx)
            pbxc.note_write(ldx)
            for ch in range(2):
                s = ch * 512
                ps = C.ps[("pc", "pd")[ch % 2]]
                bps = C.buf(("pc", "pd")[ch % 2])
                tile, bw = w_dt.tile(0, pdt)
                mm = P.op("tensor", lambda e, tile=tile, s=s, phs=phs, ps=ps: e.matmul(
                    ps[0:128, 0:512], tile, xdblb[0:64, phs + s:phs + s + 512],
                    start=True, stop=True),
                    waits=bw.read_waits() + bxdblb.read_waits() + bps.write_waits(), tag="mb_dt")
                bw.note_read(mm)
                bxdblb.note_read(mm)
                bps.note_write(mm)
                o1 = P.op("scalar", lambda e, s=s, pdt=pdt, ps=ps, pdl=pdl: e.activation(
                    pdl[:, s:s + 512], ps[0:128, 0:512], AF.Relu,
                    bias=dtb[:, pdt:pdt + 1]),
                    waits=bps.w + C.buf("mb_dtb").read_waits() + pbdl.write_waits(),
                    tag="mb_sp_r")
                bps.note_read(o1)
                pbdl.note_write_acc(o1)
                o2 = P.op("scalar", lambda e, pdt=pdt, ps=ps: e.activation(
                    spt[:, :], ps[0:128, 0:512], AF.Abs, bias=dtb[:, pdt:pdt + 1]),
                    waits=bps.w + C.buf("mb_dtb").read_waits() + bspt.write_waits(),
                    tag="mb_sp_abs")
                bps.note_read(o2)
                bspt.note_write(o2)
                o3 = P.op("scalar", lambda e: e.activation(spt[:, :], spt[:, :], AF.Exp,
                                                           scale=fv[:, 4:5]),
                          waits=bspt.w + bfv.read_waits(), tag="mb_sp_e")
                bspt.note_write_acc(o3)
                o4 = P.op("scalar", lambda e: e.activation(spt[:, :], spt[:, :], AF.Ln, bias=1.0),
                          waits=[o3], tag="mb_sp_l")
                bspt.note_write_acc(o4)
                o6 = P.op("vector", lambda e, s=s, pdl=pdl: e.tensor_add(
                    pdl[:, s:s + 512], pdl[:, s:s + 512], spt[:, :]),
                    waits=bspt.w + pbdl.write_waits() + pbdl.read_waits(), tag="mb_sp_f")
                bspt.note_read(o6)
                pbdl.note_write_acc(o6)
            uo = P.op("vector", lambda e, pub=pub, pdl=pdl, pxc=pxc: e.tensor_mul(
                pub[:, 0:1024], pdl[:, 0:1024], pxc[:, 0:1024]),
                      waits=pbdl.w + pbxc.w + pbub.write_waits(), tag="mb_u")
            pbdl.note_read(uo)
            pbxc.note_read(uo)
            pbub.note_write(uo)

        for dt in range(16):
            sl = dt % 2
            xcb2, bxc2 = xc2s[sl]
            zg, bzg = zgs[sl]
            delta, bdelta = deltas[sl]
            ub, bub = ubs[sl]
            if dt == 0 and half == 0:
                prep(0, 0)
                emit_ldz(0, 0)
            for n in range(NH):
                if n == 3:
                    if dt < 15:
                        prep(half, dt + 1)
                    elif half == 0:
                        prep(1, 0)
                dslot = n % 2
                dA_c, bdA_c = (dA, bdA) if dslot == 0 else (dA2, bdA2)
                da = P.op("scalar", lambda e, dt=dt, n=n, dA_c=dA_c, delta=delta: e.activation(
                    dA_c[:, 0:1024], delta[:, 0:1024], AF.Exp,
                    scale=acol[:, dt * 16 + n:dt * 16 + n + 1]),
                    waits=bdelta.read_waits() + bacol.read_waits() + bdA_c.write_waits(),
                    tag="mb_dA")
                bdelta.note_read(da)
                bacol.note_read(da)
                bdA_c.note_write(da)
                dslot = n % 2
                dBu_c, bdBu_c = (dBu, bdBu) if dslot == 0 else (dBu2, bdBu2)
                db = P.op("vector", lambda e, n=n, dBu_c=dBu_c, ub=ub: e.tensor_mul(
                    dBu_c[:, 0:1024], ub[:, 0:1024], bbsb[:, n * 1024:(n + 1) * 1024]),
                    waits=bub.read_waits() + bbb.read_waits() + bdBu_c.write_waits(), tag="mb_dBu")
                bub.note_read(db)
                bbb.note_read(db)
                bdBu_c.note_write(db)
                if half == 0:
                    ini = 0.0
                    iw = []
                else:
                    ini = hstate[:, dt * 16 + n:dt * 16 + n + 1]
                    iw = bhst.read_waits()
                h_c, bh_c = (hbf, bh) if dslot == 0 else (hbf2, bh2)
                hc_c, bhc_c = (hc, bhc) if dslot == 0 else (hc2, bhc2)
                sc = P.op("vector", lambda e, ini=ini, dA_c=dA_c, dBu_c=dBu_c, h_c=h_c: e.tensor_tensor_scan(
                    h_c[:, 0:1024], dA_c[:, 0:1024], dBu_c[:, 0:1024], ini, ALU.mult, ALU.add),
                    waits=bdA_c.w + bdBu_c.w + iw + bh_c.write_waits(), tag="mb_scan")
                bdA_c.note_read(sc)
                bdBu_c.note_read(sc)
                if half == 1:
                    bhst.note_read(sc)
                bh_c.note_write(sc)
                if half == 0:
                    hs_cp = P.op("scalar", lambda e, dt=dt, n=n, h_c=h_c: e.activation(
                        hstate[:, dt * 16 + n:dt * 16 + n + 1], h_c[:, 1023:1024], AF.Copy),
                        waits=bh_c.w + bhst.write_waits(), tag="mb_hst")
                    bh_c.note_read(hs_cp)
                    bhst.note_write_acc(hs_cp)
                hco = P.op("vector", lambda e, n=n, h_c=h_c, hc_c=hc_c: e.tensor_mul(
                    hc_c[:, 0:1024], h_c[:, 0:1024], cbsb[:, n * 1024:(n + 1) * 1024]),
                    waits=bh_c.w + bcb.read_waits() + bhc_c.write_waits(), tag="mb_hc")
                bh_c.note_read(hco)
                bcb.note_read(hco)
                bhc_c.note_write(hco)
                for q in range(2):
                    mm = P.op("tensor", lambda e, q=q, n=n, hc_c=hc_c: e.matmul(
                        py[:, q * 512:(q + 1) * 512], identb[:, :],
                        hc_c[:, q * 512:(q + 1) * 512],
                        start=(n == 0), stop=(n == NH - 1)),
                        waits=C.buf("identity_bf16").read_waits() + bhc_c.w +
                              (bpy.write_waits() if n == 0 else []), tag="mb_y")
                    C.buf("identity_bf16").note_read(mm)
                    bhc_c.note_read(mm)
                    bpy.note_write_acc(mm)
            g1a = P.op("scalar", lambda e, dt=dt, xcb2=xcb2: e.activation(
                gate1[:, 0:1024], xcb2[:, 0:1024], AF.Copy, scale=dcol[:, dt:dt + 1]),
                waits=bxc2.read_waits() + C.buf("mb_dcol").read_waits() +
                      bg1.write_waits(), tag="mb_g1a")
            bxc2.note_read(g1a)
            bg1.note_write(g1a)
            g1b = P.op("vector", lambda e: e.tensor_add(
                gate1[:, 0:1024], gate1[:, 0:1024], py[:, 0:1024]),
                waits=bg1.w + bpy.w, tag="mb_g1b")
            bpy.note_read(g1b)
            bg1.note_write_acc(g1b)
            go = P.op("vector", lambda e, zg=zg: e.tensor_mul(hc[:, 0:1024], gate1[:, 0:1024],
                                                              zg[:, 0:1024]),
                      waits=bg1.w + bzg.read_waits() + bhc.write_waits(), tag="mb_gate")
            bg1.note_read(go)
            bzg.note_read(go)
            bhc.note_write(go)
            st = P.dma("sync", g_d[dt * 128:(dt + 1) * 128, hs:hs + 1024], hc[:, 0:1024],
                       waits=bhc.w + bgd.write_waits(), tag="mb_g_st")
            bhc.note_read(st)
            bgd.note_write_acc(st)
            if dt < 15:
                emit_ldz(half, dt + 1)
            elif half == 0:
                emit_ldz(1, 0)

    for dtk in range(16):
        dst, bdst = (bbsb, bbb) if dtk < 8 else (cbsb, bcb)
        off = (dtk % 8) * 2048
        ld = P.dma("gpsimd", dst[:, off:off + 2048], g_d[dtk * 128:(dtk + 1) * 128, :],
                   waits=bgd.w + bdst.write_waits(), tag="mb_g_ld")
        bgd.note_read(ld)
        bdst.note_write_acc(ld)
    PS4 = ("pa", "pb", "pc", "pd")
    for mt in range(DT):
        tiles = w_out.mt_tiles(mt)
        for kt in range(16):
            tile, bw = tiles[kt]
            hsb, bsrc = (bbsb, bbb) if kt < 8 else (cbsb, bcb)
            for ch in range(4):
                s = ch * 512
                ps = C.ps[PS4[ch]]
                bps = C.buf(PS4[ch])
                srcap = hsb[:, (kt % 8) * 2048 + s:(kt % 8) * 2048 + s + 512]
                mm = P.op("tensor", lambda e, tile=tile, srcap=srcap, ps=ps, kt=kt: e.matmul(
                    ps[0:128, 0:512], tile, srcap, start=(kt == 0), stop=(kt == 15)),
                    waits=bw.read_waits() + bsrc.read_waits() +
                          (bps.write_waits() if kt == 0 else []), tag="mb_out")
                bw.note_read(mm)
                bsrc.note_read(mm)
                bps.note_write_acc(mm)
        for ch in range(4):
            s = ch * 512
            ps = C.ps[PS4[ch]]
            bps = C.buf(PS4[ch])
            ev = P.op("vector", lambda e, mt=mt, s=s, ps=ps: e.tensor_add(
                xres[:, mt * T + s:mt * T + s + 512], xres[:, mt * T + s:mt * T + s + 512],
                ps[0:128, 0:512]),
                waits=bps.w + bxres.write_waits(), tag="mb_res")
            bps.note_read(ev)
            bxres.note_write_acc(ev)


def prep_ffn_rwkv(C, W, din):
    C.salloc("ffn_xr", (128, DT * 512), BF16)
    C.salloc("ffn_xk", (128, DT * 512), BF16)
    C.salloc("ffn_d", (128, 512), BF16)
    C.salloc("ffn_r", (128, DT * 512), BF16)
    C.salloc("ffn_k2", (128, DT * 512), BF16)
    mu_cols = np.concatenate([W["L0_ffn_mu_r"].reshape(DT, 128).T,
                              W["L0_ffn_mu_k"].reshape(DT, 128).T], axis=1)
    dmu = din("mu_rk_in", np.ascontiguousarray(mu_cols, dtype=np.float32))
    emit_load_const(C, "mu_rk", dmu, mu_cols.shape, F32)
    _ensure_ws(C, 2048)
    wr = WeightLoader(C, "wr", din("W_r", W["L0_ffn_W_r"]), D, D, True)
    wk = WeightLoader(C, "wk", din("W_k", W["L0_ffn_W_k"]), D, D, True)
    wv = WvPrefetch(C, "wvs", din("W_v", W["L0_ffn_W_v"]))
    wr.prefetch_all()
    wk.prefetch_all()
    wv.prefetch_all()
    return lambda: emit_ffn_rwkv(C, wr, wk, wv)


def prep_ffn_small(C, W, din):
    v = np.zeros((64, 50), np.float32)
    v[:, 0:16] = W["L1_ffn_W1"]
    v[:, 16:32] = W["L1_ffn_b1"]
    v[:, 32:48] = W["L1_ffn_W2"]
    v[:, 48] = W["L1_ffn_bin_"]
    v[:, 49] = W["L1_ffn_b2"]
    dv = din("l1ffn_vec_in", v)
    emit_load_const(C, "l1ffn_vec", dv, v.shape, F32)
    bo = np.ascontiguousarray(W["L1_ffn_bout"].reshape(DT, 128).T, dtype=np.float32)
    dbo = din("l1ffn_bout_in", bo)
    emit_load_const(C, "l1ffn_bout", dbo, bo.shape, F32)
    C.salloc("l1ffn_s", (64, T), F32)
    C.salloc("l1ffn_h", (64, T), F32)
    C.salloc("l1ffn_h2", (64, T), F32)
    C.salloc("l1ffn_g", (64, T), F32)
    C.salloc("l1ffn_gb", (64, T), BF16)
    w_in = WeightLoader(C, "win", din("Win", W["L1_ffn_Win"]), D, 64, False)
    w_out = WeightLoader(C, "wout", din("Wout", W["L1_ffn_Wout"]), 64, D, False)
    w_in.prefetch_all()
    w_out.prefetch_all()
    return lambda: emit_ffn_small(C, w_in, w_out)


SUBLAYER_PREP = {
    "l0_ffn": prep_ffn_rwkv,
    "l0_ffn2": prep_ffn_rwkv,
    "l1_ffn": prep_ffn_small,
    "l1_ffn2": prep_ffn_small,
    "l0_attn_a": prep_attn_l0a,
    "l0_attn_b": prep_attn_l0b,
    "l1_attn_b": prep_attn_l1b,
    "l1_mamba": prep_mamba,
}


def build_stage(stage, W):
    nc = bass.Bass("TRN2", target_bir_lowering=False, debug=False)
    st = contextlib.ExitStack()
    plan = Plan(nc)
    C = Ctx(nc, plan, st)
    feeds = {}

    def din(name, arr):
        arr = np.ascontiguousarray(arr, dtype=np.float32)
        ap = nc.dram_tensor(name, list(arr.shape), F32, kind="ExternalInput").ap()
        feeds[name] = arr
        return ap

    with st:
        sems = {e: st.enter_context(nc.semaphore("s_" + e)) for e in ENGINES}
        dma_sems = {q: st.enter_context(nc.semaphore("d_" + q)) for q in DMA_QUEUES}

        for name, arr in _CONSTS.items():
            dt = BF16 if arr.dtype == BF16NP else F32
            d = nc.dram_tensor("c_" + name, list(arr.shape), dt, kind="ExternalInput").ap()
            feeds["c_" + name] = arr
            emit_load_const(C, name, d, arr.shape, dt)

        C.sbuf("x_res", (128, DT * T), F32)
        final_stores = []

        if stage == "in_tx":
            C.psum("pbig", (128, 1024), F32)
            x_in = nc.dram_tensor("x_in", [T, D], F32, kind="ExternalInput").ap()
            feeds["x_in"] = "X_TOKEN"
            xfm_out = nc.dram_tensor("xfm_out", [128, DT * T], F32, kind="ExternalOutput").ap()
            emit_in_transform(C, x_in)
            bx = C.buf("x_res")
            sto = plan.dma("sync", xfm_out[:, :], C.sb["x_res"][:, :],
                           waits=bx.w, tag="xfm_st")
            final_stores.append(sto)
        elif stage == "out_tx":
            C.psum("pbig", (128, 1024), F32)
            xfm_in = nc.dram_tensor("xfm_in", [128, DT * T], F32, kind="ExternalInput").ap()
            feeds["xfm_in"] = "XFM"
            out = nc.dram_tensor("out", [T, D], F32, kind="ExternalOutput").ap()
            ld = plan.dma("sync", C.sb["x_res"][:, :], xfm_in[:, :], tag="xfm_ld")
            C.buf("x_res").note_write(ld)
            final_stores += emit_out_transform(C, out)
        else:
            xfm_in = nc.dram_tensor("xfm_in", [128, DT * T], F32, kind="ExternalInput").ap()
            feeds["xfm_in"] = "XFM"
            xfm_out = nc.dram_tensor("xfm_out", [128, DT * T], F32, kind="ExternalOutput").ap()
            ld = plan.dma("sync", C.sb["x_res"][:, :], xfm_in[:, :], tag="xfm_ld")
            C.buf("x_res").note_write(ld)

            C.sbuf("x_ln", (128, DT * T), BF16)
            if stage == "l1_mamba":
                C.sbuf("mb_arena", (128, 9728), F32)
                C.carve("mb_arena", "ln_t1", 0, 1024, F32)
                C.carve("mb_arena", "ln_sq", 1024, 512, F32)
                C.carve("mb_arena", "ln_rows", 1536, 1536, F32)
            else:
                C.sbuf("ln_sq", (128, 512), F32)
                C.sbuf("ln_rows", (1, 1536), F32)
                C.sbuf("ln_t1", (128, 1024), F32)
            C.psum("pbig", (128, 1024), F32)
            C.psum("pa", (128, 512), F32)
            C.psum("pb", (128, 512), F32)
            C.psum("pc", (128, 512), F32)
            C.psum("pd", (128, 512), F32)
            C.psum("pe", (128, 512), F32)

            run_sub = SUBLAYER_PREP[stage](C, W, din)
            emit_ln(C, "x_res", "x_ln")
            run_sub()

            bx = C.buf("x_res")
            sto = plan.dma("sync", xfm_out[:, :], C.sb["x_res"][:, :],
                           waits=bx.write_waits(), tag="xfm_st")
            final_stores.append(sto)

        fw = plan.final_wait_spec(final_stores, engine="sync")
        with nc.Block() as block:
            plan.finalize_and_emit(block, sems, dma_sems, fw)
    return nc, feeds


def run_stage(stage, W, percore):
    nc, feeds = build_stage(stage, W)
    in_maps = []
    for core in range(NCORES):
        m = {}
        for name, arr in feeds.items():
            m[name] = percore[arr][core] if isinstance(arr, str) else arr
        in_maps.append(m)
    return run_bass_kernel_spmd(nc, in_maps, core_ids=list(range(NCORES)))


def build_full(W, seq=None):
    if seq is None:
        seq = SUBLAYER_SEQ
    nc = bass.Bass("TRN2", target_bir_lowering=False, debug=False)
    st = contextlib.ExitStack()
    plan = Plan(nc)
    C = Ctx(nc, plan, st)
    C.fused = True
    feeds = {}

    def din_raw(name, arr):
        arr = np.ascontiguousarray(arr, dtype=np.float32)
        ap = nc.dram_tensor(name, list(arr.shape), F32, kind="ExternalInput").ap()
        feeds[name] = arr
        return ap

    with st:
        sems = {e: st.enter_context(nc.semaphore("s_" + e)) for e in ENGINES}
        dma_sems = {q: st.enter_context(nc.semaphore("d_" + q)) for q in DMA_QUEUES}

        x_in = nc.dram_tensor("x_in", [T, D], F32, kind="ExternalInput").ap()
        feeds["x_in"] = "X_TOKEN"
        out = nc.dram_tensor("out", [T, D], F32, kind="ExternalOutput").ap()

        for name, arr in _CONSTS.items():
            dt = BF16 if arr.dtype == BF16NP else F32
            d = nc.dram_tensor("c_" + name, list(arr.shape), dt, kind="ExternalInput").ap()
            feeds["c_" + name] = arr
            emit_load_const(C, name, d, arr.shape, dt, persist=True)

        C.sbuf("x_res", (128, DT * T), F32)
        C.sbuf("x_ln", (128, DT * T), BF16)
        _ensure_ws(C, 2048)
        C.psum("pa", (128, 512), F32)
        C.psum("pb", (128, 512), F32)
        C.psum("pc", (128, 512), F32)
        C.psum("pd", (128, 512), F32)
        C.psum("pe", (128, 512), F32)
        C.psum("pbig", (128, 1024), F32)
        C.psum("wt_ps", (128, 1024), BF16)

        rem = nc.sbuf_bytes_remaining
        cols = rem // 4 - 64
        C.sbuf("arena", (128, cols), F32)
        C.arena_cols = cols
        C.carve("arena", "ln_t1", 0, 1024, F32)

        C.begin_sublayer("in_tx")
        emit_in_transform(C, x_in)

        for stage in seq:
            C.begin_sublayer(stage)
            pref = stage + "_"
            din = lambda n, a, pref=pref: din_raw(pref + n, a)
            run_sub = SUBLAYER_PREP[stage](C, W, din)
            emit_ln(C, "x_res", "x_ln")
            run_sub()

        C.begin_sublayer("out_tx")
        stores = emit_out_transform(C, out)

        fw = plan.final_wait_spec(stores, engine="sync")
        with nc.Block() as block:
            plan.finalize_and_emit(block, sems, dma_sems, fw)
    return nc, feeds


def kernel(**inputs):
    x = np.ascontiguousarray(np.asarray(inputs["L0_attn_a_x"], dtype=np.float32))
    W = {k: np.ascontiguousarray(np.asarray(v, dtype=np.float32)) for k, v in inputs.items()}
    _ = inputs.get("L0_attn_b_c")

    import os
    if os.environ.get("KERNEL_UNFUSED"):
        res = run_stage("in_tx", W, {"X_TOKEN": [x[b] for b in range(NCORES)]})
        xfm = [res.results[b]["xfm_out"] for b in range(NCORES)]
        for stage in SUBLAYER_SEQ:
            res = run_stage(stage, W, {"XFM": xfm})
            xfm = [res.results[b]["xfm_out"] for b in range(NCORES)]
        res = run_stage("out_tx", W, {"XFM": xfm})
        return np.stack([res.results[b]["out"] for b in range(NCORES)])

    nc, feeds = build_full(W)
    in_maps = []
    for core in range(NCORES):
        m = {}
        for name, arr in feeds.items():
            m[name] = x[core] if isinstance(arr, str) else arr
        in_maps.append(m)
    res = run_bass_kernel_spmd(nc, in_maps, core_ids=list(range(NCORES)))
    return np.stack([res.results[b]["out"] for b in range(NCORES)])


# version 40
# speedup vs baseline: 1.0725x; 1.0609x over previous
import contextlib
import numpy as np
import ml_dtypes
import concourse.bass as bass
import concourse.mybir as mybir
from concourse.bass_utils import run_bass_kernel_spmd

F32 = mybir.dt.float32
BF16 = mybir.dt.bfloat16
AF = mybir.ActivationFunctionType
ALU = mybir.AluOpType
AX = mybir.AxisListType
BF16NP = ml_dtypes.bfloat16

T = 2048
D = 1024
DT = D // 128
TT = T // 128
NCORES = 8
NH = 16
DH = 64
LN_EPS = 1e-5

ENGINES = ("tensor", "vector", "scalar", "gpsimd", "sync")
DMA_QUEUES = ("sync", "gpsimd", "scalar")


class Plan:

    def __init__(self, nc):
        self.nc = nc
        self.ops = []
        self.by_engine = {e: [] for e in ENGINES}
        self.same_engine_wait = True
        self.dma_count = {q: 0 for q in DMA_QUEUES}

    def op(self, eng, fn, waits=(), tag=""):
        oid = len(self.ops)
        rec = dict(oid=oid, eng=eng, fn=fn, waits=[w for w in waits if w is not None],
                   dma=False, tag=tag, inc=False)
        self.ops.append(rec)
        self.by_engine[eng].append(rec)
        return oid

    def dma(self, queue, out, in_, waits=(), tag=""):
        oid = len(self.ops)

        def fn(e, out=out, in_=in_):
            return e.dma_start(out=out, in_=in_)
        self.dma_count[queue] += 16
        rec = dict(oid=oid, eng=queue, fn=fn, waits=[w for w in waits if w is not None],
                   dma=True, tag=tag, inc=True, dma_target=self.dma_count[queue])
        self.ops.append(rec)
        self.by_engine[queue].append(rec)
        return oid

    def finalize_and_emit(self, block, sems, dma_sems, final_waits):
        for rec in self.ops:
            for w in rec["waits"]:
                self.ops[w]["inc"] = True
        for e in ENGINES:
            c = 0
            for rec in self.by_engine[e]:
                if not rec["dma"] and rec["inc"]:
                    c += 1
                    rec["eng_target"] = c

        def wait_list(rec):
            agg = {}
            for w in rec["waits"]:
                p = self.ops[w]
                if p["dma"]:
                    key = ("dma", p["eng"])
                    val = p["dma_target"]
                else:
                    if p["eng"] == rec["eng"] and rec["eng"] == "tensor":
                        continue
                    if p["eng"] == rec["eng"] and not self.same_engine_wait:
                        continue
                    key = ("eng", p["eng"])
                    val = p["eng_target"]
                if agg.get(key, 0) < val:
                    agg[key] = val
            return agg

        def run_engine(e, ename):
            seen = {}
            for rec in self.by_engine[ename]:
                for (kind, src), val in wait_list(rec).items():
                    if seen.get((kind, src), -1) >= val:
                        continue
                    seen[(kind, src)] = val
                    sem = dma_sems[src] if kind == "dma" else sems[src]
                    e.wait_ge(sem, val)
                ins = rec["fn"](e)
                if rec["dma"]:
                    ins.then_inc(dma_sems[rec["eng"]], 16)
                elif rec["inc"]:
                    ins.then_inc(sems[rec["eng"]], 1)
            for (kind, src), val in final_waits.get(ename, {}).items():
                if seen.get((kind, src), -1) >= val:
                    continue
                sem = dma_sems[src] if kind == "dma" else sems[src]
                e.wait_ge(sem, val)

        block.sync(lambda e: run_engine(e, "sync"))
        block.vector(lambda e: run_engine(e, "vector"))
        block.scalar(lambda e: run_engine(e, "scalar"))
        block.gpsimd(lambda e: run_engine(e, "gpsimd"))
        block.tensor(lambda e: run_engine(e, "tensor"))

    def final_wait_spec(self, opids, engine="sync"):
        agg = {}
        for oid in opids:
            p = self.ops[oid]
            assert p["dma"]
            key = ("dma", p["eng"])
            val = p["dma_target"]
            if agg.get(key, 0) < val:
                agg[key] = val
        return {engine: agg}


class Buf:
    __slots__ = ("w", "r")

    def __init__(self):
        self.w = []
        self.r = []

    def read_waits(self):
        return list(self.w)

    def write_waits(self):
        return list(self.w) + list(self.r)

    def note_read(self, oid):
        self.r.append(oid)

    def note_write(self, oid):
        self.w = [oid]
        self.r = []

    def note_write_acc(self, oid):
        self.w.append(oid)
        self.r = []


def ceil_div(a, b):
    return (a + b - 1) // b


def host_constants():
    c = {}
    c["identity_f32"] = np.eye(128, dtype=np.float32)
    c["identity_bf16"] = np.eye(128).astype(BF16NP)
    c["ones_f32"] = np.ones((128, 128), dtype=np.float32)
    c["ones_bf16"] = np.ones((128, 128)).astype(BF16NP)
    m = (np.arange(128)[None, :] >= np.arange(128)[:, None]).astype(np.float32)
    c["tri_bf16"] = m.astype(BF16NP)
    fv = np.zeros((128, 8), np.float32)
    fv[:, 0] = LN_EPS
    fv[:, 1] = 1.0 / D
    fv[:, 2] = -0.5
    fv[:, 3] = 1.0
    fv[:, 4] = -1.0
    fv[:, 5] = 0.125
    c["fvals"] = fv
    rev = np.zeros((128, 128), np.float32)
    rev[np.arange(128), 127 - np.arange(128)] = 1.0
    c["rev_bf16"] = rev.astype(BF16NP)
    return c


class Ctx:
    def __init__(self, nc, plan, st):
        self.nc = nc
        self.plan = plan
        self.st = st
        self.sb = {}
        self.ps = {}
        self.bufs = {}
        self.fused = False
        self.arena_cols = 0
        self.cursor = 0
        self.sub_names = []
        self.offsets = {}
        self.prev_tokens = []
        self.max_cursor = 0

    def begin_sublayer(self, tag=""):
        if not self.fused:
            return
        toks = list(self.prev_tokens)
        for n in self.sub_names:
            b = self.bufs.get(n)
            if b:
                toks += b.w + b.r
            self.sb.pop(n, None)
            self.bufs.pop(n, None)
        self.prev_tokens = sorted(set(toks))
        self.sub_names = []
        self.cursor = 1024
        self.sub_tag = tag

    def salloc(self, name, shape, dt=F32):
        if not self.fused:
            return self.sbuf(name, shape, dt)
        p = shape[0]
        n = shape[1]
        f32n = n if dt == F32 else (n + 1) // 2
        f32n = (f32n + 15) & ~15
        assert self.cursor + f32n <= self.arena_cols,             f"arena overflow at {name} ({self.sub_tag}): {self.cursor}+{f32n}>{self.arena_cols}"
        self.offsets[name] = self.cursor
        h = self.sb["arena"]
        if dt == F32:
            ap = h[0:p, self.cursor:self.cursor + n]
        else:
            ap = h[0:p, self.cursor:self.cursor + (n + 1) // 2].bitcast(dt)[0:p, 0:n]
        self.cursor += f32n
        self.max_cursor = max(self.max_cursor, self.cursor)
        self.sb[name] = ap
        b = Buf()
        b.w = list(self.prev_tokens)
        self.bufs[name] = b
        self.sub_names.append(name)
        return ap

    def salloc_in(self, host, name, shape, dt=F32, off_f32=0):
        if not self.fused:
            return self.sbuf(name, shape, dt)
        base = self.offsets[host] + off_f32
        p, n = shape
        h = self.sb["arena"]
        if dt == F32:
            ap = h[0:p, base:base + n]
        else:
            ap = h[0:p, base:base + (n + 1) // 2].bitcast(dt)[0:p, 0:n]
        self.sb[name] = ap
        b = Buf()
        b.w = list(self.prev_tokens)
        self.bufs[name] = b
        self.sub_names.append(name)
        return ap

    def salloc_block(self, nelem):
        if not self.fused:
            return None
        base = self.cursor
        assert base + nelem <= self.arena_cols, f"arena overflow block {nelem}"
        self.cursor += nelem
        self.max_cursor = max(self.max_cursor, self.cursor)
        return base

    def carve_rel(self, base, name, off, nelem, dt=F32, p=128):
        if base is None:
            return self.carve("mb_arena", name, off, nelem, dt)[0:p, :]
        h = self.sb["arena"]
        if dt == F32:
            ap = h[0:p, base + off:base + off + nelem]
        else:
            ap = h[0:p, base + off:base + off + (nelem + 1) // 2].bitcast(dt)[0:p, 0:nelem]
        self.sb[name] = ap
        b = Buf()
        b.w = list(self.prev_tokens)
        self.bufs[name] = b
        self.sub_names.append(name)
        return ap

    def sbuf(self, name, shape, dt=F32):
        h = self.st.enter_context(self.nc.sbuf_tensor(name, list(shape), dt))
        self.sb[name] = h
        self.bufs[name] = Buf()
        return h

    def psum(self, name, shape, dt=F32):
        h = self.st.enter_context(self.nc.psum_tensor(name, list(shape), dt))
        self.ps[name] = h
        self.bufs[name] = Buf()
        return h

    def buf(self, name):
        return self.bufs.setdefault(name, Buf())

    def carve(self, arena, name, c0, nelem, dt=F32):
        h = self.sb[arena]
        if dt == F32:
            ap = h[:, c0:c0 + nelem]
        else:
            ap = h[:, c0:c0 + (nelem + 1) // 2].bitcast(dt)[:, 0:nelem]
        self.sb[name] = ap
        self.bufs[name] = Buf()
        return ap

    def arena_reset(self, names, handoff_from):
        toks = []
        for n in handoff_from:
            b = self.bufs.get(n)
            if b:
                toks += b.w + b.r
        for n in names:
            self.bufs[n].w = list(toks)



def emit_load_const(C, name, dram_ap, shape, dt, persist=False):
    if name in C.sb:
        h = C.sb[name]
    elif persist or not C.fused:
        h = C.sbuf(name, shape, dt)
    else:
        h = C.salloc(name, shape, dt)
    oid = C.plan.dma("sync", h[0:shape[0], :], dram_ap[:, :],
                     waits=C.buf(name).write_waits(), tag=f"load_{name}")
    C.buf(name).note_write(oid)
    return h


def emit_in_transform(C, x_tok_dram, xname="x_res"):
    P = C.plan
    x_res = C.sb[xname]
    stage = C.salloc("io_stage", (128, 2 * 1024), F32)
    pt = C.ps["pbig"]
    ident = C.sb["identity_f32"]
    bident = C.buf("identity_f32")
    bst = C.buf("io_stage")
    bpt = C.buf("pbig")
    bx = C.buf(xname)
    xv = x_tok_dram.rearrange("(a p) d -> p a d", p=128)
    for j in range(DT):
        for g in range(2):
            sl = (j * 2 + g) % 2
            ld = P.dma(DMA_QUEUES[(j * 2 + g) % 3], stage[:, sl * 1024:(sl + 1) * 1024]
                       .rearrange("p (a c) -> p a c", a=8),
                       xv[:, g * 8:(g + 1) * 8, j * 128:(j + 1) * 128],
                       waits=bst.write_waits(), tag="in_ld")
            bst.note_write_acc(ld)
            for ii in range(8):
                tp = P.op("tensor", lambda e, ii=ii, sl=sl: e.transpose(
                    pt[:, ii * 128:(ii + 1) * 128],
                    stage[:, sl * 1024 + ii * 128:sl * 1024 + (ii + 1) * 128],
                    ident[:, :]),
                    waits=[ld] + bident.read_waits() + bpt.write_waits(), tag="in_tp")
                bst.note_read(tp)
                bident.note_read(tp)
                bpt.note_write_acc(tp)
            ev = P.op("vector", lambda e, j=j, g=g: e.tensor_copy(
                x_res[:, j * T + g * 1024:j * T + (g + 1) * 1024], pt[:, :]),
                waits=bpt.w + bx.write_waits(), tag="in_ev")
            bpt.note_read(ev)
            bx.note_write_acc(ev)


def make_out_transform(C, out_dram, xname="x_res"):
    P = C.plan
    x_res = C.sb[xname]
    stage = C.salloc("io_stage2", (128, 2 * D), F32)
    pt = C.ps["pbig"]
    ident = C.sb["identity_f32"]
    bident = C.buf("identity_f32")
    bst = C.buf("io_stage2")
    bpt = C.buf("pbig")
    bx = C.buf(xname)
    stores = []

    def emit_i(i):
        sl = i % 2
        for g in range(2):
            for jj in range(4):
                j = g * 4 + jj
                tp = P.op("tensor", lambda e, i=i, j=j, jj=jj: e.transpose(
                    pt[:, jj * 128:(jj + 1) * 128],
                    x_res[:, j * T + i * 128: j * T + (i + 1) * 128], ident[:, :]),
                    waits=bx.read_waits() + bident.read_waits() + bpt.write_waits(),
                    tag="out_tp")
                bx.note_read(tp)
                bident.note_read(tp)
                bpt.note_write_acc(tp)
            ev = P.op("vector", lambda e, sl=sl, g=g: e.tensor_copy(
                stage[:, sl * D + g * 512:sl * D + (g + 1) * 512], pt[:, 0:512]),
                waits=bpt.w + bst.write_waits(), tag="out_ev")
            bpt.note_read(ev)
            bst.note_write_acc(ev)
        sto = P.dma(DMA_QUEUES[i % 3], out_dram[i * 128:(i + 1) * 128, :],
                    stage[:, sl * D:(sl + 1) * D],
                    waits=bst.w, tag="out_st")
        bst.note_read(sto)
        stores.append(sto)
    return emit_i, stores


def emit_out_transform(C, out_dram, xname="x_res"):
    emit_i, stores = make_out_transform(C, out_dram, xname)
    for i in range(TT):
        emit_i(i)
    return stores



def emit_ln(C, src_name, dst_name):
    if "ln_sq" not in C.sb:
        C.salloc("ln_sq", (128, 512), F32)
    if "ln_rows" not in C.sb:
        C.salloc("ln_rows", (1, 1536), F32)
    """LayerNorm feature-major: src f32 [128, DT*T] -> dst bf16 [128, DT*T].
    Chunked by 512 tokens. Uses: ln_sq [128,512] f32, ln_rows [1, 3*T] f32,
    ln_ps psum [1, 1024] f32, ln_bc psum [128, 1024] f32, pa.. unused."""
    P = C.plan
    x = C.sb[src_name]
    y = C.sb[dst_name]
    bx = C.buf(src_name)
    by = C.buf(dst_name)
    ones = C.sb["ones_f32"]
    bones = C.buf("ones_f32")
    sq = C.sb["ln_sq"]; bsq = C.buf("ln_sq")
    rows = C.sb["ln_rows"]; brows = C.buf("ln_rows")
    psr_a = C.ps["pc"]; bps_a = C.buf("pc")
    psr_b = C.ps["pd"]; bps_b = C.buf("pd")
    pb = C.ps["pbig"]; bpb = C.buf("pbig")
    t1 = C.sb["ln_t1"]; bt1 = C.buf("ln_t1")

    for ch in range(4):
        s = ch * 512
        sqb = sq.bitcast(BF16)
        onesb_ = C.sb["ones_bf16"]
        bonesb_ = C.buf("ones_bf16")
        for j in range(DT):
            cv = P.op("scalar", lambda e, j=j, s=s, sqb=sqb: e.activation(
                sqb[:, 0:512], x[:, j * T + s:j * T + s + 512], AF.Copy),
                waits=bx.read_waits() + bsq.write_waits(), tag="ln_cast")
            bx.note_read(cv)
            bsq.note_write(cv)
            sqo = P.op("vector", lambda e, sqb=sqb: e.tensor_mul(
                sqb[:, 512:1024], sqb[:, 0:512], sqb[:, 0:512]),
                waits=bsq.w, tag="ln_sq")
            bsq.note_write_acc(sqo)
            mm = P.op("tensor", lambda e, j=j, sqb=sqb: e.matmul(
                psr_a[0:1, 0:512], onesb_[:, 0:1], sqb[:, 0:512],
                start=(j == 0), stop=(j == DT - 1)),
                waits=bsq.w + bonesb_.read_waits() +
                      (bps_a.write_waits() if j == 0 else []), tag="ln_sum")
            bsq.note_read(mm)
            bonesb_.note_read(mm)
            bps_a.note_write_acc(mm)
            mm2 = P.op("tensor", lambda e, j=j, sqb=sqb: e.matmul(
                psr_b[0:1, 0:512], onesb_[:, 0:1], sqb[:, 512:1024],
                start=(j == 0), stop=(j == DT - 1)),
                waits=bsq.w + bonesb_.read_waits() +
                      (bps_b.write_waits() if j == 0 else []), tag="ln_sumsq")
            bsq.note_read(mm2)
            bps_b.note_write_acc(mm2)
        R_MEAN, R_RSTD, R_M2 = 0, 512, 1024
        fv = C.sb["fvals"]
        bfv = C.buf("fvals")
        m_mean = P.op("scalar", lambda e, s=s: e.activation(
            rows[0:1, R_MEAN:R_MEAN + 512], psr_a[0:1, 0:512], AF.Copy, scale=fv[0:1, 1:2]),
            waits=bps_a.w + brows.write_waits() + bfv.read_waits(), tag="ln_mean")
        bps_a.note_read(m_mean)
        brows.note_write_acc(m_mean)
        m_var = P.op("scalar", lambda e, s=s: e.activation(
            rows[0:1, R_RSTD:R_RSTD + 512], psr_b[0:1, 0:512], AF.Copy, scale=fv[0:1, 1:2]),
            waits=bps_b.w + [m_mean], tag="ln_var0")
        bps_b.note_read(m_var)
        brows.note_write_acc(m_var)
        sqm = P.op("vector", lambda e, s=s: e.tensor_mul(
            rows[0:1, R_M2:R_M2 + 512], rows[0:1, R_MEAN:R_MEAN + 512],
            rows[0:1, R_MEAN:R_MEAN + 512]),
            waits=[m_mean], tag="ln_m2a")
        subv = P.op("vector", lambda e, s=s: e.tensor_sub(
            rows[0:1, R_RSTD:R_RSTD + 512], rows[0:1, R_RSTD:R_RSTD + 512],
            rows[0:1, R_M2:R_M2 + 512]),
            waits=[sqm, m_var], tag="ln_m2b")
        lnv = P.op("scalar", lambda e, s=s: e.activation(
            rows[0:1, R_RSTD:R_RSTD + 512], rows[0:1, R_RSTD:R_RSTD + 512],
            AF.Ln, bias=fv[0:1, 0:1]), waits=[subv], tag="ln_ln")
        rstd = P.op("scalar", lambda e, s=s: e.activation(
            rows[0:1, R_RSTD:R_RSTD + 512], rows[0:1, R_RSTD:R_RSTD + 512],
            AF.Exp, scale=fv[0:1, 2:3]), waits=[lnv], tag="ln_exp")
        m2 = P.op("vector", lambda e, s=s: e.tensor_mul(
            rows[0:1, R_M2:R_M2 + 512], rows[0:1, R_MEAN:R_MEAN + 512],
            rows[0:1, R_RSTD:R_RSTD + 512]),
            waits=[rstd, sqm, m_mean], tag="ln_m2")
        brows.note_write_acc(m2)
        brows.w = [m2, rstd]
        b1 = P.op("tensor", lambda e, s=s: e.matmul(
            pb[:, 0:512], ones[0:1, :], rows[0:1, R_RSTD:R_RSTD + 512],
            start=True, stop=True),
            waits=[rstd, m2] + bones.read_waits() + bpb.write_waits(), tag="ln_bc1")
        b2 = P.op("tensor", lambda e, s=s: e.matmul(
            pb[:, 512:1024], ones[0:1, :], rows[0:1, R_M2:R_M2 + 512],
            start=True, stop=True),
            waits=[m2] + bones.read_waits() + bpb.write_waits(), tag="ln_bc2")
        brows.note_read(b1)
        brows.note_read(b2)
        bones.note_read(b1)
        bpb.note_write(b1)
        bpb.w = [b1, b2]
        for j in range(DT):
            mu = P.op("vector", lambda e, j=j, s=s: e.tensor_mul(
                t1[:, 0:512], x[:, j * T + s:j * T + s + 512], pb[:, 0:512]),
                waits=bx.read_waits() + bpb.w + bt1.write_waits(), tag="ln_n1")
            bx.note_read(mu)
            bpb.note_read(mu)
            bt1.note_write(mu)
            su = P.op("vector", lambda e, j=j, s=s: e.tensor_sub(
                y[:, j * T + s:j * T + s + 512], t1[:, 0:512], pb[:, 512:1024]),
                waits=bt1.w + bpb.w + by.write_waits(), tag="ln_n2")
            bt1.note_read(su)
            bpb.note_read(su)
            by.note_write_acc(su)


class WeightLoader:

    def __init__(self, C, name, dram_ap, rows, cols, transpose, queue="gpsimd"):
        self.C, self.name = C, name
        self.dram = dram_ap
        self.rows, self.cols = rows, cols
        self.transpose = transpose
        self.queue = queue
        if transpose:
            self.KT, self.MT = ceil_div(cols, 128), ceil_div(rows, 128)
        else:
            self.KT, self.MT = ceil_div(rows, 128), ceil_div(cols, 128)

    def prefetch_all(self):
        C, P = self.C, self.C.plan
        if not self.transpose:
            h = C.salloc(f"w_{self.name}", (128, self.KT * self.cols), BF16)
            b = C.buf(f"w_{self.name}")
            for kt in range(self.KT):
                r0 = kt * 128
                r1 = min(self.rows, r0 + 128)
                ld = P.dma(self.queue, h[0:r1 - r0, kt * self.cols:(kt + 1) * self.cols],
                           self.dram[r0:r1, :], waits=b.write_waits(), tag=f"w_{self.name}_ld")
                b.note_write_acc(ld)
            self.h, self.b = h, b
        else:
            nm = f"w_{self.name}"
            if nm in C.sb:
                h = C.sb[nm]
                b = C.buf(nm)
            else:
                h = C.salloc(nm, (128, self.KT * self.rows), BF16)
                b = C.buf(nm)
            if "ws_nat" in C.sb:
                natname = "ws_nat"
            else:
                if "wt_nat" not in C.sb:
                    C.sbuf("wt_nat", (128, 2048), BF16)
                natname = "wt_nat"
            if "wt_ps" not in C.ps:
                C.psum("wt_ps", (128, 1024), BF16)
            nat = C.sb[natname]
            bn = C.buf(natname)
            pt = C.ps["wt_ps"]
            bp = C.buf("wt_ps")
            identb = C.sb["identity_bf16"]
            bib = C.buf("identity_bf16")
            ncolblk = ceil_div(self.cols, 128)
            for mt in range(self.MT):
                r0 = mt * 128
                r1 = min(self.rows, r0 + 128)
                nr = r1 - r0
                ld = P.dma(self.queue, nat[0:nr, 0:self.cols], self.dram[r0:r1, :],
                           waits=bn.write_waits(), tag=f"w_{self.name}_ld")
                bn.note_write(ld)
                for g0 in range(0, ncolblk, 8):
                    g1 = min(ncolblk, g0 + 8)
                    for kt in range(g0, g1):
                        c0 = kt * 128
                        c1 = min(self.cols, c0 + 128)
                        tp = P.op("tensor", lambda e, kt=kt, c0=c0, c1=c1, g0=g0, nr=nr: e.transpose(
                            pt[0:c1 - c0, (kt - g0) * 128:(kt - g0) * 128 + nr],
                            nat[0:nr, c0:c1], identb[0:nr, 0:nr]),
                            waits=bn.read_waits() + bib.read_waits() + bp.write_waits(),
                            tag=f"w_{self.name}_tp")
                        bn.note_read(tp)
                        bib.note_read(tp)
                        bp.note_write_acc(tp)
                    for kt in range(g0, g1):
                        c0 = kt * 128
                        c1 = min(self.cols, c0 + 128)
                        ev = P.op("vector", lambda e, kt=kt, g0=g0, c0=c0, c1=c1, r0=r0, r1=r1: e.tensor_copy(
                            h[0:c1 - c0, kt * self.rows + r0: kt * self.rows + r1],
                            pt[0:c1 - c0, (kt - g0) * 128:(kt - g0) * 128 + (r1 - r0)]),
                            waits=bp.w + b.write_waits(), tag=f"w_{self.name}_ev")
                        bp.note_read(ev)
                        b.note_write_acc(ev)
            self.h, self.b = h, b

    def tile(self, kt, mt):
        if self.transpose:
            k1 = min(self.cols - kt * 128, 128)
            m0 = mt * 128
            m1 = min(self.rows, m0 + 128)
            return self.h[0:k1, kt * self.rows + m0: kt * self.rows + m1], self.b
        else:
            k1 = min(self.rows - kt * 128, 128)
            m0 = mt * 128
            m1 = min(self.cols, m0 + 128)
            return self.h[0:k1, kt * self.cols + m0: kt * self.cols + m1], self.b


def emit_proj_fm(C, wl, xsrc_name, out_cb, out_rows, psname, tag="proj"):
    P = C.plan
    x = C.sb[xsrc_name]
    bx = C.buf(xsrc_name)
    KT = wl.KT
    MT = ceil_div(out_rows, 128)
    for mt in range(MT):
        for ch in range(4):
            pname = psname[(mt * 4 + ch) % len(psname)]
            ps = C.ps[pname]
            bps = C.buf(pname)
            s = ch * 512
            for kt in range(KT):
                tile, bw = wl.tile(kt, mt)
                mm = P.op("tensor", lambda e, tile=tile, kt=kt, s=s, ps=ps: e.matmul(
                    ps[0:128, 0:512], tile, x[:, kt * T + s: kt * T + s + 512],
                    start=(kt == 0), stop=(kt == KT - 1)),
                    waits=bw.read_waits() + bx.read_waits() +
                          (bps.write_waits() if kt == 0 else []), tag=tag + "_mm")
                bw.note_read(mm)
                bx.note_read(mm)
                bps.note_write_acc(mm)
            out_cb(mt, ch, ps, bps)


class WvPrefetch:

    def __init__(self, C, name, dram):
        self.C, self.name, self.dram = C, name, dram

    def prefetch_all(self):
        C, P = self.C, self.C.plan
        assert C.ws_slotw >= 2048
        nat = C.sb["ws_nat"]
        wt = C.sb["ws_wt"]
        vt = C.sb["ws_vT"]
        bvt = C.buf("ws_vT")
        pt = C.ps["wt_ps"]
        bp = C.buf("wt_ps")
        identb = C.sb["identity_bf16"]
        self.dsts = [(nat, C.buf("ws_nat")), (wt, C.buf("ws_wt"))]
        for mt in range(8):
            ld = P.dma("gpsimd", vt[:, 0:1024], self.dram[mt * 128:(mt + 1) * 128, :],
                       waits=bvt.write_waits(), tag=f"{self.name}_ld")
            bvt.note_write(ld)
            for g in range(2):
                for kk in range(4):
                    kt = g * 4 + kk
                    tp = P.op("tensor", lambda e, kt=kt, kk=kk: e.transpose(
                        pt[:, kk * 128:(kk + 1) * 128], vt[:, kt * 128:(kt + 1) * 128],
                        identb[:, :]),
                        waits=bvt.w + C.buf("identity_bf16").read_waits() + bp.write_waits(),
                        tag=f"{self.name}_tp")
                    bvt.note_read(tp)
                    C.buf("identity_bf16").note_read(tp)
                    bp.note_write_acc(tp)
                dsth, bdst = self.dsts[g]
                ev = P.op("vector", lambda e, mt=mt, dsth=dsth: e.tensor_copy(
                    dsth.rearrange("p (k w) -> p k w", w=1024)[:, :, mt * 128:(mt + 1) * 128],
                    pt[:, 0:512].rearrange("p (k w) -> p k w", w=128)),
                    waits=bp.w + bdst.write_waits(), tag=f"{self.name}_ev")
                bp.note_read(ev)
                bdst.note_write_acc(ev)

    def mt_tiles(self, mt, row0=0):
        out = []
        for kt in range(8):
            dsth, bdst = self.dsts[kt // 4]
            c0 = (kt % 4) * 1024 + mt * 128
            out.append((dsth[:, c0:c0 + 128], bdst))
        return out


def emit_ffn_rwkv(C, wr, wk, wv):
    P = C.plan
    CH = 512
    xl = C.sb["x_ln"]; bxl = C.buf("x_ln")
    if C.fused:
        xr2 = C.salloc_in("ln_sq", "ffn_xr2", (128, DT * 512), BF16, 0)
        C.arena_reset(["ffn_xr2"], ["ln_sq", "ln_rows", "ffn_xr2"])
    else:
        xr2 = C.sb["ffn_xr2"] if "ffn_xr2" in C.sb else C.sbuf("ffn_xr2", (128, DT * 512), BF16)
    xrs = [(C.sb["ffn_xr"], C.buf("ffn_xr")), (xr2, C.buf("ffn_xr2"))]
    xks = [(C.sb["ffn_xk"], C.buf("ffn_xk")), (C.sb["ffn_xk2"], C.buf("ffn_xk2"))]
    dd = C.sb["ffn_d"]; bd = C.buf("ffn_d")
    mu = C.sb["mu_rk"]; bmu = C.buf("mu_rk")
    r = C.sb["ffn_r"]; br = C.buf("ffn_r")
    k2 = C.sb["ffn_k2"]; bk2 = C.buf("ffn_k2")
    xres = C.sb["x_res"]; bxres = C.buf("x_res")
    t1 = C.sb["ln_t1"]; bt1 = C.buf("ln_t1")

    def chunk_proj(wl, src, bsrc, evac, tag):
        for mt in range(DT):
            pname = ("pa", "pb")[mt % 2]
            ps = C.ps[pname]
            bps = C.buf(pname)
            tiles = wl.mt_tiles(mt) if hasattr(wl, "mt_tiles") else None
            for kt in range(DT):
                tile, bw = tiles[kt] if tiles else wl.tile(kt, mt)
                mm = P.op("tensor", lambda e, tile=tile, kt=kt, ps=ps: e.matmul(
                    ps[0:128, 0:CH], tile, src[:, kt * CH:(kt + 1) * CH],
                    start=(kt == 0), stop=(kt == DT - 1)),
                    waits=bw.read_waits() + bsrc.read_waits() +
                          (bps.write_waits() if kt == 0 else []), tag=tag + "_mm")
                bw.note_read(mm)
                bsrc.note_read(mm)
                bps.note_write_acc(mm)
            evac(mt, ps, bps)

    def emit_ts(ch):
        s = ch * CH
        xr, bxr = xrs[ch % 2]
        xk, bxk = xks[ch % 2]
        for j in range(DT):
            if ch == 0:
                c0 = P.op("vector", lambda e, j=j: e.tensor_copy(dd[:, 0:1], xl[:, j * T:j * T + 1]),
                          waits=bxl.read_waits() + bd.write_waits(), tag="ffn_d0")
                bxl.note_read(c0)
                bd.note_write(c0)
                c1 = P.op("vector", lambda e, j=j: e.tensor_sub(
                    dd[:, 1:CH], xl[:, j * T + 1:j * T + CH], xl[:, j * T:j * T + CH - 1]),
                    waits=bxl.read_waits() + [c0], tag="ffn_d1")
                bxl.note_read(c1)
                bd.note_write_acc(c1)
            else:
                c1 = P.op("vector", lambda e, j=j, s=s: e.tensor_sub(
                    dd[:, 0:CH], xl[:, j * T + s:j * T + s + CH],
                    xl[:, j * T + s - 1:j * T + s + CH - 1]),
                    waits=bxl.read_waits() + bd.write_waits(), tag="ffn_d1")
                bxl.note_read(c1)
                bd.note_write(c1)
            for (dst, bdst, mucol, tg) in ((xr, bxr, j, "ffn_xr"), (xk, bxk, DT + j, "ffn_xk")):
                if ch == 0:
                    m0 = P.op("vector", lambda e, j=j, dst=dst, mucol=mucol: e.tensor_scalar(
                        dst[:, j * CH:j * CH + 1], dd[:, 0:1], mu[:, mucol:mucol + 1], None, ALU.mult),
                        waits=bd.w + bmu.read_waits() + bdst.write_waits(), tag=tg + "0")
                    bd.note_read(m0)
                    bmu.note_read(m0)
                    bdst.note_write(m0)
                    m1 = P.op("vector", lambda e, j=j, dst=dst, mucol=mucol: e.scalar_tensor_tensor(
                        dst[:, j * CH + 1:(j + 1) * CH], dd[:, 1:CH], mu[:, mucol:mucol + 1],
                        xl[:, j * T:j * T + CH - 1], ALU.mult, ALU.add),
                        waits=bd.w + bxl.read_waits() + bmu.read_waits() + [m0], tag=tg + "1")
                    bd.note_read(m1)
                    bxl.note_read(m1)
                    bmu.note_read(m1)
                    bdst.note_write_acc(m1)
                else:
                    m1 = P.op("vector", lambda e, j=j, s=s, dst=dst, mucol=mucol: e.scalar_tensor_tensor(
                        dst[:, j * CH:(j + 1) * CH], dd[:, 0:CH], mu[:, mucol:mucol + 1],
                        xl[:, j * T + s - 1:j * T + s + CH - 1], ALU.mult, ALU.add),
                        waits=bd.w + bxl.read_waits() + bmu.read_waits() + bdst.write_waits(), tag=tg + "1")
                    bd.note_read(m1)
                    bxl.note_read(m1)
                    bmu.note_read(m1)
                    bdst.note_write(m1)

    emit_ts(0)
    for ch in range(T // CH):
        s = ch * CH
        xr, bxr = xrs[ch % 2]
        xk, bxk = xks[ch % 2]
        if ch + 1 < T // CH:
            emit_ts(ch + 1)

        def evac_r(mt, ps, bps):
            ev = P.op("scalar", lambda e, mt=mt, ps=ps: e.activation(
                r[:, mt * CH:(mt + 1) * CH], ps[0:128, 0:CH], AF.Sigmoid),
                waits=bps.w + br.write_waits(), tag="ffn_r_ev")
            bps.note_read(ev)
            br.note_write_acc(ev)
        chunk_proj(wr, xr, bxr, evac_r, "ffn_r")

        def evac_k(mt, ps, bps):
            e1 = P.op("scalar", lambda e, ps=ps: e.activation(
                t1[:, 0:CH], ps[0:128, 0:CH], AF.Relu),
                waits=bps.w + bt1.write_waits(), tag="ffn_k_relu")
            bps.note_read(e1)
            bt1.note_write(e1)
            e2 = P.op("vector", lambda e, mt=mt: e.tensor_mul(
                k2[:, mt * CH:(mt + 1) * CH], t1[:, 0:CH], t1[:, 0:CH]),
                waits=bt1.w + bk2.write_waits(), tag="ffn_k_sq")
            bt1.note_read(e2)
            bk2.note_write_acc(e2)
        chunk_proj(wk, xk, bxk, evac_k, "ffn_k")

        def evac_v(mt, ps, bps):
            e1 = P.op("vector", lambda e, mt=mt, ps=ps: e.tensor_mul(
                t1[:, 512:1024], ps[0:128, 0:CH], r[:, mt * CH:(mt + 1) * CH]),
                waits=bps.w + br.read_waits() + bt1.write_waits(), tag="ffn_v_mul")
            bps.note_read(e1)
            br.note_read(e1)
            bt1.note_write(e1)
            e2 = P.op("vector", lambda e, mt=mt, s=s: e.tensor_add(
                xres[:, mt * T + s:mt * T + s + CH], xres[:, mt * T + s:mt * T + s + CH],
                t1[:, 512:1024]),
                waits=bt1.w + bxres.write_waits(), tag="ffn_v_add")
            bt1.note_read(e2)
            bxres.note_write_acc(e2)
        chunk_proj(wv, k2, bk2, evac_v, "ffn_v")


def emit_ffn_small(C, w_in, w_out, tail_cb=None):
    P = C.plan
    xl = C.sb["x_ln"]; bxl = C.buf("x_ln")
    xres = C.sb["x_res"]; bxres = C.buf("x_res")
    vec = C.sb["l1ffn_vec"]; bvec = C.buf("l1ffn_vec")
    bout = C.sb["l1ffn_bout"]; bbout = C.buf("l1ffn_bout")
    s_t = C.sb["l1ffn_s"]; bs = C.buf("l1ffn_s")
    h_t = C.sb["l1ffn_h"]; bh = C.buf("l1ffn_h")
    h_t2 = C.sb["l1ffn_h2"]; bh2 = C.buf("l1ffn_h2")
    g_t = C.sb["l1ffn_g"]; bg = C.buf("l1ffn_g")
    gb_t = C.sb["l1ffn_gb"]; bgb = C.buf("l1ffn_gb")
    t1 = C.sb["ln_t1"]; bt1 = C.buf("ln_t1")

    for ch in range(4):
        s = ch * 512
        ps = C.ps["pa"]
        bps = C.buf("pa")
        for kt in range(DT):
            tile, bw = w_in.tile(kt, 0)
            mm = P.op("tensor", lambda e, tile=tile, kt=kt, s=s, ps=ps: e.matmul(
                ps[0:64, 0:512], tile, xl[:, kt * T + s:kt * T + s + 512],
                start=(kt == 0), stop=(kt == DT - 1)),
                waits=bw.read_waits() + bxl.read_waits() +
                      (bps.write_waits() if kt == 0 else []), tag="l1f_s")
            bw.note_read(mm)
            bxl.note_read(mm)
            bps.note_write_acc(mm)
        ev = P.op("vector", lambda e, s=s, ps=ps: e.tensor_scalar_add(
            s_t[:, s:s + 512], ps[0:64, 0:512], vec[:, 48:49]),
            waits=bps.w + bvec.read_waits() + bs.write_waits(), tag="l1f_s_ev")
        bps.note_read(ev)
        bvec.note_read(ev)
        bs.note_write_acc(ev)
    for i in range(16):
        hc_t, bhc_t = (h_t, bh) if i % 2 == 0 else (h_t2, bh2)
        hi = P.op("scalar", lambda e, i=i, hc_t=hc_t: e.activation(
            hc_t[:, :], s_t[:, :], AF.Relu, scale=vec[:, i:i + 1], bias=vec[:, 16 + i:16 + i + 1]),
            waits=bs.w + bvec.read_waits() + bhc_t.write_waits(), tag="l1f_h")
        bs.note_read(hi)
        bvec.note_read(hi)
        bhc_t.note_write(hi)
        if i == 0:
            gi = P.op("vector", lambda e, hc_t=hc_t: e.tensor_scalar(
                g_t[:, :], hc_t[:, :], vec[:, 32:33], vec[:, 49:50], ALU.mult, ALU.add),
                waits=bhc_t.w + bvec.read_waits() + bg.write_waits(), tag="l1f_g0")
        else:
            gi = P.op("vector", lambda e, i=i, hc_t=hc_t: e.scalar_tensor_tensor(
                g_t[:, :], hc_t[:, :], vec[:, 32 + i:32 + i + 1], g_t[:, :], ALU.mult, ALU.add),
                waits=bhc_t.w + bvec.read_waits() + bg.write_waits() + bg.read_waits(), tag="l1f_g")
        bhc_t.note_read(gi)
        bvec.note_read(gi)
        bg.note_write(gi)
    gc = P.op("vector", lambda e: e.tensor_copy(gb_t[:, :], g_t[:, :]),
              waits=bg.w + bgb.write_waits(), tag="l1f_gc")
    bg.note_read(gc)
    bgb.note_write(gc)
    for ch in range(4):
        s = ch * 512
        for mt in range(DT):
            ps = C.ps[("pb", "pc")[mt % 2]]
            bps = C.buf(("pb", "pc")[mt % 2])
            tile, bw = w_out.tile(0, mt)
            mm = P.op("tensor", lambda e, tile=tile, s=s, ps=ps: e.matmul(
                ps[0:128, 0:512], tile, gb_t[:, s:s + 512], start=True, stop=True),
                waits=bw.read_waits() + bgb.read_waits() + bps.write_waits(), tag="l1f_o")
            bw.note_read(mm)
            bgb.note_read(mm)
            bps.note_write(mm)
            e1 = P.op("vector", lambda e, mt=mt, ps=ps: e.tensor_scalar_add(
                t1[:, 0:512], ps[0:128, 0:512], bout[:, mt:mt + 1]),
                waits=bps.w + bbout.read_waits() + bt1.write_waits(), tag="l1f_o_b")
            bps.note_read(e1)
            bbout.note_read(e1)
            bt1.note_write(e1)
            e2 = P.op("vector", lambda e, mt=mt, s=s: e.tensor_add(
                xres[:, mt * T + s:mt * T + s + 512], xres[:, mt * T + s:mt * T + s + 512],
                t1[:, 0:512]),
                waits=bt1.w + bxres.write_waits(), tag="l1f_o_add")
            bt1.note_read(e2)
            bxres.note_write_acc(e2)
        if tail_cb is not None:
            for k in range(4):
                tail_cb(ch * 4 + k)


_CONSTS = host_constants()

SUBLAYER_SEQ = ["l0_attn_a", "l0_ffn", "l0_attn_b", "l0_ffn2",
                "l1_mamba", "l1_ffn", "l1_attn_b", "l1_ffn2"]



def _ensure_ws(C, slotw):
    if "ws_nat" not in C.sb:
        C.ws_slotw = slotw
        C.sbuf("ws_nat", (128, 2 * slotw), BF16)
        C.sbuf("ws_wt", (128, 2 * slotw), BF16)
        C.sbuf("ws_vT", (128, 8 * 128), BF16)


class WStreamT:

    def __init__(self, C, name, dram, rows, cols):
        self.C, self.name, self.dram = C, name, dram
        self.rows, self.cols = rows, cols
        self.KT = cols // 128
        _ensure_ws(C, max(cols, 1024))
        assert cols <= C.ws_slotw
        self.slot = 0

    def mt_tiles(self, mt, row0=0):
        C, P = self.C, self.C.plan
        sl = self.slot
        self.slot ^= 1
        nat = C.sb["ws_nat"]
        bn = C.buf("ws_nat")
        wt = C.sb["ws_wt"]
        bwt = C.buf("ws_wt")
        pt = C.ps["wt_ps"]
        bp = C.buf("wt_ps")
        identb = C.sb["identity_bf16"]
        r0 = row0 + mt * 128
        SLOTW = self.C.ws_slotw
        ld = P.dma("gpsimd", nat[:, sl * SLOTW: sl * SLOTW + self.cols],
                   self.dram[r0:r0 + 128, :], waits=bn.write_waits(), tag=f"{self.name}_ld")
        bn.note_write_acc(ld)
        for g0 in range(0, self.KT, 8):
            g1 = min(self.KT, g0 + 8)
            for kt in range(g0, g1):
                tp = P.op("tensor", lambda e, kt=kt, sl=sl, g0=g0: e.transpose(
                    pt[:, (kt - g0) * 128:(kt - g0 + 1) * 128],
                    nat[:, sl * SLOTW + kt * 128: sl * SLOTW + (kt + 1) * 128], identb[:, :]),
                    waits=[ld] + C.buf("identity_bf16").read_waits() + bp.write_waits(),
                    tag=f"{self.name}_tp")
                bn.note_read(tp)
                bp.note_write_acc(tp)
            ev = P.op("vector", lambda e, sl=sl, g0=g0, g1=g1: e.tensor_copy(
                wt[:, sl * SLOTW + g0 * 128: sl * SLOTW + g1 * 128],
                pt[:, 0:(g1 - g0) * 128]),
                waits=bp.w + bwt.write_waits(), tag=f"{self.name}_ev")
            bp.note_read(ev)
            bwt.note_write_acc(ev)
        return [(wt[:, sl * SLOTW + kt * 128: sl * SLOTW + (kt + 1) * 128], bwt)
                for kt in range(self.KT)]

    def v_block(self, g, row0, width=512):
        C, P = self.C, self.C.plan
        nat = C.sb["ws_nat"]
        bn = C.buf("ws_nat")
        vt = C.sb["ws_vT"]
        bvt = C.buf("ws_vT")
        pt = C.ps["wt_ps"]
        bp = C.buf("wt_ps")
        identb = C.sb["identity_bf16"]
        for rb in range(width // 128):
            r0 = row0 + g * width + rb * 128
            sl = self.slot
            self.slot ^= 1
            SLOTW = self.C.ws_slotw
            ld = P.dma("gpsimd", nat[:, sl * SLOTW: sl * SLOTW + self.cols],
                       self.dram[r0:r0 + 128, :], waits=bn.write_waits(), tag=f"{self.name}_vld")
            bn.note_write_acc(ld)
            for kt in range(self.KT):
                tp = P.op("tensor", lambda e, kt=kt, sl=sl: e.transpose(
                    pt[:, kt * 128:(kt + 1) * 128],
                    nat[:, sl * SLOTW + kt * 128: sl * SLOTW + (kt + 1) * 128], identb[:, :]),
                    waits=[ld] + bp.write_waits(), tag=f"{self.name}_vtp")
                bn.note_read(tp)
                bp.note_write_acc(tp)
            ev = P.op("vector", lambda e, rb=rb, width=width: e.tensor_copy(
                vt.rearrange("p (k w) -> p k w", w=width)[:, :, rb * 128:(rb + 1) * 128],
                pt.rearrange("p (k w) -> p k w", w=128)[:, 0:self.KT, :]),
                waits=bp.w + bvt.write_waits(), tag=f"{self.name}_vev")
            bp.note_read(ev)
            bvt.note_write_acc(ev)
        return vt, bvt

    def v_slice(self, vt, kt, width=512):
        return vt[:, kt * width:(kt + 1) * width]


class WStreamN:

    def __init__(self, C, name, dram, cin, cout):
        self.C, self.name, self.dram = C, name, dram
        self.cin, self.cout = cin, cout
        self.KT = cin // 128
        _ensure_ws(C, max(cin, 1024))
        assert cin <= C.ws_slotw
        self.slot = 0

    def mt_tiles(self, mt, row0=0):
        C, P = self.C, self.C.plan
        sl = self.slot
        self.slot ^= 1
        SLOTW = self.C.ws_slotw
        wt = C.sb["ws_wt"]
        bwt = C.buf("ws_wt")
        c0 = row0 + mt * 128
        for kt in range(self.KT):
            ld = P.dma("gpsimd", wt[:, sl * SLOTW + kt * 128: sl * SLOTW + (kt + 1) * 128],
                       self.dram[kt * 128:(kt + 1) * 128, c0:c0 + 128],
                       waits=bwt.write_waits() if kt == 0 else [], tag=f"{self.name}_ld")
            bwt.note_write_acc(ld)
        return [(wt[:, sl * SLOTW + kt * 128: sl * SLOTW + (kt + 1) * 128], bwt)
                for kt in range(self.KT)]

    def v_block(self, g, row0, width=512):
        C, P = self.C, self.C.plan
        vt = C.sb["ws_vT"]
        bvt = C.buf("ws_vT")
        c0 = row0 + g * width
        for kt in range(self.KT):
            ld = P.dma("gpsimd", vt[:, kt * width:(kt + 1) * width],
                       self.dram[kt * 128:(kt + 1) * 128, c0:c0 + width],
                       waits=bvt.write_waits() if kt == 0 else [], tag=f"{self.name}_vld")
            bvt.note_write_acc(ld)
        return vt, bvt

    def v_slice(self, vt, kt, width=512):
        return vt[:, kt * width:(kt + 1) * width]


def emit_attn(C, kind, streams, has_bias, causal, rope=False, relbias=False):
    P = C.plan
    xl = C.sb["x_ln"]; bxl = C.buf("x_ln")
    xres = C.sb["x_res"]; bxres = C.buf("x_res")
    v = C.sb["att_v"]; bv = C.buf("att_v")
    o = C.sb["att_o"]; bo = C.buf("att_o")
    pbuf = C.sb["att_p"]; bp = C.buf("att_p")
    for _nm in ("att_p0", "att_p1", "att_p2", "att_p3", "att_p4", "att_p5"):
        C.buf(_nm).w = sorted(set(C.buf(_nm).w + bp.write_waits()))
    den = C.sb["att_den"]; bden = C.buf("att_den")
    rbs = C.sb["att_rb"]; brb = C.buf("att_rb")
    onesf = C.sb["ones_f32"]; bonesf = C.buf("ones_f32")
    onesb = C.sb["ones_bf16"]; bonesb = C.buf("ones_bf16")
    tri = C.sb["tri_bf16"]; btri = C.buf("tri_bf16")
    fv = C.sb["fvals"]; bfv = C.buf("fvals")
    qk = [C.sb["att_qk0"], C.sb["att_qk1"]]
    bqk = [C.buf("att_qk0"), C.buf("att_qk1")]

    (s_qk, q_row0, k_row0) = streams["qk"]
    (s_v, v_row0) = streams["v"]
    (s_o,) = streams["o"]
    relbias_fn = streams.get("relbias_fn")
    if relbias:
        d2 = C.sb["rb_d2"]
        bd2 = C.buf("rb_d2")
    if has_bias:
        qkb = C.sb["att_qkb"]; bqkb = C.buf("att_qkb")
    if rope:
        cosb = C.sb["rope_cos"]; bcos = C.buf("rope_cos")
        sinb = C.sb["rope_sin"]; bsin = C.buf("rope_sin")
        rmat = C.sb["rope_R"]; brm = C.buf("rope_R")
        t1 = C.sb["ln_t1"]; bt1 = C.buf("ln_t1")

    ms = P.op("vector", lambda e: e.memset(v[:, :], 1.0), waits=bv.write_waits(), tag="att_v_ones")
    bv.note_write(ms)

    VSL = TT * 130
    for hpi in range(DT):
        sl = hpi % 2
        dst, bdst = qk[sl], bqk[sl]
        PS4 = ("pa", "pb", "pc", "pd")
        vtiles = s_v.mt_tiles(hpi, row0=v_row0)
        for kt in range(DT):
            tile, bw = vtiles[kt]
            for ch in range(4):
                s = ch * 512
                ps = C.ps[PS4[ch]]
                bps = C.buf(PS4[ch])
                mm = P.op("tensor", lambda e, tile=tile, kt=kt, s=s, ps=ps: e.matmul(
                    ps[0:128, 0:512], tile, xl[:, kt * T + s: kt * T + s + 512],
                    start=(kt == 0), stop=(kt == DT - 1)),
                    waits=bw.read_waits() + bxl.read_waits() +
                          (bps.write_waits() if kt == 0 else []), tag="att_vf_mm")
                bw.note_read(mm)
                bxl.note_read(mm)
                bps.note_write_acc(mm)
        for ch in range(4):
            s = ch * 512
            ps = C.ps[PS4[ch]]
            bps = C.buf(PS4[ch])
            if has_bias:
                ev = P.op("vector", lambda e, s=s, ps=ps, hpi=hpi: e.tensor_scalar_add(
                    dst[:, s:s + 512], ps[0:128, 0:512], qkb[:, 16 + hpi:16 + hpi + 1]),
                    waits=bps.w + bqkb.read_waits() + bdst.write_waits(), tag="att_vf_ev")
                bqkb.note_read(ev)
            else:
                ev = P.op("vector", lambda e, s=s, ps=ps: e.tensor_copy(
                    dst[:, s:s + 512], ps[0:128, 0:512]),
                    waits=bps.w + bdst.write_waits(), tag="att_vf_ev")
            bps.note_read(ev)
            bdst.note_write_acc(ev)
        identb_ = C.sb["identity_bf16"]
        wtp = C.ps["wt_ps"]
        bwtp = C.buf("wt_ps")
        for g in range(2):
            for ii in range(8):
                i = g * 8 + ii
                tp = P.op("tensor", lambda e, i=i, ii=ii: e.transpose(
                    wtp[:, ii * 128:(ii + 1) * 128], dst[:, i * 128:(i + 1) * 128],
                    identb_[:, :]),
                    waits=bdst.read_waits() + bdst.w + C.buf("identity_bf16").read_waits() +
                          bwtp.write_waits(), tag="att_vt")
                bdst.note_read(tp)
                C.buf("identity_bf16").note_read(tp)
                bwtp.note_write_acc(tp)
            ev = P.op("vector", lambda e, g=g, sl=sl: e.tensor_copy(
                v[:, sl * VSL + g * 8 * 130: sl * VSL + (g + 1) * 8 * 130]
                    .rearrange("p (a h c) -> p a h c", h=2, c=65)[:, :, :, 0:64],
                wtp[:, 0:1024].rearrange("p (a h c) -> p a h c", h=2, c=64)),
                waits=bwtp.w + bv.write_waits(), tag="att_v_ev")
            bwtp.note_read(ev)
            bv.note_write_acc(ev)
        PS4 = ("pa", "pb", "pc", "pd")
        for half, row0, bcol0 in ((0, q_row0, 0), (1, k_row0, DT)):
            tiles = s_qk.mt_tiles(hpi, row0=row0)
            for kt in range(DT):
                tile, bw = tiles[kt]
                for ch in range(4):
                    s = ch * 512
                    ps = C.ps[PS4[ch]]
                    bps = C.buf(PS4[ch])
                    mm = P.op("tensor", lambda e, tile=tile, kt=kt, s=s, ps=ps: e.matmul(
                        ps[0:128, 0:512], tile, xl[:, kt * T + s: kt * T + s + 512],
                        start=(kt == 0), stop=(kt == DT - 1)),
                        waits=bw.read_waits() + bxl.read_waits() +
                              (bps.write_waits() if kt == 0 else []), tag="att_qk_mm")
                    bw.note_read(mm)
                    bxl.note_read(mm)
                    bps.note_write_acc(mm)
            for ch in range(4):
                s = ch * 512
                ps = C.ps[PS4[ch]]
                bps = C.buf(PS4[ch])
                col = half * T + s
                if has_bias:
                    ev = P.op("vector", lambda e, dst=dst, col=col, ps=ps, bc=bcol0, hpi=hpi:
                        e.tensor_scalar_add(dst[:, col:col + 512], ps[0:128, 0:512],
                                            qkb[:, bc + hpi:bc + hpi + 1]),
                        waits=bps.w + bqkb.read_waits() + bdst.write_waits(), tag="att_qk_ev")
                    bqkb.note_read(ev)
                else:
                    ev = P.op("vector", lambda e, dst=dst, col=col, ps=ps: e.tensor_copy(
                        dst[:, col:col + 512], ps[0:128, 0:512]),
                        waits=bps.w + bdst.write_waits(), tag="att_qk_ev")
                bps.note_read(ev)
                bdst.note_write_acc(ev)
        if rope:
            for half in range(2):
                for ch in range(4):
                    s = half * T + ch * 512
                    rs = ch * 512
                    ps = C.ps[("pa", "pb")[ch % 2]]
                    bps = C.buf(("pa", "pb")[ch % 2])
                    mm = P.op("tensor", lambda e, dst=dst, s=s, ps=ps: e.matmul(
                        ps[0:128, 0:512], rmat[:, :], dst[:, s:s + 512],
                        start=True, stop=True),
                        waits=brm.read_waits() + bdst.read_waits() + bps.write_waits(), tag="rope_mm")
                    brm.note_read(mm)
                    bdst.note_read(mm)
                    bps.note_write(mm)
                    e1 = P.op("vector", lambda e, rs=rs, ps=ps: e.tensor_mul(
                        t1[:, 0:512], ps[0:128, 0:512], sinb[:, rs:rs + 512]),
                        waits=bps.w + bsin.read_waits() + bt1.write_waits(), tag="rope_m1")
                    bps.note_read(e1)
                    bsin.note_read(e1)
                    bt1.note_write(e1)
                    e2 = P.op("vector", lambda e, dst=dst, s=s, rs=rs: e.tensor_mul(
                        t1[:, 512:1024], dst[:, s:s + 512], cosb[:, rs:rs + 512]),
                        waits=bdst.read_waits() + bcos.read_waits() + [e1], tag="rope_m2")
                    bdst.note_read(e2)
                    bcos.note_read(e2)
                    bt1.note_write_acc(e2)
                    e3 = P.op("vector", lambda e, dst=dst, s=s: e.tensor_add(
                        dst[:, s:s + 512], t1[:, 0:512], t1[:, 512:1024]),
                        waits=bt1.w + bdst.write_waits(), tag="rope_m3")
                    bt1.note_read(e3)
                    bdst.note_write_acc(e3)

        if relbias:
            relbias_fn(2 * hpi + 0, 0)
            relbias_fn(2 * hpi + 1, 1)
        bp_slots = [C.buf(f"att_p{i}") for i in range(6)]
        AVPS = ("pa", "pb")
        SPS = ("pc", "pd", "pe")
        for qc in range(4):
            ki_hi = (4 * qc + 4) if causal else TT

            def emit_S(ki, hh):
                m = ki - 4 * qc
                v0 = m * 128 if (causal and m >= 0) else 0
                hoff = 64 * hh
                ps_s = C.ps[SPS[(2 * ki + hh) % 3]]
                bss = C.buf(SPS[(2 * ki + hh) % 3])
                mm = P.op("tensor", lambda e, dst=dst, hoff=hoff, ki=ki, qc=qc, v0=v0, ps_s=ps_s:
                    e.matmul(
                        ps_s[0:128, v0:512],
                        dst[hoff:hoff + 64, T + ki * 128:T + (ki + 1) * 128],
                        dst[hoff:hoff + 64, qc * 512 + v0:(qc + 1) * 512],
                        start=True, stop=True),
                    waits=bdst.read_waits() + bss.write_waits(), tag="att_s")
                bdst.note_read(mm)
                bss.note_write(mm)

            def emit_P(ki, hh):
                m = ki - 4 * qc
                v0 = m * 128 if (causal and m >= 0) else 0
                ps_s = C.ps[SPS[(2 * ki + hh) % 3]]
                bss = C.buf(SPS[(2 * ki + hh) % 3])
                slot = (ki % 3) + 3 * hh
                bsl = bp_slots[slot]
                ex = P.op("scalar", lambda e, slot=slot, v0=v0, ps_s=ps_s: e.activation(
                    pbuf[:, slot * 512 + v0:(slot + 1) * 512], ps_s[0:128, v0:512], AF.Exp,
                    scale=fv[:, 5:6]),
                    waits=bss.w + bsl.write_waits() + bfv.read_waits(), tag="att_exp")
                bss.note_read(ex)
                bsl.note_write(ex)
                if causal and 0 <= m < 4:
                    tm = P.op("vector", lambda e, slot=slot, v0=v0: e.tensor_mul(
                        pbuf[:, slot * 512 + v0:slot * 512 + v0 + 128],
                        pbuf[:, slot * 512 + v0:slot * 512 + v0 + 128], tri[:, :]),
                        waits=bsl.w + btri.read_waits(), tag="att_tri")
                    btri.note_read(tm)
                    bsl.note_write_acc(tm)
                if relbias and (ki * 128 + 127) - (qc * 512) >= 1026:
                    c0 = qc * 512 + 2047 - (ki * 128 + 127)
                    wm = P.op("vector", lambda e, slot=slot, c0=c0, hh=hh: e.tensor_mul(
                        pbuf[:, slot * 512:(slot + 1) * 512],
                        pbuf[:, slot * 512:(slot + 1) * 512],
                        d2[:, hh * 2048 + c0:hh * 2048 + c0 + 512]),
                        waits=bsl.w + bd2.read_waits(), tag="att_wedge")
                    bd2.note_read(wm)
                    bsl.note_write_acc(wm)

            def emit_AV(ki, hh):
                m = ki - 4 * qc
                v0 = m * 128 if (causal and m >= 0) else 0
                slot = (ki % 3) + 3 * hh
                bsl = bp_slots[slot]
                ps_av = C.ps[AVPS[hh]]
                bav = C.buf(AVPS[hh])
                av = P.op("tensor", lambda e, sl=sl, hh=hh, ki=ki, v0=v0, slot=slot, ps_av=ps_av, last=(ki == ki_hi - 1):
                    e.matmul(
                        ps_av[0:65, v0:512],
                        v[:, sl * VSL + ki * 130 + hh * 65: sl * VSL + ki * 130 + (hh + 1) * 65],
                        pbuf[:, slot * 512 + v0:(slot + 1) * 512],
                        start=(ki == 0), stop=last),
                    waits=bv.read_waits() + bsl.w +
                          (bav.write_waits() if ki == 0 else []), tag="att_av")
                bv.note_read(av)
                bsl.note_read(av)
                bav.note_write_acc(av)

            for hh in range(2):
                emit_S(0, hh)
            for hh in range(2):
                emit_P(0, hh)
            for ki in range(ki_hi):
                if ki + 1 < ki_hi:
                    emit_S(ki + 1, 0)
                    emit_S(ki + 1, 1)
                emit_AV(ki, 0)
                emit_AV(ki, 1)
                if ki + 1 < ki_hi:
                    emit_P(ki + 1, 0)
                    emit_P(ki + 1, 1)
            for hh in range(2):
                hoff = 64 * hh
                ps_av = C.ps[AVPS[hh]]
                bav = C.buf(AVPS[hh])
                d1 = P.op("scalar", lambda e, ps_av=ps_av: e.activation(
                    den[64:65, 0:512], ps_av[64:65, 0:512], AF.Ln),
                    waits=bav.w + bden.write_waits(), tag="att_den")
                bav.note_read(d1)
                bden.note_write(d1)
                denb = den.bitcast(BF16)
                d2o = P.op("scalar", lambda e, denb=denb: e.activation(
                    denb[64:65, 1024:1536], den[64:65, 0:512], AF.Exp, scale=fv[64:65, 4:5]),
                    waits=bden.w + bfv.read_waits(), tag="att_rec")
                bfv.note_read(d2o)
                bden.note_write_acc(d2o)
                rb = P.op("tensor", lambda e, denb=denb: e.matmul(
                    C.ps["pbig"][0:64, 0:512], onesb[64:65, 0:64], denb[64:65, 1024:1536],
                    start=True, stop=True),
                    waits=bden.w + bonesb.read_waits() + C.buf("pbig").write_waits(), tag="att_rb_mm")
                bden.note_read(rb)
                bonesb.note_read(rb)
                C.buf("pbig").note_write(rb)
                rc = P.op("vector", lambda e: e.tensor_copy(rbs[0:64, 0:512], C.ps["pbig"][0:64, 0:512]),
                          waits=C.buf("pbig").w + brb.write_waits(), tag="att_rb_cp")
                C.buf("pbig").note_read(rc)
                brb.note_write(rc)
                oe = P.op("vector", lambda e, hoff=hoff, hpi=hpi, qc=qc, ps_av=ps_av: e.tensor_mul(
                    o[hoff:hoff + 64, hpi * T + qc * 512:hpi * T + (qc + 1) * 512],
                    ps_av[0:64, 0:512], rbs[0:64, 0:512]),
                    waits=bav.w + brb.w + bo.write_waits(), tag="att_o_ev")
                bav.note_read(oe)
                brb.note_read(oe)
                bo.note_write_acc(oe)

    ob = C.sb["att_ob"]; bob = C.buf("att_ob")
    PS4 = ("pa", "pb", "pc", "pd")
    for mt in range(DT):
        tiles = s_o.mt_tiles(mt)
        for kt in range(DT):
            tile, bw = tiles[kt]
            for ch in range(4):
                s = ch * 512
                ps = C.ps[PS4[ch]]
                bps = C.buf(PS4[ch])
                mm = P.op("tensor", lambda e, tile=tile, kt=kt, s=s, ps=ps: e.matmul(
                    ps[0:128, 0:512], tile, o[:, kt * T + s: kt * T + s + 512],
                    start=(kt == 0), stop=(kt == DT - 1)),
                    waits=bw.read_waits() + bo.read_waits() +
                          (bps.write_waits() if kt == 0 else []), tag="att_o_mm")
                bw.note_read(mm)
                bo.note_read(mm)
                bps.note_write_acc(mm)
        for ch in range(4):
            s = ch * 512
            ps = C.ps[PS4[ch]]
            bps = C.buf(PS4[ch])
            ev = P.op("vector", lambda e, mt=mt, s=s, ps=ps: e.scalar_tensor_tensor(
                xres[:, mt * T + s:mt * T + s + 512], ps[0:128, 0:512], ob[:, mt:mt + 1],
                xres[:, mt * T + s:mt * T + s + 512], ALU.add, ALU.add),
                waits=bps.w + bob.read_waits() + bxres.write_waits(), tag="att_res")
            bps.note_read(ev)
            bob.note_read(ev)
            bxres.note_write_acc(ev)


def _alloc_attn_common(C, has_bias):
    C.salloc("att_qk0", (128, 2 * T), BF16)
    C.salloc("att_qk1", (128, 2 * T), BF16)
    C.salloc("att_v", (128, 2 * TT * 130), BF16)
    C.salloc("att_o", (128, DT * T), BF16)
    C.salloc("att_p", (128, 6 * 512), BF16)
    C.salloc("att_den", (65, 768), F32)
    C.salloc("att_rb", (64, 512), F32)
    if "wt_ps" not in C.ps:
        C.psum("wt_ps", (128, 1024), BF16)


def _load_bias_cols(C, din, name, vec):
    arr = np.ascontiguousarray(vec.reshape(DT, 128).T, dtype=np.float32)
    d = din(name + "_in", arr)
    return emit_load_const(C, name, d, arr.shape, F32)


def _load_vbias(C, din, name, vec):
    arr = np.ascontiguousarray(vec.reshape(1, 1024), dtype=np.float32)
    d = din(name + "_in", arr)
    h = C.salloc(name, (1, 1024), BF16)
    oid = C.plan.dma("gpsimd", h[0:1, :], d[0:1, :],
                     waits=C.buf(name).write_waits(), tag=f"load_{name}")
    C.buf(name).note_write(oid)
    return h


def prep_attn_l0a(C, W, din):
    _alloc_attn_common(C, True)
    b = W["L0_attn_a_b_qkv"]
    qkb = np.concatenate([b[0:1024].reshape(DT, 128).T, b[1024:2048].reshape(DT, 128).T,
                          b[2048:3072].reshape(DT, 128).T], axis=1)
    dq = din("att_qkb_in", np.ascontiguousarray(qkb, dtype=np.float32))
    emit_load_const(C, "att_qkb", dq, qkb.shape, F32)
    _load_bias_cols(C, din, "att_ob", W["L0_attn_a_b_o"])
    s_qkv = WStreamT(C, "wqkv", din("W_qkv", W["L0_attn_a_W_qkv"]), 3072, 1024)
    s_o = WStreamT(C, "wo", din("W_o", W["L0_attn_a_W_o"]), 1024, 1024)
    streams = {"qk": (s_qkv, 0, 1024), "v": (s_qkv, 2048), "o": (s_o,)}
    return lambda: emit_attn(C, "l0a", streams, has_bias=True, causal=True)


def prep_attn_l1b(C, W, din):
    _alloc_attn_common(C, False)
    zeros = np.zeros((128, DT), np.float32)
    dz = din("att_ob_in", zeros)
    emit_load_const(C, "att_ob", dz, zeros.shape, F32)
    pos = np.arange(T, dtype=np.float64)
    didx = np.arange(128) % 64
    inv = 1.0 / (10000.0 ** ((didx % 32) * 2.0 / 64.0))
    ang = inv[:, None] * pos[None, :]
    cosb = np.cos(ang).astype(np.float32)
    sinb = np.sin(ang).astype(np.float32)
    nc_din_bf16(C, din, "rope_cos", cosb)
    nc_din_bf16(C, din, "rope_sin", sinb)
    R = np.zeros((128, 128), np.float32)
    for half in range(2):
        b0 = 64 * half
        for m in range(64):
            if m < 32:
                R[b0 + m + 32, b0 + m] = -1.0
            else:
                R[b0 + m - 32, b0 + m] = 1.0
    dr = nc_din_bf16(C, din, "rope_R", R)
    s_qkv = WStreamT(C, "w1", din("W1", W["L1_attn_b_W1"]), 3072, 1024)
    s_o = WStreamT(C, "w2", din("W2", W["L1_attn_b_W2"]), 1024, 1024)
    streams = {"qk": (s_qkv, 0, 1024), "v": (s_qkv, 2048), "o": (s_o,)}
    return lambda: emit_attn(C, "l1b", streams, has_bias=False, causal=True, rope=True)


def nc_din_bf16(C, din, name, arr_f32):
    arr = np.ascontiguousarray(arr_f32, dtype=np.float32)
    d = din(name + "_in", arr)
    h = C.salloc(name, arr.shape, BF16)
    oid = C.plan.dma("gpsimd", h[0:arr.shape[0], :], d[:, :],
                     waits=C.buf(name).write_waits(), tag=f"load_{name}")
    C.buf(name).note_write(oid)
    return h


class QKDispatch:

    def __init__(self, sq, sk):
        self.sq, self.sk = sq, sk

    def mt_tiles(self, mt, row0=0):
        return (self.sq if row0 == 0 else self.sk).mt_tiles(mt, row0=0)


def prep_attn_l0b(C, W, din):
    _alloc_attn_common(C, True)
    qkb = np.concatenate([W["L0_attn_b_bq"].reshape(DT, 128).T,
                          W["L0_attn_b_bk"].reshape(DT, 128).T,
                          W["L0_attn_b_bv"].reshape(DT, 128).T], axis=1)
    dq = din("att_qkb_in", np.ascontiguousarray(qkb, dtype=np.float32))
    emit_load_const(C, "att_qkb", dq, qkb.shape, F32)
    _load_bias_cols(C, din, "att_ob", W["L0_attn_b_bo"])
    sq = WStreamN(C, "wq", din("Wq", W["L0_attn_b_Wq"]), 1024, 1024)
    sk = WStreamN(C, "wk", din("Wk", W["L0_attn_b_Wk"]), 1024, 1024)
    sv = WStreamN(C, "wv", din("Wv", W["L0_attn_b_Wv"]), 1024, 1024)
    so = WStreamN(C, "wo2", din("Wo", W["L0_attn_b_Wo"]), 1024, 1024)
    P = C.plan
    nc = C.nc
    rt_d = din("rel_table", W["L0_attn_b_rel_table"])
    eg = C.salloc_in("att_o", "rb_eg", (16, 2176), F32, 0)
    beg = C.buf("rb_eg")
    rt = C.salloc_in("att_o", "rb_rt", (128, 8 * 16), F32, 2176)
    brt = C.buf("rb_rt")
    C.salloc("rb_d2", (128, 2 * 2048), BF16)
    egpad_d = nc.dram_tensor("rb_egpad_d", [16, 2175], BF16, kind="Internal")
    egpad_ap = egpad_d.ap()
    ident = C.sb["identity_f32"]
    pbig = C.ps["pbig"]
    bpb = C.buf("pbig")

    def emit_rel_pre():
        ms = P.op("vector", lambda e: e.memset(eg[:, :], 1.0), waits=beg.write_waits(), tag="rb_ms")
        beg.note_write(ms)
        for t in range(8):
            r0 = t * 128
            r1 = min(1023, r0 + 128)
            nr = r1 - r0
            ld = P.dma("sync", rt[0:nr, t * 16:(t + 1) * 16], rt_d[r0:r1, :],
                       waits=brt.write_waits(), tag="rb_ld")
            brt.note_write_acc(ld)
            tp = P.op("tensor", lambda e, t=t, nr=nr: e.transpose(
                pbig[0:16, t * 128:t * 128 + nr], rt[0:nr, t * 16:(t + 1) * 16],
                ident[0:nr, 0:nr]),
                waits=[ld] + C.buf("identity_f32").read_waits() + bpb.write_waits(), tag="rb_tp")
            brt.note_read(tp)
            bpb.note_write_acc(tp)
        ev = P.op("vector", lambda e: e.tensor_copy(eg[0:16, 0:1023], pbig[0:16, 0:1023]),
                  waits=bpb.w + [ms], tag="rb_ev")
        bpb.note_read(ev)
        beg.note_write_acc(ev)
        ch = P.op("vector", lambda e: e.tensor_copy(eg[0:16, 2175:2176], eg[0:16, 1022:1023]),
                  waits=beg.w, tag="rb_ch")
        beg.note_read(ch)
        beg.note_write_acc(ch)
        sb = P.op("vector", lambda e: e.tensor_single_scalar(
            eg[0:16, 0:1023], eg[0:16, 0:1023], eg[0:16, 2175:2176], ALU.subtract),
            waits=beg.w, tag="rb_sub")
        beg.note_write_acc(sb)
        ex = P.op("scalar", lambda e: e.activation(eg[0:16, 0:1023], eg[0:16, 0:1023], AF.Exp),
                  waits=beg.w, tag="rb_exp")
        beg.note_write_acc(ex)
        st = P.dma("gpsimd", egpad_ap[:, :], eg[0:16, 0:2175], waits=beg.w + beg.r, tag="rb_st")
        beg.note_read(st)
        C.buf("rb_egpad_d").note_write(st)

    rev = C.sb["rev_bf16"]

    def relbias_fn(h, dslot=0):
        old_toks = []
        for nm in ("att_den", "att_rb", "rb_d2f"):
            b = C.bufs.get(nm)
            if b:
                old_toks += b.w + b.r
        if C.fused:
            d2f = C.salloc_in("att_den", "rb_d2f", (128, 2048), BF16, 0)
        else:
            d2f = C.sb["rb_d2f"] if "rb_d2f" in C.sb else C.sbuf("rb_d2f", (128, 2048), BF16)
        bd2 = C.buf("rb_d2")
        bd2f = C.buf("rb_d2f")
        bd2f.w = sorted(set(bd2f.w + old_toks))
        d2h = C.sb["rb_d2"]
        srcap = bass.AP(egpad_d, h * 2175, [[1, 128], [1, 2048]])
        dm = P.dma("scalar", d2f[:, :], srcap,
                   waits=C.buf("rb_egpad_d").read_waits() + bd2f.write_waits(), tag="rb_d2f")
        bd2f.note_write(dm)
        for cch in range(2):
            s = cch * 1024
            for half in range(2):
                mm = P.op("tensor", lambda e, s=s, half=half, d2f=d2f: e.matmul(
                    C.ps["pbig"][:, half * 512:(half + 1) * 512], rev[:, :],
                    d2f[:, s + half * 512:s + (half + 1) * 512], start=True, stop=True),
                    waits=bd2f.w + C.buf("rev_bf16").read_waits() +
                          C.buf("pbig").write_waits(), tag="rb_rev")
                bd2f.note_read(mm)
                C.buf("rev_bf16").note_read(mm)
                C.buf("pbig").note_write_acc(mm)
            ev = P.op("vector", lambda e, s=s, dslot=dslot: e.tensor_copy(
                d2h[:, dslot * 2048 + s:dslot * 2048 + s + 1024], C.ps["pbig"][:, 0:1024]),
                waits=C.buf("pbig").w + bd2.write_waits(), tag="rb_rev_ev")
            C.buf("pbig").note_read(ev)
            bd2.note_write_acc(ev)

    emit_rel_pre()
    if C.fused:
        C.arena_reset(["att_o"], ["rb_eg", "rb_rt", "att_o"])
    streams = {"qk": (QKDispatch(sq, sk), 0, 1), "v": (sv, 0), "o": (so,),
               "relbias_fn": relbias_fn}
    return lambda: emit_attn(C, "l0b", streams, has_bias=True, causal=False, relbias=True)



def prep_mamba(C, W, din):
    nc, P = C.nc, C.plan
    def cols(name, vec, per):
        arr = np.ascontiguousarray(vec.reshape(-1, 128).T, dtype=np.float32)
        d = din(name + "_in", arr)
        return emit_load_const(C, name, d, arr.shape, F32)
    cols("mb_cvb", W["L1_attn_a_conv_b"], 16)
    cols("mb_dtb", W["L1_attn_a_dt_proj_b"], 16)
    cols("mb_dcol", W["L1_attn_a_D"], 16)
    cvw = np.ascontiguousarray(
        W["L1_attn_a_conv_w"].reshape(16, 128, 4).transpose(1, 0, 2).reshape(128, 64),
        dtype=np.float32)
    emit_load_const(C, "mb_cvw", din("mb_cvw_in", cvw), cvw.shape, F32)
    alog = np.ascontiguousarray(
        W["L1_attn_a_A_log"].reshape(16, 128, 16).transpose(1, 0, 2).reshape(128, 256),
        dtype=np.float32)
    C.salloc("mb_cb", (128, 16 * 1024), BF16)
    if C.fused:
        C.sb["mb_alog"] = C.salloc_in("mb_cb", "mb_alog", (128, 256), F32, 0)
        alog_ap = emit_load_const(C, "mb_alog", din("mb_alog_in", alog), alog.shape, F32)
    else:
        alog_ap = emit_load_const(C, "mb_alog", din("mb_alog_in", alog), alog.shape, F32)
    acol = C.salloc("mb_acol", (128, 256), F32)
    e1 = P.op("scalar", lambda e: e.activation(acol[:, :], alog_ap[0:128, :], AF.Exp),
              waits=C.buf("mb_alog").read_waits(), tag="mb_acol_e")
    C.buf("mb_alog").note_read(e1)
    C.buf("mb_acol").note_write(e1)
    e2 = P.op("vector", lambda e: e.tensor_scalar_mul(acol[:, :], acol[:, :], C.sb["fvals"][:, 4:5]),
              waits=C.buf("mb_acol").w + C.buf("fvals").read_waits(), tag="mb_acol_n")
    C.buf("mb_acol").note_write(e2)
    if C.fused:
        C.arena_reset(["mb_cb"], ["mb_alog", "mb_cb"])
    eh = np.zeros((96, 32 * 128), np.float32)
    for n in range(16):
        eh[64 + n, n * 128:(n + 1) * 128] = 1.0
        eh[80 + n, (16 + n) * 128:(17 + n) * 128] = 1.0
    ehot_d = din("mb_ehot_in", eh)
    _ensure_ws(C, 2048)
    w_in = WStreamT(C, "win4", din("in_proj_w", W["L1_attn_a_in_proj_w"]), 4096, 1024)
    w_xp = WeightLoader(C, "wxp", din("x_proj_w", W["L1_attn_a_x_proj_w"]), 96, 2048, True)
    w_dt = WeightLoader(C, "wdt", din("dt_proj_w", W["L1_attn_a_dt_proj_w"]), 2048, 64, True)
    w_out = WStreamT(C, "wout4", din("out_proj_w", W["L1_attn_a_out_proj_w"]), 1024, 2048)
    w_dt.prefetch_all()
    xc_d = nc.dram_tensor("mb_xc_d", [2048, 2048], BF16, kind="Internal").ap()
    z_d = nc.dram_tensor("mb_z_d", [2048, 2048], BF16, kind="Internal").ap()
    g_d = nc.dram_tensor("mb_g_d", [2048, 2048], BF16, kind="Internal").ap()
    return lambda: emit_mamba(C, w_in, w_xp, w_dt, w_out, xc_d, z_d, g_d, ehot_d)


def emit_mamba(C, w_in, w_xp, w_dt, w_out, xc_d, z_d, g_d, ehot_d):
    P = C.plan
    xl = C.sb["x_ln"]; bxl = C.buf("x_ln")
    xres = C.sb["x_res"]; bxres = C.buf("x_res")
    fv = C.sb["fvals"]; bfv = C.buf("fvals")
    acol = C.sb["mb_acol"]; bacol = C.buf("mb_acol")
    cvw = C.sb["mb_cvw"]; cvb = C.sb["mb_cvb"]
    dtb = C.sb["mb_dtb"]; dcol = C.sb["mb_dcol"]
    ehot = C.sb["ws_nat"]; behot = C.buf("ws_nat")
    cbsb = C.sb["mb_cb"]; bcb = C.buf("mb_cb")
    bxcd = C.buf("mb_xc_d"); bzd = C.buf("mb_z_d"); bgd = C.buf("mb_g_d")
    identb = C.sb["identity_bf16"]

    mb0 = C.salloc_block(9728) if C.fused else None
    C.carve_rel(mb0, "w_wxp", 4100, 1536, BF16)
    w_xp.prefetch_all()
    xe = C.carve_rel(mb0, "mb_xe", 0, 2052, F32)
    bxe = C.buf("mb_xe")
    acc = C.carve_rel(mb0, "mb_acc", 2052, 2048, F32)
    bacc = C.buf("mb_acc")
    zstg = C.carve_rel(mb0, "mb_zstg", 2052, 2048, BF16)
    bzstg = C.buf("mb_zstg")
    xdblb = C.carve_rel(mb0, "mb_xdblb", 8704, 2048, BF16)
    bxdblb = C.buf("mb_xdblb")
    C.arena_reset(["mb_xe", "mb_acc", "mb_zstg", "mb_xdblb"],
                  ["ln_t1", "ln_sq", "ln_rows"])
    XPPS = (("pc", 0), ("pd", 0), ("pbig", 0), ("pbig", 512))
    for mt in range(32):
        dt = mt % 16
        is_x = mt < 16
        if mt == 16:
            C.arena_reset(["mb_zstg"], ["mb_xe", "mb_acc"])
        tiles = w_in.mt_tiles(mt)
        if is_x:
            z3 = P.op("vector", lambda e: e.memset(xe[:, 0:3], 0.0),
                      waits=bxe.write_waits(), tag="mb_xe0")
            bxe.note_write(z3)
        PS3 = ("pa", "pb", "pe")
        for ch in range(4):
            s = ch * 512
            ps = C.ps[PS3[ch % 3]]
            bps = C.buf(PS3[ch % 3])
            for kt in range(DT):
                tile, bw = tiles[kt]
                mm = P.op("tensor", lambda e, tile=tile, kt=kt, s=s, ps=ps: e.matmul(
                    ps[0:128, 0:512], tile, xl[:, kt * T + s:kt * T + s + 512],
                    start=(kt == 0), stop=(kt == DT - 1)),
                    waits=bw.read_waits() + bxl.read_waits() +
                          (bps.write_waits() if kt == 0 else []), tag="mb_inp")
                bw.note_read(mm)
                bxl.note_read(mm)
                bps.note_write_acc(mm)
            if is_x:
                ev = P.op("scalar", lambda e, s=s, ps=ps: e.activation(
                    xe[:, 3 + s:3 + s + 512], ps[0:128, 0:512], AF.Copy),
                    waits=bps.w + bxe.write_waits(), tag="mb_xe_ev")
                bps.note_read(ev)
                bxe.note_write_acc(ev)
            else:
                ev = P.op("scalar", lambda e, s=s, ps=ps: e.activation(
                    zstg[:, s:s + 512], ps[0:128, 0:512], AF.Silu),
                    waits=bps.w + bzstg.write_waits(), tag="mb_z_ev")
                bps.note_read(ev)
                bzstg.note_write_acc(ev)
        if is_x:
            c0 = P.op("vector", lambda e, dt=dt: e.tensor_scalar(
                acc[:, :], xe[:, 3:2051], cvw[:, dt * 4 + 3:dt * 4 + 4], None, ALU.mult),
                waits=bxe.w + C.buf("mb_cvw").read_waits() + bacc.write_waits(), tag="mb_cv3")
            bxe.note_read(c0)
            bacc.note_write(c0)
            for j in (2, 1, 0):
                cj = P.op("vector", lambda e, dt=dt, j=j: e.scalar_tensor_tensor(
                    acc[:, :], xe[:, j:j + 2048], cvw[:, dt * 4 + j:dt * 4 + j + 1],
                    acc[:, :], ALU.mult, ALU.add),
                    waits=bxe.w + bacc.w + C.buf("mb_cvw").read_waits(), tag="mb_cv")
                bxe.note_read(cj)
                bacc.note_write(cj)
            xco = xe[:, 0:1024].bitcast(BF16)
            sl = P.op("scalar", lambda e, dt=dt: e.activation(
                xco[:, 0:2048], acc[:, :], AF.Silu, bias=cvb[:, dt:dt + 1]),
                waits=bacc.w + C.buf("mb_cvb").read_waits() + bxe.write_waits(), tag="mb_silu")
            bacc.note_read(sl)
            bxe.note_write(sl)
            st = P.dma("sync", xc_d[dt * 128:(dt + 1) * 128, :], xco[:, 0:2048],
                       waits=[sl] + bxcd.write_waits(), tag="mb_xc_st")
            bxe.note_read(st)
            bxcd.note_write_acc(st)
            for ch in range(4):
                s = ch * 512
                pnm, po = XPPS[ch]
                ps = C.ps[pnm]
                bps = C.buf(pnm)
                tile, bw = w_xp.tile(dt, 0)
                mm = P.op("tensor", lambda e, tile=tile, s=s, ps=ps, po=po, dt=dt: e.matmul(
                    ps[0:96, po:po + 512], tile, xco[:, s:s + 512],
                    start=(dt == 0), stop=(dt == 15), skip_group_check=True),
                    waits=bw.read_waits() + [sl] +
                          (bps.write_waits() if dt == 0 else []), tag="mb_xp")
                bw.note_read(mm)
                bxe.note_read(mm)
                bps.note_write_acc(mm)
        else:
            st = P.dma("sync", z_d[dt * 128:(dt + 1) * 128, :], zstg[:, 0:2048],
                       waits=bzstg.w + bzd.write_waits(), tag="mb_z_st")
            bzstg.note_read(st)
            bzd.note_write_acc(st)
    for ch in range(4):
        pnm, po = XPPS[ch]
        ps = C.ps[pnm]
        bps = C.buf(pnm)
        ev = P.op("scalar", lambda e, ch=ch, ps=ps, po=po: e.activation(
            xdblb[0:96, ch * 512:(ch + 1) * 512], ps[0:96, po:po + 512], AF.Copy),
            waits=bps.w + bxdblb.write_waits(), tag="mb_xdbl_ev")
        bps.note_read(ev)
        bxdblb.note_write_acc(ev)
    ehl = P.dma("gpsimd", ehot[0:96, 0:4096], ehot_d[:, :],
                waits=behot.write_waits(), tag="mb_ehot_ld")
    behot.note_write(ehl)

    dA = C.carve_rel(mb0, "mb_dA", 0, 1024, F32)
    bdA = C.buf("mb_dA")
    dBu = C.carve_rel(mb0, "mb_dBu", 1024, 1024, BF16)
    bdBu = C.buf("mb_dBu")
    hbf = C.carve_rel(mb0, "mb_h", 1536, 1024, BF16)
    bh = C.buf("mb_h")
    hc = C.carve_rel(mb0, "mb_hc", 2048, 1024, BF16)
    bhc = C.buf("mb_hc")
    gate1 = C.carve_rel(mb0, "mb_g1", 4096, 1024, F32)
    bg1 = C.buf("mb_g1")
    spt = C.carve_rel(mb0, "mb_spt", 6144, 512, F32)
    bspt = C.buf("mb_spt")
    zg1, bzg1 = C.carve_rel(mb0, "mb_zg0", 2560, 1024, BF16), C.buf("mb_zg0")
    zgs = [(zg1, bzg1), (zg1, bzg1)]
    ubs = [(C.carve_rel(mb0, "mb_u0", 3072, 1024, BF16), C.buf("mb_u0")),
           (C.carve_rel(mb0, "mb_u1", 6656, 1024, BF16), C.buf("mb_u1"))]
    xc2s = [(C.carve_rel(mb0, "mb_xc20", 3584, 1024, BF16), C.buf("mb_xc20")),
            (C.carve_rel(mb0, "mb_xc21", 7168, 1024, BF16), C.buf("mb_xc21"))]
    deltas = [(C.carve_rel(mb0, "mb_delta0", 5120, 1024, F32), C.buf("mb_delta0")),
              (C.carve_rel(mb0, "mb_delta1", 7680, 1024, F32), C.buf("mb_delta1"))]
    if C.fused:
        dA2 = C.salloc_in("ln_sq", "mb_dA2", (128, 1024), F32, 0)
        dBu2 = C.salloc_in("ln_sq", "mb_dBu2", (128, 1024), BF16, 1024)
        hbf2 = C.salloc_in("ln_sq", "mb_h2", (128, 1024), BF16, 1536)
        hc2 = C.salloc("mb_hc2", (128, 1024), BF16)
        C.arena_reset(["mb_dA2", "mb_dBu2", "mb_h2"],
                      ["ln_sq", "ln_rows", "mb_dA2", "mb_dBu2", "mb_h2"])
    else:
        dA2, dBu2, hbf2, hc2 = dA, dBu, hbf, hc
    bdA2 = C.buf("mb_dA2") if C.fused else bdA
    bdBu2 = C.buf("mb_dBu2") if C.fused else bdBu
    bh2 = C.buf("mb_h2") if C.fused else bh
    bhc2 = C.buf("mb_hc2") if C.fused else bhc
    C.arena_reset(["mb_dA", "mb_dBu", "mb_h", "mb_hc", "mb_zg0",
                   "mb_u0", "mb_u1", "mb_xc20", "mb_xc21", "mb_g1", "mb_spt",
                   "mb_delta0", "mb_delta1"],
                  ["mb_xe", "mb_acc", "mb_zstg", "w_wxp"])
    hstate = C.salloc("mb_hst", (128, 256), F32)
    bhst = C.buf("mb_hst")
    bbsb, bbb = xl, bxl
    py = C.ps["pbig"]
    bpy = C.buf("pbig")

    for half in range(2):
        hs = half * 1024
        for n in range(NH):
            for (base, dst, bdst) in ((0, bbsb, bbb), (16, cbsb, bcb)):
                for q in range(2):
                    ps = C.ps[("pa", "pb")[q]]
                    bps = C.buf(("pa", "pb")[q])
                    mm = P.op("tensor", lambda e, base=base, n=n, q=q, hs=hs, ps=ps: e.matmul(
                        ps[0:128, 0:512],
                        ehot[0:96, (base + n) * 128:(base + n + 1) * 128],
                        xdblb[0:96, hs + q * 512:hs + (q + 1) * 512],
                        start=True, stop=True),
                        waits=behot.read_waits() + bxdblb.read_waits() + bps.write_waits(),
                        tag="mb_sel")
                    behot.note_read(mm)
                    bxdblb.note_read(mm)
                    bps.note_write(mm)
                    ev = P.op("scalar", lambda e, n=n, q=q, dst=dst, ps=ps: e.activation(
                        dst[:, n * 1024 + q * 512:n * 1024 + (q + 1) * 512],
                        ps[0:128, 0:512], AF.Copy),
                        waits=bps.w + bdst.write_waits(), tag="mb_sel_ev")
                    bps.note_read(ev)
                    bdst.note_write_acc(ev)
        def emit_ldz(phalf, pdt):
            phs = phalf * 1024
            pzg, pbzg = zgs[pdt % 2]
            ldz = P.dma("gpsimd", pzg[:, 0:1024],
                        z_d[pdt * 128:(pdt + 1) * 128, phs:phs + 1024],
                        waits=bzd.w + pbzg.write_waits(), tag="mb_z_ld")
            bzd.note_read(ldz)
            pbzg.note_write(ldz)

        def prep(phalf, pdt):
            phs = phalf * 1024
            psl = pdt % 2
            pxc, pbxc = xc2s[psl]
            pdl, pbdl = deltas[psl]
            pub, pbub = ubs[psl]
            ldx = P.dma("gpsimd", pxc[:, 0:1024],
                        xc_d[pdt * 128:(pdt + 1) * 128, phs:phs + 1024],
                        waits=bxcd.w + pbxc.write_waits(), tag="mb_xc_ld")
            bxcd.note_read(ldx)
            pbxc.note_write(ldx)
            for ch in range(2):
                s = ch * 512
                ps = C.ps[("pc", "pd")[ch % 2]]
                bps = C.buf(("pc", "pd")[ch % 2])
                tile, bw = w_dt.tile(0, pdt)
                mm = P.op("tensor", lambda e, tile=tile, s=s, phs=phs, ps=ps: e.matmul(
                    ps[0:128, 0:512], tile, xdblb[0:64, phs + s:phs + s + 512],
                    start=True, stop=True),
                    waits=bw.read_waits() + bxdblb.read_waits() + bps.write_waits(), tag="mb_dt")
                bw.note_read(mm)
                bxdblb.note_read(mm)
                bps.note_write(mm)
                o1 = P.op("scalar", lambda e, s=s, pdt=pdt, ps=ps, pdl=pdl: e.activation(
                    pdl[:, s:s + 512], ps[0:128, 0:512], AF.Relu,
                    bias=dtb[:, pdt:pdt + 1]),
                    waits=bps.w + C.buf("mb_dtb").read_waits() + pbdl.write_waits(),
                    tag="mb_sp_r")
                bps.note_read(o1)
                pbdl.note_write_acc(o1)
                o2 = P.op("scalar", lambda e, pdt=pdt, ps=ps: e.activation(
                    spt[:, :], ps[0:128, 0:512], AF.Abs, bias=dtb[:, pdt:pdt + 1]),
                    waits=bps.w + C.buf("mb_dtb").read_waits() + bspt.write_waits(),
                    tag="mb_sp_abs")
                bps.note_read(o2)
                bspt.note_write(o2)
                o3 = P.op("scalar", lambda e: e.activation(spt[:, :], spt[:, :], AF.Exp,
                                                           scale=fv[:, 4:5]),
                          waits=bspt.w + bfv.read_waits(), tag="mb_sp_e")
                bspt.note_write_acc(o3)
                o4 = P.op("scalar", lambda e: e.activation(spt[:, :], spt[:, :], AF.Ln, bias=1.0),
                          waits=[o3], tag="mb_sp_l")
                bspt.note_write_acc(o4)
                o6 = P.op("vector", lambda e, s=s, pdl=pdl: e.tensor_add(
                    pdl[:, s:s + 512], pdl[:, s:s + 512], spt[:, :]),
                    waits=bspt.w + pbdl.write_waits() + pbdl.read_waits(), tag="mb_sp_f")
                bspt.note_read(o6)
                pbdl.note_write_acc(o6)
            uo = P.op("vector", lambda e, pub=pub, pdl=pdl, pxc=pxc: e.tensor_mul(
                pub[:, 0:1024], pdl[:, 0:1024], pxc[:, 0:1024]),
                      waits=pbdl.w + pbxc.w + pbub.write_waits(), tag="mb_u")
            pbdl.note_read(uo)
            pbxc.note_read(uo)
            pbub.note_write(uo)

        for dt in range(16):
            sl = dt % 2
            xcb2, bxc2 = xc2s[sl]
            zg, bzg = zgs[sl]
            delta, bdelta = deltas[sl]
            ub, bub = ubs[sl]
            if dt == 0 and half == 0:
                prep(0, 0)
                emit_ldz(0, 0)
            for n in range(NH):
                if n == 3:
                    if dt < 15:
                        prep(half, dt + 1)
                    elif half == 0:
                        prep(1, 0)
                dslot = n % 2
                dA_c, bdA_c = (dA, bdA) if dslot == 0 else (dA2, bdA2)
                da = P.op("scalar", lambda e, dt=dt, n=n, dA_c=dA_c, delta=delta: e.activation(
                    dA_c[:, 0:1024], delta[:, 0:1024], AF.Exp,
                    scale=acol[:, dt * 16 + n:dt * 16 + n + 1]),
                    waits=bdelta.read_waits() + bacol.read_waits() + bdA_c.write_waits(),
                    tag="mb_dA")
                bdelta.note_read(da)
                bacol.note_read(da)
                bdA_c.note_write(da)
                dslot = n % 2
                dBu_c, bdBu_c = (dBu, bdBu) if dslot == 0 else (dBu2, bdBu2)
                db = P.op("vector", lambda e, n=n, dBu_c=dBu_c, ub=ub: e.tensor_mul(
                    dBu_c[:, 0:1024], ub[:, 0:1024], bbsb[:, n * 1024:(n + 1) * 1024]),
                    waits=bub.read_waits() + bbb.read_waits() + bdBu_c.write_waits(), tag="mb_dBu")
                bub.note_read(db)
                bbb.note_read(db)
                bdBu_c.note_write(db)
                if half == 0:
                    ini = 0.0
                    iw = []
                else:
                    ini = hstate[:, dt * 16 + n:dt * 16 + n + 1]
                    iw = bhst.read_waits()
                h_c, bh_c = (hbf, bh) if dslot == 0 else (hbf2, bh2)
                hc_c, bhc_c = (hc, bhc) if dslot == 0 else (hc2, bhc2)
                sc = P.op("vector", lambda e, ini=ini, dA_c=dA_c, dBu_c=dBu_c, h_c=h_c: e.tensor_tensor_scan(
                    h_c[:, 0:1024], dA_c[:, 0:1024], dBu_c[:, 0:1024], ini, ALU.mult, ALU.add),
                    waits=bdA_c.w + bdBu_c.w + iw + bh_c.write_waits(), tag="mb_scan")
                bdA_c.note_read(sc)
                bdBu_c.note_read(sc)
                if half == 1:
                    bhst.note_read(sc)
                bh_c.note_write(sc)
                if half == 0:
                    hs_cp = P.op("scalar", lambda e, dt=dt, n=n, h_c=h_c: e.activation(
                        hstate[:, dt * 16 + n:dt * 16 + n + 1], h_c[:, 1023:1024], AF.Copy),
                        waits=bh_c.w + bhst.write_waits(), tag="mb_hst")
                    bh_c.note_read(hs_cp)
                    bhst.note_write_acc(hs_cp)
                hco = P.op("vector", lambda e, n=n, h_c=h_c, hc_c=hc_c: e.tensor_mul(
                    hc_c[:, 0:1024], h_c[:, 0:1024], cbsb[:, n * 1024:(n + 1) * 1024]),
                    waits=bh_c.w + bcb.read_waits() + bhc_c.write_waits(), tag="mb_hc")
                bh_c.note_read(hco)
                bcb.note_read(hco)
                bhc_c.note_write(hco)
                for q in range(2):
                    mm = P.op("tensor", lambda e, q=q, n=n, hc_c=hc_c: e.matmul(
                        py[:, q * 512:(q + 1) * 512], identb[:, :],
                        hc_c[:, q * 512:(q + 1) * 512],
                        start=(n == 0), stop=(n == NH - 1)),
                        waits=C.buf("identity_bf16").read_waits() + bhc_c.w +
                              (bpy.write_waits() if n == 0 else []), tag="mb_y")
                    C.buf("identity_bf16").note_read(mm)
                    bhc_c.note_read(mm)
                    bpy.note_write_acc(mm)
            g1a = P.op("scalar", lambda e, dt=dt, xcb2=xcb2: e.activation(
                gate1[:, 0:1024], xcb2[:, 0:1024], AF.Copy, scale=dcol[:, dt:dt + 1]),
                waits=bxc2.read_waits() + C.buf("mb_dcol").read_waits() +
                      bg1.write_waits(), tag="mb_g1a")
            bxc2.note_read(g1a)
            bg1.note_write(g1a)
            g1b = P.op("vector", lambda e: e.tensor_add(
                gate1[:, 0:1024], gate1[:, 0:1024], py[:, 0:1024]),
                waits=bg1.w + bpy.w, tag="mb_g1b")
            bpy.note_read(g1b)
            bg1.note_write_acc(g1b)
            go = P.op("vector", lambda e, zg=zg: e.tensor_mul(hc[:, 0:1024], gate1[:, 0:1024],
                                                              zg[:, 0:1024]),
                      waits=bg1.w + bzg.read_waits() + bhc.write_waits(), tag="mb_gate")
            bg1.note_read(go)
            bzg.note_read(go)
            bhc.note_write(go)
            st = P.dma("sync", g_d[dt * 128:(dt + 1) * 128, hs:hs + 1024], hc[:, 0:1024],
                       waits=bhc.w + bgd.write_waits(), tag="mb_g_st")
            bhc.note_read(st)
            bgd.note_write_acc(st)
            if dt < 15:
                emit_ldz(half, dt + 1)
            elif half == 0:
                emit_ldz(1, 0)

    for dtk in range(16):
        dst, bdst = (bbsb, bbb) if dtk < 8 else (cbsb, bcb)
        off = (dtk % 8) * 2048
        ld = P.dma("gpsimd", dst[:, off:off + 2048], g_d[dtk * 128:(dtk + 1) * 128, :],
                   waits=bgd.w + bdst.write_waits(), tag="mb_g_ld")
        bgd.note_read(ld)
        bdst.note_write_acc(ld)
    PS4 = ("pa", "pb", "pc", "pd")
    for mt in range(DT):
        tiles = w_out.mt_tiles(mt)
        for kt in range(16):
            tile, bw = tiles[kt]
            hsb, bsrc = (bbsb, bbb) if kt < 8 else (cbsb, bcb)
            for ch in range(4):
                s = ch * 512
                ps = C.ps[PS4[ch]]
                bps = C.buf(PS4[ch])
                srcap = hsb[:, (kt % 8) * 2048 + s:(kt % 8) * 2048 + s + 512]
                mm = P.op("tensor", lambda e, tile=tile, srcap=srcap, ps=ps, kt=kt: e.matmul(
                    ps[0:128, 0:512], tile, srcap, start=(kt == 0), stop=(kt == 15)),
                    waits=bw.read_waits() + bsrc.read_waits() +
                          (bps.write_waits() if kt == 0 else []), tag="mb_out")
                bw.note_read(mm)
                bsrc.note_read(mm)
                bps.note_write_acc(mm)
        for ch in range(4):
            s = ch * 512
            ps = C.ps[PS4[ch]]
            bps = C.buf(PS4[ch])
            ev = P.op("vector", lambda e, mt=mt, s=s, ps=ps: e.tensor_add(
                xres[:, mt * T + s:mt * T + s + 512], xres[:, mt * T + s:mt * T + s + 512],
                ps[0:128, 0:512]),
                waits=bps.w + bxres.write_waits(), tag="mb_res")
            bps.note_read(ev)
            bxres.note_write_acc(ev)


def prep_ffn_rwkv(C, W, din):
    C.salloc("ffn_xr", (128, DT * 512), BF16)
    C.salloc("ffn_xk", (128, DT * 512), BF16)
    C.salloc("ffn_xk2", (128, DT * 512), BF16)
    C.salloc("ffn_d", (128, 512), BF16)
    C.salloc("ffn_r", (128, DT * 512), BF16)
    C.salloc("ffn_k2", (128, DT * 512), BF16)
    mu_cols = np.concatenate([W["L0_ffn_mu_r"].reshape(DT, 128).T,
                              W["L0_ffn_mu_k"].reshape(DT, 128).T], axis=1)
    dmu = din("mu_rk_in", np.ascontiguousarray(mu_cols, dtype=np.float32))
    emit_load_const(C, "mu_rk", dmu, mu_cols.shape, F32)
    _ensure_ws(C, 2048)
    wr = WeightLoader(C, "wr", din("W_r", W["L0_ffn_W_r"]), D, D, True)
    wk = WeightLoader(C, "wk", din("W_k", W["L0_ffn_W_k"]), D, D, True)
    wv = WvPrefetch(C, "wvs", din("W_v", W["L0_ffn_W_v"]))
    wr.prefetch_all()
    wk.prefetch_all()
    wv.prefetch_all()
    return lambda: emit_ffn_rwkv(C, wr, wk, wv)


def prep_ffn_small(C, W, din):
    v = np.zeros((64, 50), np.float32)
    v[:, 0:16] = W["L1_ffn_W1"]
    v[:, 16:32] = W["L1_ffn_b1"]
    v[:, 32:48] = W["L1_ffn_W2"]
    v[:, 48] = W["L1_ffn_bin_"]
    v[:, 49] = W["L1_ffn_b2"]
    dv = din("l1ffn_vec_in", v)
    emit_load_const(C, "l1ffn_vec", dv, v.shape, F32)
    bo = np.ascontiguousarray(W["L1_ffn_bout"].reshape(DT, 128).T, dtype=np.float32)
    dbo = din("l1ffn_bout_in", bo)
    emit_load_const(C, "l1ffn_bout", dbo, bo.shape, F32)
    C.salloc("l1ffn_s", (64, T), F32)
    C.salloc("l1ffn_h", (64, T), F32)
    C.salloc("l1ffn_h2", (64, T), F32)
    C.salloc("l1ffn_g", (64, T), F32)
    C.salloc("l1ffn_gb", (64, T), BF16)
    w_in = WeightLoader(C, "win", din("Win", W["L1_ffn_Win"]), D, 64, False)
    w_out = WeightLoader(C, "wout", din("Wout", W["L1_ffn_Wout"]), 64, D, False)
    w_in.prefetch_all()
    w_out.prefetch_all()
    return lambda tail_cb=None: emit_ffn_small(C, w_in, w_out, tail_cb)


SUBLAYER_PREP = {
    "l0_ffn": prep_ffn_rwkv,
    "l0_ffn2": prep_ffn_rwkv,
    "l1_ffn": prep_ffn_small,
    "l1_ffn2": prep_ffn_small,
    "l0_attn_a": prep_attn_l0a,
    "l0_attn_b": prep_attn_l0b,
    "l1_attn_b": prep_attn_l1b,
    "l1_mamba": prep_mamba,
}


def build_stage(stage, W):
    nc = bass.Bass("TRN2", target_bir_lowering=False, debug=False)
    st = contextlib.ExitStack()
    plan = Plan(nc)
    C = Ctx(nc, plan, st)
    feeds = {}

    def din(name, arr):
        arr = np.ascontiguousarray(arr, dtype=np.float32)
        ap = nc.dram_tensor(name, list(arr.shape), F32, kind="ExternalInput").ap()
        feeds[name] = arr
        return ap

    with st:
        sems = {e: st.enter_context(nc.semaphore("s_" + e)) for e in ENGINES}
        dma_sems = {q: st.enter_context(nc.semaphore("d_" + q)) for q in DMA_QUEUES}

        for name, arr in _CONSTS.items():
            dt = BF16 if arr.dtype == BF16NP else F32
            d = nc.dram_tensor("c_" + name, list(arr.shape), dt, kind="ExternalInput").ap()
            feeds["c_" + name] = arr
            emit_load_const(C, name, d, arr.shape, dt)

        C.sbuf("x_res", (128, DT * T), F32)
        final_stores = []

        if stage == "in_tx":
            C.psum("pbig", (128, 1024), F32)
            x_in = nc.dram_tensor("x_in", [T, D], F32, kind="ExternalInput").ap()
            feeds["x_in"] = "X_TOKEN"
            xfm_out = nc.dram_tensor("xfm_out", [128, DT * T], F32, kind="ExternalOutput").ap()
            emit_in_transform(C, x_in)
            bx = C.buf("x_res")
            sto = plan.dma("sync", xfm_out[:, :], C.sb["x_res"][:, :],
                           waits=bx.w, tag="xfm_st")
            final_stores.append(sto)
        elif stage == "out_tx":
            C.psum("pbig", (128, 1024), F32)
            xfm_in = nc.dram_tensor("xfm_in", [128, DT * T], F32, kind="ExternalInput").ap()
            feeds["xfm_in"] = "XFM"
            out = nc.dram_tensor("out", [T, D], F32, kind="ExternalOutput").ap()
            ld = plan.dma("sync", C.sb["x_res"][:, :], xfm_in[:, :], tag="xfm_ld")
            C.buf("x_res").note_write(ld)
            final_stores += emit_out_transform(C, out)
        else:
            xfm_in = nc.dram_tensor("xfm_in", [128, DT * T], F32, kind="ExternalInput").ap()
            feeds["xfm_in"] = "XFM"
            xfm_out = nc.dram_tensor("xfm_out", [128, DT * T], F32, kind="ExternalOutput").ap()
            ld = plan.dma("sync", C.sb["x_res"][:, :], xfm_in[:, :], tag="xfm_ld")
            C.buf("x_res").note_write(ld)

            C.sbuf("x_ln", (128, DT * T), BF16)
            if stage == "l1_mamba":
                C.sbuf("mb_arena", (128, 9728), F32)
                C.carve("mb_arena", "ln_t1", 0, 1024, F32)
                C.carve("mb_arena", "ln_sq", 1024, 512, F32)
                C.carve("mb_arena", "ln_rows", 1536, 1536, F32)
            else:
                C.sbuf("ln_sq", (128, 512), F32)
                C.sbuf("ln_rows", (1, 1536), F32)
                C.sbuf("ln_t1", (128, 1024), F32)
            C.psum("pbig", (128, 1024), F32)
            C.psum("pa", (128, 512), F32)
            C.psum("pb", (128, 512), F32)
            C.psum("pc", (128, 512), F32)
            C.psum("pd", (128, 512), F32)
            C.psum("pe", (128, 512), F32)

            run_sub = SUBLAYER_PREP[stage](C, W, din)
            emit_ln(C, "x_res", "x_ln")
            run_sub()

            bx = C.buf("x_res")
            sto = plan.dma("sync", xfm_out[:, :], C.sb["x_res"][:, :],
                           waits=bx.write_waits(), tag="xfm_st")
            final_stores.append(sto)

        fw = plan.final_wait_spec(final_stores, engine="sync")
        with nc.Block() as block:
            plan.finalize_and_emit(block, sems, dma_sems, fw)
    return nc, feeds


def run_stage(stage, W, percore):
    nc, feeds = build_stage(stage, W)
    in_maps = []
    for core in range(NCORES):
        m = {}
        for name, arr in feeds.items():
            m[name] = percore[arr][core] if isinstance(arr, str) else arr
        in_maps.append(m)
    return run_bass_kernel_spmd(nc, in_maps, core_ids=list(range(NCORES)))


def build_full(W, seq=None):
    if seq is None:
        seq = SUBLAYER_SEQ
    nc = bass.Bass("TRN2", target_bir_lowering=False, debug=False)
    st = contextlib.ExitStack()
    plan = Plan(nc)
    C = Ctx(nc, plan, st)
    C.fused = True
    feeds = {}

    def din_raw(name, arr):
        arr = np.ascontiguousarray(arr, dtype=np.float32)
        ap = nc.dram_tensor(name, list(arr.shape), F32, kind="ExternalInput").ap()
        feeds[name] = arr
        return ap

    with st:
        sems = {e: st.enter_context(nc.semaphore("s_" + e)) for e in ENGINES}
        dma_sems = {q: st.enter_context(nc.semaphore("d_" + q)) for q in DMA_QUEUES}

        x_in = nc.dram_tensor("x_in", [T, D], F32, kind="ExternalInput").ap()
        feeds["x_in"] = "X_TOKEN"
        out = nc.dram_tensor("out", [T, D], F32, kind="ExternalOutput").ap()

        for name, arr in _CONSTS.items():
            dt = BF16 if arr.dtype == BF16NP else F32
            d = nc.dram_tensor("c_" + name, list(arr.shape), dt, kind="ExternalInput").ap()
            feeds["c_" + name] = arr
            emit_load_const(C, name, d, arr.shape, dt, persist=True)

        C.sbuf("x_res", (128, DT * T), F32)
        C.sbuf("x_ln", (128, DT * T), BF16)
        _ensure_ws(C, 2048)
        C.psum("pa", (128, 512), F32)
        C.psum("pb", (128, 512), F32)
        C.psum("pc", (128, 512), F32)
        C.psum("pd", (128, 512), F32)
        C.psum("pe", (128, 512), F32)
        C.psum("pbig", (128, 1024), F32)
        C.psum("wt_ps", (128, 1024), BF16)

        rem = nc.sbuf_bytes_remaining
        cols = rem // 4 - 64
        C.sbuf("arena", (128, cols), F32)
        C.arena_cols = cols
        C.carve("arena", "ln_t1", 0, 1024, F32)

        C.begin_sublayer("in_tx")
        emit_in_transform(C, x_in)

        stores = None
        for stage in seq:
            C.begin_sublayer(stage)
            pref = stage + "_"
            din = lambda n, a, pref=pref: din_raw(pref + n, a)
            run_sub = SUBLAYER_PREP[stage](C, W, din)
            emit_ln(C, "x_res", "x_ln")
            if stage == seq[-1] and stage == "l1_ffn2":
                emit_i, stores = make_out_transform(C, out)
                run_sub(emit_i)
            else:
                run_sub()

        if stores is None:
            C.begin_sublayer("out_tx")
            stores = emit_out_transform(C, out)

        fw = plan.final_wait_spec(stores, engine="sync")
        with nc.Block() as block:
            plan.finalize_and_emit(block, sems, dma_sems, fw)
    return nc, feeds


def kernel(**inputs):
    x = np.ascontiguousarray(np.asarray(inputs["L0_attn_a_x"], dtype=np.float32))
    W = {k: np.ascontiguousarray(np.asarray(v, dtype=np.float32)) for k, v in inputs.items()}
    _ = inputs.get("L0_attn_b_c")

    import os
    if os.environ.get("KERNEL_UNFUSED"):
        res = run_stage("in_tx", W, {"X_TOKEN": [x[b] for b in range(NCORES)]})
        xfm = [res.results[b]["xfm_out"] for b in range(NCORES)]
        for stage in SUBLAYER_SEQ:
            res = run_stage(stage, W, {"XFM": xfm})
            xfm = [res.results[b]["xfm_out"] for b in range(NCORES)]
        res = run_stage("out_tx", W, {"XFM": xfm})
        return np.stack([res.results[b]["out"] for b in range(NCORES)])

    nc, feeds = build_full(W)
    in_maps = []
    for core in range(NCORES):
        m = {}
        for name, arr in feeds.items():
            m[name] = x[core] if isinstance(arr, str) else arr
        in_maps.append(m)
    res = run_bass_kernel_spmd(nc, in_maps, core_ids=list(range(NCORES)))
    return np.stack([res.results[b]["out"] for b in range(NCORES)])
